# version 08
# baseline (speedup 1.0000x reference)
import contextlib
import numpy as np
import ml_dtypes
import concourse.bass as bass
import concourse.mybir as mybir
from concourse.bass_core_rust import IndirectOffsetOnAxis

F32 = mybir.dt.float32
BF16 = mybir.dt.bfloat16
U32 = mybir.dt.uint32
AF = mybir.ActivationFunctionType
ALU = mybir.AluOpType
AX = mybir.AxisListType

T = 4096
D = 1024
ENGS = ("sync", "gpsimd", "tensor", "vector", "scalar")
QT = 512
NQT = T // QT
NCH = D // 128


def emit_early_wcaches(nc, p, io, gates=()):
    wch = {}
    m = None
    nd = [0]
    gates = list(gates)

    def gw():
        w = [gates[min(nd[0] // 2, len(gates) - 1)]] if gates else []
        nd[0] += 1
        return w
    if "Win2" in io:
        wch["win2"] = nc.dram_tensor("b1_win2bf", [4096, 512], BF16, kind="Internal")
        for i in range(4):
            m = p.dma("gpsimd", wch["win2"].ap()[i * 1024:(i + 1) * 1024, :],
                      io["Win2"][i * 1024:(i + 1) * 1024, :], sem="q_wch", waits=gw())
    for wn in ["q_w", "k_w"]:
        if wn in io:
            wch[wn] = nc.dram_tensor(f"b2_{wn}bf", [1024, 1024], BF16, kind="Internal")
            for i in range(2):
                m = p.dma("gpsimd", wch[wn].ap()[i * 512:(i + 1) * 512, :],
                          io[wn][i * 512:(i + 1) * 512, :], sem="q_wch", waits=gw())
    if "w1my" in io:
        wch["w1bf"] = nc.dram_tensor("b3_w1bf", [2 * 1024, 4096], BF16, kind="Internal")
        wch["w2bf"] = nc.dram_tensor("b3_w2bf", [2 * 4096, 1024], BF16, kind="Internal")
        for ex in range(2):
            for i in range(2):
                m = p.dma("gpsimd",
                          wch["w1bf"].ap()[ex * 1024 + i * 512:ex * 1024 + (i + 1) * 512, :],
                          io["w1my"][ex, i * 512:(i + 1) * 512, :], sem="q_wch", waits=gw())
        for ex in range(2):
            for i in range(2):
                m = p.dma("gpsimd",
                          wch["w2bf"].ap()[ex * 4096 + i * 2048:ex * 4096 + (i + 1) * 2048, :],
                          io["w2my"][ex, i * 2048:(i + 1) * 2048, :], sem="q_wch", waits=gw())
    io["_wch"] = wch
    io["_wch_m"] = m


class Prog:
    def __init__(self, nc):
        self.nc = nc
        self.ops = []
        self.ctr = {}
        self.sems = {}
        self.sem_names = []

    def sem(self, name):
        if name not in self.ctr:
            self.ctr[name] = 0
            self.sem_names.append(name)
        return name

    SELF_SEM = {"vector": "dve", "scalar": "act", "gpsimd": "gp"}

    def get(self, name):
        self.sem(name)
        return self.ctr[name]

    def emit(self, eng, fn, waits=(), inc=None, inc_n=1, is_dma=False):
        waits = list((s, v) for (s, v) in waits if v)
        selfsem = self.SELF_SEM.get(eng)
        if selfsem is not None and not is_dma:
            assert inc in (None, selfsem), (eng, inc)
            prev = self.get(selfsem)
            if prev:
                waits.append((selfsem, prev))
            inc = selfsem
        incs = []
        val = None
        if inc is not None:
            self.sem(inc)
            self.ctr[inc] += inc_n
            val = self.ctr[inc]
            incs.append((inc, inc_n))
        for (s, v) in waits:
            self.sem(s)
        self.ops.append((eng, fn, tuple(waits), tuple(incs)))
        return val

    def dma(self, queue, out, in_, waits=(), sem=None, **kw):
        semn = self.sem(sem if sem else ("q_" + queue))
        return self.emit(queue, lambda e: e.dma_start(out=out, in_=in_, **kw),
                         waits=waits, inc=semn, inc_n=16, is_dma=True)

    def barrier(self):
        waits = [(name, self.ctr[name]) for name in self.sem_names if self.ctr[name] > 0]
        for eng in ENGS:
            self.emit(eng, lambda e: None, waits=waits, is_dma=True)

    def build(self, stack):
        E = stack.enter_context
        for name in self.sem_names:
            self.sems[name] = E(self.nc.semaphore(name))
        block = E(self.nc.Block())
        prog = self

        def make(engname):
            def body(eng):
                for (e, fn, waits, incs) in prog.ops:
                    if e != engname:
                        continue
                    for (s, v) in waits:
                        eng.wait_ge(prog.sems[s], v)
                    inst = fn(eng)
                    for (s, n) in incs:
                        if inst is None:
                            inst = eng.nop()
                        inst = inst.then_inc(prog.sems[s], n)
            return body

        block.sync(make("sync"))
        block.gpsimd(make("gpsimd"))
        block.tensor(make("tensor"))
        block.vector(make("vector"))
        block.scalar(make("scalar"))


def host_constants():
    c = {}
    c["c_ident"] = np.eye(128, dtype=np.float32)
    P = np.zeros((128, 128), dtype=np.float32)
    for h0 in (0, 64):
        for d_ in range(32):
            P[h0 + d_ + 32, h0 + d_] = -1.0
            P[h0 + d_, h0 + d_ + 32] = 1.0
    c["c_prot"] = P
    hd, base = 64, 10000.0
    inv_freq = 1.0 / base ** (np.arange(0, hd, 2, dtype=np.float32) / hd)
    t = np.arange(T, dtype=np.float32)
    freqs = np.einsum("i,j->ij", t, inv_freq)
    emb = np.concatenate([freqs, freqs], axis=-1)
    cos = np.cos(emb).astype(np.float32).T
    sin = np.sin(emb).astype(np.float32).T
    c["c_cos"] = np.concatenate([cos, cos], axis=0)
    c["c_sin"] = np.concatenate([sin, sin], axis=0)
    dk = np.arange(128)[:, None]
    dq = np.arange(128)[None, :]
    c["c_tri"] = (dk <= dq).astype(np.float32)
    c["c_ones1"] = np.ones((1, 128), dtype=np.float32)
    c["c_onescol"] = np.ones((128, 1), dtype=np.float32)
    c["c_negrow"] = np.full((1, 512), -1.0, dtype=np.float32)
    return c


CONST_SPECS = {
    "c_ident": (128, 128), "c_prot": (128, 128), "c_cos": (128, T), "c_sin": (128, T),
    "c_tri": (128, 128), "c_ones1": (1, 128), "c_onescol": (128, 1), "c_negrow": (1, 512),
}


class B0:

    def __init__(self, nc, p, stack, io):
        self.nc, self.p, self.stack, self.io = nc, p, stack, io
        self.E = stack.enter_context

    def alloc(self):
        nc, E = self.nc, self.E
        s = self
        s.xbf = [E(nc.sbuf_tensor(f"b0_xbf{c}", [128, T], BF16)) for c in range(NCH)]
        s.attnT = [E(nc.sbuf_tensor(f"b0_attnT{c}", [128, T], BF16)) for c in range(NCH)]
        s.qraw = E(nc.sbuf_tensor("b0_qraw", [128, T], BF16))
        s.kraw = E(nc.sbuf_tensor("b0_kraw", [128, T], BF16))
        s.qr = E(nc.sbuf_tensor("b0_qr", [128, T], BF16))
        s.kr = E(nc.sbuf_tensor("b0_kr", [128, T], BF16))
        s.vhat = E(nc.sbuf_tensor("b0_vhat", [128, 32 * 130], BF16))
        s.wsl = [E(nc.sbuf_tensor("b0_wsl0", [128, 3 * 1024], BF16))]
        s.ptile = [E(nc.sbuf_tensor(f"b0_ptile{i}", [128, 1024], BF16)) for i in range(3)]
        s.pscr = E(nc.sbuf_tensor("b0_pscr", [128, 256], BF16))
        s.ropes = E(nc.sbuf_tensor("b0_ropes", [128, 2 * QT], F32))
        s.numbuf = E(nc.sbuf_tensor("b0_numbuf", [128, QT], F32))
        s.lrec = E(nc.sbuf_tensor("b0_lrec", [1, 2 * QT], BF16))
        s.lcat = s.ropes[0:1, 0:2 * QT]
        s.lln = s.ropes[32:33, 0:2 * QT]
        s.resbuf = s.vhat[:, 0:NCH * QT]

        _krf = s.kr[:].bitcast(F32)
        _qrf = s.qr[:].bitcast(F32)
        s.rowbuf = _krf[0:1, 0:2048]
        s.rowbf = None
        s.x1buf = [_qrf[:, 1024 + i * QT:1024 + (i + 1) * QT] for i in range(2)]
        s.x1tmp = None
        s.biases = E(nc.sbuf_tensor("b0_biases", [128, 4 * NCH], F32))
        s.bvrow = E(nc.sbuf_tensor("b0_bvrow", [1, 1024], BF16))
        s.woT = [(s.qraw if c < 4 else s.kraw)[:, (c % 4) * 1024:(c % 4 + 1) * 1024] for c in range(NCH)]
        s.wtmp = [s.qr[:, 0:1024], s.qr[:, 1024:2048]]
        s.stripes = [s.attnT[i][:, 0:1024] for i in range(NCH)]
        s.cst = {}
        for name in ["c_ident", "c_prot", "c_cos", "c_sin", "c_tri", "c_ones1", "c_onescol"]:
            s.cst[name] = E(nc.sbuf_tensor("b0_" + name, list(CONST_SPECS[name]), BF16))
        s.rowbf = s.lrec[0:1, 0:2 * QT]
        s.x1tmp = s.numbuf[:, 0:QT]
        s.rowslot4 = _qrf[0:1, 0:QT]
        s.rowslotE = _qrf[0:1, QT:2 * QT]
        s.sqscr = [s.ptile[0][:, 0:QT], s.ptile[0][:, QT:2 * QT]]
        s.ps_s = [E(nc.psum_tensor(f"b0_ps_s{i}", [128, 1024], F32)) for i in range(2)]
        s.ps_va = E(nc.psum_tensor("b0_ps_va", [128, 512], F32))
        s.ps_vb = E(nc.psum_tensor("b0_ps_vb", [128, 512], F32))
        s.ps_mm = E(nc.psum_tensor("b0_ps_mm", [128, 512], F32))
        s.ps_bc = E(nc.psum_tensor("b0_ps_bc", [128, 512], F32))
        s.ps_tr = [s.ps_s[0].ap().bitcast(BF16), s.ps_s[1].ap().bitcast(BF16)]
        s.ps_bc2 = s.ps_vb

    def load_consts(self):
        p, s, io = self.p, self, self.io
        s.mc = {}
        for name in ["c_ident", "c_prot", "c_cos", "c_sin", "c_tri", "c_ones1", "c_onescol"]:
            p.dma("gpsimd", s.cst[name][:], io[name], sem="q_ig")
        for c in range(NCH):
            p.dma("gpsimd", s.xbf[c][:], io["xT"][c * 128:(c + 1) * 128, :], sem="q_ig")
        for j, bn in enumerate(["bq", "bk", "bv", "bo"]):
            for c in range(NCH):
                p.dma("sync", s.biases[:, j * NCH + c:j * NCH + c + 1],
                      io[bn][c * 128:(c + 1) * 128].rearrange("(p o) -> p o", o=1), sem="q_is")
        s.m_bias = p.get("q_is")
        p.dma("gpsimd", s.bvrow[0:1, :], io["bv"].rearrange("(o p) -> o p", o=1), sem="q_ig")
        m_ig = p.get("q_ig")
        for name in CONST_SPECS:
            s.mc[name] = m_ig
        s.m_bv = m_ig
        s.m_xall = m_ig
        for c in range(NCH):
            s.mc[f"x{c}"] = m_ig

    def bias_ap(self, j, c):
        return self.biases[:, j * NCH + c:j * NCH + c + 1]

    def prep_weights(self):
        p, s, nc, io = self.p, self, self.nc, self.io
        s.wT = {}
        s.prep_done = {}
        ptr_free = [None] * 4
        wtmp_free = [None, None]
        prev_store = None
        for wn in ["wq", "wk", "wv", "wo"]:
            s.wT[wn] = nc.dram_tensor(f"b0_{wn}T", [1024, 1024], BF16, kind="Internal")
            last_evac = None
            for er in range(8):
                buf = s.wtmp[er % 2]
                qp = "q_pa" if er % 2 == 0 else "q_pb"
                mld = p.dma("gpsimd", buf[:], io[wn][er * 128:(er + 1) * 128, :],
                            waits=[("dve", wtmp_free[er % 2])] if wtmp_free[er % 2] else [], sem=qp)
                for dc in range(8):
                    g = dc % 4
                    tr = s.ps_tr[g // 2]
                    go = (g % 2) * 1024
                    waits = [(qp, mld), ("q_ig", s.mc["c_ident"])]
                    if ptr_free[g]:
                        waits.append(("dve", ptr_free[g]))
                    mt = p.emit("tensor",
                                lambda e, buf=buf, dc=dc, tr=tr, go=go: e.transpose(
                                    tr[:, go:go + 128],
                                    buf[:, dc * 128:(dc + 1) * 128], s.cst["c_ident"][:]),
                                waits=waits, inc="pe")
                    vwaits = [("pe", mt)]
                    if er == 0 and prev_store:
                        vwaits.append(("q_ps", prev_store))
                    mv = p.emit("vector",
                                lambda e, dc=dc, er=er, tr=tr, go=go: e.tensor_copy(
                                    s.stripes[dc][:, er * 128:(er + 1) * 128],
                                    tr[:, go:go + 128]),
                                waits=vwaits, inc="dve")
                    ptr_free[g] = mv
                    last_evac = mv
                wtmp_free[er % 2] = last_evac
            mstore = None
            for dc in range(8):
                mstore = p.dma("sync", s.wT[wn].ap()[dc * 128:(dc + 1) * 128, :], s.stripes[dc],
                               waits=[("dve", last_evac)], sem="q_ps")
            s.prep_done[wn] = mstore
            prev_store = mstore
        return

    def run(self):
        p, s = self.p, self
        s.alloc()
        s.load_consts()
        s.prep_weights()
        for hp in range(8):
            self.pair(hp)
        self.oproj_ln()
        p.emit("sync", lambda e: None,
               waits=[("q_sta", p.get("q_sta")), ("q_stb", p.get("q_stb"))])

    def pair(self, hp):
        p, s = self.p, self
        wb = s.wsl[0]
        wsl_ap = lambda j, c: wb[:].rearrange("p (w c e) -> p w c e", w=3, e=128)[:, j, c]
        guard_pe = p.get("pe")
        guard_dve0 = p.get("dve")
        mw = []
        for j, wn in enumerate(["wq", "wk", "wv"]):
            src = s.wT[wn].ap().rearrange("(c p) e -> p c e", p=128)[:, :, hp * 128:(hp + 1) * 128]
            dst = wb[:].rearrange("p (w c e) -> p w c e", w=3, e=128)[:, j]
            mw.append(p.dma("sync", dst, src, sem="q_wsl",
                            waits=[("q_ps", s.prep_done[wn]), ("pe", guard_pe),
                                   ("dve", guard_dve0)]))
        if not hasattr(s, "wsl_miles"):
            s.wsl_miles = []
        s.wsl_miles.append(mw[2])
        ps_mm_free = [None, None]

        def qkv(dst, j, bj):
            qp2 = lambda qt: (s.ps_mm if qt % 2 == 0 else s.ps_bc)
            for qt in range(NQT):
                for c in range(NCH):
                    w = []
                    if c == 0:
                        w = [("q_wsl", mw[2]), ("q_ig", s.m_xall), ("q_is", s.m_bias)]
                        if ps_mm_free[qt % 2]:
                            w.append(("dve", ps_mm_free[qt % 2]))
                    mpe = p.emit("tensor",
                                 lambda e, c=c, qt=qt: e.matmul(
                                     qp2(qt)[:], wsl_ap(j, c), s.xbf[c][:, qt * QT:(qt + 1) * QT],
                                     start=(c == 0), stop=(c == NCH - 1)),
                                 waits=w, inc="pe" if c == NCH - 1 else None)
                mv = p.emit("vector",
                            lambda e, qt=qt: e.tensor_scalar(
                                dst[:, qt * QT:(qt + 1) * QT], qp2(qt)[:], s.bias_ap(bj, hp),
                                None, ALU.add),
                            waits=[("pe", mpe)], inc="dve")
                ps_mm_free[qt % 2] = mv

        qkv(s.qraw, 0, 0)
        qkv(s.kraw, 1, 1)

        for (srcb, dstb) in ((s.qraw, s.qr), (s.kraw, s.kr)):
            for qt in range(NQT):
                w = [("dve", p.get("dve")), ("q_ig", s.mc["c_prot"])]
                if ps_mm_free[0]:
                    w.append(("dve", ps_mm_free[0]))
                if ps_mm_free[1]:
                    w.append(("dve", ps_mm_free[1]))
                    ps_mm_free[1] = None
                mt = p.emit("tensor",
                            lambda e, srcb=srcb, qt=qt: e.matmul(
                                s.ps_mm[:], s.cst["c_prot"][:], srcb[:, qt * QT:(qt + 1) * QT],
                                start=True, stop=True),
                            waits=w, inc="pe")
                p.emit("vector",
                       lambda e, srcb=srcb, qt=qt: e.tensor_tensor(
                           s.ropes[:, 0:QT], srcb[:, qt * QT:(qt + 1) * QT],
                           s.cst["c_cos"][:, qt * QT:(qt + 1) * QT], ALU.mult),
                       waits=[("q_ig", s.mc["c_cos"])], inc="dve")
                mv1 = p.emit("vector",
                             lambda e, qt=qt: e.tensor_tensor(
                                 s.ropes[:, QT:2 * QT], s.ps_mm[:],
                                 s.cst["c_sin"][:, qt * QT:(qt + 1) * QT], ALU.mult),
                             waits=[("pe", mt), ("q_ig", s.mc["c_sin"])], inc="dve")
                ps_mm_free[0] = mv1
                p.emit("vector",
                       lambda e, dstb=dstb, qt=qt: e.tensor_add(
                           dstb[:, qt * QT:(qt + 1) * QT], s.ropes[:, 0:QT], s.ropes[:, QT:2 * QT]),
                       inc="dve")

        vh3 = s.vhat[:].rearrange("p (tt c) -> p tt c", c=130)
        guard_pe = p.get("pe")
        p.emit("vector", lambda e: e.memset(vh3[:, :, 64:65], 1.0), waits=[("pe", guard_pe)], inc="dve")
        p.emit("vector", lambda e: e.memset(vh3[:, :, 129:130], 1.0), inc="dve")
        m_ones = p.get("dve")
        for tt in range(32):
            for c in range(NCH):
                w = []
                if c == 0:
                    w = [("q_wsl", mw[2])]
                    if ps_mm_free[0]:
                        w.append(("dve", ps_mm_free[0]))
                p.emit("tensor",
                       lambda e, c=c, tt=tt: e.matmul(
                           s.ps_mm[:, 0:128], s.xbf[c][:, tt * 128:(tt + 1) * 128], wsl_ap(2, c),
                           start=(c == 0), stop=False),
                       waits=w)
            mpe = p.emit("tensor",
                         lambda e: e.matmul(
                             s.ps_mm[:, 0:128], s.cst["c_ones1"][0:1, :],
                             s.bvrow[0:1, hp * 128:(hp + 1) * 128], start=False, stop=True),
                         waits=[("q_ig", s.mc["c_ones1"]), ("q_ig", s.m_bv)], inc="pe")
            p.emit("vector",
                   lambda e, tt=tt: e.tensor_copy(vh3[:, tt, 0:64], s.ps_mm[:, 0:64]),
                   waits=[("pe", mpe)], inc="dve")
            mv = p.emit("vector",
                        lambda e, tt=tt: e.tensor_copy(vh3[:, tt, 65:129], s.ps_mm[:, 64:128]),
                        inc="dve")
            ps_mm_free[0] = mv

        ps_s_free = [None, None]
        pscr_free = [None]
        pend_bc = [None]
        mbc_prev = [None]
        for qt in range(NQT):
            K = 4 * (qt + 1)
            ptile_free = [None] * 3
            guard_dve = p.get("dve")
            guard_act = p.get("act")
            mva = None
            kb_order = list(range(4 * qt, K)) + list(range(0, 4 * qt))
            stage = {}

            def emit_st(idx):
                kb = kb_order[idx]
                diag = kb >= 4 * qt
                c0 = 128 * (kb - 4 * qt) if diag else 0
                par = idx % 2
                pt = s.ptile[idx % 3]
                w = [("dve", guard_dve)] if idx == 0 else []
                if ps_s_free[par]:
                    w.append(("act", ps_s_free[par]))
                p.emit("tensor",
                       lambda e, kb=kb, qt=qt, par=par, c0=c0: e.matmul(
                           s.ps_s[par][:, c0:QT],
                           s.kr[0:64, kb * 128:(kb + 1) * 128],
                           s.qr[0:64, qt * QT + c0:(qt + 1) * QT], start=True, stop=True),
                       waits=w)
                mpe = p.emit("tensor",
                             lambda e, kb=kb, qt=qt, par=par, c0=c0: e.matmul(
                                 s.ps_s[par][:, QT + c0:2 * QT],
                                 s.kr[64:128, kb * 128:(kb + 1) * 128],
                                 s.qr[64:128, qt * QT + c0:(qt + 1) * QT], start=True, stop=True),
                             inc="pe")
                ewaits = [("pe", mpe)]
                if ptile_free[idx % 3]:
                    ewaits.append(("pe", ptile_free[idx % 3]))
                last = (idx == K - 1)
                if not diag:
                    mact = p.emit("scalar",
                                  lambda e, pt=pt, par=par: e.activation(
                                      pt[:], s.ps_s[par][:], AF.Exp, scale=0.125),
                                  waits=ewaits, inc="act")
                    ps_s_free[par] = mact
                    stage[idx] = (("act", mact), 0)
                else:
                    if c0 + 128 < QT:
                        p.emit("scalar",
                               lambda e, pt=pt, par=par, c0=c0: e.activation(
                                   pt[:].rearrange("p (h q) -> p h q", h=2)[:, :, c0 + 128:QT],
                                   s.ps_s[par][:].rearrange("p (h q) -> p h q", h=2)[:, :, c0 + 128:QT],
                                   AF.Exp, scale=0.125),
                               waits=ewaits, inc="act")
                        ewaits = []
                    if pscr_free[0]:
                        ewaits = list(ewaits) + [("dve", pscr_free[0])]
                    mact = p.emit("scalar",
                                  lambda e, par=par, c0=c0: e.activation(
                                      s.pscr[:].rearrange("p (h q) -> p h q", h=2),
                                      s.ps_s[par][:].rearrange("p (h q) -> p h q", h=2)[:, :, c0:c0 + 128],
                                      AF.Exp, scale=0.125),
                                  waits=ewaits, inc="act")
                    ps_s_free[par] = mact
                    if last and c0 > 0:
                        p.emit("vector", lambda e, pt=pt, c0=c0: e.memset(pt[:, 0:c0], 0.0),
                               waits=[("act", mact)], inc="dve")
                        p.emit("vector", lambda e, pt=pt, c0=c0: e.memset(pt[:, QT:QT + c0], 0.0),
                               inc="dve")
                    p.emit("vector",
                           lambda e, pt=pt, c0=c0: e.tensor_tensor(
                               pt[:, c0:c0 + 128], s.pscr[:, 0:128], s.cst["c_tri"][:], ALU.mult),
                           waits=[("act", mact), ("q_ig", s.mc["c_tri"])], inc="dve")
                    mmask = p.emit("vector",
                                   lambda e, pt=pt, c0=c0: e.tensor_tensor(
                                       pt[:, QT + c0:QT + c0 + 128], s.pscr[:, 128:256], s.cst["c_tri"][:],
                                       ALU.mult),
                                   inc="dve")
                    pscr_free[0] = mmask
                    stage[idx] = (("dve", mmask), c0 if not last else 0)

            def emit_av(idx):
                nonlocal mva
                kb = kb_order[idx]
                pt = s.ptile[idx % 3]
                mdep, nlo = stage.pop(idx)
                first = (idx == 0)
                last = (idx == K - 1)
                p.emit("tensor",
                       lambda e, kb=kb, pt=pt, nlo=nlo, first=first, last=last: e.matmul(
                           s.ps_va.ap()[0:65, nlo:QT], vh3[:, kb, 0:65], pt[:, nlo:QT],
                           start=first, stop=last),
                       waits=[mdep])
                mva = p.emit("tensor",
                             lambda e, kb=kb, pt=pt, nlo=nlo, first=first, last=last: e.matmul(
                                 s.ps_vb.ap()[0:65, nlo:QT], vh3[:, kb, 65:130], pt[:, QT + nlo:2 * QT],
                                 start=first, stop=last),
                             inc="pe")
                ptile_free[idx % 3] = mva

            emit_st(0)
            for idx in range(K):
                if idx + 1 < K:
                    emit_st(idx + 1)
                emit_av(idx)
                if idx == 0 and pend_bc[0] is not None:
                    pend_bc[0]()
                    pend_bc[0] = None
            p.emit("vector", lambda e: e.tensor_copy(s.numbuf[0:64, :], s.ps_va[0:64, :]),
                   waits=[("pe", mva)], inc="dve")
            mv2 = p.emit("vector", lambda e: e.tensor_copy(s.numbuf[64:128, :], s.ps_vb[0:64, :]), inc="dve")
            p.emit("vector", lambda e: e.tensor_copy(s.lcat[0:1, 0:QT], s.ps_va[64:65, :]),
                   waits=[("act", guard_act)], inc="dve")
            ml2 = p.emit("vector", lambda e: e.tensor_copy(s.lcat[0:1, QT:2 * QT], s.ps_vb[64:65, :]),
                         inc="dve")
            mln = p.emit("scalar", lambda e: e.activation(s.lln[0:1, :], s.lcat[0:1, :], AF.Ln),
                         waits=[("dve", ml2)], inc="act")
            mrw = [("act", mln)]
            if mbc_prev[0]:
                mrw.append(("pe", mbc_prev[0]))
            mrec = p.emit("scalar", lambda e: e.activation(s.lrec[0:1, :], s.lln[0:1, :], AF.Exp, scale=-1.0),
                          waits=mrw, inc="act")
            guard_dve2 = p.get("dve")

            def bc_part(qt=qt, mrec=mrec, mv2=mv2, guard_dve2=guard_dve2):
                p.emit("tensor",
                       lambda e: e.matmul(s.ps_bc[0:64, :], s.cst["c_ones1"][0:1, 0:64],
                                          s.lrec[0:1, 0:QT], start=True, stop=True),
                       waits=[("act", mrec), ("q_ig", s.mc["c_ones1"]), ("dve", guard_dve2)])
                mbc = p.emit("tensor",
                             lambda e: e.matmul(s.ps_bc[64:128, :], s.cst["c_ones1"][0:1, 0:64],
                                                s.lrec[0:1, QT:2 * QT], start=True, stop=True,
                                                tile_position=(0, 64)),
                             inc="pe")
                mbc_prev[0] = mbc
                p.emit("vector",
                       lambda e, hp=hp, qt=qt: e.tensor_tensor(
                           s.attnT[hp][:, qt * QT:(qt + 1) * QT], s.numbuf[:], s.ps_bc[:], ALU.mult),
                       waits=[("pe", mbc), ("dve", mv2), ("q_ps", s.prep_done["wo"])], inc="dve")

            pend_bc[0] = bc_part
        pend_bc[0]()
        pend_bc[0] = None

    def oproj_ln(self):
        p, s = self.p, self
        io = self.io
        mwo = None
        guard_pe = p.get("pe")
        guard_dve0 = p.get("dve")
        for c in range(NCH):
            mwo = p.dma("sync", s.woT[c], s.wT["wo"].ap()[c * 128:(c + 1) * 128, :], sem="q_wo",
                        waits=[("q_ps", s.prep_done["wo"]), ("pe", guard_pe), ("dve", guard_dve0)])
        eps = 1e-5
        ps_mm_free = [None, None]
        mdma_prev = {}
        for qt in range(NQT):
            guard_dve = p.get("dve")
            for oc in range(NCH):
                pso = s.ps_mm[:] if oc % 2 == 0 else s.ps_s[0].ap()[:, 0:QT]
                for ec in range(NCH):
                    w = []
                    if ec == 0:
                        w = [("q_wo", mwo), ("dve", guard_dve)]
                        if ps_mm_free[oc % 2]:
                            w.append(("dve", ps_mm_free[oc % 2]))
                    mpe = p.emit("tensor",
                                 lambda e, oc=oc, ec=ec, qt=qt, pso=pso: e.matmul(
                                     pso, s.woT[ec][:, oc * 128:(oc + 1) * 128],
                                     s.attnT[ec][:, qt * QT:(qt + 1) * QT],
                                     start=(ec == 0), stop=(ec == NCH - 1)),
                                 waits=w, inc="pe" if ec == NCH - 1 else None)
                mres = p.emit("vector",
                              lambda e, oc=oc, qt=qt, pso=pso: e.scalar_tensor_tensor(
                                  s.resbuf[:, oc * QT:(oc + 1) * QT], pso, s.bias_ap(3, oc),
                                  s.xbf[oc][:, qt * QT:(qt + 1) * QT], ALU.add, ALU.add),
                              waits=[("pe", mpe)], inc="dve")
                ps_mm_free[oc % 2] = mres
            guard_pe2 = p.get("pe")
            guard_dve3 = p.get("dve")
            for oc in range(NCH):
                w = [("dve", guard_dve3), ("pe", guard_pe2)] if oc == 0 else []
                p.emit("tensor",
                       lambda e, oc=oc: e.matmul(
                           s.ps_va.ap()[0:1, 0:QT], s.cst["c_onescol"][:, 0:1],
                           s.resbuf[:, oc * QT:(oc + 1) * QT],
                           start=(oc == 0), stop=(oc == NCH - 1)),
                       waits=w, inc="pe" if oc == NCH - 1 else None)
            sq_free = [None, None]
            for oc in range(NCH):
                sp = s.sqscr[oc % 2]
                w = [("dve", guard_dve3)] if oc == 0 else []
                if sq_free[oc % 2]:
                    w = list(w) + [("pe", sq_free[oc % 2])]
                ma = p.emit("scalar",
                            lambda e, oc=oc, sp=sp: e.activation(
                                sp, s.resbuf[:, oc * QT:(oc + 1) * QT], AF.Square),
                            waits=w, inc="act")
                mpe_sq = p.emit("tensor",
                                lambda e, oc=oc, sp=sp: e.matmul(
                                    s.ps_vb.ap()[0:1, 0:QT], s.cst["c_onescol"][:, 0:1], sp,
                                    start=(oc == 0), stop=(oc == NCH - 1)),
                                waits=[("act", ma)], inc="pe")
                sq_free[oc % 2] = mpe_sq
            mpe_stats = p.get("pe")
            p.emit("vector", lambda e: e.tensor_scalar_mul(s.rowbuf[0:1, 0:QT],
                                                           s.ps_va.ap()[0:1, 0:QT], 1.0 / 1024),
                   waits=[("pe", mpe_stats)], inc="dve")
            p.emit("vector", lambda e: e.tensor_scalar_mul(s.rowbuf[0:1, QT:2 * QT],
                                                           s.ps_vb.ap()[0:1, 0:QT], 1.0 / 1024),
                   inc="dve")
            p.emit("vector", lambda e: e.tensor_tensor(s.rowbuf[0:1, 2 * QT:3 * QT],
                                                       s.rowbuf[0:1, 0:QT], s.rowbuf[0:1, 0:QT], ALU.mult),
                   inc="dve")
            p.emit("vector", lambda e: e.tensor_scalar_add(s.rowslotE,
                                                           s.rowbuf[0:1, 2 * QT:3 * QT], -eps), inc="dve")
            mvar = p.emit("vector", lambda e: e.tensor_sub(s.rowbuf[0:1, 3 * QT:4 * QT],
                                                           s.rowbuf[0:1, QT:2 * QT],
                                                           s.rowslotE), inc="dve")
            mln2 = p.emit("scalar", lambda e: e.activation(s.rowslot4,
                                                           s.rowbuf[0:1, 3 * QT:4 * QT], AF.Ln),
                          waits=[("dve", mvar)], inc="act")
            mrstd = p.emit("scalar", lambda e: e.activation(s.rowbf[0:1, 0:QT],
                                                            s.rowslot4, AF.Exp, scale=-0.5),
                           waits=[("act", mln2)], inc="act")
            mb = p.emit("vector", lambda e: e.tensor_tensor(s.rowbf[0:1, QT:2 * QT], s.rowbf[0:1, 0:QT],
                                                            s.rowbuf[0:1, 0:QT], ALU.mult),
                        waits=[("act", mrstd)], inc="dve")
            p.emit("tensor", lambda e: e.matmul(s.ps_bc[:, 0:QT], s.cst["c_ones1"][0:1, :],
                                                s.rowbf[0:1, 0:QT], start=True, stop=True),
                   waits=[("dve", mb)])
            mab = p.emit("tensor", lambda e: e.matmul(s.ps_bc2.ap()[:, 0:QT], s.cst["c_ones1"][0:1, :],
                                                      s.rowbf[0:1, QT:2 * QT], start=True, stop=True),
                         inc="pe")
            for oc in range(NCH):
                xb = s.x1buf[oc % 2]
                w = [("pe", mab)]
                if mdma_prev.get(oc % 2):
                    w.append(("q_sta" if oc % 2 == 0 else "q_stb", mdma_prev[oc % 2]))
                p.emit("vector",
                       lambda e, oc=oc: e.tensor_tensor(
                           s.x1tmp, s.resbuf[:, oc * QT:(oc + 1) * QT], s.ps_bc[:, 0:QT], ALU.mult),
                       waits=w, inc="dve")
                mx1 = p.emit("vector",
                             lambda e, xb=xb: e.tensor_sub(xb, s.x1tmp, s.ps_bc2.ap()[:, 0:QT]),
                             inc="dve")
                mdma_prev[oc % 2] = p.dma("sync",
                                          io["x1T"][oc * 128:(oc + 1) * 128, qt * QT:(qt + 1) * QT],
                                          xb, waits=[("dve", mx1)],
                                          sem="q_sta" if oc % 2 == 0 else "q_stb")


class B1:

    def __init__(self, nc, p, stack, io):
        self.nc, self.p, self.stack, self.io = nc, p, stack, io
        self.E = stack.enter_context

    def alloc(self):
        nc, E = self.nc, self.E
        s = self
        s.xbf = [E(nc.sbuf_tensor(f"b1_xbf{c}", [128, T], BF16)) for c in range(NCH)]
        s.hT = E(nc.sbuf_tensor("b1_hT", [128, 32 * QT], BF16))
        s.win1 = E(nc.sbuf_tensor("b1_win1", [128, 8 * QT], BF16))
        s.wout2T = E(nc.sbuf_tensor("b1_wout2T", [128, 4 * 1024], BF16))
        s.w1buf = [E(nc.sbuf_tensor(f"b1_w1buf{i}", [128, 2048], BF16)) for i in range(2)]
        s.w2buf = [E(nc.sbuf_tensor(f"b1_w2buf{i}", [128, 2048], BF16)) for i in range(2)]
        s.tT = E(nc.sbuf_tensor("b1_tT", [16, T], BF16))
        s.aT = E(nc.sbuf_tensor("b1_aT", [16, T], BF16))
        s.gbuf = [E(nc.sbuf_tensor(f"b1_g{i}", [128, 4 * QT], BF16)) for i in range(2)]
        s.zg = [E(nc.sbuf_tensor(f"b1_zg{i}", [128, 4 * QT], BF16)) for i in range(2)]
        s.t_tb = E(nc.sbuf_tensor("b1_ttb", [128, QT], F32))
        s.e_tmp = E(nc.sbuf_tensor("b1_etmp", [128, QT], F32))
        s.e_c = E(nc.sbuf_tensor("b1_ec", [128, QT], F32))
        s.e_z = E(nc.sbuf_tensor("b1_ez", [128, QT], F32))
        s.a_tb = E(nc.sbuf_tensor("b1_atb", [128, QT], BF16))
        s.esm = E(nc.sbuf_tensor("b1_esm", [128, 16 * 32], F32))
        s.e_zz = E(nc.sbuf_tensor("b1_ezz", [128, QT], F32))
        s.resbuf = E(nc.sbuf_tensor("b1_res", [128, NCH * QT], BF16))
        s.sqscr = [E(nc.sbuf_tensor(f"b1_sq{i}", [128, QT], BF16)) for i in range(2)]
        s.rowbuf = E(nc.sbuf_tensor("b1_rowbuf", [1, 4 * QT], F32))
        s.rowslot4 = E(nc.sbuf_tensor("b1_rs4", [1, QT], F32))
        s.rowslotE = E(nc.sbuf_tensor("b1_rsE", [1, QT], F32))
        s.rowbf = E(nc.sbuf_tensor("b1_rowbf", [1, 2 * QT], BF16))
        s.x1buf = [E(nc.sbuf_tensor(f"b1_x1buf{i}", [128, QT], F32)) for i in range(2)]
        s.x1tmp = E(nc.sbuf_tensor("b1_x1tmp", [128, QT], F32))
        s.projW = E(nc.sbuf_tensor("b1_projW", [128, 8 * 16], BF16))
        s.C1sb = E(nc.sbuf_tensor("b1_C1", [16, QT], BF16))
        s.C2sb = E(nc.sbuf_tensor("b1_C2", [16, QT], BF16))
        s.bcols = E(nc.sbuf_tensor("b1_bcols", [128, 8], F32))
        s.wtmp = [E(nc.sbuf_tensor(f"b1_wtmp{i}", [128, 512], BF16)) for i in range(2)]
        s.cst = {}
        for name in ["c_ident", "c_ones1", "c_onescol"]:
            s.cst[name] = E(nc.sbuf_tensor("b1_" + name, list(CONST_SPECS[name]), BF16))
        s.ps = [E(nc.psum_tensor(f"b1_ps{i}", [128, 512], F32)) for i in range(6)]
        s.ps_bc = E(nc.psum_tensor("b1_ps_bc", [128, 512], F32))
        s.ps_bc2 = E(nc.psum_tensor("b1_ps_bc2", [128, 512], F32))
        s.ps_tr = [s.ps[i].ap().bitcast(BF16) for i in (2, 3, 4, 5)]

    def load_consts(self):
        p, s, io = self.p, self, self.io
        s.mc = {}
        for name in ["c_ident", "c_ones1", "c_onescol"]:
            p.dma("gpsimd", s.cst[name][:], io[name], sem="q_ig")
        for c in range(NCH):
            p.dma("gpsimd", s.xbf[c][:], io["x1T"][c * 128:(c + 1) * 128, :], sem="q_ig")
        for c in range(NCH):
            p.dma("gpsimd", s.win1[:, c * QT:(c + 1) * QT], io["Win1"][c * 128:(c + 1) * 128, :],
                  sem="q_ig")
            p.dma("gpsimd", s.projW[:, c * 16:(c + 1) * 16], io["proj"][c * 128:(c + 1) * 128, :],
                  sem="q_ig")
        p.dma("gpsimd", s.C1sb[:], io["C1"], sem="q_ig")
        p.dma("gpsimd", s.C2sb[:], io["C2"], sem="q_ig")
        for rc in range(4):
            p.dma("sync", s.bcols[:, rc:rc + 1],
                  io["Win1"][1024, rc * 128:(rc + 1) * 128].rearrange("(p o) -> p o", o=1), sem="q_is")
            p.dma("sync", s.bcols[:, 4 + rc:5 + rc],
                  io["Win2"][4096, rc * 128:(rc + 1) * 128].rearrange("(p o) -> p o", o=1), sem="q_is")
        s.m_bias = p.get("q_is")
        m_ig = p.get("q_ig")
        for name in ["c_ident", "c_ones1", "c_onescol"]:
            s.mc[name] = m_ig
        s.m_xall = m_ig
        s.m_smallw = m_ig

    def prep_weights(self):
        p, s, nc, io = self.p, self, self.nc, self.io
        s.wout1T_dram = nc.dram_tensor("b1_wout1T", [512, 4096], BF16, kind="Internal")
        s.win2_dram = io["_wch"]["win2"]
        ptr_free = [None] * 4
        wtmp_free = [None, None]
        for er in range(32):
            buf = s.wtmp[er % 2]
            qp = "q_pa" if er % 2 == 0 else "q_pb"
            mld = p.dma("gpsimd", buf[:], io["Wout1"][er * 128:(er + 1) * 128, :],
                        waits=[("dve", wtmp_free[er % 2])] if wtmp_free[er % 2] else [], sem=qp)
            for rc in range(4):
                tr = s.ps_tr[rc]
                go = 0
                w = [(qp, mld), ("q_ig", s.mc["c_ident"])]
                if ptr_free[rc]:
                    w.append(("dve", ptr_free[rc]))
                mt = p.emit("tensor",
                            lambda e, buf=buf, rc=rc, tr=tr, go=go: e.transpose(
                                tr[:, go:go + 128], buf[:, rc * 128:(rc + 1) * 128],
                                s.cst["c_ident"][:]),
                            waits=w, inc="pe")
                mv = p.emit("vector",
                            lambda e, rc=rc, er=er, tr=tr, go=go: e.tensor_copy(
                                s.hT[:, rc * 4096 + er * 128:rc * 4096 + (er + 1) * 128],
                                tr[:, go:go + 128]),
                            waits=[("pe", mt)], inc="dve")
                ptr_free[rc] = mv
                wtmp_free[er % 2] = mv
        mst = None
        for rc in range(4):
            mst = p.dma("sync", s.wout1T_dram.ap()[rc * 128:(rc + 1) * 128, :],
                        s.hT[:, rc * 4096:(rc + 1) * 4096], waits=[("dve", p.get("dve"))], sem="q_ps")
        s.m_w1T = mst
        for er in range(8):
            buf = s.wtmp[er % 2]
            qp = "q_pa" if er % 2 == 0 else "q_pb"
            mld = p.dma("gpsimd", buf[:], io["Wout2"][er * 128:(er + 1) * 128, :],
                        waits=[("dve", wtmp_free[er % 2])] if wtmp_free[er % 2] else [], sem=qp)
            for rc in range(4):
                tr = s.ps_tr[rc]
                go = 0
                w = [(qp, mld)]
                if ptr_free[rc]:
                    w.append(("dve", ptr_free[rc]))
                mt = p.emit("tensor",
                            lambda e, buf=buf, rc=rc, tr=tr, go=go: e.transpose(
                                tr[:, go:go + 128], buf[:, rc * 128:(rc + 1) * 128],
                                s.cst["c_ident"][:]),
                            waits=w, inc="pe")
                mv = p.emit("vector",
                            lambda e, rc=rc, er=er, tr=tr, go=go: e.tensor_copy(
                                s.wout2T[:, rc * 1024 + er * 128:rc * 1024 + (er + 1) * 128],
                                tr[:, go:go + 128]),
                            waits=[("pe", mt)], inc="dve")
                ptr_free[rc] = mv
                wtmp_free[er % 2] = mv
        s.m_w2T = p.get("dve")
        s.m_w2c = io["_wch_m"]

    def gates(self):
        p, s = self.p, self
        sm = lambda i: s.esm[:, i * 32:(i + 1) * 32]
        sm3 = lambda i: s.esm[:, i * 32:(i + 1) * 32].rearrange("p (c o) -> p c o", o=1)
        for qt in range(NQT):
            for c in range(NCH):
                w = []
                if c == 0:
                    w = [("q_ig", s.m_smallw), ("q_ig", s.m_xall)]
                    if qt >= 2:
                        w.append(("dve", s._tevac[qt - 2]))
                p.emit("tensor",
                       lambda e, c=c, qt=qt: e.matmul(
                           s.ps[qt % 2].ap()[0:16, :], s.projW[:, c * 16:(c + 1) * 16],
                           s.xbf[c][:, qt * QT:(qt + 1) * QT],
                           start=(c == 0), stop=(c == NCH - 1)),
                       waits=w, inc="pe" if c == NCH - 1 else None)
            mpe = p.get("pe")
            mv = p.emit("vector",
                        lambda e, qt=qt: e.tensor_copy(s.tT[:, qt * QT:(qt + 1) * QT],
                                                       s.ps[qt % 2].ap()[0:16, :]),
                        waits=[("pe", mpe)], inc="dve")
            if not hasattr(s, "_tevac"):
                s._tevac = {}
            s._tevac[qt] = mv
        guard = p.get("dve")
        for tt in range(32):
            g = tt % 4
            tr = s.ps_tr[g]
            go = 0
            w = [("dve", guard)] if tt < 4 else [("dve", s._ttb[tt - 4])]
            mt = p.emit("tensor",
                        lambda e, tt=tt, tr=tr, go=go: e.transpose(
                            tr[:, go:go + 16], s.tT[0:16, tt * 128:(tt + 1) * 128],
                            s.cst["c_ident"][0:16, 0:16]),
                        waits=w, inc="pe")
            mv = p.emit("vector",
                        lambda e, tt=tt, tr=tr, go=go: e.tensor_copy(
                            s.t_tb[:, tt * 16:(tt + 1) * 16], tr[:, go:go + 16]),
                        waits=[("pe", mt)], inc="dve")
            if not hasattr(s, "_ttb"):
                s._ttb = {}
            s._ttb[tt] = mv
        t3 = s.t_tb[:].rearrange("p (c e) -> p c e", e=16)
        tmp3 = s.e_tmp[:].rearrange("p (c e) -> p c e", e=16)
        c3 = s.e_c[:].rearrange("p (c e) -> p c e", e=16)
        z3 = s.e_z[:].rearrange("p (c e) -> p c e", e=16)
        zz3 = s.e_zz[:].rearrange("p (c e) -> p c e", e=16)
        bc = lambda i: sm3(i).broadcast_to([128, 32, 16])
        p.emit("vector", lambda e: e.tensor_reduce(sm3(0), t3, AX.X, ALU.add),
               waits=[("dve", p.get("dve"))], inc="dve")
        p.emit("vector", lambda e: e.tensor_scalar_mul(sm(1), sm(0), 1.0 / 16), inc="dve")
        p.emit("vector", lambda e: e.tensor_tensor(tmp3, t3, bc(1), ALU.subtract), inc="dve")
        p.emit("vector", lambda e: e.tensor_mul(s.e_c[:], s.e_tmp[:], s.e_tmp[:]), inc="dve")
        p.emit("vector", lambda e: e.tensor_reduce(sm3(2), c3, AX.X, ALU.add), inc="dve")
        mv = p.emit("vector", lambda e: e.tensor_scalar(sm(3), sm(2), 1.0 / 16, 1e-5, ALU.mult, ALU.add),
                    inc="dve")
        mla = p.emit("scalar", lambda e: e.activation(sm(4), sm(3), AF.Ln),
                     waits=[("dve", mv)], inc="act")
        mrs = p.emit("scalar", lambda e: e.activation(sm(5), sm(4), AF.Exp, scale=-0.5),
                     waits=[("act", mla)], inc="act")
        p.emit("vector", lambda e: e.tensor_tensor(z3, tmp3, bc(5), ALU.mult),
               waits=[("act", mrs)], inc="dve")
        p.emit("vector", lambda e: e.tensor_reduce(sm3(6), z3, AX.X, ALU.max), inc="dve")
        p.emit("vector", lambda e: e.tensor_scalar_mul(sm(7), sm(6), 0.5), inc="dve")
        p.emit("vector", lambda e: e.scalar_tensor_tensor(zz3, z3, 0.5, bc(7), ALU.mult, ALU.subtract),
               inc="dve")
        p.emit("vector", lambda e: e.memset(sm(8), -1.0), inc="dve")
        tcur, tnxt = 8, 9
        for it in range(24):
            p.emit("vector", lambda e, tcur=tcur: e.tensor_tensor(tmp3, zz3, bc(tcur), ALU.subtract),
                   inc="dve")
            p.emit("vector", lambda e: e.tensor_scalar_max(s.e_c[:], s.e_tmp[:], 0.0), inc="dve")
            p.emit("vector", lambda e: e.tensor_mul(s.e_tmp[:], s.e_c[:], s.e_c[:]), inc="dve")
            p.emit("vector", lambda e: e.tensor_reduce(sm3(10), tmp3, AX.X, ALU.add), inc="dve")
            p.emit("vector", lambda e: e.tensor_reduce(sm3(11), c3, AX.X, ALU.add), inc="dve")
            p.emit("vector", lambda e: e.tensor_scalar_add(sm(12), sm(10), -1.0), inc="dve")
            p.emit("vector", lambda e: e.reciprocal(sm(13), sm(11)), inc="dve")
            p.emit("vector", lambda e: e.tensor_tensor(sm(14), sm(12), sm(13), ALU.mult), inc="dve")
            p.emit("vector", lambda e, tcur=tcur, tnxt=tnxt: e.scalar_tensor_tensor(
                sm(tnxt), sm(14), 0.5, sm(tcur), ALU.mult, ALU.add), inc="dve")
            tcur, tnxt = tnxt, tcur
        p.emit("vector", lambda e, tcur=tcur: e.tensor_tensor(tmp3, zz3, bc(tcur), ALU.subtract),
               inc="dve")
        p.emit("vector", lambda e: e.tensor_scalar_max(s.e_c[:], s.e_tmp[:], 0.0), inc="dve")
        ma = p.emit("vector", lambda e: e.tensor_mul(s.a_tb[:], s.e_c[:], s.e_c[:]), inc="dve")
        for tt in range(32):
            g = tt % 4
            tr = s.ps_tr[g]
            go = 0
            w = [("dve", ma)] if tt < 4 else [("dve", s._atb[tt - 4])]
            mt = p.emit("tensor",
                        lambda e, tt=tt, tr=tr, go=go: e.transpose(
                            tr[0:16, go:go + 128], s.a_tb[:, tt * 16:(tt + 1) * 16],
                            s.cst["c_ident"][:]),
                        waits=w, inc="pe")
            mv = p.emit("vector",
                        lambda e, tt=tt, tr=tr, go=go: e.tensor_copy(
                            s.aT[0:16, tt * 128:(tt + 1) * 128], tr[0:16, go:go + 128]),
                        waits=[("pe", mt)], inc="dve")
            if not hasattr(s, "_atb"):
                s._atb = {}
            s._atb[tt] = mv
        s.m_gates = p.get("dve")

    def mumoe(self):
        p, s, io = self.p, self, self.io
        bias1 = lambda rc: s.bcols[:, rc:rc + 1]
        bias2 = lambda rc: s.bcols[:, 4 + rc:5 + rc]
        w1ld_prev = {}
        w2ld_prev = {}
        mdma_prev = {}
        eps = 1e-5
        for qt in range(NQT):
            guard = p.get("dve")
            for j, Csb in ((0, s.C1sb), (1, s.C2sb)):
                mgv = None
                for rc in range(4):
                    w = [("dve", s.m_gates)] if (qt == 0 and rc == 0 and j == 0) else []
                    if rc == 0:
                        w.append(("dve", guard))
                    if mgv:
                        w = list(w) + [("dve", mgv)]
                    mpe = p.emit("tensor",
                                 lambda e, j=j, rc=rc, qt=qt, Csb=Csb: e.matmul(
                                     s.ps[4 + j][:], Csb[:, rc * 128:(rc + 1) * 128],
                                     s.aT[0:16, qt * QT:(qt + 1) * QT],
                                     start=True, stop=True),
                                 waits=w, inc="pe")
                    mgv = p.emit("vector",
                                 lambda e, j=j, rc=rc: e.tensor_copy(
                                     s.gbuf[j][:, rc * QT:(rc + 1) * QT], s.ps[4 + j][:]),
                                 waits=[("pe", mpe)], inc="dve")
            if not hasattr(s, "_gz"):
                s._gz = {}
            s._gz[0] = p.get("dve")
            s._gz[1] = p.get("dve")
            zfree = getattr(s, "_z1free", {})
            for rc in range(4):
                for c in range(NCH):
                    w = []
                    if c == 0:
                        if qt == 0 and rc == 0:
                            w = [("q_ig", s.m_smallw), ("q_is", s.m_bias)]
                        if zfree.get(rc % 2):
                            w.append(("dve", zfree[rc % 2]))
                    p.emit("tensor",
                           lambda e, rc=rc, c=c, qt=qt: e.matmul(
                               s.ps[rc % 2][:], s.win1[:, c * QT + rc * 128:c * QT + (rc + 1) * 128],
                               s.xbf[c][:, qt * QT:(qt + 1) * QT],
                               start=(c == 0), stop=(c == NCH - 1)),
                           waits=w, inc="pe" if c == NCH - 1 else None)
                mpe = p.get("pe")
                zfree[rc % 2] = p.emit("vector",
                                       lambda e, rc=rc: e.scalar_tensor_tensor(
                                           s.zg[0][:, rc * QT:(rc + 1) * QT], s.ps[rc % 2][:], bias1(rc),
                                           s.gbuf[0][:, rc * QT:(rc + 1) * QT], ALU.add, ALU.mult),
                                       waits=[("pe", mpe)], inc="dve")
            s._z1free = zfree
            mzg1 = p.get("dve")
            for go in range(8):
                wb = s.w1buf[go % 2]
                w = [("q_ps", s.m_w1T), ("dve", mzg1)]
                if w1ld_prev.get(go % 2):
                    w.append(("pe", w1ld_prev[go % 2]))
                mld = p.dma("sync", wb[:],
                            s.wout1T_dram.ap().rearrange("(rc p) o -> p rc o", p=128)[
                                :, :, go * 512:(go + 1) * 512], waits=w,
                            sem="q_w1a" if go % 2 == 0 else "q_w1b")
                hact_loc = {}
                for oc4 in range(4):
                    oc = go * 4 + oc4
                    for rc in range(4):
                        w2 = [(("q_w1a" if go % 2 == 0 else "q_w1b"), mld)] if rc == 0 else []
                        if rc == 0 and oc4 == 0 and s._hact.get((qt - 1, go)):
                            w2.append(("act", s._hact[(qt - 1, go)]))
                        if rc == 0 and oc4 >= 2:
                            w2.append(("act", hact_loc[oc4 - 2]))
                        if rc == 0 and oc4 < 2 and s._hact.get((qt, go - 1)):
                            w2.append(("act", s._hact[(qt, go - 1)]))
                        p.emit("tensor",
                               lambda e, oc4=oc4, rc=rc, wb=wb: e.matmul(
                                   s.ps[2 + oc4 % 2][:],
                                   wb[:].rearrange("p (rc o) -> p rc o", rc=4)[:, rc,
                                                                              oc4 * 128:(oc4 + 1) * 128],
                                   s.zg[0][:, rc * QT:(rc + 1) * QT],
                                   start=(rc == 0), stop=(rc == 3)),
                               waits=w2, inc="pe" if rc == 3 else None)
                    mpe = p.get("pe")
                    mact = p.emit("scalar",
                                  lambda e, oc=oc, oc4=oc4: e.activation(
                                      s.hT[:, oc * QT:(oc + 1) * QT], s.ps[2 + oc4 % 2][:], AF.Gelu),
                                  waits=[("pe", mpe)], inc="act")
                    hact_loc[oc4] = mact
                w1ld_prev[go % 2] = mpe
                s._hact[(qt, go)] = mact
            for go in range(8):
                wb = s.w2buf[go % 2]
                w = [("q_wch", s.m_w2c), ("act", s._hact[(qt, 7)])]
                if w2ld_prev.get(go % 2):
                    w.append(("pe", w2ld_prev[go % 2]))
                mld = p.dma("sync", wb[:],
                            s.win2_dram.ap().rearrange("(g p) r -> p g r", p=128)[
                                :, go * 4:(go + 1) * 4, :], waits=w,
                            sem="q_w2a" if go % 2 == 0 else "q_w2b")
                for oc4 in range(4):
                    oc = go * 4 + oc4
                    for rc in range(4):
                        w2 = []
                        if rc == 0 and oc4 == 0:
                            w2 = [(("q_w2a" if go % 2 == 0 else "q_w2b"), mld)]
                            if go == 0:
                                w2.append(("dve", s._zg2free))
                        mpe = p.emit("tensor",
                                     lambda e, oc=oc, oc4=oc4, rc=rc, wb=wb: e.matmul(
                                         s.ps[rc][:],
                                         wb[:].rearrange("p (g r) -> p g r", g=4)[
                                             :, oc4, rc * 128:(rc + 1) * 128],
                                         s.hT[:, oc * QT:(oc + 1) * QT],
                                         start=(oc == 0), stop=(oc == 31)),
                                     waits=w2, inc="pe" if (rc == 3 and oc4 == 3) else None)
                w2ld_prev[go % 2] = mpe
            mpe_z2 = p.get("pe")
            for rc in range(4):
                p.emit("vector",
                       lambda e, rc=rc: e.scalar_tensor_tensor(
                           s.zg[1][:, rc * QT:(rc + 1) * QT], s.ps[rc][:], bias2(rc),
                           s.gbuf[1][:, rc * QT:(rc + 1) * QT], ALU.add, ALU.mult),
                       waits=[("pe", mpe_z2)] if rc == 0 else [], inc="dve")
            s._zg2free = p.get("dve")
            guard2 = p.get("dve")
            ps_mm_free = [None, None]
            for dc in range(NCH):
                for rc in range(4):
                    w = []
                    if rc == 0:
                        w = [("dve", guard2)]
                        if ps_mm_free[dc % 2]:
                            w.append(("dve", ps_mm_free[dc % 2]))
                    p.emit("tensor",
                           lambda e, dc=dc, rc=rc: e.matmul(
                               s.ps[4 + dc % 2][:],
                               s.wout2T[:, rc * 1024 + dc * 128:rc * 1024 + (dc + 1) * 128],
                               s.zg[1][:, rc * QT:(rc + 1) * QT],
                               start=(rc == 0), stop=(rc == 3)),
                           waits=w, inc="pe" if rc == 3 else None)
                mpe = p.get("pe")
                mres = p.emit("vector",
                              lambda e, dc=dc, qt=qt: e.tensor_tensor(
                                  s.resbuf[:, dc * QT:(dc + 1) * QT], s.ps[4 + dc % 2][:],
                                  s.xbf[dc][:, qt * QT:(qt + 1) * QT], ALU.add),
                              waits=[("pe", mpe)], inc="dve")
                ps_mm_free[dc % 2] = mres
            guard_pe2 = p.get("pe")
            guard_dve3 = p.get("dve")
            for oc in range(NCH):
                w = [("dve", guard_dve3), ("pe", guard_pe2)] if oc == 0 else []
                p.emit("tensor",
                       lambda e, oc=oc: e.matmul(
                           s.ps_bc.ap()[0:1, 0:QT], s.cst["c_onescol"][:, 0:1],
                           s.resbuf[:, oc * QT:(oc + 1) * QT],
                           start=(oc == 0), stop=(oc == NCH - 1)),
                       waits=w, inc="pe" if oc == NCH - 1 else None)
            sq_free = [None, None]
            for oc in range(NCH):
                sp = s.sqscr[oc % 2]
                w = [("dve", guard_dve3)] if oc == 0 else []
                if sq_free[oc % 2]:
                    w = list(w) + [("pe", sq_free[oc % 2])]
                mact2 = p.emit("scalar",
                               lambda e, oc=oc, sp=sp: e.activation(
                                   sp[:], s.resbuf[:, oc * QT:(oc + 1) * QT], AF.Square),
                               waits=w, inc="act")
                mpe_sq = p.emit("tensor",
                                lambda e, oc=oc, sp=sp: e.matmul(
                                    s.ps_bc2.ap()[0:1, 0:QT], s.cst["c_onescol"][:, 0:1], sp[:],
                                    start=(oc == 0), stop=(oc == NCH - 1)),
                                waits=[("act", mact2)], inc="pe")
                sq_free[oc % 2] = mpe_sq
            mpe_stats = p.get("pe")
            p.emit("vector", lambda e: e.tensor_scalar_mul(s.rowbuf[0:1, 0:QT],
                                                           s.ps_bc.ap()[0:1, 0:QT], 1.0 / 1024),
                   waits=[("pe", mpe_stats)], inc="dve")
            p.emit("vector", lambda e: e.tensor_scalar_mul(s.rowbuf[0:1, QT:2 * QT],
                                                           s.ps_bc2.ap()[0:1, 0:QT], 1.0 / 1024),
                   inc="dve")
            p.emit("vector", lambda e: e.tensor_tensor(s.rowbuf[0:1, 2 * QT:3 * QT],
                                                       s.rowbuf[0:1, 0:QT], s.rowbuf[0:1, 0:QT],
                                                       ALU.mult), inc="dve")
            p.emit("vector", lambda e: e.tensor_scalar_add(s.rowslotE[0:1, :],
                                                           s.rowbuf[0:1, 2 * QT:3 * QT], -eps),
                   inc="dve")
            mvar = p.emit("vector", lambda e: e.tensor_sub(s.rowbuf[0:1, 3 * QT:4 * QT],
                                                           s.rowbuf[0:1, QT:2 * QT],
                                                           s.rowslotE[0:1, :]), inc="dve")
            mln2 = p.emit("scalar", lambda e: e.activation(s.rowslot4[0:1, :],
                                                           s.rowbuf[0:1, 3 * QT:4 * QT], AF.Ln),
                          waits=[("dve", mvar)], inc="act")
            mrstd = p.emit("scalar", lambda e: e.activation(s.rowbf[0:1, 0:QT],
                                                            s.rowslot4[0:1, :], AF.Exp, scale=-0.5),
                           waits=[("act", mln2)], inc="act")
            mb = p.emit("vector", lambda e: e.tensor_tensor(s.rowbf[0:1, QT:2 * QT],
                                                            s.rowbf[0:1, 0:QT],
                                                            s.rowbuf[0:1, 0:QT], ALU.mult),
                        waits=[("act", mrstd)], inc="dve")
            p.emit("tensor", lambda e: e.matmul(s.ps_bc[:, 0:QT], s.cst["c_ones1"][0:1, :],
                                                s.rowbf[0:1, 0:QT], start=True, stop=True),
                   waits=[("dve", mb), ("q_ig", s.mc["c_ones1"])])
            mab = p.emit("tensor", lambda e: e.matmul(s.ps_bc2.ap()[:, 0:QT], s.cst["c_ones1"][0:1, :],
                                                      s.rowbf[0:1, QT:2 * QT], start=True, stop=True),
                         inc="pe")
            for oc in range(NCH):
                xb = s.x1buf[oc % 2]
                w = [("pe", mab)]
                if mdma_prev.get(oc % 2):
                    w.append(("q_sta" if oc % 2 == 0 else "q_stb", mdma_prev[oc % 2]))
                p.emit("vector",
                       lambda e, oc=oc: e.tensor_tensor(
                           s.x1tmp[:], s.resbuf[:, oc * QT:(oc + 1) * QT], s.ps_bc[:, 0:QT],
                           ALU.mult),
                       waits=w, inc="dve")
                mx1 = p.emit("vector",
                             lambda e, xb=xb: e.tensor_sub(xb[:], s.x1tmp[:], s.ps_bc2.ap()[:, 0:QT]),
                             inc="dve")
                mdma_prev[oc % 2] = p.dma("sync",
                                          io["x2T"][oc * 128:(oc + 1) * 128, qt * QT:(qt + 1) * QT],
                                          xb[:], waits=[("dve", mx1)],
                                          sem="q_sta" if oc % 2 == 0 else "q_stb")

    def run(self):
        p, s = self.p, self
        s.alloc()
        s._hact = {}
        s._zg2free = None
        s.load_consts()
        s.prep_weights()
        self.gates()
        self.mumoe()
        p.emit("sync", lambda e: None,
               waits=[("q_sta", p.get("q_sta")), ("q_stb", p.get("q_stb"))])


class B2:

    def __init__(self, nc, p, stack, io):
        self.nc, self.p, self.stack, self.io = nc, p, stack, io
        self.E = stack.enter_context

    def alloc(self):
        nc, E = self.nc, self.E
        s = self
        s.xbf = [E(nc.sbuf_tensor(f"b2_xbf{c}", [128, T], BF16)) for c in range(NCH)]
        s.vhat = E(nc.sbuf_tensor("b2_vhat", [128, 32 * 1040], BF16))
        s.wbuf = E(nc.sbuf_tensor("b2_wbuf", [128, 8 * 1024], BF16))
        s.kbuf = E(nc.sbuf_tensor("b2_kbuf", [128, 8 * 1024], BF16))
        s.qr = E(nc.sbuf_tensor("b2_qr", [128, T], BF16))
        s.kr = E(nc.sbuf_tensor("b2_kr", [128, T], BF16))
        s.sqbuf = E(nc.sbuf_tensor("b2_sqbuf", [128, T], BF16))
        s.kptok = [E(nc.sbuf_tensor(f"b2_kptok{i}", [128, 256], BF16)) for i in range(2)]
        s.kvbf = E(nc.sbuf_tensor("b2_kvbf", [128, 130], BF16))
        s.kvf32 = E(nc.sbuf_tensor("b2_kvf32", [128, 2 * 130], F32))
        s.atile = [E(nc.sbuf_tensor(f"b2_atile{i}", [128, 128], BF16)) for i in range(2)]
        s.attntok = E(nc.sbuf_tensor("b2_attntok", [128, 32 * 64], BF16))

        s.dbuf = E(nc.sbuf_tensor("b2_dbuf", [128, 8], F32))
        s.rbuf = E(nc.sbuf_tensor("b2_rbuf", [128, 8], F32))
        s.mapW = E(nc.sbuf_tensor("b2_mapW", [128, 256], BF16))
        s.negones = E(nc.sbuf_tensor("b2_negones", [128, 256], BF16))
        s.biases = E(nc.sbuf_tensor("b2_biases", [128, 4 * NCH], F32))
        s.vbrow = E(nc.sbuf_tensor("b2_vbrow", [1, 1024], BF16))
        s.gb2 = E(nc.sbuf_tensor("b2_gb2", [2, 1024], BF16))
        s.onesrow = E(nc.sbuf_tensor("b2_onesrow", [2, QT], BF16))
        s.resbuf = s.vhat[:, 0:NCH * QT]
        s.res2buf = s.vhat[:, NCH * QT:2 * NCH * QT]
        _qf = s.qr[:].bitcast(F32)
        _kf = s.kr[:].bitcast(F32)
        s.sqscr = [s.attntok[:, 0:QT], s.attntok[:, QT:2 * QT]]
        s.ybuf = s.attntok[:, 2 * QT:4 * QT].bitcast(F32)
        s.rowbuf = _qf[0:1, 0:2048]
        s.rowslot4 = _qf[32:33, 0:QT]
        s.rowslotE = E(nc.sbuf_tensor("b2_rsE", [1, QT], F32))[:]
        s.rowbf = s.kr[0:1, 0:2 * QT]
        s.x1buf = [_kf[:, 512 + i * QT:512 + (i + 1) * QT] for i in range(2)]
        s.x1tmp = _kf[:, 1536:2048]
        s.cst = {}
        for name in ["c_ident", "c_ones1", "c_onescol", "c_tri", "c_negrow"]:
            shp = CONST_SPECS.get(name, (1, QT))
            s.cst[name] = E(nc.sbuf_tensor("b2_" + name, list(shp), BF16))
        s.ps_m = [E(nc.psum_tensor(f"b2_ps_m{i}", [128, 1024], F32)) for i in range(2)]
        s.ps_a = E(nc.psum_tensor("b2_ps_a", [128, 512], F32))
        s.ps_o4 = E(nc.psum_tensor("b2_ps_o4", [128, 512], F32))
        s.ps_kv = E(nc.psum_tensor("b2_ps_kv", [128, 512], F32))
        s.ps_bc = E(nc.psum_tensor("b2_ps_bc", [128, 512], F32))
        s.ps_tr = [s.ps_m[0].ap().bitcast(BF16), s.ps_m[1].ap().bitcast(BF16)]
        s.ps_bc2 = s.ps_kv

    def load_consts(self):
        p, s, io = self.p, self, self.io
        for name in ["c_ident", "c_ones1", "c_onescol", "c_tri", "c_negrow"]:
            p.dma("gpsimd", s.cst[name][:], io[name], sem="q_ig")
        for c in range(NCH):
            p.dma("gpsimd", s.xbf[c][:], io["x2T"][c * 128:(c + 1) * 128, :], sem="q_ig")
        for j, bn in enumerate(["q_b", "k_b", "v_b", "out_b"]):
            for c in range(NCH):
                p.dma("sync", s.biases[:, j * NCH + c:j * NCH + c + 1],
                      io[bn][c * 128:(c + 1) * 128].rearrange("(p o) -> p o", o=1), sem="q_is")
        p.dma("gpsimd", s.gb2[0:1, :], io["gamma"].rearrange("(o p) -> o p", o=1), sem="q_ig")
        p.dma("gpsimd", s.gb2[1:2, :], io["beta"].rearrange("(o p) -> o p", o=1), sem="q_ig")
        p.dma("gpsimd", s.vbrow[0:1, :], io["v_b"].rearrange("(o p) -> o p", o=1), sem="q_ig")
        p.dma("gpsimd", s.onesrow[1:2, :], io["c_negrow"], sem="q_ig")
        s.m_bias = p.get("q_is")
        s.m_ig = p.get("q_ig")

    def bias_ap(self, j, c):
        return self.biases[:, j * NCH + c:j * NCH + c + 1]

    def prep(self):
        p, s, nc, io = self.p, self, self.nc, self.io
        m1 = p.dma("gpsimd", s.sqbuf[:, 0:64], io["proj"][0:128, :], sem="q_pa")
        m2 = p.dma("gpsimd", s.sqbuf[:, 64:128], io["proj"][128:256, :], sem="q_pb")
        mn = p.emit("vector", lambda e: e.memset(s.negones[:], -1.0 / 128))
        prev_evac = {}
        for half in range(2):
            qw = ("q_pa", m1) if half == 0 else ("q_pb", m2)
            for tp in (0, 64):
                tw = [qw, ("q_ig", s.m_ig)]
                if prev_evac.get(tp):
                    tw.append(("dve", prev_evac[tp]))
                mt = p.emit("tensor",
                            lambda e, half=half, tp=tp: e.transpose(
                                (s.ps_tr[0][0:64, 0:128] if tp == 0 else s.ps_tr[0][64:128, 1024:1152]),
                                s.sqbuf[:, half * 64:half * 64 + 64],
                                s.cst["c_ident"][:], tile_position=(0, tp)),
                            waits=tw, inc="pe")
                prev_evac[tp] = p.emit("vector",
                       lambda e, half=half, tp=tp: e.tensor_scalar_mul(
                           s.mapW[tp:tp + 64, half * 128:(half + 1) * 128],
                           (s.ps_tr[0][:, 0:128] if tp == 0 else s.ps_tr[0][:, 1024:1152])[tp:tp + 64, :],
                           0.125),
                       waits=[("pe", mt)], inc="dve")
        s.m_mapw = p.get("dve")
        s.wc = {wn: io["_wch"][wn] for wn in ["q_w", "k_w"]}
        s.m_wc = io["_wch_m"]

    def build_vhat(self):
        p, s, io = self.p, self, self.io
        vh3 = s.vhat[:].rearrange("p (tt pr c) -> p tt pr c", pr=8, c=130)
        p.emit("vector", lambda e: e.memset(vh3[:, :, :, 64:65], 1.0))
        p.emit("vector", lambda e: e.memset(vh3[:, :, :, 129:130], 1.0))
        mones = p.get("dve")
        for c in range(NCH):
            p.dma("gpsimd", s.kbuf[:, c * 1024:(c + 1) * 1024],
                  io["v_w"][c * 128:(c + 1) * 128, :], sem="q_ig2")
        mvl = p.get("q_ig2")
        ps_free = {}
        for tt in range(32):
            for half in range(2):
                for c in range(NCH):
                    w = []
                    if c == 0:
                        w = [("q_ig2", mvl), ("q_ig", s.m_ig)]
                        if tt == 0 and half == 0:
                            w.append(("dve", mones))
                        if ps_free.get(tt % 2):
                            w.append(("dve", ps_free[tt % 2]))
                    p.emit("tensor",
                           lambda e, tt=tt, c=c, half=half: e.matmul(
                               s.ps_m[tt % 2].ap()[:, half * 512:(half + 1) * 512],
                               s.xbf[c][:, tt * 128:(tt + 1) * 128],
                               s.kbuf[:, c * 1024 + half * 512:c * 1024 + (half + 1) * 512],
                               start=(c == 0), stop=False),
                           waits=w)
                mpx = p.emit("tensor",
                             lambda e, tt=tt, half=half: e.matmul(
                                 s.ps_m[tt % 2].ap()[:, half * 512:(half + 1) * 512],
                                 s.cst["c_ones1"][0:1, :],
                                 s.vbrow[0:1, half * 512:(half + 1) * 512],
                                 start=False, stop=True),
                             inc="pe")
                src = s.ps_m[tt % 2].ap()[:, half * 512:(half + 1) * 512].rearrange(
                    "p (pr h d) -> p pr h d", pr=4, h=2, d=64)
                dst3 = vh3[:, tt, half * 4:(half + 1) * 4, :]
                p.emit("vector",
                       lambda e, src=src, dst3=dst3: e.tensor_copy(dst3[:, :, 0:64], src[:, :, 0, :]),
                       waits=[("pe", mpx)], inc="dve")
                mv = p.emit("vector",
                            lambda e, src=src, dst3=dst3: e.tensor_copy(dst3[:, :, 65:129], src[:, :, 1, :]),
                            inc="dve")
                ps_free[tt % 2] = mv
        s.m_vhat = p.get("dve")

    def pair_qk(self, hp, mat_prev=None):
        p, s = self.p, self
        guard_pe = p.get("pe")
        guard_dve = p.get("dve")
        mws = []
        for j, wn in enumerate(["q_w", "k_w"]):
            src = s.wc[wn].ap().rearrange("(c p) e -> p c e", p=128)[:, :, hp * 128:(hp + 1) * 128]
            dst = s.sqbuf[:].rearrange("p (w c e) -> p w c e", w=4, e=128)[:, j]
            ww = [("q_wch", s.m_wc),
                  ("pe", guard_pe), ("dve", guard_dve)]
            if mat_prev:
                ww.append((mat_prev[0], mat_prev[1]))
            mws.append(p.dma("sync", dst, src, sem="q_wsl", waits=ww))
        mwsl = mws[-1]
        wsl_ap = lambda j, c: s.sqbuf[:].rearrange("p (w c e) -> p w c e", w=4, e=128)[:, j, c]
        ps_free = {}
        for j, dstb in ((0, s.qr), (1, s.kr)):
            for qt in range(NQT):
                for c in range(NCH):
                    w = []
                    if c == 0:
                        w = [("q_wsl", mwsl), ("q_ig", s.m_ig), ("q_is", s.m_bias)]
                        if ps_free.get(qt % 2):
                            w.append(("dve", ps_free[qt % 2]))
                    mpe = p.emit("tensor",
                                 lambda e, j=j, qt=qt, c=c: e.matmul(
                                     s.ps_m[qt % 2].ap()[:, 0:512],
                                     wsl_ap(j, c),
                                     s.xbf[c][:, qt * QT:(qt + 1) * QT],
                                     start=(c == 0), stop=(c == NCH - 1)),
                                 waits=w, inc="pe" if c == NCH - 1 else None)
                mv = p.emit("vector",
                            lambda e, j=j, qt=qt, dstb=dstb, hp=hp: e.tensor_scalar(
                                dstb[:, qt * QT:(qt + 1) * QT], s.ps_m[qt % 2].ap()[:, 0:512],
                                s.bias_ap(j, hp), None, ALU.add),
                            waits=[("pe", mpe)], inc="dve")
                ps_free[qt % 2] = mv

    def head(self, hp, half, adram, mat_prev):
        p, s = self.p, self
        h = 2 * hp + half
        hb = half * 64
        qpT = lambda mc: s.wbuf[:, mc * 4096:(mc + 1) * 4096]
        kpT = lambda mc: s.kbuf[:, mc * 4096:(mc + 1) * 4096]
        vh3 = s.vhat[:].rearrange("p (tt pr c) -> p tt pr c", pr=8, c=130)
        guard = p.get("dve")
        guard_pe = p.get("pe")
        for j, (srcb, dstT) in enumerate(((s.qr, qpT), (s.kr, kpT))):
            sqw = [("pe", guard_pe)]
            if mat_prev:
                sqw.append((mat_prev[0], mat_prev[1]))
            msq = p.emit("vector",
                         lambda e, srcb=srcb, hb=hb: e.tensor_mul(
                             s.sqbuf[hb:hb + 64, :] if hb == 0 else s.sqbuf[64:128, :],
                             srcb[hb:hb + 64, :] if hb == 0 else srcb[64:128, :],
                             srcb[hb:hb + 64, :] if hb == 0 else srcb[64:128, :]),
                         waits=sqw, inc="dve")
            if j == 0:
                ps_free = {}
            mlast_mm = None
            for st in range(NQT):
                for mc in range(2):
                    w = []
                    if mc == 0:
                        w = [("dve", msq), ("dve", s.m_mapw)]
                        if ps_free.get(st % 2):
                            w.append(("act", ps_free[st % 2]))
                    p.emit("tensor",
                           lambda e, st=st, mc=mc, hb=hb, srcb=srcb: e.matmul(
                               s.ps_m[st % 2].ap()[:, mc * 512:(mc + 1) * 512],
                               s.mapW[hb:hb + 64, mc * 128:(mc + 1) * 128],
                               srcb[hb:hb + 64, st * QT:(st + 1) * QT],
                               start=True, stop=False),
                           waits=w)
                    mlast_mm = p.emit("tensor",
                                      lambda e, st=st, mc=mc, hb=hb: e.matmul(
                                          s.ps_m[st % 2].ap()[:, mc * 512:(mc + 1) * 512],
                                          s.negones[hb:hb + 64, mc * 128:(mc + 1) * 128],
                                          s.sqbuf[hb:hb + 64, st * QT:(st + 1) * QT],
                                          start=False, stop=True),
                                      inc="pe")
                dst = (s.wbuf if j == 0 else s.kbuf)[:].rearrange(
                    "p (mc x) -> p mc x", mc=2)[:, :, st * QT:(st + 1) * QT]
                mex = p.emit("scalar",
                             lambda e, st=st, dst=dst: e.activation(
                                 dst, s.ps_m[st % 2].ap().rearrange("p (mc x) -> p mc x", mc=2),
                                 AF.Exp),
                             waits=[("pe", mlast_mm)], inc="act")
                ps_free[st % 2] = mex
            if j == 0:
                guard_pe = p.get("pe")
        m_exp_done = p.get("act")
        ms_ktok = {}
        m_ktok_last = None
        m_mask_last = None
        m_atile_free = {}
        m_kptok_free = {}
        m_kv_evac = None
        m_norm = None
        m_out_last = None
        for i in range(32):
            par = i % 2
            w = [("act", m_exp_done)] if i == 0 else [("pe", m_out_last)]
            if m_ktok_last:
                w.append(("dve", m_ktok_last))
            for mc in range(2):
                mt = p.emit("tensor",
                            lambda e, i=i, mc=mc: e.transpose(
                                s.ps_tr[0][:, mc * 1024:mc * 1024 + 128],
                                kpT(mc)[:, i * 128:(i + 1) * 128],
                                s.cst["c_ident"][:]),
                            waits=w if mc == 0 else [], inc="pe" if mc == 1 else None)
            kw = [("pe", mt)]
            if m_kptok_free.get(par):
                kw.append(("pe", m_kptok_free[par]))
            mktok = p.emit("vector",
                           lambda e, i=i, par=par: e.tensor_copy(
                               s.kptok[par][:].rearrange("p (mc x) -> p mc x", mc=2),
                               s.ps_tr[0].rearrange("p (mc x) -> p mc x", mc=2)[:, :, 0:128]),
                           waits=kw, inc="dve")
            ms_ktok[par] = mktok
            m_ktok_last = mktok
            w = [("dve", m_mask_last)] if m_mask_last else []
            for mc in range(2):
                ma = p.emit("tensor",
                            lambda e, i=i, mc=mc: e.matmul(
                                s.ps_a.ap()[:, 0:128],
                                kpT(mc)[:, i * 128:(i + 1) * 128],
                                qpT(mc)[:, i * 128:(i + 1) * 128],
                                start=(mc == 0), stop=(mc == 1)),
                            waits=w if mc == 0 else [], inc="pe" if mc == 1 else None)
            mw2 = [("pe", ma), ("q_ig", s.m_ig)]
            if m_atile_free.get(par):
                mw2.append(("pe", m_atile_free[par]))
            mmask = p.emit("vector",
                           lambda e, par=par: e.tensor_tensor(
                               s.atile[par][:], s.ps_a.ap()[:, 0:128], s.cst["c_tri"][:], ALU.mult),
                           waits=mw2, inc="dve")
            m_mask_last = mmask
            ob = (i % 4) * 65
            w = [("dve", mmask)]
            if i % 4 == 0 and m_norm:
                w.append(("dve", m_norm))
            m_out_last = p.emit("tensor",
                                lambda e, i=i, ob=ob, par=par, hp=hp, half=half: e.matmul(
                                    s.ps_o4.ap()[:, ob:ob + 65], s.atile[par][:],
                                    vh3[:, i, hp, half * 65:half * 65 + 65],
                                    start=True, stop=(i == 0)),
                                waits=w, inc="pe" if i == 0 else None)
            if i > 0:
                for mc in range(2):
                    m_out_last = p.emit("tensor",
                                        lambda e, i=i, ob=ob, mc=mc: e.matmul(
                                            s.ps_o4.ap()[:, ob:ob + 65],
                                            qpT(mc)[:, i * 128:(i + 1) * 128],
                                            s.kvbf[:, mc * 65:(mc + 1) * 65],
                                            start=False, stop=(mc == 1)),
                                        waits=[m_kv_evac] if mc == 0 else [],
                                        inc="pe" if mc == 1 else None)
            m_atile_free[par] = m_out_last
            w = [("dve", mktok)]
            if m_kv_evac:
                w.append(m_kv_evac)
            for mc in range(2):
                mkv = p.emit("tensor",
                             lambda e, i=i, mc=mc, par=par, hp=hp, half=half: e.matmul(
                                 s.ps_kv.ap()[:, mc * 65:(mc + 1) * 65],
                                 s.kptok[par][:, mc * 128:(mc + 1) * 128],
                                 vh3[:, i, hp, half * 65:half * 65 + 65],
                                 start=True, stop=True),
                             waits=w if mc == 0 else [], inc="pe" if mc == 1 else None)
            m_kptok_free[par] = mkv
            if i == 0:
                madd = p.emit("vector",
                              lambda e, par=par: e.tensor_copy(s.kvf32[:, par * 130:(par + 1) * 130],
                                                               s.ps_kv.ap()[:, 0:130]),
                              waits=[("pe", mkv)], inc="dve")
            else:
                madd = p.emit("vector",
                              lambda e, par=par: e.tensor_add(s.kvf32[:, par * 130:(par + 1) * 130],
                                                              s.kvf32[:, (1 - par) * 130:(2 - par) * 130],
                                                              s.ps_kv.ap()[:, 0:130]),
                              waits=[("pe", mkv)], inc="dve")
            m_kv_evac = ("act", p.emit("scalar",
                                       lambda e, par=par: e.activation(
                                           s.kvbf[:], s.kvf32[:, par * 130:(par + 1) * 130],
                                           AF.Copy),
                                       waits=[("dve", madd)], inc="act"))
            if i % 4 == 3:
                g0 = i - 3
                p.emit("vector",
                       lambda e: e.tensor_copy(
                           s.dbuf[:, 0:4],
                           s.ps_o4.ap()[:, 0:260].rearrange("p (g c) -> p g c", c=65)[:, :, 64:65]
                           .rearrange("p g o -> p (g o)")),
                       waits=[("pe", m_out_last)], inc="dve")
                p.emit("vector", lambda e: e.reciprocal(s.rbuf[:, 0:4], s.dbuf[:, 0:4]), inc="dve")
                m_norm = p.emit("vector",
                                lambda e, g0=g0: e.tensor_tensor(
                                    s.attntok[:, g0 * 64:(g0 + 4) * 64].rearrange(
                                        "p (g d) -> p g d", g=4),
                                    s.ps_o4.ap()[:, 0:260].rearrange("p (g c) -> p g c", c=65)[:, :, 0:64],
                                    s.rbuf[:, 0:4].rearrange("p (g o) -> p g o", o=1).broadcast_to(
                                        [128, 4, 64]),
                                    ALU.mult),
                                inc="dve")
        ab = (1 - half) * 64
        matp = {}
        for tt in range(32):
            par = tt % 2
            w = [("dve", m_norm)] if tt < 2 else []
            if matp.get(par):
                w.append(("dve", matp[par]))
            if tt == 0 and mat_prev:
                w.append((mat_prev[0], mat_prev[1]))
            mt = p.emit("tensor",
                        lambda e, tt=tt, par=par, ab=ab: e.transpose(
                            s.ps_tr[1][ab:ab + 64, par * 1024:par * 1024 + 128],
                            s.attntok[:, tt * 64:(tt + 1) * 64],
                            s.cst["c_ident"][:], tile_position=(0, ab)),
                        waits=w, inc="pe")
            mv = p.emit("vector",
                        lambda e, tt=tt, par=par, ab=ab: e.tensor_copy(
                            s.sqbuf[ab:ab + 64, tt * 128:(tt + 1) * 128],
                            s.ps_tr[1][ab:ab + 64, par * 1024:par * 1024 + 128]),
                        waits=[("pe", mt)], inc="dve")
            matp[par] = mv
        mat = p.dma("sync", adram[h * 64:(h + 1) * 64, :], s.sqbuf[ab:ab + 64, :],
                    sem="q_ata" if h % 2 == 0 else "q_atb", waits=[("dve", p.get("dve"))])
        return (("q_ata" if h % 2 == 0 else "q_atb"), mat)

    def oproj_ln(self, adram):
        p, s, io = self.p, self, self.io
        guard_pe = p.get("pe")
        guard_dve = p.get("dve")
        for c in range(NCH):
            p.dma("gpsimd", s.kbuf[:, c * 1024:(c + 1) * 1024],
                  io["out_w"][c * 128:(c + 1) * 128, :], sem="q_ow",
                  waits=[("pe", guard_pe), ("dve", guard_dve)] if c == 0 else [])
        m_ow = p.get("q_ow")
        eps = 1e-5
        mab_prev = {}
        mdma_prev = {}
        mat_ld = {}
        for qt in range(NQT):
            par = qt % 2
            w = ([("pe", guard_pe), ("dve", guard_dve),
                  ("q_ata", p.get("q_ata")), ("q_atb", p.get("q_atb"))]
                 if qt < 2 else [("pe", mat_ld[par])])
            mld = p.dma("sync", s.wbuf[:, par * 4096:(par + 1) * 4096].rearrange(
                            "p (c t) -> p c t", c=8),
                        adram.rearrange("(c p) t -> p c t", p=128)[:, :, qt * QT:(qt + 1) * QT],
                        sem="q_aa" if par == 0 else "q_ab", waits=w)
            guard2 = p.get("dve")
            ps_free = {}
            for oc in range(NCH):
                for ec in range(NCH):
                    w2 = []
                    if ec == 0:
                        w2 = [(("q_aa" if par == 0 else "q_ab"), mld), ("q_ow", m_ow)]
                        if ps_free.get(oc % 2):
                            w2.append(("dve", ps_free[oc % 2]))
                    mpe = p.emit("tensor",
                                 lambda e, oc=oc, ec=ec, par=par: e.matmul(
                                     s.ps_m[oc % 2].ap()[:, 0:512],
                                     s.kbuf[:, ec * 1024 + oc * 128:ec * 1024 + (oc + 1) * 128],
                                     s.wbuf[:, par * 4096 + ec * 512:par * 4096 + (ec + 1) * 512],
                                     start=(ec == 0), stop=(ec == NCH - 1)),
                                 waits=w2, inc="pe" if ec == NCH - 1 else None)
                mres = p.emit("vector",
                              lambda e, oc=oc, qt=qt: e.scalar_tensor_tensor(
                                  s.resbuf[:, oc * QT:(oc + 1) * QT], s.ps_m[oc % 2].ap()[:, 0:512],
                                  self.bias_ap(3, oc),
                                  s.xbf[oc][:, qt * QT:(qt + 1) * QT], ALU.add, ALU.add),
                              waits=[("pe", mpe)], inc="dve")
                ps_free[oc % 2] = mres
            mat_ld[par] = p.get("pe")
            self._ln_stats(s.resbuf, qt)
            mb = p.emit("vector", lambda e: e.tensor_tensor(s.onesrow[0:1, :], s.rowbf[0:1, 0:QT],
                                                            s.rowbuf[0:1, 0:QT], ALU.mult),
                        inc="dve")
            mab_last = None
            for oc in range(NCH):
                w2 = [("dve", mb)]
                if mab_last:
                    w2.append(("dve", mab_last))
                p.emit("tensor",
                       lambda e, oc=oc: e.matmul(
                           s.ps_bc[:, 0:QT], s.gb2[0:1, oc * 128:(oc + 1) * 128],
                           s.rowbf[0:1, 0:QT], start=True, stop=True),
                       waits=w2)
                mabx = p.emit("tensor",
                              lambda e, oc=oc: e.matmul(
                                  s.ps_bc2.ap()[:, 0:QT], s.gb2[:, oc * 128:(oc + 1) * 128],
                                  s.onesrow[:], start=True, stop=True),
                              inc="pe")
                p.emit("vector",
                       lambda e, oc=oc: e.tensor_tensor(
                           s.x1tmp, s.resbuf[:, oc * QT:(oc + 1) * QT], s.ps_bc[:, 0:QT], ALU.mult),
                       waits=[("pe", mabx)], inc="dve")
                p.emit("vector",
                       lambda e: e.tensor_sub(s.ybuf, s.x1tmp, s.ps_bc2.ap()[:, 0:QT]),
                       inc="dve")
                mab_last = p.emit("vector",
                                  lambda e, oc=oc, qt=qt: e.tensor_tensor(
                                      s.res2buf[:, oc * QT:(oc + 1) * QT], s.ybuf,
                                      s.xbf[oc][:, qt * QT:(qt + 1) * QT], ALU.add),
                                  inc="dve")
            self._ln_stats(s.res2buf, qt)
            mbb = p.emit("vector", lambda e: e.tensor_tensor(s.rowbf[0:1, QT:2 * QT],
                                                             s.rowbf[0:1, 0:QT],
                                                             s.rowbuf[0:1, 0:QT], ALU.mult),
                         inc="dve")
            p.emit("tensor", lambda e: e.matmul(s.ps_bc[:, 0:QT], s.cst["c_ones1"][0:1, :],
                                                s.rowbf[0:1, 0:QT], start=True, stop=True),
                   waits=[("dve", mbb)])
            mab = p.emit("tensor", lambda e: e.matmul(s.ps_bc2.ap()[:, 0:QT], s.cst["c_ones1"][0:1, :],
                                                      s.rowbf[0:1, QT:2 * QT], start=True, stop=True),
                         inc="pe")
            for oc in range(NCH):
                xb = s.x1buf[oc % 2]
                w2 = [("pe", mab)]
                if mdma_prev.get(oc % 2):
                    w2.append(("q_sta" if oc % 2 == 0 else "q_stb", mdma_prev[oc % 2]))
                p.emit("vector",
                       lambda e, oc=oc: e.tensor_tensor(
                           s.x1tmp, s.res2buf[:, oc * QT:(oc + 1) * QT], s.ps_bc[:, 0:QT],
                           ALU.mult),
                       waits=w2, inc="dve")
                mx1 = p.emit("vector",
                             lambda e, xb=xb: e.tensor_sub(xb, s.x1tmp, s.ps_bc2.ap()[:, 0:QT]),
                             inc="dve")
                mdma_prev[oc % 2] = p.dma("sync",
                                          io["x3T"][oc * 128:(oc + 1) * 128, qt * QT:(qt + 1) * QT],
                                          xb, waits=[("dve", mx1)],
                                          sem="q_sta" if oc % 2 == 0 else "q_stb")

    def _ln_stats(self, resb, qt, eps=1e-5):
        p, s = self.p, self
        guard_pe2 = p.get("pe")
        guard_dve3 = p.get("dve")
        for oc in range(NCH):
            w = [("dve", guard_dve3), ("pe", guard_pe2)] if oc == 0 else []
            p.emit("tensor",
                   lambda e, oc=oc, resb=resb: e.matmul(
                       s.ps_bc.ap()[0:1, 0:QT], s.cst["c_onescol"][:, 0:1],
                       resb[:, oc * QT:(oc + 1) * QT],
                       start=(oc == 0), stop=(oc == NCH - 1)),
                   waits=w, inc="pe" if oc == NCH - 1 else None)
        sq_free = {}
        for oc in range(NCH):
            sp = s.sqscr[oc % 2]
            w = [("dve", guard_dve3)] if oc == 0 else []
            if sq_free.get(oc % 2):
                w = list(w) + [("pe", sq_free[oc % 2])]
            mact2 = p.emit("scalar",
                           lambda e, oc=oc, sp=sp, resb=resb: e.activation(
                               sp, resb[:, oc * QT:(oc + 1) * QT], AF.Square),
                           waits=w, inc="act")
            mpe_sq = p.emit("tensor",
                            lambda e, oc=oc, sp=sp: e.matmul(
                                s.ps_bc2.ap()[0:1, 0:QT], s.cst["c_onescol"][:, 0:1], sp,
                                start=(oc == 0), stop=(oc == NCH - 1)),
                            waits=[("act", mact2)], inc="pe")
            sq_free[oc % 2] = mpe_sq
        mpe_stats = p.get("pe")
        p.emit("vector", lambda e: e.tensor_scalar_mul(s.rowbuf[0:1, 0:QT],
                                                       s.ps_bc.ap()[0:1, 0:QT], 1.0 / 1024),
               waits=[("pe", mpe_stats)], inc="dve")
        p.emit("vector", lambda e: e.tensor_scalar_mul(s.rowbuf[0:1, QT:2 * QT],
                                                       s.ps_bc2.ap()[0:1, 0:QT], 1.0 / 1024),
               inc="dve")
        p.emit("vector", lambda e: e.tensor_tensor(s.rowbuf[0:1, 2 * QT:3 * QT],
                                                   s.rowbuf[0:1, 0:QT], s.rowbuf[0:1, 0:QT],
                                                   ALU.mult), inc="dve")
        p.emit("vector", lambda e: e.tensor_scalar_add(s.rowslotE,
                                                       s.rowbuf[0:1, 2 * QT:3 * QT], -eps),
               inc="dve")
        mvar = p.emit("vector", lambda e: e.tensor_sub(s.rowbuf[0:1, 3 * QT:4 * QT],
                                                       s.rowbuf[0:1, QT:2 * QT],
                                                       s.rowslotE), inc="dve")
        mln2 = p.emit("scalar", lambda e: e.activation(s.rowslot4,
                                                       s.rowbuf[0:1, 3 * QT:4 * QT], AF.Ln),
                      waits=[("dve", mvar)], inc="act")
        p.emit("scalar", lambda e: e.activation(s.rowbf[0:1, 0:QT],
                                                s.rowslot4, AF.Exp, scale=-0.5),
               waits=[("act", mln2)], inc="act")
        p.emit("vector", lambda e: e.engine_nop(), waits=[("act", p.get("act"))], inc="dve")

    def run(self):
        p, s, nc = self.p, self, self.nc
        s.alloc()
        s.load_consts()
        self.prep()
        self.build_vhat()
        adram = nc.dram_tensor("b2_attnT", [1024, T], BF16, kind="Internal").ap()
        mat_prev = None
        for hp in range(8):
            self.pair_qk(hp, mat_prev)
            for half in range(2):
                mat_prev = self.head(hp, half, adram, mat_prev)
        self.oproj_ln(adram)
        p.emit("sync", lambda e: None,
               waits=[("q_sta", p.get("q_sta")), ("q_stb", p.get("q_stb"))])


class B3:

    def __init__(self, nc, p, stack, io, core_id):
        self.nc, self.p, self.stack, self.io = nc, p, stack, io
        self.E = stack.enter_context
        self.cid = core_id

    def alloc(self):
        nc, E = self.nc, self.E
        s = self
        s.xbf = [E(nc.sbuf_tensor(f"b3_xbf{c}", [128, T], BF16)) for c in range(NCH)]
        s.hT = E(nc.sbuf_tensor("b3_hT", [128, 16 * 1024], BF16))
        s.xtok = E(nc.sbuf_tensor("b3_xtok", [128, 2048], BF16))
        s.wst = [E(nc.sbuf_tensor(f"b3_wst{i}", [128, 4096], BF16)) for i in range(2)]
        s.gl = E(nc.sbuf_tensor("b3_gl", [16, T], F32))
        s.gltb = E(nc.sbuf_tensor("b3_gltb", [128, QT], F32))
        s.etb = [E(nc.sbuf_tensor(f"b3_etb{i}", [128, QT], F32)) for i in range(2)]
        s.esm = E(nc.sbuf_tensor("b3_esm", [128, 16 * 32], F32))
        s.idxu = E(nc.sbuf_tensor("b3_idxu", [128, 8 * 32], U32))
        s.tokid = E(nc.sbuf_tensor("b3_tokid", [128, 32], F32))
        s.offmask = E(nc.sbuf_tensor("b3_offmask", [128, 16], F32))
        s.iotatb = E(nc.sbuf_tensor("b3_iotatb", [128, QT], F32))
        s.big16 = E(nc.sbuf_tensor("b3_big16", [128, QT], F32))
        s.wgf = E(nc.sbuf_tensor("b3_wgf", [128, 8 * 16], F32))
        s.ctriF = E(nc.sbuf_tensor("b3_ctriF", [128, 128], F32))
        s.conescolF = E(nc.sbuf_tensor("b3_onescolF", [128, 1], F32))
        s.cidentF = E(nc.sbuf_tensor("b3_cidentF", [128, 128], F32))
        s.pt16 = E(nc.sbuf_tensor("b3_pt16", [16, 96], F32))
        s.onesrowF = E(nc.sbuf_tensor("b3_ones1F", [1, 128], F32))
        s.invrow = E(nc.sbuf_tensor("b3_invrow", [128, 2], U32))
        s.prow = E(nc.sbuf_tensor("b3_prow", [1, QT], F32))
        s.cnt = E(nc.sbuf_tensor("b3_cnt", [16, 1], F32))
        s.cntall = E(nc.sbuf_tensor("b3_cntall", [128, 1], F32))
        s.offs = E(nc.sbuf_tensor("b3_offs", [16, 1], F32))
        s.invf = E(nc.sbuf_tensor("b3_invf", [128, 16], F32))
        s.invu = E(nc.sbuf_tensor("b3_invu", [128, 16], U32))
        s.zero = E(nc.sbuf_tensor("b3_zero", [128, 144], F32))
        s.ptrash = E(nc.sbuf_tensor("b3_ptrash", [128, 32], F32))
        s.qgrid = E(nc.sbuf_tensor("b3_qgrid", [128, 16], F32))
        s.qgrid2 = E(nc.sbuf_tensor("b3_qgrid2", [128, 16], F32))
        s.gout = E(nc.sbuf_tensor("b3_gout", [128, 1024], BF16))
        s.sdump = E(nc.sbuf_tensor("b3_sdump", [128, 1024], BF16))
        s.resf = E(nc.sbuf_tensor("b3_resf", [128, 1024], F32))
        s.outf = [E(nc.sbuf_tensor(f"b3_outf{i}", [128, 1024], F32)) for i in range(2)]
        s.otok = [E(nc.sbuf_tensor(f"b3_otok{i}", [128, 1024], BF16)) for i in range(2)]
        s.lnm = E(nc.sbuf_tensor("b3_lnm", [128, 8], F32))
        s.cst = {}
        s.ident = E(nc.sbuf_tensor("b3_ident", [128, 128], BF16))
        s.ps = [E(nc.psum_tensor(f"b3_ps{i}", [128, 512], F32)) for i in range(6)]
        s.ps_r = E(nc.psum_tensor("b3_ps_r", [128, 512], F32))
        s.ps_r2 = E(nc.psum_tensor("b3_ps_r2", [128, 512], F32))
        s.ps_tr = [s.ps[i].ap().bitcast(BF16) for i in range(4)]

    def load_consts(self):
        p, s, io = self.p, self, self.io
        for c in range(NCH):
            p.dma("gpsimd", s.xbf[c][:], io["x3T"][c * 128:(c + 1) * 128, :], sem="q_ig")
        p.dma("gpsimd", s.ident[:], io["c_ident"], sem="q_ig")
        p.dma("sync", s.tokid[:], io["c_tokid"], sem="q_is")
        p.dma("sync", s.offmask[:], io["c_offmask"], sem="q_is")
        p.dma("sync", s.iotatb[:], io["c_iotatb"], sem="q_is")
        p.dma("sync", s.ctriF[:], io["c_tri"], sem="q_is")
        p.dma("sync", s.conescolF[:], io["c_onescol"], sem="q_is")
        p.dma("sync", s.cidentF[:], io["c_ident"], sem="q_is")
        for c in range(NCH):
            p.dma("sync", s.wgf[:, c * 16:(c + 1) * 16], io["Wg"][c * 128:(c + 1) * 128, :], sem="q_is")
        p.dma("sync", s.onesrowF[:], io["c_ones1"], sem="q_is")
        p.dma("sync", s.invrow[:], io["c_invrow"], sem="q_is")
        p.dma("sync", s.ptrash[:], io["c_ptrash"], sem="q_is")
        p.dma("sync", s.qgrid[:], io["c_qgrid"], sem="q_is")
        p.dma("sync", s.qgrid2[:], io["c_qgrid2"], sem="q_is")
        s.m_ig = p.get("q_ig")
        s.m_is = p.get("q_is")
        p.emit("vector", lambda e: e.memset(s.big16[:], 16.0))
        p.emit("vector", lambda e: e.memset(s.zero[:], 0.0))
        s.m_memset = p.get("dve")

    def routing(self):
        p, s, nc, io = self.p, self, self.nc, self.io
        sm = lambda i: s.esm[:, i * 32:(i + 1) * 32]
        sm3 = lambda i: s.esm[:, i * 32:(i + 1) * 32].rearrange("p (c o) -> p c o", o=1)
        bc = lambda i: sm3(i).broadcast_to([128, 32, 16])
        s.agx_in = nc.dram_tensor("b3_agx_in", [T, 1024], BF16, kind="Internal")
        s.agx_out = nc.dram_tensor("b3_agx_out", [8 * T, 1024], BF16, kind="Internal",
                                   addr_space="Shared")
        mstore_prev = {}
        for tt in range(32):
            for c in range(NCH):
                g = c % 4
                w = [("q_ig", s.m_ig)]
                if c >= 4:
                    w.append(s._xtev[(tt, c - 4)])
                elif tt > 0:
                    w.append(s._xtev[(tt - 1, c + 4)])
                mt = p.emit("tensor",
                            lambda e, tt=tt, c=c, g=g: e.transpose(
                                s.ps_tr[g][:, 0:128], s.xbf[c][:, tt * 128:(tt + 1) * 128],
                                s.ident[:]),
                            waits=w, inc="pe")
                if not hasattr(s, "_xtev"):
                    s._xtev = {}
                vw = [("pe", mt)]
                if tt > 0 and c <= 1:
                    par = tt % 2
                    if mstore_prev.get(par):
                        vw.append((("q_xta" if par == 0 else "q_xtb"), mstore_prev[par]))
                dst_lo = (tt % 2) * 1024 + c * 128
                if c % 2 == 0:
                    mcp = p.emit("vector",
                                 lambda e, dst_lo=dst_lo, g=g: e.tensor_copy(
                                     s.xtok[:, dst_lo:dst_lo + 128], s.ps_tr[g][:, 0:128]),
                                 waits=vw, inc="dve")
                    s._xtev[(tt, c)] = ("dve", mcp)
                else:
                    mcp = p.emit("scalar",
                                 lambda e, dst_lo=dst_lo, g=g: e.activation(
                                     s.xtok[:, dst_lo:dst_lo + 128], s.ps_tr[g][:, 0:128],
                                     AF.Copy),
                                 waits=vw, inc="act")
                    s._xtev[(tt, c)] = ("act", mcp)
            par = tt % 2
            mstore_prev[par] = p.dma("sync", s.agx_in.ap()[tt * 128:(tt + 1) * 128, :],
                                     s.xtok[:, par * 1024:(par + 1) * 1024],
                                     waits=[s._xtev[(tt, NCH - 2)], s._xtev[(tt, NCH - 1)]],
                                     sem="q_xta" if par == 0 else "q_xtb")
        m_xtok = (mstore_prev[0], mstore_prev[1])
        mag2 = p.emit("gpsimd",
                      lambda e: e.collective_compute(
                          "AllGather", ALU.bypass, replica_groups=[list(range(8))],
                          ins=[s.agx_in.ap().opt()], outs=[s.agx_out.ap().opt()]),
                      waits=[("q_xta", m_xtok[0]), ("q_xtb", m_xtok[1])], inc="cc2", is_dma=True)
        xf = s.hT[:].bitcast(F32)[:, 0:4096]
        mgl_prev = None
        for qt in range(NQT):
            mld = p.dma("sync", xf.rearrange("p (c t) -> p c t", c=NCH),
                        io["x3T"].rearrange("(c p) t -> p c t", p=128)[:, :, qt * QT:(qt + 1) * QT],
                        sem="q_xf", waits=[("pe", mgl_prev)] if mgl_prev else [])
            for c in range(NCH):
                w = [("q_xf", mld), ("q_is", s.m_is)] if c == 0 else []
                if c == 0 and getattr(s, "_mglv", None):
                    w.append(("dve", s._mglv))
                mgl = p.emit("tensor",
                             lambda e, qt=qt, c=c: e.matmul(
                                 s.ps_r.ap()[0:16, 0:512], s.wgf[:, c * 16:(c + 1) * 16],
                                 xf[:, c * 512:(c + 1) * 512],
                                 start=(c == 0), stop=(c == NCH - 1)),
                             waits=w, inc="pe" if c == NCH - 1 else None)
            mgl_prev = mgl
            s._mglv = p.emit("vector",
                             lambda e, qt=qt: e.tensor_copy(s.gl[0:16, qt * QT:(qt + 1) * QT],
                                                            s.ps_r.ap()[0:16, 0:512]),
                             waits=[("pe", mgl)], inc="dve")
        mgt = {}
        for tt in range(32):
            g = tt % 4
            w = [("dve", p.get("dve"))] if tt < 4 else [("dve", mgt[tt - 4])]
            mt = p.emit("tensor",
                        lambda e, tt=tt, g=g: e.transpose(
                            s.ps[g].ap()[:, 0:16], s.gl[0:16, tt * 128:(tt + 1) * 128],
                            s.cidentF[0:16, 0:16]),
                        waits=w, inc="pe")
            mgt[tt] = p.emit("vector",
                             lambda e, tt=tt, g=g: e.tensor_copy(
                                 s.gltb[:, tt * 16:(tt + 1) * 16], s.ps[g].ap()[:, 0:16]),
                             waits=[("pe", mt)], inc="dve")
        gl3 = s.gltb[:].rearrange("p (c e) -> p c e", e=16)
        t0 = s.etb[0][:].rearrange("p (c e) -> p c e", e=16)
        t1 = s.etb[1][:].rearrange("p (c e) -> p c e", e=16)
        iota3 = s.iotatb[:].rearrange("p (c e) -> p c e", e=16)
        p.emit("vector", lambda e: e.tensor_reduce(sm3(0), gl3, AX.X, ALU.max), inc="dve")
        p.emit("vector", lambda e: e.tensor_tensor(t0, gl3, bc(0), ALU.subtract), inc="dve")
        mex = p.emit("scalar", lambda e: e.activation(s.etb[1][:], s.etb[0][:], AF.Exp),
                     waits=[("dve", p.get("dve"))], inc="act")
        p.emit("vector", lambda e: e.tensor_reduce(sm3(1), t1, AX.X, ALU.add),
               waits=[("act", mex)], inc="dve")
        p.emit("vector", lambda e: e.reciprocal(sm(2), sm(1)), inc="dve")
        p.emit("vector", lambda e: e.tensor_tensor(t0, gl3, bc(0), ALU.is_ge), inc="dve")
        p.emit("vector", lambda e: e.tensor_scalar_add(s.gltb[:], s.iotatb[:], -16.0),
               waits=[("q_is", s.m_is)], inc="dve")
        p.emit("vector", lambda e: e.tensor_tensor(t1, gl3, t0, ALU.mult), inc="dve")
        p.emit("vector", lambda e: e.tensor_reduce(sm3(13), t1, AX.X, ALU.min), inc="dve")
        p.emit("vector", lambda e: e.tensor_scalar_add(sm(3), sm(13), 16.0), inc="dve")
        moh = p.emit("vector", lambda e: e.tensor_tensor(t0, iota3, bc(3), ALU.is_equal), inc="dve")
        for tt in range(32):
            w = [("dve", moh)] if tt == 0 else []
            if tt >= 2:
                w.append(("dve", s._cumev[tt - 2]))
            p.emit("tensor",
                   lambda e, tt=tt: e.matmul(
                       s.ps[4 + tt % 2].ap()[:, 0:16], s.ctriF[:],
                       t0[:, tt, :], start=True, stop=True),
                   waits=w, inc=None)
            mcc = p.emit("tensor",
                         lambda e, tt=tt: e.matmul(
                             s.ps_r2.ap()[0:16, tt:tt + 1], t0[:, tt, :], s.conescolF[:, 0:1],
                             start=True, stop=True),
                         inc="pe")
            if not hasattr(s, "_cumev"):
                s._cumev = {}
            s._cumev[tt] = p.emit("vector",
                                  lambda e, tt=tt: e.tensor_copy(
                                      s.etb[1][:].rearrange("p (c e) -> p c e", e=16)[:, tt, :],
                                      s.ps[4 + tt % 2].ap()[:, 0:16]),
                                  waits=[("pe", mcc)], inc="dve")
        m_cum = p.get("dve")
        mtt = p.emit("vector", lambda e: e.tensor_copy(s.pt16[0:16, 0:32], s.ps_r2.ap()[0:16, 0:32]),
                     waits=[("pe", p.get("pe"))], inc="dve")
        p.emit("vector", lambda e: e.tensor_reduce(s.cnt[0:16, 0:1], s.pt16[0:16, 0:32],
                                                   AX.X, ALU.add),
               inc="dve")
        p.emit("vector", lambda e: e.tensor_tensor_scan(s.pt16[0:16, 32:64], s.pt16[0:16, 0:32],
                                                        s.pt16[0:16, 0:32], 0.0, ALU.add, ALU.bypass),
               inc="dve")
        mpref = p.emit("vector", lambda e: e.tensor_sub(s.pt16[0:16, 64:96], s.pt16[0:16, 32:64],
                                                        s.pt16[0:16, 0:32]), inc="dve")
        s.agc_in = nc.dram_tensor("b3_agc_in", [16, 1], F32, kind="Internal")
        s.agc_out = nc.dram_tensor("b3_agc_out", [128, 1], F32, kind="Internal", addr_space="Shared")
        mcnt = p.dma("sync", s.agc_in.ap(), s.cnt[0:16, 0:1], sem="q_cn",
                     waits=[("dve", p.get("dve"))])
        mag1 = p.emit("gpsimd",
                      lambda e: e.collective_compute(
                          "AllGather", ALU.bypass, replica_groups=[list(range(8))],
                          ins=[s.agc_in.ap().opt()], outs=[s.agc_out.ap().opt()]),
                      waits=[("q_cn", mcnt), ("cc2", 1)], inc="cc1", is_dma=True)
        mld = p.dma("sync", s.cntall[:, 0:1], s.agc_out.ap(), sem="q_cn", waits=[("cc1", 1)])
        moff = p.emit("tensor",
                      lambda e: e.matmul(s.ps_r.ap()[0:16, 0:1], s.offmask[:], s.cntall[:, 0:1],
                                         start=True, stop=True),
                      waits=[("q_cn", mld), ("q_is", s.m_is)], inc="pe")
        p.emit("vector", lambda e: e.tensor_copy(s.offs[0:16, 0:1], s.ps_r.ap()[0:16, 0:1]),
               waits=[("pe", moff)], inc="dve")
        p.emit("vector", lambda e: e.tensor_tensor(
            s.pt16[0:16, 0:32].rearrange("p (o c) -> p o c", o=1),
            s.pt16[0:16, 64:96].rearrange("p (o c) -> p o c", o=1),
            s.offs[0:16, 0:1].rearrange("p (c o) -> p c o", o=1).broadcast_to([16, 1, 32]),
            ALU.add), inc="dve")
        mpoff = p.emit("vector", lambda e: e.tensor_scalar_add(s.pt16[0:16, 32:64],
                                                               s.pt16[0:16, 0:32], -1.0), inc="dve")
        s.poff_dram = nc.dram_tensor("b3_poff", [32, 16], F32, kind="Internal")
        mtp = p.emit("tensor",
                     lambda e: e.transpose(s.ps_r.ap()[0:32, 0:16], s.pt16[0:16, 32:64],
                                           s.cidentF[0:16, 0:16]),
                     waits=[("dve", mpoff)], inc="pe")
        mtv = p.emit("vector", lambda e: e.tensor_copy(s.big16[0:32, 0:16], s.ps_r.ap()[0:32, 0:16]),
                     waits=[("pe", mtp)], inc="dve")
        mst = p.dma("sync", s.poff_dram.ap(), s.big16[0:32, 0:16], sem="q_cn", waits=[("dve", mtv)])
        mld2 = p.dma("sync", s.prow[0:1, :], s.poff_dram.ap().rearrange("(o a) b -> o (a b)", o=1),
                     sem="q_cn", waits=[("q_cn", mst)])
        mbc = p.emit("tensor",
                     lambda e: e.matmul(s.ps_r.ap()[:, 0:512], s.onesrowF[0:1, :], s.prow[0:1, :],
                                        start=True, stop=True),
                     waits=[("q_cn", mld2), ("q_is", s.m_is)], inc="pe")
        p.emit("vector", lambda e: e.tensor_tensor(
            s.gltb[:], s.etb[1][:], s.ps_r.ap()[:, 0:512], ALU.add),
               waits=[("pe", mbc), ("dve", m_cum)], inc="dve")
        p.emit("vector", lambda e: e.tensor_tensor(t1, gl3, t0, ALU.mult), inc="dve")
        mslot = p.emit("vector", lambda e: e.tensor_reduce(sm3(4), t1, AX.X, ALU.add), inc="dve")
        p.emit("vector", lambda e: e.tensor_single_scalar(sm(5), sm(4), 2048.0, ALU.is_lt), inc="dve")
        p.emit("vector", lambda e: e.tensor_tensor(sm(6), sm(2), sm(5), ALU.mult), inc="dve")
        grid3 = s.etb[0][:].rearrange("p (c j) -> p c j", j=16)
        p.emit("vector", lambda e: e.tensor_tensor(
            grid3, sm3(4).broadcast_to([128, 32, 16]),
            s.qgrid[:].rearrange("p (o j) -> p o j", o=1).broadcast_to([128, 32, 16]),
            ALU.is_ge), inc="dve")
        p.emit("vector", lambda e: e.tensor_reduce(sm3(7), grid3, AX.X, ALU.add), inc="dve")
        p.emit("vector", lambda e: e.tensor_tensor(
            grid3, sm3(4).broadcast_to([128, 32, 16]),
            s.qgrid[:].rearrange("p (o j) -> p o j", o=1).broadcast_to([128, 32, 16])
            if False else
            s.qgrid2[:].rearrange("p (o j) -> p o j", o=1).broadcast_to([128, 32, 16]),
            ALU.is_ge), inc="dve")
        p.emit("vector", lambda e: e.tensor_reduce(sm3(8), grid3, AX.X, ALU.add), inc="dve")
        p.emit("vector", lambda e: e.tensor_tensor(sm(9), sm(7), sm(8), ALU.add), inc="dve")
        p.emit("vector", lambda e: e.scalar_tensor_tensor(sm(7), sm(9), -128.0, sm(4),
                                                          ALU.mult, ALU.add), inc="dve")
        p.emit("vector", lambda e: e.scalar_tensor_tensor(sm(10), sm(7), 16.0, sm(9),
                                                          ALU.mult, ALU.add), inc="dve")
        p.emit("vector", lambda e: e.scalar_tensor_tensor(sm(11), sm(3), 2048.0, sm(10),
                                                          ALU.mult, ALU.add), inc="dve")
        p.emit("vector", lambda e: e.tensor_tensor(sm(13), sm(11), s.ptrash[:], ALU.subtract),
               inc="dve")
        p.emit("vector", lambda e: e.tensor_tensor(sm(8), sm(13), sm(5), ALU.mult), inc="dve")
        p.emit("vector", lambda e: e.tensor_tensor(sm(12), sm(8), s.ptrash[:], ALU.add), inc="dve")
        mdu = p.emit("vector", lambda e: e.tensor_copy(s.idxu[:, 0:32], sm(12)), inc="dve")
        p.emit("vector", lambda e: e.scalar_tensor_tensor(sm(14), sm(3), 2048.0, sm(4),
                                                          ALU.mult, ALU.add), inc="dve")
        p.emit("vector", lambda e: e.tensor_tensor(sm(15), sm(14), sm(5), ALU.mult), inc="dve")
        mgu = p.emit("vector", lambda e: e.tensor_copy(s.idxu[:, 32:64], sm(15)), inc="dve")
        s.inv_dram = nc.dram_tensor("b3_inv", [36864, 1], F32, kind="Internal")
        s.inv_out = nc.dram_tensor("b3_inv_out", [36864, 1], F32, kind="Internal",
                                   addr_space="Shared")
        mz = None
        for i in range(2):
            mz = p.dma("sync", s.inv_dram.ap().rearrange("(a p f) o -> a p (f o)", a=2, p=128)[i],
                       s.zero[:, 0:144], sem="q_zf", waits=[("dve", s.m_memset)])
        msc = None
        for tt in range(32):
            w = ([("dve", mdu), ("q_zf", p.get("q_zf")), ("q_is", s.m_is)]
                 if tt == 0 else [("q_sc", msc)])
            msc = p.emit("gpsimd",
                         lambda e, tt=tt: e.indirect_dma_start(
                             out=s.inv_dram.ap(),
                             out_offset=IndirectOffsetOnAxis(s.idxu[:, tt:tt + 1], 0),
                             in_=s.tokid[:, tt:tt + 1], in_offset=None),
                         waits=w, inc="q_sc", inc_n=16, is_dma=True)
        mar = p.emit("gpsimd",
                     lambda e: e.collective_compute(
                         "AllReduce", ALU.add, replica_groups=[list(range(8))],
                         ins=[s.inv_dram.ap().opt()], outs=[s.inv_out.ap().opt()]),
                     waits=[("q_sc", p.get("q_sc")), ("cc1", 1)], inc="cc3", is_dma=True)
        s.m_routing = (mgu, mar)

    def experts(self):
        p, s, nc, io = self.p, self, self.nc, self.io
        s.w1bf = io["_wch"]["w1bf"]
        s.w2bf = io["_wch"]["w2bf"]
        s.m_wcache = io["_wch_m"]
        s.ago_in = nc.dram_tensor("b3_ago_in", [T, 1024], BF16, kind="Internal")
        s.ago_out = nc.dram_tensor("b3_ago_out", [8 * T, 1024], BF16, kind="Internal",
                                   addr_space="Shared")
        inv2d = s.inv_out.ap().rearrange("(r j) o -> r (j o)", j=16)
        dispT = lambda dc: s.xbf[4 + dc // 2][:, (dc % 2) * 2048:(dc % 2 + 1) * 2048]
        exp_guard = (None, None)
        for ex in range(2):
            mgi = p.emit("gpsimd",
                         lambda e, ex=ex: e.indirect_dma_start(
                             out=s.invf[:], out_offset=None,
                             in_=inv2d, in_offset=IndirectOffsetOnAxis(s.invrow[:, ex:ex + 1], 0)),
                         waits=[("cc3", 1), ("q_is", s.m_is)] + (
                             [("dve", exp_guard[1])] if exp_guard[1] else []),
                         inc="q_gi", inc_n=16, is_dma=True)
            mcv = p.emit("vector", lambda e: e.tensor_copy(s.invu[:], s.invf[:]),
                         waits=[("q_gi", p.get("q_gi"))], inc="dve")
            mg_miles = []
            for g in range(16):
                dst = s.xbf[g // 4][:, (g % 4) * 1024:(g % 4 + 1) * 1024]
                w = [("dve", mcv), ("cc2", 1)]
                if exp_guard[0]:
                    w.append(("pe", exp_guard[0]))
                mg_miles.append(p.emit("gpsimd",
                                       lambda e, g=g, dst=dst: e.indirect_dma_start(
                                           out=dst, out_offset=None,
                                           in_=s.agx_out.ap(),
                                           in_offset=IndirectOffsetOnAxis(s.invu[:, g:g + 1], 0)),
                                       waits=w if g == 0 else [], inc="q_gx", inc_n=16,
                                       is_dma=True))
            m_gx = mg_miles[-1]
            mtr = {}
            for g in range(16):
                for dc in range(8):
                    q = (g * 8 + dc) % 4
                    w = ([("q_gx", m_gx), ("act", p.get("act"))]
                         if (g == 0 and dc == 0) else [])
                    if mtr.get(q):
                        w.append(mtr[q])
                    mt = p.emit("tensor",
                                lambda e, g=g, dc=dc, q=q: e.transpose(
                                    s.ps_tr[q][:, 0:128],
                                    s.xbf[g // 4][:, (g % 4) * 1024 + dc * 128:
                                                  (g % 4) * 1024 + (dc + 1) * 128],
                                    s.ident[:]),
                                waits=w, inc="pe")
                    db, dlo = 4 + dc // 2, (dc % 2) * 2048 + g * 128
                    if dc % 2 == 0:
                        mtr[q] = ("dve", p.emit("vector",
                                                lambda e, db=db, dlo=dlo, q=q: e.tensor_copy(
                                                    s.xbf[db][:, dlo:dlo + 128],
                                                    s.ps_tr[q][:, 0:128]),
                                                waits=[("pe", mt)], inc="dve"))
                    else:
                        mtr[q] = ("act", p.emit("scalar",
                                                lambda e, db=db, dlo=dlo, q=q: e.activation(
                                                    s.xbf[db][:, dlo:dlo + 128],
                                                    s.ps_tr[q][:, 0:128], AF.Copy),
                                                waits=[("pe", mt)], inc="act"))
            m_dispT = (p.get("dve"), p.get("act"))
            for st in range(4):
                wld_prev = {}
                relu_prev = {}
                mpe = None
                for fc8 in range(8):
                    par = fc8 % 2
                    w = [("q_wch", s.m_wcache)]
                    if wld_prev.get(par):
                        w.append(("pe", wld_prev[par]))
                    if st > 0 or ex > 0:
                        w.append(("pe", s._w2done)) if getattr(s, "_w2done", None) else None
                    mldw = p.dma("sync", s.wst[par][:].rearrange("p (c f) -> p c f", c=8),
                                 s.w1bf.ap()[ex * 1024:(ex + 1) * 1024,
                                             fc8 * 512:(fc8 + 1) * 512]
                                 .rearrange("(c p) f -> p c f", p=128),
                                 sem="q_w1a" if par == 0 else "q_w1b", waits=w)
                    for f2 in range(4):
                        fc = fc8 * 4 + f2
                        for dc in range(8):
                            w2 = []
                            if dc == 0:
                                w2 = [(("q_w1a" if par == 0 else "q_w1b"), mldw)]
                                if fc8 == 0 and f2 == 0:
                                    w2.append(("dve", m_dispT[0]))
                                    w2.append(("act", m_dispT[1]))
                                if relu_prev.get(f2 % 2):
                                    w2.append(("act", relu_prev[f2 % 2]))
                            mpe = p.emit("tensor",
                                         lambda e, f2=f2, dc=dc, st=st, par=par: e.matmul(
                                             s.ps[4 + f2 % 2].ap()[:, 0:512],
                                             s.wst[par][:].rearrange("p (c f) -> p c f", c=8)[
                                                 :, dc, f2 * 128:(f2 + 1) * 128],
                                             dispT(dc)[:, st * 512:(st + 1) * 512],
                                             start=(dc == 0), stop=(dc == 7)),
                                         waits=w2, inc="pe" if dc == 7 else None)
                        relu_prev[f2 % 2] = p.emit("scalar",
                                                   lambda e, fc=fc, f2=f2: e.activation(
                                                       s.hT[:, fc * 512:(fc + 1) * 512],
                                                       s.ps[4 + f2 % 2].ap()[:, 0:512], AF.Relu),
                                                   waits=[("pe", mpe)], inc="act")
                    wld_prev[par] = mpe
                m_h = p.get("act")
                psof = lambda oc: (s.ps[oc] if oc < 6 else (s.ps_r if oc == 6 else s.ps_r2))
                mld2 = None
                mpo = None
                for w2g in range(8):
                    par = w2g % 2
                    w = [("act", m_h)]
                    if w2g >= 2:
                        w.append(("pe", s._w2mm[par]))
                    mld2 = p.dma("sync", s.wst[par][:].rearrange("p (c o) -> p c o", c=4),
                                 s.w2bf.ap()[ex * 4096 + w2g * 512:ex * 4096 + (w2g + 1) * 512, :]
                                 .rearrange("(c p) o -> p c o", p=128),
                                 sem="q_w2a" if par == 0 else "q_w2b", waits=w)
                    for oc in range(8):
                        for c4 in range(4):
                            fch = w2g * 4 + c4
                            w3 = []
                            if c4 == 0 and oc == 0:
                                w3 = [(("q_w2a" if par == 0 else "q_w2b"), mld2)]
                                if w2g == 0:
                                    w3.append(("dve", getattr(s, "_oev_last", None)))
                            mpo = p.emit("tensor",
                                         lambda e, oc=oc, c4=c4, fch=fch, par=par: e.matmul(
                                             psof(oc).ap()[:, 0:512],
                                             s.wst[par][:].rearrange("p (c o) -> p c o", c=4)[
                                                 :, c4, oc * 128:(oc + 1) * 128],
                                             s.hT[:, fch * 512:(fch + 1) * 512],
                                             start=(fch == 0), stop=(fch == 31)),
                                         waits=[x for x in w3 if x],
                                         inc="pe" if (oc == 7 and c4 == 3) else None)
                    if not hasattr(s, "_w2mm"):
                        s._w2mm = {}
                    s._w2mm[par] = mpo
                s._w2done = mpo
                oev = None
                for oc in range(8):
                    oev = p.emit("vector",
                                 lambda e, oc=oc: e.tensor_copy(
                                     s.xbf[0][:, oc * 512:(oc + 1) * 512], psof(oc).ap()[:, 0:512]),
                                 waits=[("pe", s._w2done)] if oc == 0 else [], inc="dve")
                s._oev_last = oev
                mot = getattr(s, "_mot", None)
                for tl in range(4):
                    for oc in range(8):
                        q = (tl * 8 + oc) % 4
                        w = [("dve", oev)] if (tl == 0 and oc == 0) else []
                        if mtr.get(("o", q)):
                            w.append(mtr[("o", q)])
                        mt = p.emit("tensor",
                                    lambda e, tl=tl, oc=oc, q=q: e.transpose(
                                        s.ps_tr[q][:, 0:128],
                                        s.xbf[0][:, oc * 512 + tl * 128:oc * 512 + (tl + 1) * 128],
                                        s.ident[:]),
                                    waits=w, inc="pe")
                        cw = [("pe", mt)]
                        if oc <= 1 and mot:
                            cw.append(("q_oa", mot))
                        if oc % 2 == 0:
                            mtr[("o", q)] = ("dve", p.emit("vector",
                                                           lambda e, oc=oc, q=q: e.tensor_copy(
                                                               s.gout[:, oc * 128:(oc + 1) * 128],
                                                               s.ps_tr[q][:, 0:128]),
                                                           waits=cw, inc="dve"))
                        else:
                            mtr[("o", q)] = ("act", p.emit("scalar",
                                                           lambda e, oc=oc, q=q: e.activation(
                                                               s.gout[:, oc * 128:(oc + 1) * 128],
                                                               s.ps_tr[q][:, 0:128], AF.Copy),
                                                           waits=cw, inc="act"))
                    mot = p.dma("sync",
                                s.ago_in.ap()[ex * 2048 + st * 512 + tl * 128:
                                              ex * 2048 + st * 512 + (tl + 1) * 128, :],
                                s.gout[:], sem="q_oa",
                                waits=[("dve", p.get("dve")), ("act", p.get("act"))])
                    s._mot = mot
            exp_guard = (p.get("pe"), p.get("dve"))
        s.m_ago = p.get("q_oa")
        mag3 = p.emit("gpsimd",
                      lambda e: e.collective_compute(
                          "AllGather", ALU.bypass, replica_groups=[list(range(8))],
                          ins=[s.ago_in.ap().opt()], outs=[s.ago_out.ap().opt()]),
                      waits=[("q_oa", s.m_ago), ("cc3", 1)], inc="cc4", is_dma=True)

    def final(self):
        p, s, io = self.p, self, self.io
        gout2 = s.wst[1][:, 0:1024]
        resf2 = s.wst[0][:].bitcast(F32)[:, 0:1024]
        mres_prev = {}
        mact_prev = {}
        mo_prev = {}
        eps = 1e-5
        for tt in range(32):
            par = tt % 2
            gb = s.gout[:] if par == 0 else gout2
            rb = s.resf[:] if par == 0 else resf2
            xt = s.xtok[:, par * 1024:(par + 1) * 1024]
            Lt = s.lnm if par == 0 else s.invf
            L = lambda i: Lt[:, i:i + 1]
            w = ([("cc4", 1), ("dve", p.get("dve")), ("act", p.get("act"))]
                 if tt == 0 else [])
            if mres_prev.get(par):
                w.append(("dve", mres_prev[par]))
            qgb = "q_gb" if par == 0 else "q_gb2"
            qxr = "q_xr" if par == 0 else "q_xr2"
            mgr = p.emit("gpsimd",
                         lambda e, tt=tt, gb=gb: e.indirect_dma_start(
                             out=gb, out_offset=None,
                             in_=s.ago_out.ap(),
                             in_offset=IndirectOffsetOnAxis(s.idxu[:, 32 + tt:33 + tt], 0)),
                         waits=w, inc=qgb, inc_n=16, is_dma=True)
            mxl = p.dma("gpsimd", xt, s.agx_in.ap()[tt * 128:(tt + 1) * 128, :],
                        sem=qxr, waits=[("q_xta", p.get("q_xta")), ("q_xtb", p.get("q_xtb"))]
                        if tt == 0 else [])
            rw = [(qgb, p.get(qgb)), (qxr, p.get(qxr))]
            if mact_prev.get(par):
                rw.append(("act", mact_prev[par]))
            mres = p.emit("vector",
                          lambda e, tt=tt, gb=gb, rb=rb, xt=xt: e.scalar_tensor_tensor(
                              rb, gb, s.esm[:, 6 * 32 + tt:6 * 32 + tt + 1],
                              xt, ALU.mult, ALU.add),
                          waits=rw, inc="dve")
            mres_prev[par] = mres
            p.emit("scalar", lambda e, rb=rb, L=L: e.activation(s.sdump[:], rb, AF.Square,
                                                                accum_out=L(2)),
                   waits=[("dve", mres)], inc="act")
            mcp = p.emit("scalar", lambda e, rb=rb, L=L: e.activation(s.sdump[:], rb, AF.Copy,
                                                                      accum_out=L(0)),
                         inc="act")
            mact_prev[par] = mcp
            p.emit("scalar", lambda e, L=L: e.activation(L(1), L(0), AF.Copy, scale=1.0 / 1024),
                   inc="act")
            mmsq = p.emit("scalar", lambda e, L=L: e.activation(L(3), L(2), AF.Copy,
                                                                scale=1.0 / 1024, bias=eps),
                          inc="act")
            p.emit("vector", lambda e, L=L: e.tensor_tensor(L(4), L(1), L(1), ALU.mult),
                   waits=[("act", mmsq)], inc="dve")
            mvar = p.emit("vector", lambda e, L=L: e.tensor_sub(L(5), L(3), L(4)), inc="dve")
            p.emit("scalar", lambda e, L=L: e.activation(L(6), L(5), AF.Ln),
                   waits=[("dve", mvar)], inc="act")
            mrs = p.emit("scalar", lambda e, L=L: e.activation(L(7), L(6), AF.Exp, scale=-0.5),
                         inc="act")
            ob = s.outf[par]
            w4 = [("act", mrs)]
            if mo_prev.get(par):
                w4.append((("q_fa" if par == 0 else "q_fb"), mo_prev[par]))
            mfin = p.emit("vector",
                          lambda e, ob=ob, rb=rb, L=L: e.tensor_scalar(
                              ob[:], rb, L(1), L(7), ALU.subtract, ALU.mult),
                          waits=w4, inc="dve")
            mo_prev[par] = p.dma("sync", io["outTOK"][tt * 128:(tt + 1) * 128, :], ob[:],
                                 waits=[("dve", mfin)], sem="q_fa" if par == 0 else "q_fb")
        p.emit("sync", lambda e: None,
               waits=[("q_fa", p.get("q_fa")), ("q_fb", p.get("q_fb"))])

    def run(self):
        p, s = self.p, self
        s.alloc()
        s.load_consts()
        self.routing()
        self.experts()
        self.final()


from concourse.bass_utils import run_bass_kernel_spmd

_NCACHE = {}


def _build_fused():
    nc = bass.Bass("TRN2", target_bir_lowering=False)
    io = {}
    io["xT"] = nc.dram_tensor("xT", [D, T], F32, kind="ExternalInput").ap()
    for wn in ["wq", "wk", "wv", "wo"]:
        io[wn] = nc.dram_tensor(wn, [D, D], F32, kind="ExternalInput").ap()
    for bn in ["bq", "bk", "bv", "bo"]:
        io[bn] = nc.dram_tensor(bn, [D], F32, kind="ExternalInput").ap()
    for cn in CONST_SPECS:
        io[cn] = nc.dram_tensor(cn, list(CONST_SPECS[cn]), F32, kind="ExternalInput").ap()
    io["proj_f"] = nc.dram_tensor("proj_f", [D, 16], F32, kind="ExternalInput").ap()
    io["C1"] = nc.dram_tensor("C1", [16, 512], F32, kind="ExternalInput").ap()
    io["C2"] = nc.dram_tensor("C2", [16, 512], F32, kind="ExternalInput").ap()
    io["Win1"] = nc.dram_tensor("Win1", [1025, 512], F32, kind="ExternalInput").ap()
    io["Win2"] = nc.dram_tensor("Win2", [4097, 512], F32, kind="ExternalInput").ap()
    io["Wout1"] = nc.dram_tensor("Wout1", [4096, 512], F32, kind="ExternalInput").ap()
    io["Wout2"] = nc.dram_tensor("Wout2", [1024, 512], F32, kind="ExternalInput").ap()
    for wn in ["q_w", "k_w", "v_w", "out_w"]:
        io[wn] = nc.dram_tensor(wn, [D, D], F32, kind="ExternalInput").ap()
    for bn in ["q_b", "k_b", "v_b", "out_b"]:
        io[bn] = nc.dram_tensor(bn, [D], F32, kind="ExternalInput").ap()
    io["proj"] = nc.dram_tensor("proj2", [256, 64], F32, kind="ExternalInput").ap()
    io["gamma"] = nc.dram_tensor("gamma", [D], F32, kind="ExternalInput").ap()
    io["beta"] = nc.dram_tensor("beta", [D], F32, kind="ExternalInput").ap()
    io["Wg"] = nc.dram_tensor("Wg", [D, 16], F32, kind="ExternalInput").ap()
    io["w1my"] = nc.dram_tensor("w1my", [2, 1024, 4096], F32, kind="ExternalInput").ap()
    io["w2my"] = nc.dram_tensor("w2my", [2, 4096, 1024], F32, kind="ExternalInput").ap()
    io["c_tokid"] = nc.dram_tensor("c_tokid", [128, 32], F32, kind="ExternalInput").ap()
    io["c_offmask"] = nc.dram_tensor("c_offmask", [128, 16], F32, kind="ExternalInput").ap()
    io["c_iotatb"] = nc.dram_tensor("c_iotatb", [128, 512], F32, kind="ExternalInput").ap()
    io["c_invrow"] = nc.dram_tensor("c_invrow", [128, 2], U32, kind="ExternalInput").ap()
    io["c_ptrash"] = nc.dram_tensor("c_ptrash", [128, 32], F32, kind="ExternalInput").ap()
    io["c_qgrid"] = nc.dram_tensor("c_qgrid", [128, 16], F32, kind="ExternalInput").ap()
    io["c_qgrid2"] = nc.dram_tensor("c_qgrid2", [128, 16], F32, kind="ExternalInput").ap()
    io["outTOK"] = nc.dram_tensor("outTOK", [T, D], F32, kind="ExternalOutput").ap()
    io["x1T"] = nc.dram_tensor("h_x1T", [D, T], F32, kind="Internal").ap()
    io["x2T"] = nc.dram_tensor("h_x2T", [D, T], F32, kind="Internal").ap()
    io["x3T"] = nc.dram_tensor("h_x3T", [D, T], F32, kind="Internal").ap()

    p = Prog(nc)
    with contextlib.ExitStack() as outer:
        with contextlib.ExitStack() as st0:
            b0 = B0(nc, p, st0, io)
            b0.run()
        emit_early_wcaches(nc, p, io, gates=[("q_wsl", m) for m in b0.wsl_miles])
        p.barrier()
        io1 = dict(io)
        io1["proj"] = io["proj_f"]
        with contextlib.ExitStack() as st1:
            B1(nc, p, st1, io1).run()
        p.barrier()
        io2 = dict(io)
        with contextlib.ExitStack() as st2:
            B2(nc, p, st2, io2).run()
        p.barrier()
        io3 = dict(io)
        with contextlib.ExitStack() as st3:
            B3(nc, p, st3, io3, 0).run()
        p.build(outer)
    return nc


def _get(name, builder):
    if name not in _NCACHE:
        _NCACHE[name] = builder()
    return _NCACHE[name]


def kernel(**ins):
    consts = host_constants()
    p_ = np.arange(128, dtype=np.float32)[:, None]
    x = np.asarray(ins["L0_attn_x"])
    nc = _get("fused", _build_fused)
    maps = []
    for c in range(8):
        m = dict(
            xT=np.ascontiguousarray(x[c].T),
            wq=ins["L0_attn_Wq"], wk=ins["L0_attn_Wk"], wv=ins["L0_attn_Wv"],
            wo=ins["L0_attn_Wo"],
            bq=ins["L0_attn_bq"], bk=ins["L0_attn_bk"], bv=ins["L0_attn_bv"],
            bo=ins["L0_attn_bo"],
            proj_f=ins["L0_ffn_proj"], C1=ins["L0_ffn_C1"], C2=ins["L0_ffn_C2"],
            Win1=ins["L0_ffn_Win1"], Win2=ins["L0_ffn_Win2"],
            Wout1=ins["L0_ffn_Wout1"], Wout2=ins["L0_ffn_Wout2"],
            q_w=ins["L1_attn_q_w"], k_w=ins["L1_attn_k_w"], v_w=ins["L1_attn_v_w"],
            out_w=ins["L1_attn_out_w"],
            q_b=ins["L1_attn_q_b"], k_b=ins["L1_attn_k_b"], v_b=ins["L1_attn_v_b"],
            out_b=ins["L1_attn_out_b"],
            proj2=ins["L1_attn_proj"], gamma=ins["L1_attn_gamma"], beta=ins["L1_attn_beta"],
            Wg=ins["L1_ffn_Wg"],
            w1my=np.ascontiguousarray(np.asarray(ins["L1_ffn_w1"])[2 * c:2 * c + 2]),
            w2my=np.ascontiguousarray(np.asarray(ins["L1_ffn_w2"])[2 * c:2 * c + 2]),
            c_tokid=(4096.0 * c + 128.0 * np.arange(32, dtype=np.float32)[None, :]
                     + p_).astype(np.float32),
            c_offmask=np.zeros((128, 16), dtype=np.float32),
            c_iotatb=np.tile(np.arange(16, dtype=np.float32)[None, :], (128, 32)).reshape(128, 512),
            c_invrow=((2 * c + np.arange(2, dtype=np.uint32)[None, :]) * 128
                      + np.arange(128, dtype=np.uint32)[:, None]).astype(np.uint32),
            c_ptrash=(32768.0 + 128.0 * np.arange(32, dtype=np.float32)[None, :]
                      + p_).astype(np.float32),
            c_qgrid=np.tile((128.0 * np.arange(1, 17, dtype=np.float32)).reshape(1, 16), (128, 1)),
            c_qgrid2=np.tile(np.concatenate(
                [128.0 * np.arange(17, 32, dtype=np.float32), [1e9]]).astype(
                    np.float32).reshape(1, 16), (128, 1)),
        )
        for b in range(c):
            for e in range(16):
                m["c_offmask"][16 * b + e, e] = 1.0
        m.update(consts)
        maps.append(m)
    res = run_bass_kernel_spmd(nc, maps, core_ids=list(range(8)))
    out = np.stack([np.asarray(res.results[c]["outTOK"]) for c in range(8)], axis=0)
    return out.astype(np.float32)


# version 09
# speedup vs baseline: 1.0257x; 1.0257x over previous
import contextlib
import numpy as np
import ml_dtypes
import concourse.bass as bass
import concourse.mybir as mybir
from concourse.bass_core_rust import IndirectOffsetOnAxis

F32 = mybir.dt.float32
BF16 = mybir.dt.bfloat16
U32 = mybir.dt.uint32
AF = mybir.ActivationFunctionType
ALU = mybir.AluOpType
AX = mybir.AxisListType

T = 4096
D = 1024
ENGS = ("sync", "gpsimd", "tensor", "vector", "scalar")
QT = 512
NQT = T // QT
NCH = D // 128


def emit_early_wcaches(nc, p, io, gates=()):
    wch = {}
    m = None
    nd = [0]
    gates = list(gates)

    def gw():
        w = [gates[min(nd[0] // 2, len(gates) - 1)]] if gates else []
        nd[0] += 1
        return w
    if "Win2" in io:
        wch["win2"] = nc.dram_tensor("b1_win2bf", [4096, 512], BF16, kind="Internal")
        for i in range(4):
            m = p.dma("gpsimd", wch["win2"].ap()[i * 1024:(i + 1) * 1024, :],
                      io["Win2"][i * 1024:(i + 1) * 1024, :], sem="q_wch", waits=gw())
    for wn in ["q_w", "k_w"]:
        if wn in io:
            wch[wn] = nc.dram_tensor(f"b2_{wn}bf", [1024, 1024], BF16, kind="Internal")
            for i in range(2):
                m = p.dma("gpsimd", wch[wn].ap()[i * 512:(i + 1) * 512, :],
                          io[wn][i * 512:(i + 1) * 512, :], sem="q_wch", waits=gw())
    if "w1my" in io:
        wch["w1bf"] = nc.dram_tensor("b3_w1bf", [2 * 1024, 4096], BF16, kind="Internal")
        wch["w2bf"] = nc.dram_tensor("b3_w2bf", [2 * 4096, 1024], BF16, kind="Internal")
        for ex in range(2):
            for i in range(2):
                m = p.dma("gpsimd",
                          wch["w1bf"].ap()[ex * 1024 + i * 512:ex * 1024 + (i + 1) * 512, :],
                          io["w1my"][ex, i * 512:(i + 1) * 512, :], sem="q_wch", waits=gw())
        for ex in range(2):
            for i in range(2):
                m = p.dma("gpsimd",
                          wch["w2bf"].ap()[ex * 4096 + i * 2048:ex * 4096 + (i + 1) * 2048, :],
                          io["w2my"][ex, i * 2048:(i + 1) * 2048, :], sem="q_wch", waits=gw())
    io["_wch"] = wch
    io["_wch_m"] = m


class Prog:
    def __init__(self, nc):
        self.nc = nc
        self.ops = []
        self.ctr = {}
        self.sems = {}
        self.sem_names = []

    def sem(self, name):
        if name not in self.ctr:
            self.ctr[name] = 0
            self.sem_names.append(name)
        return name

    SELF_SEM = {"vector": "dve", "scalar": "act", "gpsimd": "gp"}

    def get(self, name):
        self.sem(name)
        return self.ctr[name]

    def emit(self, eng, fn, waits=(), inc=None, inc_n=1, is_dma=False):
        waits = list((s, v) for (s, v) in waits if v)
        selfsem = self.SELF_SEM.get(eng)
        if selfsem is not None and not is_dma:
            assert inc in (None, selfsem), (eng, inc)
            prev = self.get(selfsem)
            if prev:
                waits.append((selfsem, prev))
            inc = selfsem
        incs = []
        val = None
        if inc is not None:
            self.sem(inc)
            self.ctr[inc] += inc_n
            val = self.ctr[inc]
            incs.append((inc, inc_n))
        for (s, v) in waits:
            self.sem(s)
        self.ops.append((eng, fn, tuple(waits), tuple(incs)))
        return val

    def dma(self, queue, out, in_, waits=(), sem=None, **kw):
        semn = self.sem(sem if sem else ("q_" + queue))
        return self.emit(queue, lambda e: e.dma_start(out=out, in_=in_, **kw),
                         waits=waits, inc=semn, inc_n=16, is_dma=True)

    def barrier(self):
        waits = [(name, self.ctr[name]) for name in self.sem_names if self.ctr[name] > 0]
        for eng in ENGS:
            self.emit(eng, lambda e: None, waits=waits, is_dma=True)

    def build(self, stack):
        E = stack.enter_context
        for name in self.sem_names:
            self.sems[name] = E(self.nc.semaphore(name))
        block = E(self.nc.Block())
        prog = self

        def make(engname):
            def body(eng):
                for (e, fn, waits, incs) in prog.ops:
                    if e != engname:
                        continue
                    for (s, v) in waits:
                        eng.wait_ge(prog.sems[s], v)
                    inst = fn(eng)
                    for (s, n) in incs:
                        if inst is None:
                            inst = eng.nop()
                        inst = inst.then_inc(prog.sems[s], n)
            return body

        block.sync(make("sync"))
        block.gpsimd(make("gpsimd"))
        block.tensor(make("tensor"))
        block.vector(make("vector"))
        block.scalar(make("scalar"))


def host_constants():
    c = {}
    c["c_ident"] = np.eye(128, dtype=np.float32)
    P = np.zeros((128, 128), dtype=np.float32)
    for h0 in (0, 64):
        for d_ in range(32):
            P[h0 + d_ + 32, h0 + d_] = -1.0
            P[h0 + d_, h0 + d_ + 32] = 1.0
    c["c_prot"] = P
    hd, base = 64, 10000.0
    inv_freq = 1.0 / base ** (np.arange(0, hd, 2, dtype=np.float32) / hd)
    t = np.arange(T, dtype=np.float32)
    freqs = np.einsum("i,j->ij", t, inv_freq)
    emb = np.concatenate([freqs, freqs], axis=-1)
    cos = np.cos(emb).astype(np.float32).T
    sin = np.sin(emb).astype(np.float32).T
    c["c_cos"] = np.concatenate([cos, cos], axis=0)
    c["c_sin"] = np.concatenate([sin, sin], axis=0)
    dk = np.arange(128)[:, None]
    dq = np.arange(128)[None, :]
    c["c_tri"] = (dk <= dq).astype(np.float32)
    c["c_ones1"] = np.ones((1, 128), dtype=np.float32)
    c["c_onescol"] = np.ones((128, 1), dtype=np.float32)
    c["c_negrow"] = np.full((1, 512), -1.0, dtype=np.float32)
    return c


CONST_SPECS = {
    "c_ident": (128, 128), "c_prot": (128, 128), "c_cos": (128, T), "c_sin": (128, T),
    "c_tri": (128, 128), "c_ones1": (1, 128), "c_onescol": (128, 1), "c_negrow": (1, 512),
}


class B0:

    def __init__(self, nc, p, stack, io):
        self.nc, self.p, self.stack, self.io = nc, p, stack, io
        self.E = stack.enter_context

    def alloc(self):
        nc, E = self.nc, self.E
        s = self
        s.xbf = [E(nc.sbuf_tensor(f"b0_xbf{c}", [128, T], BF16)) for c in range(NCH)]
        s.attnT = [E(nc.sbuf_tensor(f"b0_attnT{c}", [128, T], BF16)) for c in range(NCH)]
        s.qraw = E(nc.sbuf_tensor("b0_qraw", [128, T], BF16))
        s.kraw = E(nc.sbuf_tensor("b0_kraw", [128, T], BF16))
        s.qr = E(nc.sbuf_tensor("b0_qr", [128, T], BF16))
        s.kr = E(nc.sbuf_tensor("b0_kr", [128, T], BF16))
        s.vhat = E(nc.sbuf_tensor("b0_vhat", [128, 32 * 130], BF16))
        s.wsl = [E(nc.sbuf_tensor("b0_wsl0", [128, 3 * 1024], BF16))]
        s.ptile = [E(nc.sbuf_tensor(f"b0_ptile{i}", [128, 1024], BF16)) for i in range(3)]
        s.pscr = E(nc.sbuf_tensor("b0_pscr", [128, 256], BF16))
        s.ropes = E(nc.sbuf_tensor("b0_ropes", [128, 2 * QT], F32))
        s.numbuf = E(nc.sbuf_tensor("b0_numbuf", [128, QT], F32))
        s.lrec = E(nc.sbuf_tensor("b0_lrec", [1, 2 * QT], BF16))
        s.lcat = s.ropes[0:1, 0:2 * QT]
        s.lln = s.ropes[32:33, 0:2 * QT]
        s.resbuf = s.vhat[:, 0:NCH * QT]

        _krf = s.kr[:].bitcast(F32)
        _qrf = s.qr[:].bitcast(F32)
        s.rowbuf = _krf[0:1, 0:2048]
        s.rowbf = None
        s.x1buf = [_qrf[:, 1024 + i * QT:1024 + (i + 1) * QT] for i in range(2)]
        s.x1tmp = None
        s.biases = E(nc.sbuf_tensor("b0_biases", [128, 4 * NCH], F32))
        s.bvrow = E(nc.sbuf_tensor("b0_bvrow", [1, 1024], BF16))
        s.woT = [(s.qraw if c < 4 else s.kraw)[:, (c % 4) * 1024:(c % 4 + 1) * 1024] for c in range(NCH)]
        s.wtmp = [s.qr[:, 0:1024], s.qr[:, 1024:2048]]
        s.stripes = [s.attnT[i][:, 0:1024] for i in range(NCH)]
        s.cst = {}
        for name in ["c_ident", "c_prot", "c_cos", "c_sin", "c_tri", "c_ones1", "c_onescol"]:
            s.cst[name] = E(nc.sbuf_tensor("b0_" + name, list(CONST_SPECS[name]), BF16))
        s.rowbf = s.lrec[0:1, 0:2 * QT]
        s.x1tmp = s.numbuf[:, 0:QT]
        s.rowslot4 = _qrf[0:1, 0:QT]
        s.rowslotE = _qrf[0:1, QT:2 * QT]
        s.sqscr = [s.ptile[0][:, 0:QT], s.ptile[0][:, QT:2 * QT]]
        s.ps_s = [E(nc.psum_tensor(f"b0_ps_s{i}", [128, 1024], F32)) for i in range(2)]
        s.ps_va = E(nc.psum_tensor("b0_ps_va", [128, 512], F32))
        s.ps_vb = E(nc.psum_tensor("b0_ps_vb", [128, 512], F32))
        s.ps_mm = E(nc.psum_tensor("b0_ps_mm", [128, 512], F32))
        s.ps_bc = E(nc.psum_tensor("b0_ps_bc", [128, 512], F32))
        s.ps_tr = [s.ps_s[0].ap().bitcast(BF16), s.ps_s[1].ap().bitcast(BF16)]
        s.ps_bc2 = s.ps_vb

    def load_consts(self):
        p, s, io = self.p, self, self.io
        s.mc = {}
        for name in ["c_ident", "c_prot", "c_cos", "c_sin", "c_tri", "c_ones1", "c_onescol"]:
            p.dma("gpsimd", s.cst[name][:], io[name], sem="q_ig")
        for c in range(NCH):
            p.dma("gpsimd", s.xbf[c][:], io["xT"][c * 128:(c + 1) * 128, :], sem="q_ig")
        for j, bn in enumerate(["bq", "bk", "bv", "bo"]):
            for c in range(NCH):
                p.dma("sync", s.biases[:, j * NCH + c:j * NCH + c + 1],
                      io[bn][c * 128:(c + 1) * 128].rearrange("(p o) -> p o", o=1), sem="q_is")
        s.m_bias = p.get("q_is")
        p.dma("gpsimd", s.bvrow[0:1, :], io["bv"].rearrange("(o p) -> o p", o=1), sem="q_ig")
        m_ig = p.get("q_ig")
        for name in CONST_SPECS:
            s.mc[name] = m_ig
        s.m_bv = m_ig
        s.m_xall = m_ig
        for c in range(NCH):
            s.mc[f"x{c}"] = m_ig

    def bias_ap(self, j, c):
        return self.biases[:, j * NCH + c:j * NCH + c + 1]

    def prep_weights(self):
        p, s, nc, io = self.p, self, self.nc, self.io
        s.wT = {}
        s.prep_done = {}
        ptr_free = [None] * 4
        wtmp_free = [None, None]
        prev_store = None
        for wn in ["wq", "wk", "wv", "wo"]:
            s.wT[wn] = nc.dram_tensor(f"b0_{wn}T", [1024, 1024], BF16, kind="Internal")
            last_evac = None
            for er in range(8):
                buf = s.wtmp[er % 2]
                qp = "q_pa" if er % 2 == 0 else "q_pb"
                mld = p.dma("gpsimd", buf[:], io[wn][er * 128:(er + 1) * 128, :],
                            waits=[("dve", wtmp_free[er % 2])] if wtmp_free[er % 2] else [], sem=qp)
                for dc in range(8):
                    g = dc % 4
                    tr = s.ps_tr[g // 2]
                    go = (g % 2) * 1024
                    waits = [(qp, mld), ("q_ig", s.mc["c_ident"])]
                    if ptr_free[g]:
                        waits.append(("dve", ptr_free[g]))
                    mt = p.emit("tensor",
                                lambda e, buf=buf, dc=dc, tr=tr, go=go: e.transpose(
                                    tr[:, go:go + 128],
                                    buf[:, dc * 128:(dc + 1) * 128], s.cst["c_ident"][:]),
                                waits=waits, inc="pe")
                    vwaits = [("pe", mt)]
                    if er == 0 and prev_store:
                        vwaits.append(("q_ps", prev_store))
                    mv = p.emit("vector",
                                lambda e, dc=dc, er=er, tr=tr, go=go: e.tensor_copy(
                                    s.stripes[dc][:, er * 128:(er + 1) * 128],
                                    tr[:, go:go + 128]),
                                waits=vwaits, inc="dve")
                    ptr_free[g] = mv
                    last_evac = mv
                wtmp_free[er % 2] = last_evac
            mstore = None
            for dc in range(8):
                mstore = p.dma("sync", s.wT[wn].ap()[dc * 128:(dc + 1) * 128, :], s.stripes[dc],
                               waits=[("dve", last_evac)], sem="q_ps")
            s.prep_done[wn] = mstore
            prev_store = mstore
        return

    def run(self):
        p, s = self.p, self
        s.alloc()
        s.load_consts()
        s.prep_weights()
        for hp in range(8):
            self.pair(hp)
        self.oproj_ln()
        p.emit("sync", lambda e: None,
               waits=[("q_sta", p.get("q_sta")), ("q_stb", p.get("q_stb"))])

    def pair(self, hp):
        p, s = self.p, self
        wb = s.wsl[0]
        wsl_ap = lambda j, c: wb[:].rearrange("p (w c e) -> p w c e", w=3, e=128)[:, j, c]
        guard_pe = p.get("pe")
        guard_dve0 = p.get("dve")
        guard_act0 = p.get("act")
        mw = []
        for j, wn in enumerate(["wq", "wk", "wv"]):
            src = s.wT[wn].ap().rearrange("(c p) e -> p c e", p=128)[:, :, hp * 128:(hp + 1) * 128]
            dst = wb[:].rearrange("p (w c e) -> p w c e", w=3, e=128)[:, j]
            mw.append(p.dma("sync", dst, src, sem="q_wsl",
                            waits=[("q_ps", s.prep_done[wn]), ("pe", guard_pe),
                                   ("dve", guard_dve0), ("act", guard_act0)]))
        if not hasattr(s, "wsl_miles"):
            s.wsl_miles = []
        s.wsl_miles.append(mw[2])
        ps_mm_free = [None, None]

        def qkv(dst, j, bj):
            qp2 = lambda qt: (s.ps_mm if qt % 2 == 0 else s.ps_bc)
            for qt in range(NQT):
                for c in range(NCH):
                    w = []
                    if c == 0:
                        w = [("q_wsl", mw[2]), ("q_ig", s.m_xall), ("q_is", s.m_bias)]
                        if ps_mm_free[qt % 2]:
                            w.append(("dve", ps_mm_free[qt % 2]))
                    mpe = p.emit("tensor",
                                 lambda e, c=c, qt=qt: e.matmul(
                                     qp2(qt)[:], wsl_ap(j, c), s.xbf[c][:, qt * QT:(qt + 1) * QT],
                                     start=(c == 0), stop=(c == NCH - 1)),
                                 waits=w, inc="pe" if c == NCH - 1 else None)
                mv = p.emit("vector",
                            lambda e, qt=qt: e.tensor_scalar(
                                dst[:, qt * QT:(qt + 1) * QT], qp2(qt)[:], s.bias_ap(bj, hp),
                                None, ALU.add),
                            waits=[("pe", mpe)], inc="dve")
                ps_mm_free[qt % 2] = mv

        qkv(s.qraw, 0, 0)
        qkv(s.kraw, 1, 1)

        for (srcb, dstb) in ((s.qraw, s.qr), (s.kraw, s.kr)):
            for qt in range(NQT):
                w = [("dve", p.get("dve")), ("q_ig", s.mc["c_prot"])]
                if ps_mm_free[0]:
                    w.append(("dve", ps_mm_free[0]))
                if ps_mm_free[1]:
                    w.append(("dve", ps_mm_free[1]))
                    ps_mm_free[1] = None
                mt = p.emit("tensor",
                            lambda e, srcb=srcb, qt=qt: e.matmul(
                                s.ps_mm[:], s.cst["c_prot"][:], srcb[:, qt * QT:(qt + 1) * QT],
                                start=True, stop=True),
                            waits=w, inc="pe")
                p.emit("vector",
                       lambda e, srcb=srcb, qt=qt: e.tensor_tensor(
                           s.ropes[:, 0:QT], srcb[:, qt * QT:(qt + 1) * QT],
                           s.cst["c_cos"][:, qt * QT:(qt + 1) * QT], ALU.mult),
                       waits=[("q_ig", s.mc["c_cos"])], inc="dve")
                mv1 = p.emit("vector",
                             lambda e, qt=qt: e.tensor_tensor(
                                 s.ropes[:, QT:2 * QT], s.ps_mm[:],
                                 s.cst["c_sin"][:, qt * QT:(qt + 1) * QT], ALU.mult),
                             waits=[("pe", mt), ("q_ig", s.mc["c_sin"])], inc="dve")
                ps_mm_free[0] = mv1
                p.emit("vector",
                       lambda e, dstb=dstb, qt=qt: e.tensor_add(
                           dstb[:, qt * QT:(qt + 1) * QT], s.ropes[:, 0:QT], s.ropes[:, QT:2 * QT]),
                       inc="dve")

        vh3 = s.vhat[:].rearrange("p (tt c) -> p tt c", c=130)
        guard_pe = p.get("pe")
        p.emit("vector", lambda e: e.memset(vh3[:, :, 64:65], 1.0), waits=[("pe", guard_pe)], inc="dve")
        p.emit("vector", lambda e: e.memset(vh3[:, :, 129:130], 1.0), inc="dve")
        m_ones = p.get("dve")
        vfree = {0: [("dve", ps_mm_free[0])] if ps_mm_free[0] else [],
                 1: ([("dve", ps_mm_free[1])] if ps_mm_free[1] else [])}
        ps_mm_free[1] = None
        for tt in range(32):
            par = tt % 2
            pso = s.ps_mm if par == 0 else s.ps_bc
            for c in range(NCH):
                w = []
                if c == 0:
                    w = [("q_wsl", mw[2])] + vfree[par]
                p.emit("tensor",
                       lambda e, c=c, tt=tt, pso=pso: e.matmul(
                           pso[:, 0:128], s.xbf[c][:, tt * 128:(tt + 1) * 128], wsl_ap(2, c),
                           start=(c == 0), stop=False),
                       waits=w)
            mpe = p.emit("tensor",
                         lambda e, pso=pso: e.matmul(
                             pso[:, 0:128], s.cst["c_ones1"][0:1, :],
                             s.bvrow[0:1, hp * 128:(hp + 1) * 128], start=False, stop=True),
                         waits=[("q_ig", s.mc["c_ones1"]), ("q_ig", s.m_bv)], inc="pe")
            mv = p.emit("vector",
                        lambda e, tt=tt, pso=pso: e.tensor_copy(vh3[:, tt, 0:64], pso[:, 0:64]),
                        waits=[("pe", mpe)], inc="dve")
            ma = p.emit("vector",
                        lambda e, tt=tt, pso=pso: e.tensor_copy(vh3[:, tt, 65:129],
                                                                pso[:, 64:128]),
                        inc="dve")
            vfree[par] = [("dve", ma)]
        ps_mm_free[0] = None

        ps_s_free = [None, None]
        pscr_free = [None]
        pend_bc = [None]
        mbc_prev = [None]
        for qt in range(NQT):
            K = 4 * (qt + 1)
            ptile_free = [None] * 3
            guard_dve = p.get("dve")
            guard_act = p.get("act")
            mva = None
            kb_order = list(range(4 * qt, K)) + list(range(0, 4 * qt))
            stage = {}

            def emit_st(idx):
                kb = kb_order[idx]
                diag = kb >= 4 * qt
                c0 = 128 * (kb - 4 * qt) if diag else 0
                par = idx % 2
                pt = s.ptile[idx % 3]
                w = [("dve", guard_dve), ("act", guard_act)] if idx == 0 else []
                if ps_s_free[par]:
                    w.append(("act", ps_s_free[par]))
                p.emit("tensor",
                       lambda e, kb=kb, qt=qt, par=par, c0=c0: e.matmul(
                           s.ps_s[par][:, c0:QT],
                           s.kr[0:64, kb * 128:(kb + 1) * 128],
                           s.qr[0:64, qt * QT + c0:(qt + 1) * QT], start=True, stop=True),
                       waits=w)
                mpe = p.emit("tensor",
                             lambda e, kb=kb, qt=qt, par=par, c0=c0: e.matmul(
                                 s.ps_s[par][:, QT + c0:2 * QT],
                                 s.kr[64:128, kb * 128:(kb + 1) * 128],
                                 s.qr[64:128, qt * QT + c0:(qt + 1) * QT], start=True, stop=True),
                             inc="pe")
                ewaits = [("pe", mpe)]
                if ptile_free[idx % 3]:
                    ewaits.append(("pe", ptile_free[idx % 3]))
                last = (idx == K - 1)
                if not diag:
                    mact = p.emit("scalar",
                                  lambda e, pt=pt, par=par: e.activation(
                                      pt[:], s.ps_s[par][:], AF.Exp, scale=0.125),
                                  waits=ewaits, inc="act")
                    ps_s_free[par] = mact
                    stage[idx] = (("act", mact), 0)
                else:
                    if c0 + 128 < QT:
                        p.emit("scalar",
                               lambda e, pt=pt, par=par, c0=c0: e.activation(
                                   pt[:].rearrange("p (h q) -> p h q", h=2)[:, :, c0 + 128:QT],
                                   s.ps_s[par][:].rearrange("p (h q) -> p h q", h=2)[:, :, c0 + 128:QT],
                                   AF.Exp, scale=0.125),
                               waits=ewaits, inc="act")
                        ewaits = []
                    if pscr_free[0]:
                        ewaits = list(ewaits) + [("dve", pscr_free[0])]
                    mact = p.emit("scalar",
                                  lambda e, par=par, c0=c0: e.activation(
                                      s.pscr[:].rearrange("p (h q) -> p h q", h=2),
                                      s.ps_s[par][:].rearrange("p (h q) -> p h q", h=2)[:, :, c0:c0 + 128],
                                      AF.Exp, scale=0.125),
                                  waits=ewaits, inc="act")
                    ps_s_free[par] = mact
                    if last and c0 > 0:
                        p.emit("vector", lambda e, pt=pt, c0=c0: e.memset(pt[:, 0:c0], 0.0),
                               waits=[("act", mact)], inc="dve")
                        p.emit("vector", lambda e, pt=pt, c0=c0: e.memset(pt[:, QT:QT + c0], 0.0),
                               inc="dve")
                    p.emit("vector",
                           lambda e, pt=pt, c0=c0: e.tensor_tensor(
                               pt[:, c0:c0 + 128], s.pscr[:, 0:128], s.cst["c_tri"][:], ALU.mult),
                           waits=[("act", mact), ("q_ig", s.mc["c_tri"])], inc="dve")
                    mmask = p.emit("vector",
                                   lambda e, pt=pt, c0=c0: e.tensor_tensor(
                                       pt[:, QT + c0:QT + c0 + 128], s.pscr[:, 128:256], s.cst["c_tri"][:],
                                       ALU.mult),
                                   inc="dve")
                    pscr_free[0] = mmask
                    stage[idx] = (("dve", mmask), c0 if not last else 0)

            def emit_av(idx):
                nonlocal mva
                kb = kb_order[idx]
                pt = s.ptile[idx % 3]
                mdep, nlo = stage.pop(idx)
                first = (idx == 0)
                last = (idx == K - 1)
                p.emit("tensor",
                       lambda e, kb=kb, pt=pt, nlo=nlo, first=first, last=last: e.matmul(
                           s.ps_va.ap()[0:65, nlo:QT], vh3[:, kb, 0:65], pt[:, nlo:QT],
                           start=first, stop=last),
                       waits=[mdep])
                mva = p.emit("tensor",
                             lambda e, kb=kb, pt=pt, nlo=nlo, first=first, last=last: e.matmul(
                                 s.ps_vb.ap()[0:65, nlo:QT], vh3[:, kb, 65:130], pt[:, QT + nlo:2 * QT],
                                 start=first, stop=last),
                             inc="pe")
                ptile_free[idx % 3] = mva

            emit_st(0)
            for idx in range(K):
                if idx + 1 < K:
                    emit_st(idx + 1)
                emit_av(idx)
                if idx == 0 and pend_bc[0] is not None:
                    pend_bc[0]()
                    pend_bc[0] = None
            p.emit("vector", lambda e: e.tensor_copy(s.numbuf[0:64, :], s.ps_va[0:64, :]),
                   waits=[("pe", mva)], inc="dve")
            mv2 = p.emit("vector", lambda e: e.tensor_copy(s.numbuf[64:128, :], s.ps_vb[0:64, :]), inc="dve")
            p.emit("vector", lambda e: e.tensor_copy(s.lcat[0:1, 0:QT], s.ps_va[64:65, :]),
                   waits=[("act", guard_act)], inc="dve")
            ml2 = p.emit("vector", lambda e: e.tensor_copy(s.lcat[0:1, QT:2 * QT], s.ps_vb[64:65, :]),
                         inc="dve")
            mln = p.emit("scalar", lambda e: e.activation(s.lln[0:1, :], s.lcat[0:1, :], AF.Ln),
                         waits=[("dve", ml2)], inc="act")
            mrw = [("act", mln)]
            if mbc_prev[0]:
                mrw.append(("pe", mbc_prev[0]))
            mrec = p.emit("scalar", lambda e: e.activation(s.lrec[0:1, :], s.lln[0:1, :], AF.Exp, scale=-1.0),
                          waits=mrw, inc="act")
            guard_dve2 = p.get("dve")

            def bc_part(qt=qt, mrec=mrec, mv2=mv2, guard_dve2=guard_dve2):
                p.emit("tensor",
                       lambda e: e.matmul(s.ps_bc[0:64, :], s.cst["c_ones1"][0:1, 0:64],
                                          s.lrec[0:1, 0:QT], start=True, stop=True),
                       waits=[("act", mrec), ("q_ig", s.mc["c_ones1"]), ("dve", guard_dve2)])
                mbc = p.emit("tensor",
                             lambda e: e.matmul(s.ps_bc[64:128, :], s.cst["c_ones1"][0:1, 0:64],
                                                s.lrec[0:1, QT:2 * QT], start=True, stop=True,
                                                tile_position=(0, 64)),
                             inc="pe")
                mbc_prev[0] = mbc
                p.emit("vector",
                       lambda e, hp=hp, qt=qt: e.tensor_tensor(
                           s.attnT[hp][:, qt * QT:(qt + 1) * QT], s.numbuf[:], s.ps_bc[:], ALU.mult),
                       waits=[("pe", mbc), ("dve", mv2), ("q_ps", s.prep_done["wo"])], inc="dve")

            pend_bc[0] = bc_part
        pend_bc[0]()
        pend_bc[0] = None

    def oproj_ln(self):
        p, s = self.p, self
        io = self.io
        mwo = None
        guard_pe = p.get("pe")
        guard_dve0 = p.get("dve")
        for c in range(NCH):
            mwo = p.dma("sync", s.woT[c], s.wT["wo"].ap()[c * 128:(c + 1) * 128, :], sem="q_wo",
                        waits=[("q_ps", s.prep_done["wo"]), ("pe", guard_pe), ("dve", guard_dve0)])
        eps = 1e-5
        ps_mm_free = [None, None]
        mdma_prev = {}
        for qt in range(NQT):
            guard_dve = p.get("dve")
            for oc in range(NCH):
                pso = s.ps_mm[:] if oc % 2 == 0 else s.ps_s[0].ap()[:, 0:QT]
                for ec in range(NCH):
                    w = []
                    if ec == 0:
                        w = [("q_wo", mwo), ("dve", guard_dve)]
                        if ps_mm_free[oc % 2]:
                            w.append(("dve", ps_mm_free[oc % 2]))
                    mpe = p.emit("tensor",
                                 lambda e, oc=oc, ec=ec, qt=qt, pso=pso: e.matmul(
                                     pso, s.woT[ec][:, oc * 128:(oc + 1) * 128],
                                     s.attnT[ec][:, qt * QT:(qt + 1) * QT],
                                     start=(ec == 0), stop=(ec == NCH - 1)),
                                 waits=w, inc="pe" if ec == NCH - 1 else None)
                mres = p.emit("vector",
                              lambda e, oc=oc, qt=qt, pso=pso: e.scalar_tensor_tensor(
                                  s.resbuf[:, oc * QT:(oc + 1) * QT], pso, s.bias_ap(3, oc),
                                  s.xbf[oc][:, qt * QT:(qt + 1) * QT], ALU.add, ALU.add),
                              waits=[("pe", mpe)], inc="dve")
                ps_mm_free[oc % 2] = mres
            guard_pe2 = p.get("pe")
            guard_dve3 = p.get("dve")
            for oc in range(NCH):
                w = [("dve", guard_dve3), ("pe", guard_pe2)] if oc == 0 else []
                p.emit("tensor",
                       lambda e, oc=oc: e.matmul(
                           s.ps_va.ap()[0:1, 0:QT], s.cst["c_onescol"][:, 0:1],
                           s.resbuf[:, oc * QT:(oc + 1) * QT],
                           start=(oc == 0), stop=(oc == NCH - 1)),
                       waits=w, inc="pe" if oc == NCH - 1 else None)
            sq_free = [None, None]
            for oc in range(NCH):
                sp = s.sqscr[oc % 2]
                w = [("dve", guard_dve3)] if oc == 0 else []
                if sq_free[oc % 2]:
                    w = list(w) + [("pe", sq_free[oc % 2])]
                ma = p.emit("scalar",
                            lambda e, oc=oc, sp=sp: e.activation(
                                sp, s.resbuf[:, oc * QT:(oc + 1) * QT], AF.Square),
                            waits=w, inc="act")
                mpe_sq = p.emit("tensor",
                                lambda e, oc=oc, sp=sp: e.matmul(
                                    s.ps_vb.ap()[0:1, 0:QT], s.cst["c_onescol"][:, 0:1], sp,
                                    start=(oc == 0), stop=(oc == NCH - 1)),
                                waits=[("act", ma)], inc="pe")
                sq_free[oc % 2] = mpe_sq
            mpe_stats = p.get("pe")
            p.emit("vector", lambda e: e.tensor_scalar_mul(s.rowbuf[0:1, 0:QT],
                                                           s.ps_va.ap()[0:1, 0:QT], 1.0 / 1024),
                   waits=[("pe", mpe_stats)], inc="dve")
            p.emit("vector", lambda e: e.tensor_scalar_mul(s.rowbuf[0:1, QT:2 * QT],
                                                           s.ps_vb.ap()[0:1, 0:QT], 1.0 / 1024),
                   inc="dve")
            p.emit("vector", lambda e: e.tensor_tensor(s.rowbuf[0:1, 2 * QT:3 * QT],
                                                       s.rowbuf[0:1, 0:QT], s.rowbuf[0:1, 0:QT], ALU.mult),
                   inc="dve")
            p.emit("vector", lambda e: e.tensor_scalar_add(s.rowslotE,
                                                           s.rowbuf[0:1, 2 * QT:3 * QT], -eps), inc="dve")
            mvar = p.emit("vector", lambda e: e.tensor_sub(s.rowbuf[0:1, 3 * QT:4 * QT],
                                                           s.rowbuf[0:1, QT:2 * QT],
                                                           s.rowslotE), inc="dve")
            mln2 = p.emit("scalar", lambda e: e.activation(s.rowslot4,
                                                           s.rowbuf[0:1, 3 * QT:4 * QT], AF.Ln),
                          waits=[("dve", mvar)], inc="act")
            mrstd = p.emit("scalar", lambda e: e.activation(s.rowbf[0:1, 0:QT],
                                                            s.rowslot4, AF.Exp, scale=-0.5),
                           waits=[("act", mln2)], inc="act")
            mb = p.emit("vector", lambda e: e.tensor_tensor(s.rowbf[0:1, QT:2 * QT], s.rowbf[0:1, 0:QT],
                                                            s.rowbuf[0:1, 0:QT], ALU.mult),
                        waits=[("act", mrstd)], inc="dve")
            p.emit("tensor", lambda e: e.matmul(s.ps_bc[:, 0:QT], s.cst["c_ones1"][0:1, :],
                                                s.rowbf[0:1, 0:QT], start=True, stop=True),
                   waits=[("dve", mb)])
            mab = p.emit("tensor", lambda e: e.matmul(s.ps_bc2.ap()[:, 0:QT], s.cst["c_ones1"][0:1, :],
                                                      s.rowbf[0:1, QT:2 * QT], start=True, stop=True),
                         inc="pe")
            for oc in range(NCH):
                xb = s.x1buf[oc % 2]
                w = [("pe", mab)]
                if mdma_prev.get(oc % 2):
                    w.append(("q_sta" if oc % 2 == 0 else "q_stb", mdma_prev[oc % 2]))
                p.emit("vector",
                       lambda e, oc=oc: e.tensor_tensor(
                           s.x1tmp, s.resbuf[:, oc * QT:(oc + 1) * QT], s.ps_bc[:, 0:QT], ALU.mult),
                       waits=w, inc="dve")
                mx1 = p.emit("vector",
                             lambda e, xb=xb: e.tensor_sub(xb, s.x1tmp, s.ps_bc2.ap()[:, 0:QT]),
                             inc="dve")
                mdma_prev[oc % 2] = p.dma("sync",
                                          io["x1T"][oc * 128:(oc + 1) * 128, qt * QT:(qt + 1) * QT],
                                          xb, waits=[("dve", mx1)],
                                          sem="q_sta" if oc % 2 == 0 else "q_stb")


class B1:

    def __init__(self, nc, p, stack, io):
        self.nc, self.p, self.stack, self.io = nc, p, stack, io
        self.E = stack.enter_context

    def alloc(self):
        nc, E = self.nc, self.E
        s = self
        s.xbf = [E(nc.sbuf_tensor(f"b1_xbf{c}", [128, T], BF16)) for c in range(NCH)]
        s.hT = E(nc.sbuf_tensor("b1_hT", [128, 32 * QT], BF16))
        s.win1 = E(nc.sbuf_tensor("b1_win1", [128, 8 * QT], BF16))
        s.wout2T = E(nc.sbuf_tensor("b1_wout2T", [128, 4 * 1024], BF16))
        s.w1buf = [E(nc.sbuf_tensor(f"b1_w1buf{i}", [128, 2048], BF16)) for i in range(2)]
        s.w2buf = [E(nc.sbuf_tensor(f"b1_w2buf{i}", [128, 2048], BF16)) for i in range(2)]
        s.tT = E(nc.sbuf_tensor("b1_tT", [16, T], BF16))
        s.aT = E(nc.sbuf_tensor("b1_aT", [16, T], BF16))
        s.gbuf = [E(nc.sbuf_tensor(f"b1_g{i}", [128, 4 * QT], BF16)) for i in range(2)]
        s.zg = [E(nc.sbuf_tensor(f"b1_zg{i}", [128, 4 * QT], BF16)) for i in range(2)]
        s.t_tb = E(nc.sbuf_tensor("b1_ttb", [128, QT], F32))
        s.e_tmp = E(nc.sbuf_tensor("b1_etmp", [128, QT], F32))
        s.e_c = E(nc.sbuf_tensor("b1_ec", [128, QT], F32))
        s.e_z = E(nc.sbuf_tensor("b1_ez", [128, QT], F32))
        s.a_tb = E(nc.sbuf_tensor("b1_atb", [128, QT], BF16))
        s.esm = E(nc.sbuf_tensor("b1_esm", [128, 16 * 32], F32))
        s.e_zz = E(nc.sbuf_tensor("b1_ezz", [128, QT], F32))
        s.resbuf = E(nc.sbuf_tensor("b1_res", [128, NCH * QT], BF16))
        s.sqscr = [E(nc.sbuf_tensor(f"b1_sq{i}", [128, QT], BF16)) for i in range(2)]
        s.rowbuf = E(nc.sbuf_tensor("b1_rowbuf", [1, 4 * QT], F32))
        s.rowslot4 = E(nc.sbuf_tensor("b1_rs4", [1, QT], F32))
        s.rowslotE = E(nc.sbuf_tensor("b1_rsE", [1, QT], F32))
        s.rowbf = E(nc.sbuf_tensor("b1_rowbf", [1, 2 * QT], BF16))
        s.x1buf = [E(nc.sbuf_tensor(f"b1_x1buf{i}", [128, QT], F32)) for i in range(2)]
        s.x1tmp = E(nc.sbuf_tensor("b1_x1tmp", [128, QT], F32))
        s.projW = E(nc.sbuf_tensor("b1_projW", [128, 8 * 16], BF16))
        s.C1sb = E(nc.sbuf_tensor("b1_C1", [16, QT], BF16))
        s.C2sb = E(nc.sbuf_tensor("b1_C2", [16, QT], BF16))
        s.bcols = E(nc.sbuf_tensor("b1_bcols", [128, 8], F32))
        s.wtmp = [E(nc.sbuf_tensor(f"b1_wtmp{i}", [128, 512], BF16)) for i in range(2)]
        s.cst = {}
        for name in ["c_ident", "c_ones1", "c_onescol"]:
            s.cst[name] = E(nc.sbuf_tensor("b1_" + name, list(CONST_SPECS[name]), BF16))
        s.ps = [E(nc.psum_tensor(f"b1_ps{i}", [128, 512], F32)) for i in range(6)]
        s.ps_bc = E(nc.psum_tensor("b1_ps_bc", [128, 512], F32))
        s.ps_bc2 = E(nc.psum_tensor("b1_ps_bc2", [128, 512], F32))
        s.ps_tr = [s.ps[i].ap().bitcast(BF16) for i in (2, 3, 4, 5)]

    def load_consts(self):
        p, s, io = self.p, self, self.io
        s.mc = {}
        for name in ["c_ident", "c_ones1", "c_onescol"]:
            p.dma("gpsimd", s.cst[name][:], io[name], sem="q_ig")
        for c in range(NCH):
            p.dma("gpsimd", s.xbf[c][:], io["x1T"][c * 128:(c + 1) * 128, :], sem="q_ig")
        for c in range(NCH):
            p.dma("gpsimd", s.win1[:, c * QT:(c + 1) * QT], io["Win1"][c * 128:(c + 1) * 128, :],
                  sem="q_ig")
            p.dma("gpsimd", s.projW[:, c * 16:(c + 1) * 16], io["proj"][c * 128:(c + 1) * 128, :],
                  sem="q_ig")
        p.dma("gpsimd", s.C1sb[:], io["C1"], sem="q_ig")
        p.dma("gpsimd", s.C2sb[:], io["C2"], sem="q_ig")
        for rc in range(4):
            p.dma("sync", s.bcols[:, rc:rc + 1],
                  io["Win1"][1024, rc * 128:(rc + 1) * 128].rearrange("(p o) -> p o", o=1), sem="q_is")
            p.dma("sync", s.bcols[:, 4 + rc:5 + rc],
                  io["Win2"][4096, rc * 128:(rc + 1) * 128].rearrange("(p o) -> p o", o=1), sem="q_is")
        s.m_bias = p.get("q_is")
        m_ig = p.get("q_ig")
        for name in ["c_ident", "c_ones1", "c_onescol"]:
            s.mc[name] = m_ig
        s.m_xall = m_ig
        s.m_smallw = m_ig

    def prep_weights(self):
        p, s, nc, io = self.p, self, self.nc, self.io
        s.wout1T_dram = nc.dram_tensor("b1_wout1T", [512, 4096], BF16, kind="Internal")
        s.win2_dram = io["_wch"]["win2"]
        ptr_free = [None] * 4
        wtmp_free = [None, None]
        for er in range(32):
            buf = s.wtmp[er % 2]
            qp = "q_pa" if er % 2 == 0 else "q_pb"
            mld = p.dma("gpsimd", buf[:], io["Wout1"][er * 128:(er + 1) * 128, :],
                        waits=[("dve", wtmp_free[er % 2])] if wtmp_free[er % 2] else [], sem=qp)
            for rc in range(4):
                tr = s.ps_tr[rc]
                go = 0
                w = [(qp, mld), ("q_ig", s.mc["c_ident"])]
                if ptr_free[rc]:
                    w.append(("dve", ptr_free[rc]))
                mt = p.emit("tensor",
                            lambda e, buf=buf, rc=rc, tr=tr, go=go: e.transpose(
                                tr[:, go:go + 128], buf[:, rc * 128:(rc + 1) * 128],
                                s.cst["c_ident"][:]),
                            waits=w, inc="pe")
                mv = p.emit("vector",
                            lambda e, rc=rc, er=er, tr=tr, go=go: e.tensor_copy(
                                s.hT[:, rc * 4096 + er * 128:rc * 4096 + (er + 1) * 128],
                                tr[:, go:go + 128]),
                            waits=[("pe", mt)], inc="dve")
                ptr_free[rc] = mv
                wtmp_free[er % 2] = mv
        mst = None
        for rc in range(4):
            mst = p.dma("sync", s.wout1T_dram.ap()[rc * 128:(rc + 1) * 128, :],
                        s.hT[:, rc * 4096:(rc + 1) * 4096], waits=[("dve", p.get("dve"))], sem="q_ps")
        s.m_w1T = mst
        for er in range(8):
            buf = s.wtmp[er % 2]
            qp = "q_pa" if er % 2 == 0 else "q_pb"
            mld = p.dma("gpsimd", buf[:], io["Wout2"][er * 128:(er + 1) * 128, :],
                        waits=[("dve", wtmp_free[er % 2])] if wtmp_free[er % 2] else [], sem=qp)
            for rc in range(4):
                tr = s.ps_tr[rc]
                go = 0
                w = [(qp, mld)]
                if ptr_free[rc]:
                    w.append(("dve", ptr_free[rc]))
                mt = p.emit("tensor",
                            lambda e, buf=buf, rc=rc, tr=tr, go=go: e.transpose(
                                tr[:, go:go + 128], buf[:, rc * 128:(rc + 1) * 128],
                                s.cst["c_ident"][:]),
                            waits=w, inc="pe")
                mv = p.emit("vector",
                            lambda e, rc=rc, er=er, tr=tr, go=go: e.tensor_copy(
                                s.wout2T[:, rc * 1024 + er * 128:rc * 1024 + (er + 1) * 128],
                                tr[:, go:go + 128]),
                            waits=[("pe", mt)], inc="dve")
                ptr_free[rc] = mv
                wtmp_free[er % 2] = mv
        s.m_w2T = p.get("dve")
        s.m_w2c = io["_wch_m"]

    def gates(self):
        p, s = self.p, self
        sm = lambda i: s.esm[:, i * 32:(i + 1) * 32]
        sm3 = lambda i: s.esm[:, i * 32:(i + 1) * 32].rearrange("p (c o) -> p c o", o=1)
        for qt in range(NQT):
            for c in range(NCH):
                w = []
                if c == 0:
                    w = [("q_ig", s.m_smallw), ("q_ig", s.m_xall)]
                    if qt >= 2:
                        w.append(("dve", s._tevac[qt - 2]))
                p.emit("tensor",
                       lambda e, c=c, qt=qt: e.matmul(
                           s.ps[qt % 2].ap()[0:16, :], s.projW[:, c * 16:(c + 1) * 16],
                           s.xbf[c][:, qt * QT:(qt + 1) * QT],
                           start=(c == 0), stop=(c == NCH - 1)),
                       waits=w, inc="pe" if c == NCH - 1 else None)
            mpe = p.get("pe")
            mv = p.emit("vector",
                        lambda e, qt=qt: e.tensor_copy(s.tT[:, qt * QT:(qt + 1) * QT],
                                                       s.ps[qt % 2].ap()[0:16, :]),
                        waits=[("pe", mpe)], inc="dve")
            if not hasattr(s, "_tevac"):
                s._tevac = {}
            s._tevac[qt] = mv
        guard = p.get("dve")
        for tt in range(32):
            g = tt % 4
            tr = s.ps_tr[g]
            go = 0
            w = [("dve", guard)] if tt < 4 else [("dve", s._ttb[tt - 4])]
            mt = p.emit("tensor",
                        lambda e, tt=tt, tr=tr, go=go: e.transpose(
                            tr[:, go:go + 16], s.tT[0:16, tt * 128:(tt + 1) * 128],
                            s.cst["c_ident"][0:16, 0:16]),
                        waits=w, inc="pe")
            mv = p.emit("vector",
                        lambda e, tt=tt, tr=tr, go=go: e.tensor_copy(
                            s.t_tb[:, tt * 16:(tt + 1) * 16], tr[:, go:go + 16]),
                        waits=[("pe", mt)], inc="dve")
            if not hasattr(s, "_ttb"):
                s._ttb = {}
            s._ttb[tt] = mv
        t3 = s.t_tb[:].rearrange("p (c e) -> p c e", e=16)
        tmp3 = s.e_tmp[:].rearrange("p (c e) -> p c e", e=16)
        c3 = s.e_c[:].rearrange("p (c e) -> p c e", e=16)
        z3 = s.e_z[:].rearrange("p (c e) -> p c e", e=16)
        zz3 = s.e_zz[:].rearrange("p (c e) -> p c e", e=16)
        bc = lambda i: sm3(i).broadcast_to([128, 32, 16])
        p.emit("vector", lambda e: e.tensor_reduce(sm3(0), t3, AX.X, ALU.add),
               waits=[("dve", p.get("dve"))], inc="dve")
        p.emit("vector", lambda e: e.tensor_scalar_mul(sm(1), sm(0), 1.0 / 16), inc="dve")
        p.emit("vector", lambda e: e.tensor_tensor(tmp3, t3, bc(1), ALU.subtract), inc="dve")
        p.emit("vector", lambda e: e.tensor_mul(s.e_c[:], s.e_tmp[:], s.e_tmp[:]), inc="dve")
        p.emit("vector", lambda e: e.tensor_reduce(sm3(2), c3, AX.X, ALU.add), inc="dve")
        mv = p.emit("vector", lambda e: e.tensor_scalar(sm(3), sm(2), 1.0 / 16, 1e-5, ALU.mult, ALU.add),
                    inc="dve")
        mla = p.emit("scalar", lambda e: e.activation(sm(4), sm(3), AF.Ln),
                     waits=[("dve", mv)], inc="act")
        mrs = p.emit("scalar", lambda e: e.activation(sm(5), sm(4), AF.Exp, scale=-0.5),
                     waits=[("act", mla)], inc="act")
        p.emit("vector", lambda e: e.tensor_tensor(z3, tmp3, bc(5), ALU.mult),
               waits=[("act", mrs)], inc="dve")
        p.emit("vector", lambda e: e.tensor_reduce(sm3(6), z3, AX.X, ALU.max), inc="dve")
        p.emit("vector", lambda e: e.tensor_scalar_mul(sm(7), sm(6), 0.5), inc="dve")
        p.emit("vector", lambda e: e.scalar_tensor_tensor(zz3, z3, 0.5, bc(7), ALU.mult, ALU.subtract),
               inc="dve")
        p.emit("vector", lambda e: e.memset(sm(8), -1.0), inc="dve")
        tcur, tnxt = 8, 9
        for it in range(24):
            p.emit("vector", lambda e, tcur=tcur: e.tensor_tensor(tmp3, zz3, bc(tcur), ALU.subtract),
                   inc="dve")
            p.emit("vector", lambda e: e.tensor_scalar_max(s.e_c[:], s.e_tmp[:], 0.0), inc="dve")
            p.emit("vector", lambda e: e.tensor_mul(s.e_tmp[:], s.e_c[:], s.e_c[:]), inc="dve")
            p.emit("vector", lambda e: e.tensor_reduce(sm3(10), tmp3, AX.X, ALU.add), inc="dve")
            p.emit("vector", lambda e: e.tensor_reduce(sm3(11), c3, AX.X, ALU.add), inc="dve")
            p.emit("vector", lambda e: e.tensor_scalar_add(sm(12), sm(10), -1.0), inc="dve")
            p.emit("vector", lambda e: e.reciprocal(sm(13), sm(11)), inc="dve")
            p.emit("vector", lambda e: e.tensor_tensor(sm(14), sm(12), sm(13), ALU.mult), inc="dve")
            p.emit("vector", lambda e, tcur=tcur, tnxt=tnxt: e.scalar_tensor_tensor(
                sm(tnxt), sm(14), 0.5, sm(tcur), ALU.mult, ALU.add), inc="dve")
            tcur, tnxt = tnxt, tcur
        p.emit("vector", lambda e, tcur=tcur: e.tensor_tensor(tmp3, zz3, bc(tcur), ALU.subtract),
               inc="dve")
        p.emit("vector", lambda e: e.tensor_scalar_max(s.e_c[:], s.e_tmp[:], 0.0), inc="dve")
        ma = p.emit("vector", lambda e: e.tensor_mul(s.a_tb[:], s.e_c[:], s.e_c[:]), inc="dve")
        for tt in range(32):
            g = tt % 4
            tr = s.ps_tr[g]
            go = 0
            w = [("dve", ma)] if tt < 4 else [("dve", s._atb[tt - 4])]
            mt = p.emit("tensor",
                        lambda e, tt=tt, tr=tr, go=go: e.transpose(
                            tr[0:16, go:go + 128], s.a_tb[:, tt * 16:(tt + 1) * 16],
                            s.cst["c_ident"][:]),
                        waits=w, inc="pe")
            mv = p.emit("vector",
                        lambda e, tt=tt, tr=tr, go=go: e.tensor_copy(
                            s.aT[0:16, tt * 128:(tt + 1) * 128], tr[0:16, go:go + 128]),
                        waits=[("pe", mt)], inc="dve")
            if not hasattr(s, "_atb"):
                s._atb = {}
            s._atb[tt] = mv
        s.m_gates = p.get("dve")

    def mumoe(self):
        p, s, io = self.p, self, self.io
        bias1 = lambda rc: s.bcols[:, rc:rc + 1]
        bias2 = lambda rc: s.bcols[:, 4 + rc:5 + rc]
        w1ld_prev = {}
        w2ld_prev = {}
        mdma_prev = {}
        eps = 1e-5
        for qt in range(NQT):
            guard = p.get("dve")
            for j, Csb in ((0, s.C1sb), (1, s.C2sb)):
                mgv = None
                for rc in range(4):
                    w = [("dve", s.m_gates)] if (qt == 0 and rc == 0 and j == 0) else []
                    if rc == 0:
                        w.append(("dve", guard))
                    if mgv:
                        w = list(w) + [("dve", mgv)]
                    mpe = p.emit("tensor",
                                 lambda e, j=j, rc=rc, qt=qt, Csb=Csb: e.matmul(
                                     s.ps[4 + j][:], Csb[:, rc * 128:(rc + 1) * 128],
                                     s.aT[0:16, qt * QT:(qt + 1) * QT],
                                     start=True, stop=True),
                                 waits=w, inc="pe")
                    mgv = p.emit("vector",
                                 lambda e, j=j, rc=rc: e.tensor_copy(
                                     s.gbuf[j][:, rc * QT:(rc + 1) * QT], s.ps[4 + j][:]),
                                 waits=[("pe", mpe)], inc="dve")
            if not hasattr(s, "_gz"):
                s._gz = {}
            s._gz[0] = p.get("dve")
            s._gz[1] = p.get("dve")
            zfree = getattr(s, "_z1free", {})
            for rc in range(4):
                for c in range(NCH):
                    w = []
                    if c == 0:
                        if qt == 0 and rc == 0:
                            w = [("q_ig", s.m_smallw), ("q_is", s.m_bias)]
                        if zfree.get(rc % 2):
                            w.append(("dve", zfree[rc % 2]))
                    p.emit("tensor",
                           lambda e, rc=rc, c=c, qt=qt: e.matmul(
                               s.ps[rc % 2][:], s.win1[:, c * QT + rc * 128:c * QT + (rc + 1) * 128],
                               s.xbf[c][:, qt * QT:(qt + 1) * QT],
                               start=(c == 0), stop=(c == NCH - 1)),
                           waits=w, inc="pe" if c == NCH - 1 else None)
                mpe = p.get("pe")
                zfree[rc % 2] = p.emit("vector",
                                       lambda e, rc=rc: e.scalar_tensor_tensor(
                                           s.zg[0][:, rc * QT:(rc + 1) * QT], s.ps[rc % 2][:], bias1(rc),
                                           s.gbuf[0][:, rc * QT:(rc + 1) * QT], ALU.add, ALU.mult),
                                       waits=[("pe", mpe)], inc="dve")
            s._z1free = zfree
            mzg1 = p.get("dve")
            for go in range(8):
                wb = s.w1buf[go % 2]
                w = [("q_ps", s.m_w1T), ("dve", mzg1)]
                if w1ld_prev.get(go % 2):
                    w.append(("pe", w1ld_prev[go % 2]))
                mld = p.dma("sync", wb[:],
                            s.wout1T_dram.ap().rearrange("(rc p) o -> p rc o", p=128)[
                                :, :, go * 512:(go + 1) * 512], waits=w,
                            sem="q_w1a" if go % 2 == 0 else "q_w1b")
                hact_loc = {}
                for oc4 in range(4):
                    oc = go * 4 + oc4
                    for rc in range(4):
                        w2 = [(("q_w1a" if go % 2 == 0 else "q_w1b"), mld)] if rc == 0 else []
                        if rc == 0 and oc4 == 0 and s._hact.get((qt - 1, go)):
                            w2.append(("act", s._hact[(qt - 1, go)]))
                        if rc == 0 and oc4 >= 2:
                            w2.append(("act", hact_loc[oc4 - 2]))
                        if rc == 0 and oc4 < 2 and s._hact.get((qt, go - 1)):
                            w2.append(("act", s._hact[(qt, go - 1)]))
                        p.emit("tensor",
                               lambda e, oc4=oc4, rc=rc, wb=wb: e.matmul(
                                   s.ps[2 + oc4 % 2][:],
                                   wb[:].rearrange("p (rc o) -> p rc o", rc=4)[:, rc,
                                                                              oc4 * 128:(oc4 + 1) * 128],
                                   s.zg[0][:, rc * QT:(rc + 1) * QT],
                                   start=(rc == 0), stop=(rc == 3)),
                               waits=w2, inc="pe" if rc == 3 else None)
                    mpe = p.get("pe")
                    mact = p.emit("scalar",
                                  lambda e, oc=oc, oc4=oc4: e.activation(
                                      s.hT[:, oc * QT:(oc + 1) * QT], s.ps[2 + oc4 % 2][:], AF.Gelu),
                                  waits=[("pe", mpe)], inc="act")
                    hact_loc[oc4] = mact
                w1ld_prev[go % 2] = mpe
                s._hact[(qt, go)] = mact
            for go in range(8):
                wb = s.w2buf[go % 2]
                w = [("q_wch", s.m_w2c), ("act", s._hact[(qt, 7)])]
                if w2ld_prev.get(go % 2):
                    w.append(("pe", w2ld_prev[go % 2]))
                mld = p.dma("sync", wb[:],
                            s.win2_dram.ap().rearrange("(g p) r -> p g r", p=128)[
                                :, go * 4:(go + 1) * 4, :], waits=w,
                            sem="q_w2a" if go % 2 == 0 else "q_w2b")
                for oc4 in range(4):
                    oc = go * 4 + oc4
                    for rc in range(4):
                        w2 = []
                        if rc == 0 and oc4 == 0:
                            w2 = [(("q_w2a" if go % 2 == 0 else "q_w2b"), mld)]
                            if go == 0:
                                w2.append(("dve", s._zg2free))
                        mpe = p.emit("tensor",
                                     lambda e, oc=oc, oc4=oc4, rc=rc, wb=wb: e.matmul(
                                         s.ps[rc][:],
                                         wb[:].rearrange("p (g r) -> p g r", g=4)[
                                             :, oc4, rc * 128:(rc + 1) * 128],
                                         s.hT[:, oc * QT:(oc + 1) * QT],
                                         start=(oc == 0), stop=(oc == 31)),
                                     waits=w2, inc="pe" if (rc == 3 and oc4 == 3) else None)
                w2ld_prev[go % 2] = mpe
            mpe_z2 = p.get("pe")
            for rc in range(4):
                p.emit("vector",
                       lambda e, rc=rc: e.scalar_tensor_tensor(
                           s.zg[1][:, rc * QT:(rc + 1) * QT], s.ps[rc][:], bias2(rc),
                           s.gbuf[1][:, rc * QT:(rc + 1) * QT], ALU.add, ALU.mult),
                       waits=[("pe", mpe_z2)] if rc == 0 else [], inc="dve")
            s._zg2free = p.get("dve")
            guard2 = p.get("dve")
            ps_mm_free = [None, None]
            for dc in range(NCH):
                for rc in range(4):
                    w = []
                    if rc == 0:
                        w = [("dve", guard2)]
                        if ps_mm_free[dc % 2]:
                            w.append(("dve", ps_mm_free[dc % 2]))
                    p.emit("tensor",
                           lambda e, dc=dc, rc=rc: e.matmul(
                               s.ps[4 + dc % 2][:],
                               s.wout2T[:, rc * 1024 + dc * 128:rc * 1024 + (dc + 1) * 128],
                               s.zg[1][:, rc * QT:(rc + 1) * QT],
                               start=(rc == 0), stop=(rc == 3)),
                           waits=w, inc="pe" if rc == 3 else None)
                mpe = p.get("pe")
                mres = p.emit("vector",
                              lambda e, dc=dc, qt=qt: e.tensor_tensor(
                                  s.resbuf[:, dc * QT:(dc + 1) * QT], s.ps[4 + dc % 2][:],
                                  s.xbf[dc][:, qt * QT:(qt + 1) * QT], ALU.add),
                              waits=[("pe", mpe)], inc="dve")
                ps_mm_free[dc % 2] = mres
            guard_pe2 = p.get("pe")
            guard_dve3 = p.get("dve")
            for oc in range(NCH):
                w = [("dve", guard_dve3), ("pe", guard_pe2)] if oc == 0 else []
                p.emit("tensor",
                       lambda e, oc=oc: e.matmul(
                           s.ps_bc.ap()[0:1, 0:QT], s.cst["c_onescol"][:, 0:1],
                           s.resbuf[:, oc * QT:(oc + 1) * QT],
                           start=(oc == 0), stop=(oc == NCH - 1)),
                       waits=w, inc="pe" if oc == NCH - 1 else None)
            sq_free = [None, None]
            for oc in range(NCH):
                sp = s.sqscr[oc % 2]
                w = [("dve", guard_dve3)] if oc == 0 else []
                if sq_free[oc % 2]:
                    w = list(w) + [("pe", sq_free[oc % 2])]
                mact2 = p.emit("scalar",
                               lambda e, oc=oc, sp=sp: e.activation(
                                   sp[:], s.resbuf[:, oc * QT:(oc + 1) * QT], AF.Square),
                               waits=w, inc="act")
                mpe_sq = p.emit("tensor",
                                lambda e, oc=oc, sp=sp: e.matmul(
                                    s.ps_bc2.ap()[0:1, 0:QT], s.cst["c_onescol"][:, 0:1], sp[:],
                                    start=(oc == 0), stop=(oc == NCH - 1)),
                                waits=[("act", mact2)], inc="pe")
                sq_free[oc % 2] = mpe_sq
            mpe_stats = p.get("pe")
            p.emit("vector", lambda e: e.tensor_scalar_mul(s.rowbuf[0:1, 0:QT],
                                                           s.ps_bc.ap()[0:1, 0:QT], 1.0 / 1024),
                   waits=[("pe", mpe_stats)], inc="dve")
            p.emit("vector", lambda e: e.tensor_scalar_mul(s.rowbuf[0:1, QT:2 * QT],
                                                           s.ps_bc2.ap()[0:1, 0:QT], 1.0 / 1024),
                   inc="dve")
            p.emit("vector", lambda e: e.tensor_tensor(s.rowbuf[0:1, 2 * QT:3 * QT],
                                                       s.rowbuf[0:1, 0:QT], s.rowbuf[0:1, 0:QT],
                                                       ALU.mult), inc="dve")
            p.emit("vector", lambda e: e.tensor_scalar_add(s.rowslotE[0:1, :],
                                                           s.rowbuf[0:1, 2 * QT:3 * QT], -eps),
                   inc="dve")
            mvar = p.emit("vector", lambda e: e.tensor_sub(s.rowbuf[0:1, 3 * QT:4 * QT],
                                                           s.rowbuf[0:1, QT:2 * QT],
                                                           s.rowslotE[0:1, :]), inc="dve")
            mln2 = p.emit("scalar", lambda e: e.activation(s.rowslot4[0:1, :],
                                                           s.rowbuf[0:1, 3 * QT:4 * QT], AF.Ln),
                          waits=[("dve", mvar)], inc="act")
            mrstd = p.emit("scalar", lambda e: e.activation(s.rowbf[0:1, 0:QT],
                                                            s.rowslot4[0:1, :], AF.Exp, scale=-0.5),
                           waits=[("act", mln2)], inc="act")
            mb = p.emit("vector", lambda e: e.tensor_tensor(s.rowbf[0:1, QT:2 * QT],
                                                            s.rowbf[0:1, 0:QT],
                                                            s.rowbuf[0:1, 0:QT], ALU.mult),
                        waits=[("act", mrstd)], inc="dve")
            p.emit("tensor", lambda e: e.matmul(s.ps_bc[:, 0:QT], s.cst["c_ones1"][0:1, :],
                                                s.rowbf[0:1, 0:QT], start=True, stop=True),
                   waits=[("dve", mb), ("q_ig", s.mc["c_ones1"])])
            mab = p.emit("tensor", lambda e: e.matmul(s.ps_bc2.ap()[:, 0:QT], s.cst["c_ones1"][0:1, :],
                                                      s.rowbf[0:1, QT:2 * QT], start=True, stop=True),
                         inc="pe")
            for oc in range(NCH):
                xb = s.x1buf[oc % 2]
                w = [("pe", mab)]
                if mdma_prev.get(oc % 2):
                    w.append(("q_sta" if oc % 2 == 0 else "q_stb", mdma_prev[oc % 2]))
                p.emit("vector",
                       lambda e, oc=oc: e.tensor_tensor(
                           s.x1tmp[:], s.resbuf[:, oc * QT:(oc + 1) * QT], s.ps_bc[:, 0:QT],
                           ALU.mult),
                       waits=w, inc="dve")
                mx1 = p.emit("vector",
                             lambda e, xb=xb: e.tensor_sub(xb[:], s.x1tmp[:], s.ps_bc2.ap()[:, 0:QT]),
                             inc="dve")
                mdma_prev[oc % 2] = p.dma("sync",
                                          io["x2T"][oc * 128:(oc + 1) * 128, qt * QT:(qt + 1) * QT],
                                          xb[:], waits=[("dve", mx1)],
                                          sem="q_sta" if oc % 2 == 0 else "q_stb")

    def run(self):
        p, s = self.p, self
        s.alloc()
        s._hact = {}
        s._zg2free = None
        s.load_consts()
        s.prep_weights()
        self.gates()
        self.mumoe()
        p.emit("sync", lambda e: None,
               waits=[("q_sta", p.get("q_sta")), ("q_stb", p.get("q_stb"))])


class B2:

    def __init__(self, nc, p, stack, io):
        self.nc, self.p, self.stack, self.io = nc, p, stack, io
        self.E = stack.enter_context

    def alloc(self):
        nc, E = self.nc, self.E
        s = self
        s.xbf = [E(nc.sbuf_tensor(f"b2_xbf{c}", [128, T], BF16)) for c in range(NCH)]
        s.vhat = E(nc.sbuf_tensor("b2_vhat", [128, 32 * 1040], BF16))
        s.wbuf = E(nc.sbuf_tensor("b2_wbuf", [128, 8 * 1024], BF16))
        s.kbuf = E(nc.sbuf_tensor("b2_kbuf", [128, 8 * 1024], BF16))
        s.qr = E(nc.sbuf_tensor("b2_qr", [128, T], BF16))
        s.kr = E(nc.sbuf_tensor("b2_kr", [128, T], BF16))
        s.sqbuf = E(nc.sbuf_tensor("b2_sqbuf", [128, T], BF16))
        s.kptok = [E(nc.sbuf_tensor(f"b2_kptok{i}", [128, 256], BF16)) for i in range(2)]
        s.kvbf = E(nc.sbuf_tensor("b2_kvbf", [128, 130], BF16))
        s.kvf32 = E(nc.sbuf_tensor("b2_kvf32", [128, 2 * 130], F32))
        s.atile = [E(nc.sbuf_tensor(f"b2_atile{i}", [128, 128], BF16)) for i in range(2)]
        s.attntok = E(nc.sbuf_tensor("b2_attntok", [128, 32 * 64], BF16))

        s.dbuf = E(nc.sbuf_tensor("b2_dbuf", [128, 8], F32))
        s.rbuf = E(nc.sbuf_tensor("b2_rbuf", [128, 8], F32))
        s.mapW = E(nc.sbuf_tensor("b2_mapW", [128, 256], BF16))
        s.negones = E(nc.sbuf_tensor("b2_negones", [128, 256], BF16))
        s.biases = E(nc.sbuf_tensor("b2_biases", [128, 4 * NCH], F32))
        s.vbrow = E(nc.sbuf_tensor("b2_vbrow", [1, 1024], BF16))
        s.gb2 = E(nc.sbuf_tensor("b2_gb2", [2, 1024], BF16))
        s.onesrow = E(nc.sbuf_tensor("b2_onesrow", [2, QT], BF16))
        s.resbuf = s.vhat[:, 0:NCH * QT]
        s.res2buf = s.vhat[:, NCH * QT:2 * NCH * QT]
        _qf = s.qr[:].bitcast(F32)
        _kf = s.kr[:].bitcast(F32)
        s.sqscr = [s.attntok[:, 0:QT], s.attntok[:, QT:2 * QT]]
        s.ybuf = s.attntok[:, 2 * QT:4 * QT].bitcast(F32)
        s.rowbuf = _qf[0:1, 0:2048]
        s.rowslot4 = _qf[32:33, 0:QT]
        s.rowslotE = E(nc.sbuf_tensor("b2_rsE", [1, QT], F32))[:]
        s.rowbf = s.kr[0:1, 0:2 * QT]
        s.x1buf = [_kf[:, 512 + i * QT:512 + (i + 1) * QT] for i in range(2)]
        s.x1tmp = _kf[:, 1536:2048]
        s.cst = {}
        for name in ["c_ident", "c_ones1", "c_onescol", "c_tri", "c_negrow"]:
            shp = CONST_SPECS.get(name, (1, QT))
            s.cst[name] = E(nc.sbuf_tensor("b2_" + name, list(shp), BF16))
        s.ps_m = [E(nc.psum_tensor(f"b2_ps_m{i}", [128, 1024], F32)) for i in range(2)]
        s.ps_a = E(nc.psum_tensor("b2_ps_a", [128, 512], F32))
        s.ps_o4 = E(nc.psum_tensor("b2_ps_o4", [128, 512], F32))
        s.ps_kv = E(nc.psum_tensor("b2_ps_kv", [128, 512], F32))
        s.ps_bc = E(nc.psum_tensor("b2_ps_bc", [128, 512], F32))
        s.ps_tr = [s.ps_m[0].ap().bitcast(BF16), s.ps_m[1].ap().bitcast(BF16)]
        s.ps_bc2 = s.ps_kv

    def load_consts(self):
        p, s, io = self.p, self, self.io
        for name in ["c_ident", "c_ones1", "c_onescol", "c_tri", "c_negrow"]:
            p.dma("gpsimd", s.cst[name][:], io[name], sem="q_ig")
        for c in range(NCH):
            p.dma("gpsimd", s.xbf[c][:], io["x2T"][c * 128:(c + 1) * 128, :], sem="q_ig")
        for j, bn in enumerate(["q_b", "k_b", "v_b", "out_b"]):
            for c in range(NCH):
                p.dma("sync", s.biases[:, j * NCH + c:j * NCH + c + 1],
                      io[bn][c * 128:(c + 1) * 128].rearrange("(p o) -> p o", o=1), sem="q_is")
        p.dma("gpsimd", s.gb2[0:1, :], io["gamma"].rearrange("(o p) -> o p", o=1), sem="q_ig")
        p.dma("gpsimd", s.gb2[1:2, :], io["beta"].rearrange("(o p) -> o p", o=1), sem="q_ig")
        p.dma("gpsimd", s.vbrow[0:1, :], io["v_b"].rearrange("(o p) -> o p", o=1), sem="q_ig")
        p.dma("gpsimd", s.onesrow[1:2, :], io["c_negrow"], sem="q_ig")
        s.m_bias = p.get("q_is")
        s.m_ig = p.get("q_ig")

    def bias_ap(self, j, c):
        return self.biases[:, j * NCH + c:j * NCH + c + 1]

    def prep(self):
        p, s, nc, io = self.p, self, self.nc, self.io
        m1 = p.dma("gpsimd", s.sqbuf[:, 0:64], io["proj"][0:128, :], sem="q_pa")
        m2 = p.dma("gpsimd", s.sqbuf[:, 64:128], io["proj"][128:256, :], sem="q_pb")
        mn = p.emit("vector", lambda e: e.memset(s.negones[:], -1.0 / 128))
        prev_evac = {}
        for half in range(2):
            qw = ("q_pa", m1) if half == 0 else ("q_pb", m2)
            for tp in (0, 64):
                tw = [qw, ("q_ig", s.m_ig)]
                if prev_evac.get(tp):
                    tw.append(("dve", prev_evac[tp]))
                mt = p.emit("tensor",
                            lambda e, half=half, tp=tp: e.transpose(
                                (s.ps_tr[0][0:64, 0:128] if tp == 0 else s.ps_tr[0][64:128, 1024:1152]),
                                s.sqbuf[:, half * 64:half * 64 + 64],
                                s.cst["c_ident"][:], tile_position=(0, tp)),
                            waits=tw, inc="pe")
                prev_evac[tp] = p.emit("vector",
                       lambda e, half=half, tp=tp: e.tensor_scalar_mul(
                           s.mapW[tp:tp + 64, half * 128:(half + 1) * 128],
                           (s.ps_tr[0][:, 0:128] if tp == 0 else s.ps_tr[0][:, 1024:1152])[tp:tp + 64, :],
                           0.125),
                       waits=[("pe", mt)], inc="dve")
        s.m_mapw = p.get("dve")
        s.wc = {wn: io["_wch"][wn] for wn in ["q_w", "k_w"]}
        s.m_wc = io["_wch_m"]

    def build_vhat(self):
        p, s, io = self.p, self, self.io
        vh3 = s.vhat[:].rearrange("p (tt pr c) -> p tt pr c", pr=8, c=130)
        p.emit("vector", lambda e: e.memset(vh3[:, :, :, 64:65], 1.0))
        p.emit("vector", lambda e: e.memset(vh3[:, :, :, 129:130], 1.0))
        mones = p.get("dve")
        for c in range(NCH):
            p.dma("gpsimd", s.kbuf[:, c * 1024:(c + 1) * 1024],
                  io["v_w"][c * 128:(c + 1) * 128, :], sem="q_ig2")
        mvl = p.get("q_ig2")
        ps_free = {}
        for tt in range(32):
            for half in range(2):
                for c in range(NCH):
                    w = []
                    if c == 0:
                        w = [("q_ig2", mvl), ("q_ig", s.m_ig)]
                        if tt == 0 and half == 0:
                            w.append(("dve", mones))
                        if ps_free.get(tt % 2):
                            w.append(("dve", ps_free[tt % 2]))
                    p.emit("tensor",
                           lambda e, tt=tt, c=c, half=half: e.matmul(
                               s.ps_m[tt % 2].ap()[:, half * 512:(half + 1) * 512],
                               s.xbf[c][:, tt * 128:(tt + 1) * 128],
                               s.kbuf[:, c * 1024 + half * 512:c * 1024 + (half + 1) * 512],
                               start=(c == 0), stop=False),
                           waits=w)
                mpx = p.emit("tensor",
                             lambda e, tt=tt, half=half: e.matmul(
                                 s.ps_m[tt % 2].ap()[:, half * 512:(half + 1) * 512],
                                 s.cst["c_ones1"][0:1, :],
                                 s.vbrow[0:1, half * 512:(half + 1) * 512],
                                 start=False, stop=True),
                             inc="pe")
                src = s.ps_m[tt % 2].ap()[:, half * 512:(half + 1) * 512].rearrange(
                    "p (pr h d) -> p pr h d", pr=4, h=2, d=64)
                dst3 = vh3[:, tt, half * 4:(half + 1) * 4, :]
                p.emit("vector",
                       lambda e, src=src, dst3=dst3: e.tensor_copy(dst3[:, :, 0:64], src[:, :, 0, :]),
                       waits=[("pe", mpx)], inc="dve")
                mv = p.emit("vector",
                            lambda e, src=src, dst3=dst3: e.tensor_copy(dst3[:, :, 65:129], src[:, :, 1, :]),
                            inc="dve")
                ps_free[tt % 2] = mv
        s.m_vhat = p.get("dve")

    def pair_qk(self, hp, mat_prev=None):
        p, s = self.p, self
        guard_pe = p.get("pe")
        guard_dve = p.get("dve")
        mws = []
        for j, wn in enumerate(["q_w", "k_w"]):
            src = s.wc[wn].ap().rearrange("(c p) e -> p c e", p=128)[:, :, hp * 128:(hp + 1) * 128]
            dst = s.sqbuf[:].rearrange("p (w c e) -> p w c e", w=4, e=128)[:, j]
            ww = [("q_wch", s.m_wc),
                  ("pe", guard_pe), ("dve", guard_dve)]
            if mat_prev:
                ww.append((mat_prev[0], mat_prev[1]))
            mws.append(p.dma("sync", dst, src, sem="q_wsl", waits=ww))
        mwsl = mws[-1]
        wsl_ap = lambda j, c: s.sqbuf[:].rearrange("p (w c e) -> p w c e", w=4, e=128)[:, j, c]
        ps_free = {}
        for j, dstb in ((0, s.qr), (1, s.kr)):
            for qt in range(NQT):
                for c in range(NCH):
                    w = []
                    if c == 0:
                        w = [("q_wsl", mwsl), ("q_ig", s.m_ig), ("q_is", s.m_bias)]
                        if ps_free.get(qt % 2):
                            w.append(("dve", ps_free[qt % 2]))
                    mpe = p.emit("tensor",
                                 lambda e, j=j, qt=qt, c=c: e.matmul(
                                     s.ps_m[qt % 2].ap()[:, 0:512],
                                     wsl_ap(j, c),
                                     s.xbf[c][:, qt * QT:(qt + 1) * QT],
                                     start=(c == 0), stop=(c == NCH - 1)),
                                 waits=w, inc="pe" if c == NCH - 1 else None)
                mv = p.emit("vector",
                            lambda e, j=j, qt=qt, dstb=dstb, hp=hp: e.tensor_scalar(
                                dstb[:, qt * QT:(qt + 1) * QT], s.ps_m[qt % 2].ap()[:, 0:512],
                                s.bias_ap(j, hp), None, ALU.add),
                            waits=[("pe", mpe)], inc="dve")
                ps_free[qt % 2] = mv

    def head(self, hp, half, adram, mat_prev):
        p, s = self.p, self
        h = 2 * hp + half
        hb = half * 64
        qpT = lambda mc: s.wbuf[:, mc * 4096:(mc + 1) * 4096]
        kpT = lambda mc: s.kbuf[:, mc * 4096:(mc + 1) * 4096]
        vh3 = s.vhat[:].rearrange("p (tt pr c) -> p tt pr c", pr=8, c=130)
        guard = p.get("dve")
        guard_pe = p.get("pe")
        for j, (srcb, dstT) in enumerate(((s.qr, qpT), (s.kr, kpT))):
            sqw = [("pe", guard_pe)]
            if mat_prev:
                sqw.append((mat_prev[0], mat_prev[1]))
            msq = p.emit("vector",
                         lambda e, srcb=srcb, hb=hb: e.tensor_mul(
                             s.sqbuf[hb:hb + 64, :] if hb == 0 else s.sqbuf[64:128, :],
                             srcb[hb:hb + 64, :] if hb == 0 else srcb[64:128, :],
                             srcb[hb:hb + 64, :] if hb == 0 else srcb[64:128, :]),
                         waits=sqw, inc="dve")
            if j == 0:
                ps_free = {}
            mlast_mm = None
            for st in range(NQT):
                for mc in range(2):
                    w = []
                    if mc == 0:
                        w = [("dve", msq), ("dve", s.m_mapw)]
                        if ps_free.get(st % 2):
                            w.append(("act", ps_free[st % 2]))
                    p.emit("tensor",
                           lambda e, st=st, mc=mc, hb=hb, srcb=srcb: e.matmul(
                               s.ps_m[st % 2].ap()[:, mc * 512:(mc + 1) * 512],
                               s.mapW[hb:hb + 64, mc * 128:(mc + 1) * 128],
                               srcb[hb:hb + 64, st * QT:(st + 1) * QT],
                               start=True, stop=False),
                           waits=w)
                    mlast_mm = p.emit("tensor",
                                      lambda e, st=st, mc=mc, hb=hb: e.matmul(
                                          s.ps_m[st % 2].ap()[:, mc * 512:(mc + 1) * 512],
                                          s.negones[hb:hb + 64, mc * 128:(mc + 1) * 128],
                                          s.sqbuf[hb:hb + 64, st * QT:(st + 1) * QT],
                                          start=False, stop=True),
                                      inc="pe")
                dst = (s.wbuf if j == 0 else s.kbuf)[:].rearrange(
                    "p (mc x) -> p mc x", mc=2)[:, :, st * QT:(st + 1) * QT]
                mex = p.emit("scalar",
                             lambda e, st=st, dst=dst: e.activation(
                                 dst, s.ps_m[st % 2].ap().rearrange("p (mc x) -> p mc x", mc=2),
                                 AF.Exp),
                             waits=[("pe", mlast_mm)], inc="act")
                ps_free[st % 2] = mex
            if j == 0:
                guard_pe = p.get("pe")
        m_exp_done = p.get("act")
        ms_ktok = {}
        m_ktok_last = None
        m_mask_last = None
        m_atile_free = {}
        m_kptok_free = {}
        m_kv_evac = None
        m_norm = None
        m_out_last = None
        for i in range(32):
            par = i % 2
            w = [("act", m_exp_done)] if i == 0 else [("pe", m_out_last)]
            if m_ktok_last:
                w.append(("dve", m_ktok_last))
            for mc in range(2):
                mt = p.emit("tensor",
                            lambda e, i=i, mc=mc: e.transpose(
                                s.ps_tr[0][:, mc * 1024:mc * 1024 + 128],
                                kpT(mc)[:, i * 128:(i + 1) * 128],
                                s.cst["c_ident"][:]),
                            waits=w if mc == 0 else [], inc="pe" if mc == 1 else None)
            kw = [("pe", mt)]
            if m_kptok_free.get(par):
                kw.append(("pe", m_kptok_free[par]))
            mktok = p.emit("vector",
                           lambda e, i=i, par=par: e.tensor_copy(
                               s.kptok[par][:].rearrange("p (mc x) -> p mc x", mc=2),
                               s.ps_tr[0].rearrange("p (mc x) -> p mc x", mc=2)[:, :, 0:128]),
                           waits=kw, inc="dve")
            ms_ktok[par] = mktok
            m_ktok_last = mktok
            w = [("dve", m_mask_last)] if m_mask_last else []
            for mc in range(2):
                ma = p.emit("tensor",
                            lambda e, i=i, mc=mc: e.matmul(
                                s.ps_a.ap()[:, 0:128],
                                kpT(mc)[:, i * 128:(i + 1) * 128],
                                qpT(mc)[:, i * 128:(i + 1) * 128],
                                start=(mc == 0), stop=(mc == 1)),
                            waits=w if mc == 0 else [], inc="pe" if mc == 1 else None)
            mw2 = [("pe", ma), ("q_ig", s.m_ig)]
            if m_atile_free.get(par):
                mw2.append(("pe", m_atile_free[par]))
            mmask = p.emit("vector",
                           lambda e, par=par: e.tensor_tensor(
                               s.atile[par][:], s.ps_a.ap()[:, 0:128], s.cst["c_tri"][:], ALU.mult),
                           waits=mw2, inc="dve")
            m_mask_last = mmask
            ob = (i % 4) * 65
            w = [("dve", mmask)]
            if i % 4 == 0 and m_norm:
                w.append(("dve", m_norm))
            m_out_last = p.emit("tensor",
                                lambda e, i=i, ob=ob, par=par, hp=hp, half=half: e.matmul(
                                    s.ps_o4.ap()[:, ob:ob + 65], s.atile[par][:],
                                    vh3[:, i, hp, half * 65:half * 65 + 65],
                                    start=True, stop=(i == 0)),
                                waits=w, inc="pe" if i == 0 else None)
            if i > 0:
                for mc in range(2):
                    m_out_last = p.emit("tensor",
                                        lambda e, i=i, ob=ob, mc=mc: e.matmul(
                                            s.ps_o4.ap()[:, ob:ob + 65],
                                            qpT(mc)[:, i * 128:(i + 1) * 128],
                                            s.kvbf[:, mc * 65:(mc + 1) * 65],
                                            start=False, stop=(mc == 1)),
                                        waits=[m_kv_evac] if mc == 0 else [],
                                        inc="pe" if mc == 1 else None)
            m_atile_free[par] = m_out_last
            w = [("dve", mktok)]
            if m_kv_evac:
                w.append(m_kv_evac)
            for mc in range(2):
                mkv = p.emit("tensor",
                             lambda e, i=i, mc=mc, par=par, hp=hp, half=half: e.matmul(
                                 s.ps_kv.ap()[:, mc * 65:(mc + 1) * 65],
                                 s.kptok[par][:, mc * 128:(mc + 1) * 128],
                                 vh3[:, i, hp, half * 65:half * 65 + 65],
                                 start=True, stop=True),
                             waits=w if mc == 0 else [], inc="pe" if mc == 1 else None)
            m_kptok_free[par] = mkv
            if i == 0:
                madd = p.emit("vector",
                              lambda e, par=par: e.tensor_copy(s.kvf32[:, par * 130:(par + 1) * 130],
                                                               s.ps_kv.ap()[:, 0:130]),
                              waits=[("pe", mkv)], inc="dve")
            else:
                madd = p.emit("vector",
                              lambda e, par=par: e.tensor_add(s.kvf32[:, par * 130:(par + 1) * 130],
                                                              s.kvf32[:, (1 - par) * 130:(2 - par) * 130],
                                                              s.ps_kv.ap()[:, 0:130]),
                              waits=[("pe", mkv)], inc="dve")
            m_kv_evac = ("act", p.emit("scalar",
                                       lambda e, par=par: e.activation(
                                           s.kvbf[:], s.kvf32[:, par * 130:(par + 1) * 130],
                                           AF.Copy),
                                       waits=[("dve", madd)], inc="act"))
            if i % 4 == 3:
                g0 = i - 3
                p.emit("vector",
                       lambda e: e.tensor_copy(
                           s.dbuf[:, 0:4],
                           s.ps_o4.ap()[:, 0:260].rearrange("p (g c) -> p g c", c=65)[:, :, 64:65]
                           .rearrange("p g o -> p (g o)")),
                       waits=[("pe", m_out_last)], inc="dve")
                p.emit("vector", lambda e: e.reciprocal(s.rbuf[:, 0:4], s.dbuf[:, 0:4]), inc="dve")
                m_norm = p.emit("vector",
                                lambda e, g0=g0: e.tensor_tensor(
                                    s.attntok[:, g0 * 64:(g0 + 4) * 64].rearrange(
                                        "p (g d) -> p g d", g=4),
                                    s.ps_o4.ap()[:, 0:260].rearrange("p (g c) -> p g c", c=65)[:, :, 0:64],
                                    s.rbuf[:, 0:4].rearrange("p (g o) -> p g o", o=1).broadcast_to(
                                        [128, 4, 64]),
                                    ALU.mult),
                                inc="dve")
        ab = (1 - half) * 64
        matp = {}
        for tt in range(32):
            par = tt % 2
            w = [("dve", m_norm)] if tt < 2 else []
            if matp.get(par):
                w.append(("dve", matp[par]))
            if tt == 0 and mat_prev:
                w.append((mat_prev[0], mat_prev[1]))
            mt = p.emit("tensor",
                        lambda e, tt=tt, par=par, ab=ab: e.transpose(
                            s.ps_tr[1][ab:ab + 64, par * 1024:par * 1024 + 128],
                            s.attntok[:, tt * 64:(tt + 1) * 64],
                            s.cst["c_ident"][:], tile_position=(0, ab)),
                        waits=w, inc="pe")
            mv = p.emit("vector",
                        lambda e, tt=tt, par=par, ab=ab: e.tensor_copy(
                            s.sqbuf[ab:ab + 64, tt * 128:(tt + 1) * 128],
                            s.ps_tr[1][ab:ab + 64, par * 1024:par * 1024 + 128]),
                        waits=[("pe", mt)], inc="dve")
            matp[par] = mv
        mat = p.dma("sync", adram[h * 64:(h + 1) * 64, :], s.sqbuf[ab:ab + 64, :],
                    sem="q_ata" if h % 2 == 0 else "q_atb", waits=[("dve", p.get("dve"))])
        return (("q_ata" if h % 2 == 0 else "q_atb"), mat)

    def oproj_ln(self, adram):
        p, s, io = self.p, self, self.io
        guard_pe = p.get("pe")
        guard_dve = p.get("dve")
        for c in range(NCH):
            p.dma("gpsimd", s.kbuf[:, c * 1024:(c + 1) * 1024],
                  io["out_w"][c * 128:(c + 1) * 128, :], sem="q_ow",
                  waits=[("pe", guard_pe), ("dve", guard_dve)] if c == 0 else [])
        m_ow = p.get("q_ow")
        eps = 1e-5
        mab_prev = {}
        mdma_prev = {}
        mat_ld = {}
        for qt in range(NQT):
            par = qt % 2
            w = ([("pe", guard_pe), ("dve", guard_dve),
                  ("q_ata", p.get("q_ata")), ("q_atb", p.get("q_atb"))]
                 if qt < 2 else [("pe", mat_ld[par])])
            mld = p.dma("sync", s.wbuf[:, par * 4096:(par + 1) * 4096].rearrange(
                            "p (c t) -> p c t", c=8),
                        adram.rearrange("(c p) t -> p c t", p=128)[:, :, qt * QT:(qt + 1) * QT],
                        sem="q_aa" if par == 0 else "q_ab", waits=w)
            guard2 = p.get("dve")
            ps_free = {}
            for oc in range(NCH):
                for ec in range(NCH):
                    w2 = []
                    if ec == 0:
                        w2 = [(("q_aa" if par == 0 else "q_ab"), mld), ("q_ow", m_ow)]
                        if ps_free.get(oc % 2):
                            w2.append(("dve", ps_free[oc % 2]))
                    mpe = p.emit("tensor",
                                 lambda e, oc=oc, ec=ec, par=par: e.matmul(
                                     s.ps_m[oc % 2].ap()[:, 0:512],
                                     s.kbuf[:, ec * 1024 + oc * 128:ec * 1024 + (oc + 1) * 128],
                                     s.wbuf[:, par * 4096 + ec * 512:par * 4096 + (ec + 1) * 512],
                                     start=(ec == 0), stop=(ec == NCH - 1)),
                                 waits=w2, inc="pe" if ec == NCH - 1 else None)
                mres = p.emit("vector",
                              lambda e, oc=oc, qt=qt: e.scalar_tensor_tensor(
                                  s.resbuf[:, oc * QT:(oc + 1) * QT], s.ps_m[oc % 2].ap()[:, 0:512],
                                  self.bias_ap(3, oc),
                                  s.xbf[oc][:, qt * QT:(qt + 1) * QT], ALU.add, ALU.add),
                              waits=[("pe", mpe)], inc="dve")
                ps_free[oc % 2] = mres
            mat_ld[par] = p.get("pe")
            self._ln_stats(s.resbuf, qt)
            mb = p.emit("vector", lambda e: e.tensor_tensor(s.onesrow[0:1, :], s.rowbf[0:1, 0:QT],
                                                            s.rowbuf[0:1, 0:QT], ALU.mult),
                        inc="dve")
            mab_last = None
            for oc in range(NCH):
                w2 = [("dve", mb)]
                if mab_last:
                    w2.append(("dve", mab_last))
                p.emit("tensor",
                       lambda e, oc=oc: e.matmul(
                           s.ps_bc[:, 0:QT], s.gb2[0:1, oc * 128:(oc + 1) * 128],
                           s.rowbf[0:1, 0:QT], start=True, stop=True),
                       waits=w2)
                mabx = p.emit("tensor",
                              lambda e, oc=oc: e.matmul(
                                  s.ps_bc2.ap()[:, 0:QT], s.gb2[:, oc * 128:(oc + 1) * 128],
                                  s.onesrow[:], start=True, stop=True),
                              inc="pe")
                p.emit("vector",
                       lambda e, oc=oc: e.tensor_tensor(
                           s.x1tmp, s.resbuf[:, oc * QT:(oc + 1) * QT], s.ps_bc[:, 0:QT], ALU.mult),
                       waits=[("pe", mabx)], inc="dve")
                p.emit("vector",
                       lambda e: e.tensor_sub(s.ybuf, s.x1tmp, s.ps_bc2.ap()[:, 0:QT]),
                       inc="dve")
                mab_last = p.emit("vector",
                                  lambda e, oc=oc, qt=qt: e.tensor_tensor(
                                      s.res2buf[:, oc * QT:(oc + 1) * QT], s.ybuf,
                                      s.xbf[oc][:, qt * QT:(qt + 1) * QT], ALU.add),
                                  inc="dve")
            self._ln_stats(s.res2buf, qt)
            mbb = p.emit("vector", lambda e: e.tensor_tensor(s.rowbf[0:1, QT:2 * QT],
                                                             s.rowbf[0:1, 0:QT],
                                                             s.rowbuf[0:1, 0:QT], ALU.mult),
                         inc="dve")
            p.emit("tensor", lambda e: e.matmul(s.ps_bc[:, 0:QT], s.cst["c_ones1"][0:1, :],
                                                s.rowbf[0:1, 0:QT], start=True, stop=True),
                   waits=[("dve", mbb)])
            mab = p.emit("tensor", lambda e: e.matmul(s.ps_bc2.ap()[:, 0:QT], s.cst["c_ones1"][0:1, :],
                                                      s.rowbf[0:1, QT:2 * QT], start=True, stop=True),
                         inc="pe")
            for oc in range(NCH):
                xb = s.x1buf[oc % 2]
                w2 = [("pe", mab)]
                if mdma_prev.get(oc % 2):
                    w2.append(("q_sta" if oc % 2 == 0 else "q_stb", mdma_prev[oc % 2]))
                p.emit("vector",
                       lambda e, oc=oc: e.tensor_tensor(
                           s.x1tmp, s.res2buf[:, oc * QT:(oc + 1) * QT], s.ps_bc[:, 0:QT],
                           ALU.mult),
                       waits=w2, inc="dve")
                mx1 = p.emit("vector",
                             lambda e, xb=xb: e.tensor_sub(xb, s.x1tmp, s.ps_bc2.ap()[:, 0:QT]),
                             inc="dve")
                mdma_prev[oc % 2] = p.dma("sync",
                                          io["x3T"][oc * 128:(oc + 1) * 128, qt * QT:(qt + 1) * QT],
                                          xb, waits=[("dve", mx1)],
                                          sem="q_sta" if oc % 2 == 0 else "q_stb")

    def _ln_stats(self, resb, qt, eps=1e-5):
        p, s = self.p, self
        guard_pe2 = p.get("pe")
        guard_dve3 = p.get("dve")
        for oc in range(NCH):
            w = [("dve", guard_dve3), ("pe", guard_pe2)] if oc == 0 else []
            p.emit("tensor",
                   lambda e, oc=oc, resb=resb: e.matmul(
                       s.ps_bc.ap()[0:1, 0:QT], s.cst["c_onescol"][:, 0:1],
                       resb[:, oc * QT:(oc + 1) * QT],
                       start=(oc == 0), stop=(oc == NCH - 1)),
                   waits=w, inc="pe" if oc == NCH - 1 else None)
        sq_free = {}
        for oc in range(NCH):
            sp = s.sqscr[oc % 2]
            w = [("dve", guard_dve3)] if oc == 0 else []
            if sq_free.get(oc % 2):
                w = list(w) + [("pe", sq_free[oc % 2])]
            mact2 = p.emit("scalar",
                           lambda e, oc=oc, sp=sp, resb=resb: e.activation(
                               sp, resb[:, oc * QT:(oc + 1) * QT], AF.Square),
                           waits=w, inc="act")
            mpe_sq = p.emit("tensor",
                            lambda e, oc=oc, sp=sp: e.matmul(
                                s.ps_bc2.ap()[0:1, 0:QT], s.cst["c_onescol"][:, 0:1], sp,
                                start=(oc == 0), stop=(oc == NCH - 1)),
                            waits=[("act", mact2)], inc="pe")
            sq_free[oc % 2] = mpe_sq
        mpe_stats = p.get("pe")
        p.emit("vector", lambda e: e.tensor_scalar_mul(s.rowbuf[0:1, 0:QT],
                                                       s.ps_bc.ap()[0:1, 0:QT], 1.0 / 1024),
               waits=[("pe", mpe_stats)], inc="dve")
        p.emit("vector", lambda e: e.tensor_scalar_mul(s.rowbuf[0:1, QT:2 * QT],
                                                       s.ps_bc2.ap()[0:1, 0:QT], 1.0 / 1024),
               inc="dve")
        p.emit("vector", lambda e: e.tensor_tensor(s.rowbuf[0:1, 2 * QT:3 * QT],
                                                   s.rowbuf[0:1, 0:QT], s.rowbuf[0:1, 0:QT],
                                                   ALU.mult), inc="dve")
        p.emit("vector", lambda e: e.tensor_scalar_add(s.rowslotE,
                                                       s.rowbuf[0:1, 2 * QT:3 * QT], -eps),
               inc="dve")
        mvar = p.emit("vector", lambda e: e.tensor_sub(s.rowbuf[0:1, 3 * QT:4 * QT],
                                                       s.rowbuf[0:1, QT:2 * QT],
                                                       s.rowslotE), inc="dve")
        mln2 = p.emit("scalar", lambda e: e.activation(s.rowslot4,
                                                       s.rowbuf[0:1, 3 * QT:4 * QT], AF.Ln),
                      waits=[("dve", mvar)], inc="act")
        p.emit("scalar", lambda e: e.activation(s.rowbf[0:1, 0:QT],
                                                s.rowslot4, AF.Exp, scale=-0.5),
               waits=[("act", mln2)], inc="act")
        p.emit("vector", lambda e: e.engine_nop(), waits=[("act", p.get("act"))], inc="dve")

    def run(self):
        p, s, nc = self.p, self, self.nc
        s.alloc()
        s.load_consts()
        self.prep()
        self.build_vhat()
        adram = nc.dram_tensor("b2_attnT", [1024, T], BF16, kind="Internal").ap()
        mat_prev = None
        for hp in range(8):
            self.pair_qk(hp, mat_prev)
            for half in range(2):
                mat_prev = self.head(hp, half, adram, mat_prev)
        self.oproj_ln(adram)
        p.emit("sync", lambda e: None,
               waits=[("q_sta", p.get("q_sta")), ("q_stb", p.get("q_stb"))])


class B3:

    def __init__(self, nc, p, stack, io, core_id):
        self.nc, self.p, self.stack, self.io = nc, p, stack, io
        self.E = stack.enter_context
        self.cid = core_id

    def alloc(self):
        nc, E = self.nc, self.E
        s = self
        s.xbf = [E(nc.sbuf_tensor(f"b3_xbf{c}", [128, T], BF16)) for c in range(NCH)]
        s.hT = E(nc.sbuf_tensor("b3_hT", [128, 16 * 1024], BF16))
        s.xtok = E(nc.sbuf_tensor("b3_xtok", [128, 2048], BF16))
        s.wst = [E(nc.sbuf_tensor(f"b3_wst{i}", [128, 4096], BF16)) for i in range(2)]
        s.gl = E(nc.sbuf_tensor("b3_gl", [16, T], F32))
        s.gltb = E(nc.sbuf_tensor("b3_gltb", [128, QT], F32))
        s.etb = [E(nc.sbuf_tensor(f"b3_etb{i}", [128, QT], F32)) for i in range(2)]
        s.esm = E(nc.sbuf_tensor("b3_esm", [128, 16 * 32], F32))
        s.idxu = E(nc.sbuf_tensor("b3_idxu", [128, 8 * 32], U32))
        s.tokid = E(nc.sbuf_tensor("b3_tokid", [128, 32], F32))
        s.offmask = E(nc.sbuf_tensor("b3_offmask", [128, 16], F32))
        s.iotatb = E(nc.sbuf_tensor("b3_iotatb", [128, QT], F32))
        s.big16 = E(nc.sbuf_tensor("b3_big16", [128, QT], F32))
        s.wgf = E(nc.sbuf_tensor("b3_wgf", [128, 8 * 16], F32))
        s.ctriF = E(nc.sbuf_tensor("b3_ctriF", [128, 128], F32))
        s.conescolF = E(nc.sbuf_tensor("b3_onescolF", [128, 1], F32))
        s.cidentF = E(nc.sbuf_tensor("b3_cidentF", [128, 128], F32))
        s.pt16 = E(nc.sbuf_tensor("b3_pt16", [16, 96], F32))
        s.onesrowF = E(nc.sbuf_tensor("b3_ones1F", [1, 128], F32))
        s.invrow = E(nc.sbuf_tensor("b3_invrow", [128, 2], U32))
        s.prow = E(nc.sbuf_tensor("b3_prow", [1, QT], F32))
        s.cnt = E(nc.sbuf_tensor("b3_cnt", [16, 1], F32))
        s.cntall = E(nc.sbuf_tensor("b3_cntall", [128, 1], F32))
        s.offs = E(nc.sbuf_tensor("b3_offs", [16, 1], F32))
        s.invf = E(nc.sbuf_tensor("b3_invf", [128, 16], F32))
        s.invu = E(nc.sbuf_tensor("b3_invu", [128, 16], U32))
        s.zero = E(nc.sbuf_tensor("b3_zero", [128, 144], F32))
        s.ptrash = E(nc.sbuf_tensor("b3_ptrash", [128, 32], F32))
        s.qgrid = E(nc.sbuf_tensor("b3_qgrid", [128, 16], F32))
        s.qgrid2 = E(nc.sbuf_tensor("b3_qgrid2", [128, 16], F32))
        s.gout = E(nc.sbuf_tensor("b3_gout", [128, 1024], BF16))
        s.sdump = E(nc.sbuf_tensor("b3_sdump", [128, 1024], BF16))
        s.resf = E(nc.sbuf_tensor("b3_resf", [128, 1024], F32))
        s.outf = [E(nc.sbuf_tensor(f"b3_outf{i}", [128, 1024], F32)) for i in range(2)]
        s.otok = [E(nc.sbuf_tensor(f"b3_otok{i}", [128, 1024], BF16)) for i in range(2)]
        s.lnm = E(nc.sbuf_tensor("b3_lnm", [128, 8], F32))
        s.cst = {}
        s.ident = E(nc.sbuf_tensor("b3_ident", [128, 128], BF16))
        s.ps = [E(nc.psum_tensor(f"b3_ps{i}", [128, 512], F32)) for i in range(6)]
        s.ps_r = E(nc.psum_tensor("b3_ps_r", [128, 512], F32))
        s.ps_r2 = E(nc.psum_tensor("b3_ps_r2", [128, 512], F32))
        s.ps_tr = [s.ps[i].ap().bitcast(BF16) for i in range(4)]

    def load_consts(self):
        p, s, io = self.p, self, self.io
        for c in range(NCH):
            p.dma("gpsimd", s.xbf[c][:], io["x3T"][c * 128:(c + 1) * 128, :], sem="q_ig")
        p.dma("gpsimd", s.ident[:], io["c_ident"], sem="q_ig")
        p.dma("sync", s.tokid[:], io["c_tokid"], sem="q_is")
        p.dma("sync", s.offmask[:], io["c_offmask"], sem="q_is")
        p.dma("sync", s.iotatb[:], io["c_iotatb"], sem="q_is")
        p.dma("sync", s.ctriF[:], io["c_tri"], sem="q_is")
        p.dma("sync", s.conescolF[:], io["c_onescol"], sem="q_is")
        p.dma("sync", s.cidentF[:], io["c_ident"], sem="q_is")
        for c in range(NCH):
            p.dma("sync", s.wgf[:, c * 16:(c + 1) * 16], io["Wg"][c * 128:(c + 1) * 128, :], sem="q_is")
        p.dma("sync", s.onesrowF[:], io["c_ones1"], sem="q_is")
        p.dma("sync", s.invrow[:], io["c_invrow"], sem="q_is")
        p.dma("sync", s.ptrash[:], io["c_ptrash"], sem="q_is")
        p.dma("sync", s.qgrid[:], io["c_qgrid"], sem="q_is")
        p.dma("sync", s.qgrid2[:], io["c_qgrid2"], sem="q_is")
        s.m_ig = p.get("q_ig")
        s.m_is = p.get("q_is")
        p.emit("vector", lambda e: e.memset(s.big16[:], 16.0))
        p.emit("vector", lambda e: e.memset(s.zero[:], 0.0))
        s.m_memset = p.get("dve")

    def routing(self):
        p, s, nc, io = self.p, self, self.nc, self.io
        sm = lambda i: s.esm[:, i * 32:(i + 1) * 32]
        sm3 = lambda i: s.esm[:, i * 32:(i + 1) * 32].rearrange("p (c o) -> p c o", o=1)
        bc = lambda i: sm3(i).broadcast_to([128, 32, 16])
        s.agx_in = nc.dram_tensor("b3_agx_in", [T, 1024], BF16, kind="Internal")
        s.agx_out = nc.dram_tensor("b3_agx_out", [8 * T, 1024], BF16, kind="Internal",
                                   addr_space="Shared")
        mstore_prev = {}
        for tt in range(32):
            for c in range(NCH):
                g = c % 4
                w = [("q_ig", s.m_ig)]
                if c >= 4:
                    w.append(s._xtev[(tt, c - 4)])
                elif tt > 0:
                    w.append(s._xtev[(tt - 1, c + 4)])
                mt = p.emit("tensor",
                            lambda e, tt=tt, c=c, g=g: e.transpose(
                                s.ps_tr[g][:, 0:128], s.xbf[c][:, tt * 128:(tt + 1) * 128],
                                s.ident[:]),
                            waits=w, inc="pe")
                if not hasattr(s, "_xtev"):
                    s._xtev = {}
                vw = [("pe", mt)]
                if tt > 0 and c <= 1:
                    par = tt % 2
                    if mstore_prev.get(par):
                        vw.append((("q_xta" if par == 0 else "q_xtb"), mstore_prev[par]))
                dst_lo = (tt % 2) * 1024 + c * 128
                if c % 2 == 0:
                    mcp = p.emit("vector",
                                 lambda e, dst_lo=dst_lo, g=g: e.tensor_copy(
                                     s.xtok[:, dst_lo:dst_lo + 128], s.ps_tr[g][:, 0:128]),
                                 waits=vw, inc="dve")
                    s._xtev[(tt, c)] = ("dve", mcp)
                else:
                    mcp = p.emit("scalar",
                                 lambda e, dst_lo=dst_lo, g=g: e.activation(
                                     s.xtok[:, dst_lo:dst_lo + 128], s.ps_tr[g][:, 0:128],
                                     AF.Copy),
                                 waits=vw, inc="act")
                    s._xtev[(tt, c)] = ("act", mcp)
            par = tt % 2
            mstore_prev[par] = p.dma("sync", s.agx_in.ap()[tt * 128:(tt + 1) * 128, :],
                                     s.xtok[:, par * 1024:(par + 1) * 1024],
                                     waits=[s._xtev[(tt, NCH - 2)], s._xtev[(tt, NCH - 1)]],
                                     sem="q_xta" if par == 0 else "q_xtb")
        m_xtok = (mstore_prev[0], mstore_prev[1])
        mag2 = p.emit("gpsimd",
                      lambda e: e.collective_compute(
                          "AllGather", ALU.bypass, replica_groups=[list(range(8))],
                          ins=[s.agx_in.ap().opt()], outs=[s.agx_out.ap().opt()]),
                      waits=[("q_xta", m_xtok[0]), ("q_xtb", m_xtok[1])], inc="cc2", is_dma=True)
        xf = s.hT[:].bitcast(F32)[:, 0:4096]
        mgl_prev = None
        for qt in range(NQT):
            mld = p.dma("sync", xf.rearrange("p (c t) -> p c t", c=NCH),
                        io["x3T"].rearrange("(c p) t -> p c t", p=128)[:, :, qt * QT:(qt + 1) * QT],
                        sem="q_xf", waits=[("pe", mgl_prev)] if mgl_prev else [])
            for c in range(NCH):
                w = [("q_xf", mld), ("q_is", s.m_is)] if c == 0 else []
                if c == 0 and getattr(s, "_mglv", None):
                    w.append(("dve", s._mglv))
                mgl = p.emit("tensor",
                             lambda e, qt=qt, c=c: e.matmul(
                                 s.ps_r.ap()[0:16, 0:512], s.wgf[:, c * 16:(c + 1) * 16],
                                 xf[:, c * 512:(c + 1) * 512],
                                 start=(c == 0), stop=(c == NCH - 1)),
                             waits=w, inc="pe" if c == NCH - 1 else None)
            mgl_prev = mgl
            s._mglv = p.emit("vector",
                             lambda e, qt=qt: e.tensor_copy(s.gl[0:16, qt * QT:(qt + 1) * QT],
                                                            s.ps_r.ap()[0:16, 0:512]),
                             waits=[("pe", mgl)], inc="dve")
        mgt = {}
        for tt in range(32):
            g = tt % 4
            w = [("dve", p.get("dve"))] if tt < 4 else [("dve", mgt[tt - 4])]
            mt = p.emit("tensor",
                        lambda e, tt=tt, g=g: e.transpose(
                            s.ps[g].ap()[:, 0:16], s.gl[0:16, tt * 128:(tt + 1) * 128],
                            s.cidentF[0:16, 0:16]),
                        waits=w, inc="pe")
            mgt[tt] = p.emit("vector",
                             lambda e, tt=tt, g=g: e.tensor_copy(
                                 s.gltb[:, tt * 16:(tt + 1) * 16], s.ps[g].ap()[:, 0:16]),
                             waits=[("pe", mt)], inc="dve")
        gl3 = s.gltb[:].rearrange("p (c e) -> p c e", e=16)
        t0 = s.etb[0][:].rearrange("p (c e) -> p c e", e=16)
        t1 = s.etb[1][:].rearrange("p (c e) -> p c e", e=16)
        iota3 = s.iotatb[:].rearrange("p (c e) -> p c e", e=16)
        p.emit("vector", lambda e: e.tensor_reduce(sm3(0), gl3, AX.X, ALU.max), inc="dve")
        p.emit("vector", lambda e: e.tensor_tensor(t0, gl3, bc(0), ALU.subtract), inc="dve")
        mex = p.emit("scalar", lambda e: e.activation(s.etb[1][:], s.etb[0][:], AF.Exp),
                     waits=[("dve", p.get("dve"))], inc="act")
        p.emit("vector", lambda e: e.tensor_reduce(sm3(1), t1, AX.X, ALU.add),
               waits=[("act", mex)], inc="dve")
        p.emit("vector", lambda e: e.reciprocal(sm(2), sm(1)), inc="dve")
        p.emit("vector", lambda e: e.tensor_tensor(t0, gl3, bc(0), ALU.is_ge), inc="dve")
        p.emit("vector", lambda e: e.tensor_scalar_add(s.gltb[:], s.iotatb[:], -16.0),
               waits=[("q_is", s.m_is)], inc="dve")
        p.emit("vector", lambda e: e.tensor_tensor(t1, gl3, t0, ALU.mult), inc="dve")
        p.emit("vector", lambda e: e.tensor_reduce(sm3(13), t1, AX.X, ALU.min), inc="dve")
        p.emit("vector", lambda e: e.tensor_scalar_add(sm(3), sm(13), 16.0), inc="dve")
        moh = p.emit("vector", lambda e: e.tensor_tensor(t0, iota3, bc(3), ALU.is_equal), inc="dve")
        for tt in range(32):
            w = [("dve", moh)] if tt == 0 else []
            if tt >= 2:
                w.append(("dve", s._cumev[tt - 2]))
            p.emit("tensor",
                   lambda e, tt=tt: e.matmul(
                       s.ps[4 + tt % 2].ap()[:, 0:16], s.ctriF[:],
                       t0[:, tt, :], start=True, stop=True),
                   waits=w, inc=None)
            mcc = p.emit("tensor",
                         lambda e, tt=tt: e.matmul(
                             s.ps_r2.ap()[0:16, tt:tt + 1], t0[:, tt, :], s.conescolF[:, 0:1],
                             start=True, stop=True),
                         inc="pe")
            if not hasattr(s, "_cumev"):
                s._cumev = {}
            s._cumev[tt] = p.emit("vector",
                                  lambda e, tt=tt: e.tensor_copy(
                                      s.etb[1][:].rearrange("p (c e) -> p c e", e=16)[:, tt, :],
                                      s.ps[4 + tt % 2].ap()[:, 0:16]),
                                  waits=[("pe", mcc)], inc="dve")
        m_cum = p.get("dve")
        mtt = p.emit("vector", lambda e: e.tensor_copy(s.pt16[0:16, 0:32], s.ps_r2.ap()[0:16, 0:32]),
                     waits=[("pe", p.get("pe"))], inc="dve")
        p.emit("vector", lambda e: e.tensor_reduce(s.cnt[0:16, 0:1], s.pt16[0:16, 0:32],
                                                   AX.X, ALU.add),
               inc="dve")
        p.emit("vector", lambda e: e.tensor_tensor_scan(s.pt16[0:16, 32:64], s.pt16[0:16, 0:32],
                                                        s.pt16[0:16, 0:32], 0.0, ALU.add, ALU.bypass),
               inc="dve")
        mpref = p.emit("vector", lambda e: e.tensor_sub(s.pt16[0:16, 64:96], s.pt16[0:16, 32:64],
                                                        s.pt16[0:16, 0:32]), inc="dve")
        s.agc_in = nc.dram_tensor("b3_agc_in", [16, 1], F32, kind="Internal")
        s.agc_out = nc.dram_tensor("b3_agc_out", [128, 1], F32, kind="Internal", addr_space="Shared")
        mcnt = p.dma("sync", s.agc_in.ap(), s.cnt[0:16, 0:1], sem="q_cn",
                     waits=[("dve", p.get("dve"))])
        mag1 = p.emit("gpsimd",
                      lambda e: e.collective_compute(
                          "AllGather", ALU.bypass, replica_groups=[list(range(8))],
                          ins=[s.agc_in.ap().opt()], outs=[s.agc_out.ap().opt()]),
                      waits=[("q_cn", mcnt), ("cc2", 1)], inc="cc1", is_dma=True)
        mld = p.dma("sync", s.cntall[:, 0:1], s.agc_out.ap(), sem="q_cn", waits=[("cc1", 1)])
        moff = p.emit("tensor",
                      lambda e: e.matmul(s.ps_r.ap()[0:16, 0:1], s.offmask[:], s.cntall[:, 0:1],
                                         start=True, stop=True),
                      waits=[("q_cn", mld), ("q_is", s.m_is)], inc="pe")
        p.emit("vector", lambda e: e.tensor_copy(s.offs[0:16, 0:1], s.ps_r.ap()[0:16, 0:1]),
               waits=[("pe", moff)], inc="dve")
        p.emit("vector", lambda e: e.tensor_tensor(
            s.pt16[0:16, 0:32].rearrange("p (o c) -> p o c", o=1),
            s.pt16[0:16, 64:96].rearrange("p (o c) -> p o c", o=1),
            s.offs[0:16, 0:1].rearrange("p (c o) -> p c o", o=1).broadcast_to([16, 1, 32]),
            ALU.add), inc="dve")
        mpoff = p.emit("vector", lambda e: e.tensor_scalar_add(s.pt16[0:16, 32:64],
                                                               s.pt16[0:16, 0:32], -1.0), inc="dve")
        s.poff_dram = nc.dram_tensor("b3_poff", [32, 16], F32, kind="Internal")
        mtp = p.emit("tensor",
                     lambda e: e.transpose(s.ps_r.ap()[0:32, 0:16], s.pt16[0:16, 32:64],
                                           s.cidentF[0:16, 0:16]),
                     waits=[("dve", mpoff)], inc="pe")
        mtv = p.emit("vector", lambda e: e.tensor_copy(s.big16[0:32, 0:16], s.ps_r.ap()[0:32, 0:16]),
                     waits=[("pe", mtp)], inc="dve")
        mst = p.dma("sync", s.poff_dram.ap(), s.big16[0:32, 0:16], sem="q_cn", waits=[("dve", mtv)])
        mld2 = p.dma("sync", s.prow[0:1, :], s.poff_dram.ap().rearrange("(o a) b -> o (a b)", o=1),
                     sem="q_cn", waits=[("q_cn", mst)])
        mbc = p.emit("tensor",
                     lambda e: e.matmul(s.ps_r.ap()[:, 0:512], s.onesrowF[0:1, :], s.prow[0:1, :],
                                        start=True, stop=True),
                     waits=[("q_cn", mld2), ("q_is", s.m_is)], inc="pe")
        p.emit("vector", lambda e: e.tensor_tensor(
            s.gltb[:], s.etb[1][:], s.ps_r.ap()[:, 0:512], ALU.add),
               waits=[("pe", mbc), ("dve", m_cum)], inc="dve")
        p.emit("vector", lambda e: e.tensor_tensor(t1, gl3, t0, ALU.mult), inc="dve")
        mslot = p.emit("vector", lambda e: e.tensor_reduce(sm3(4), t1, AX.X, ALU.add), inc="dve")
        p.emit("vector", lambda e: e.tensor_single_scalar(sm(5), sm(4), 2048.0, ALU.is_lt), inc="dve")
        p.emit("vector", lambda e: e.tensor_tensor(sm(6), sm(2), sm(5), ALU.mult), inc="dve")
        grid3 = s.etb[0][:].rearrange("p (c j) -> p c j", j=16)
        p.emit("vector", lambda e: e.tensor_tensor(
            grid3, sm3(4).broadcast_to([128, 32, 16]),
            s.qgrid[:].rearrange("p (o j) -> p o j", o=1).broadcast_to([128, 32, 16]),
            ALU.is_ge), inc="dve")
        p.emit("vector", lambda e: e.tensor_reduce(sm3(7), grid3, AX.X, ALU.add), inc="dve")
        p.emit("vector", lambda e: e.tensor_tensor(
            grid3, sm3(4).broadcast_to([128, 32, 16]),
            s.qgrid[:].rearrange("p (o j) -> p o j", o=1).broadcast_to([128, 32, 16])
            if False else
            s.qgrid2[:].rearrange("p (o j) -> p o j", o=1).broadcast_to([128, 32, 16]),
            ALU.is_ge), inc="dve")
        p.emit("vector", lambda e: e.tensor_reduce(sm3(8), grid3, AX.X, ALU.add), inc="dve")
        p.emit("vector", lambda e: e.tensor_tensor(sm(9), sm(7), sm(8), ALU.add), inc="dve")
        p.emit("vector", lambda e: e.scalar_tensor_tensor(sm(7), sm(9), -128.0, sm(4),
                                                          ALU.mult, ALU.add), inc="dve")
        p.emit("vector", lambda e: e.scalar_tensor_tensor(sm(10), sm(7), 16.0, sm(9),
                                                          ALU.mult, ALU.add), inc="dve")
        p.emit("vector", lambda e: e.scalar_tensor_tensor(sm(11), sm(3), 2048.0, sm(10),
                                                          ALU.mult, ALU.add), inc="dve")
        p.emit("vector", lambda e: e.tensor_tensor(sm(13), sm(11), s.ptrash[:], ALU.subtract),
               inc="dve")
        p.emit("vector", lambda e: e.tensor_tensor(sm(8), sm(13), sm(5), ALU.mult), inc="dve")
        p.emit("vector", lambda e: e.tensor_tensor(sm(12), sm(8), s.ptrash[:], ALU.add), inc="dve")
        mdu = p.emit("vector", lambda e: e.tensor_copy(s.idxu[:, 0:32], sm(12)), inc="dve")
        p.emit("vector", lambda e: e.scalar_tensor_tensor(sm(14), sm(3), 2048.0, sm(4),
                                                          ALU.mult, ALU.add), inc="dve")
        p.emit("vector", lambda e: e.tensor_tensor(sm(15), sm(14), sm(5), ALU.mult), inc="dve")
        mgu = p.emit("vector", lambda e: e.tensor_copy(s.idxu[:, 32:64], sm(15)), inc="dve")
        s.inv_dram = nc.dram_tensor("b3_inv", [36864, 1], F32, kind="Internal")
        s.inv_out = nc.dram_tensor("b3_inv_out", [36864, 1], F32, kind="Internal",
                                   addr_space="Shared")
        mz = None
        for i in range(2):
            mz = p.dma("sync", s.inv_dram.ap().rearrange("(a p f) o -> a p (f o)", a=2, p=128)[i],
                       s.zero[:, 0:144], sem="q_zf", waits=[("dve", s.m_memset)])
        msc = None
        for tt in range(32):
            w = ([("dve", mdu), ("q_zf", p.get("q_zf")), ("q_is", s.m_is)]
                 if tt == 0 else [("q_sc", msc)])
            msc = p.emit("gpsimd",
                         lambda e, tt=tt: e.indirect_dma_start(
                             out=s.inv_dram.ap(),
                             out_offset=IndirectOffsetOnAxis(s.idxu[:, tt:tt + 1], 0),
                             in_=s.tokid[:, tt:tt + 1], in_offset=None),
                         waits=w, inc="q_sc", inc_n=16, is_dma=True)
        mar = p.emit("gpsimd",
                     lambda e: e.collective_compute(
                         "AllReduce", ALU.add, replica_groups=[list(range(8))],
                         ins=[s.inv_dram.ap().opt()], outs=[s.inv_out.ap().opt()]),
                     waits=[("q_sc", p.get("q_sc")), ("cc1", 1)], inc="cc3", is_dma=True)
        s.m_routing = (mgu, mar)

    def experts(self):
        p, s, nc, io = self.p, self, self.nc, self.io
        s.w1bf = io["_wch"]["w1bf"]
        s.w2bf = io["_wch"]["w2bf"]
        s.m_wcache = io["_wch_m"]
        s.ago_in = nc.dram_tensor("b3_ago_in", [T, 1024], BF16, kind="Internal")
        s.ago_out = nc.dram_tensor("b3_ago_out", [8 * T, 1024], BF16, kind="Internal",
                                   addr_space="Shared")
        inv2d = s.inv_out.ap().rearrange("(r j) o -> r (j o)", j=16)
        dispT = lambda dc: s.xbf[4 + dc // 2][:, (dc % 2) * 2048:(dc % 2 + 1) * 2048]
        exp_guard = (None, None)
        for ex in range(2):
            mgi = p.emit("gpsimd",
                         lambda e, ex=ex: e.indirect_dma_start(
                             out=s.invf[:], out_offset=None,
                             in_=inv2d, in_offset=IndirectOffsetOnAxis(s.invrow[:, ex:ex + 1], 0)),
                         waits=[("cc3", 1), ("q_is", s.m_is)] + (
                             [("dve", exp_guard[1])] if exp_guard[1] else []),
                         inc="q_gi", inc_n=16, is_dma=True)
            mcv = p.emit("vector", lambda e: e.tensor_copy(s.invu[:], s.invf[:]),
                         waits=[("q_gi", p.get("q_gi"))], inc="dve")
            mg_miles = []
            for g in range(16):
                dst = s.xbf[g // 4][:, (g % 4) * 1024:(g % 4 + 1) * 1024]
                w = [("dve", mcv), ("cc2", 1)]
                if exp_guard[0]:
                    w.append(("pe", exp_guard[0]))
                mg_miles.append(p.emit("gpsimd",
                                       lambda e, g=g, dst=dst: e.indirect_dma_start(
                                           out=dst, out_offset=None,
                                           in_=s.agx_out.ap(),
                                           in_offset=IndirectOffsetOnAxis(s.invu[:, g:g + 1], 0)),
                                       waits=w if g == 0 else [], inc="q_gx", inc_n=16,
                                       is_dma=True))
            m_gx = mg_miles[-1]
            mtr = {}
            for g in range(16):
                for dc in range(8):
                    q = (g * 8 + dc) % 4
                    w = ([("q_gx", m_gx), ("act", p.get("act"))]
                         if (g == 0 and dc == 0) else [])
                    if mtr.get(q):
                        w.append(mtr[q])
                    mt = p.emit("tensor",
                                lambda e, g=g, dc=dc, q=q: e.transpose(
                                    s.ps_tr[q][:, 0:128],
                                    s.xbf[g // 4][:, (g % 4) * 1024 + dc * 128:
                                                  (g % 4) * 1024 + (dc + 1) * 128],
                                    s.ident[:]),
                                waits=w, inc="pe")
                    db, dlo = 4 + dc // 2, (dc % 2) * 2048 + g * 128
                    if dc % 2 == 0:
                        mtr[q] = ("dve", p.emit("vector",
                                                lambda e, db=db, dlo=dlo, q=q: e.tensor_copy(
                                                    s.xbf[db][:, dlo:dlo + 128],
                                                    s.ps_tr[q][:, 0:128]),
                                                waits=[("pe", mt)], inc="dve"))
                    else:
                        mtr[q] = ("act", p.emit("scalar",
                                                lambda e, db=db, dlo=dlo, q=q: e.activation(
                                                    s.xbf[db][:, dlo:dlo + 128],
                                                    s.ps_tr[q][:, 0:128], AF.Copy),
                                                waits=[("pe", mt)], inc="act"))
            m_dispT = (p.get("dve"), p.get("act"))
            for st in range(4):
                wld_prev = {}
                relu_prev = {}
                mpe = None
                for fc8 in range(8):
                    par = fc8 % 2
                    w = [("q_wch", s.m_wcache)]
                    if wld_prev.get(par):
                        w.append(("pe", wld_prev[par]))
                    if st > 0 or ex > 0:
                        w.append(("pe", s._w2done)) if getattr(s, "_w2done", None) else None
                    mldw = p.dma("sync", s.wst[par][:].rearrange("p (c f) -> p c f", c=8),
                                 s.w1bf.ap()[ex * 1024:(ex + 1) * 1024,
                                             fc8 * 512:(fc8 + 1) * 512]
                                 .rearrange("(c p) f -> p c f", p=128),
                                 sem="q_w1a" if par == 0 else "q_w1b", waits=w)
                    for f2 in range(4):
                        fc = fc8 * 4 + f2
                        for dc in range(8):
                            w2 = []
                            if dc == 0:
                                w2 = [(("q_w1a" if par == 0 else "q_w1b"), mldw)]
                                if fc8 == 0 and f2 == 0:
                                    w2.append(("dve", m_dispT[0]))
                                    w2.append(("act", m_dispT[1]))
                                if relu_prev.get(f2 % 2):
                                    w2.append(("act", relu_prev[f2 % 2]))
                            mpe = p.emit("tensor",
                                         lambda e, f2=f2, dc=dc, st=st, par=par: e.matmul(
                                             s.ps[4 + f2 % 2].ap()[:, 0:512],
                                             s.wst[par][:].rearrange("p (c f) -> p c f", c=8)[
                                                 :, dc, f2 * 128:(f2 + 1) * 128],
                                             dispT(dc)[:, st * 512:(st + 1) * 512],
                                             start=(dc == 0), stop=(dc == 7)),
                                         waits=w2, inc="pe" if dc == 7 else None)
                        relu_prev[f2 % 2] = p.emit("scalar",
                                                   lambda e, fc=fc, f2=f2: e.activation(
                                                       s.hT[:, fc * 512:(fc + 1) * 512],
                                                       s.ps[4 + f2 % 2].ap()[:, 0:512], AF.Relu),
                                                   waits=[("pe", mpe)], inc="act")
                    wld_prev[par] = mpe
                m_h = p.get("act")
                psof = lambda oc: (s.ps[oc] if oc < 6 else (s.ps_r if oc == 6 else s.ps_r2))
                mld2 = None
                mpo = None
                for w2g in range(8):
                    par = w2g % 2
                    w = [("act", m_h)]
                    if w2g >= 2:
                        w.append(("pe", s._w2mm[par]))
                    mld2 = p.dma("sync", s.wst[par][:].rearrange("p (c o) -> p c o", c=4),
                                 s.w2bf.ap()[ex * 4096 + w2g * 512:ex * 4096 + (w2g + 1) * 512, :]
                                 .rearrange("(c p) o -> p c o", p=128),
                                 sem="q_w2a" if par == 0 else "q_w2b", waits=w)
                    for oc in range(8):
                        for c4 in range(4):
                            fch = w2g * 4 + c4
                            w3 = []
                            if c4 == 0 and oc == 0:
                                w3 = [(("q_w2a" if par == 0 else "q_w2b"), mld2)]
                                if w2g == 0:
                                    w3.append(("dve", getattr(s, "_oev_last", None)))
                            mpo = p.emit("tensor",
                                         lambda e, oc=oc, c4=c4, fch=fch, par=par: e.matmul(
                                             psof(oc).ap()[:, 0:512],
                                             s.wst[par][:].rearrange("p (c o) -> p c o", c=4)[
                                                 :, c4, oc * 128:(oc + 1) * 128],
                                             s.hT[:, fch * 512:(fch + 1) * 512],
                                             start=(fch == 0), stop=(fch == 31)),
                                         waits=[x for x in w3 if x],
                                         inc="pe" if (oc == 7 and c4 == 3) else None)
                    if not hasattr(s, "_w2mm"):
                        s._w2mm = {}
                    s._w2mm[par] = mpo
                s._w2done = mpo
                oev = None
                for oc in range(8):
                    oev = p.emit("vector",
                                 lambda e, oc=oc: e.tensor_copy(
                                     s.xbf[0][:, oc * 512:(oc + 1) * 512], psof(oc).ap()[:, 0:512]),
                                 waits=[("pe", s._w2done)] if oc == 0 else [], inc="dve")
                s._oev_last = oev
                mot = getattr(s, "_mot", None)
                for tl in range(4):
                    for oc in range(8):
                        q = (tl * 8 + oc) % 4
                        w = [("dve", oev)] if (tl == 0 and oc == 0) else []
                        if mtr.get(("o", q)):
                            w.append(mtr[("o", q)])
                        mt = p.emit("tensor",
                                    lambda e, tl=tl, oc=oc, q=q: e.transpose(
                                        s.ps_tr[q][:, 0:128],
                                        s.xbf[0][:, oc * 512 + tl * 128:oc * 512 + (tl + 1) * 128],
                                        s.ident[:]),
                                    waits=w, inc="pe")
                        cw = [("pe", mt)]
                        if oc <= 1 and mot:
                            cw.append(("q_oa", mot))
                        if oc % 2 == 0:
                            mtr[("o", q)] = ("dve", p.emit("vector",
                                                           lambda e, oc=oc, q=q: e.tensor_copy(
                                                               s.gout[:, oc * 128:(oc + 1) * 128],
                                                               s.ps_tr[q][:, 0:128]),
                                                           waits=cw, inc="dve"))
                        else:
                            mtr[("o", q)] = ("act", p.emit("scalar",
                                                           lambda e, oc=oc, q=q: e.activation(
                                                               s.gout[:, oc * 128:(oc + 1) * 128],
                                                               s.ps_tr[q][:, 0:128], AF.Copy),
                                                           waits=cw, inc="act"))
                    mot = p.dma("sync",
                                s.ago_in.ap()[ex * 2048 + st * 512 + tl * 128:
                                              ex * 2048 + st * 512 + (tl + 1) * 128, :],
                                s.gout[:], sem="q_oa",
                                waits=[("dve", p.get("dve")), ("act", p.get("act"))])
                    s._mot = mot
            exp_guard = (p.get("pe"), p.get("dve"))
        s.m_ago = p.get("q_oa")
        mag3 = p.emit("gpsimd",
                      lambda e: e.collective_compute(
                          "AllGather", ALU.bypass, replica_groups=[list(range(8))],
                          ins=[s.ago_in.ap().opt()], outs=[s.ago_out.ap().opt()]),
                      waits=[("q_oa", s.m_ago), ("cc3", 1)], inc="cc4", is_dma=True)

    def final(self):
        p, s, io = self.p, self, self.io
        gout2 = s.wst[1][:, 0:1024]
        resf2 = s.wst[0][:].bitcast(F32)[:, 0:1024]
        mres_prev = {}
        mact_prev = {}
        mo_prev = {}
        eps = 1e-5
        for tt in range(32):
            par = tt % 2
            gb = s.gout[:] if par == 0 else gout2
            rb = s.resf[:] if par == 0 else resf2
            xt = s.xtok[:, par * 1024:(par + 1) * 1024]
            Lt = s.lnm if par == 0 else s.invf
            L = lambda i: Lt[:, i:i + 1]
            w = ([("cc4", 1), ("dve", p.get("dve")), ("act", p.get("act"))]
                 if tt == 0 else [])
            if mres_prev.get(par):
                w.append(("dve", mres_prev[par]))
            qgb = "q_gb" if par == 0 else "q_gb2"
            qxr = "q_xr" if par == 0 else "q_xr2"
            mgr = p.emit("gpsimd",
                         lambda e, tt=tt, gb=gb: e.indirect_dma_start(
                             out=gb, out_offset=None,
                             in_=s.ago_out.ap(),
                             in_offset=IndirectOffsetOnAxis(s.idxu[:, 32 + tt:33 + tt], 0)),
                         waits=w, inc=qgb, inc_n=16, is_dma=True)
            mxl = p.dma("gpsimd", xt, s.agx_in.ap()[tt * 128:(tt + 1) * 128, :],
                        sem=qxr, waits=[("q_xta", p.get("q_xta")), ("q_xtb", p.get("q_xtb"))]
                        if tt == 0 else [])
            rw = [(qgb, p.get(qgb)), (qxr, p.get(qxr))]
            if mact_prev.get(par):
                rw.append(("act", mact_prev[par]))
            mres = p.emit("vector",
                          lambda e, tt=tt, gb=gb, rb=rb, xt=xt: e.scalar_tensor_tensor(
                              rb, gb, s.esm[:, 6 * 32 + tt:6 * 32 + tt + 1],
                              xt, ALU.mult, ALU.add),
                          waits=rw, inc="dve")
            mres_prev[par] = mres
            p.emit("scalar", lambda e, rb=rb, L=L: e.activation(s.sdump[:], rb, AF.Square,
                                                                accum_out=L(2)),
                   waits=[("dve", mres)], inc="act")
            mcp = p.emit("scalar", lambda e, rb=rb, L=L: e.activation(s.sdump[:], rb, AF.Copy,
                                                                      accum_out=L(0)),
                         inc="act")
            mact_prev[par] = mcp
            p.emit("scalar", lambda e, L=L: e.activation(L(1), L(0), AF.Copy, scale=1.0 / 1024),
                   inc="act")
            mmsq = p.emit("scalar", lambda e, L=L: e.activation(L(3), L(2), AF.Copy,
                                                                scale=1.0 / 1024, bias=eps),
                          inc="act")
            p.emit("vector", lambda e, L=L: e.tensor_tensor(L(4), L(1), L(1), ALU.mult),
                   waits=[("act", mmsq)], inc="dve")
            mvar = p.emit("vector", lambda e, L=L: e.tensor_sub(L(5), L(3), L(4)), inc="dve")
            p.emit("scalar", lambda e, L=L: e.activation(L(6), L(5), AF.Ln),
                   waits=[("dve", mvar)], inc="act")
            mrs = p.emit("scalar", lambda e, L=L: e.activation(L(7), L(6), AF.Exp, scale=-0.5),
                         inc="act")
            ob = s.outf[par]
            w4 = [("act", mrs)]
            if mo_prev.get(par):
                w4.append((("q_fa" if par == 0 else "q_fb"), mo_prev[par]))
            mfin = p.emit("vector",
                          lambda e, ob=ob, rb=rb, L=L: e.tensor_scalar(
                              ob[:], rb, L(1), L(7), ALU.subtract, ALU.mult),
                          waits=w4, inc="dve")
            mo_prev[par] = p.dma("sync", io["outTOK"][tt * 128:(tt + 1) * 128, :], ob[:],
                                 waits=[("dve", mfin)], sem="q_fa" if par == 0 else "q_fb")
        p.emit("sync", lambda e: None,
               waits=[("q_fa", p.get("q_fa")), ("q_fb", p.get("q_fb"))])

    def run(self):
        p, s = self.p, self
        s.alloc()
        s.load_consts()
        self.routing()
        self.experts()
        self.final()


from concourse.bass_utils import run_bass_kernel_spmd

_NCACHE = {}


def _build_fused():
    nc = bass.Bass("TRN2", target_bir_lowering=False)
    io = {}
    io["xT"] = nc.dram_tensor("xT", [D, T], F32, kind="ExternalInput").ap()
    for wn in ["wq", "wk", "wv", "wo"]:
        io[wn] = nc.dram_tensor(wn, [D, D], F32, kind="ExternalInput").ap()
    for bn in ["bq", "bk", "bv", "bo"]:
        io[bn] = nc.dram_tensor(bn, [D], F32, kind="ExternalInput").ap()
    for cn in CONST_SPECS:
        io[cn] = nc.dram_tensor(cn, list(CONST_SPECS[cn]), F32, kind="ExternalInput").ap()
    io["proj_f"] = nc.dram_tensor("proj_f", [D, 16], F32, kind="ExternalInput").ap()
    io["C1"] = nc.dram_tensor("C1", [16, 512], F32, kind="ExternalInput").ap()
    io["C2"] = nc.dram_tensor("C2", [16, 512], F32, kind="ExternalInput").ap()
    io["Win1"] = nc.dram_tensor("Win1", [1025, 512], F32, kind="ExternalInput").ap()
    io["Win2"] = nc.dram_tensor("Win2", [4097, 512], F32, kind="ExternalInput").ap()
    io["Wout1"] = nc.dram_tensor("Wout1", [4096, 512], F32, kind="ExternalInput").ap()
    io["Wout2"] = nc.dram_tensor("Wout2", [1024, 512], F32, kind="ExternalInput").ap()
    for wn in ["q_w", "k_w", "v_w", "out_w"]:
        io[wn] = nc.dram_tensor(wn, [D, D], F32, kind="ExternalInput").ap()
    for bn in ["q_b", "k_b", "v_b", "out_b"]:
        io[bn] = nc.dram_tensor(bn, [D], F32, kind="ExternalInput").ap()
    io["proj"] = nc.dram_tensor("proj2", [256, 64], F32, kind="ExternalInput").ap()
    io["gamma"] = nc.dram_tensor("gamma", [D], F32, kind="ExternalInput").ap()
    io["beta"] = nc.dram_tensor("beta", [D], F32, kind="ExternalInput").ap()
    io["Wg"] = nc.dram_tensor("Wg", [D, 16], F32, kind="ExternalInput").ap()
    io["w1my"] = nc.dram_tensor("w1my", [2, 1024, 4096], F32, kind="ExternalInput").ap()
    io["w2my"] = nc.dram_tensor("w2my", [2, 4096, 1024], F32, kind="ExternalInput").ap()
    io["c_tokid"] = nc.dram_tensor("c_tokid", [128, 32], F32, kind="ExternalInput").ap()
    io["c_offmask"] = nc.dram_tensor("c_offmask", [128, 16], F32, kind="ExternalInput").ap()
    io["c_iotatb"] = nc.dram_tensor("c_iotatb", [128, 512], F32, kind="ExternalInput").ap()
    io["c_invrow"] = nc.dram_tensor("c_invrow", [128, 2], U32, kind="ExternalInput").ap()
    io["c_ptrash"] = nc.dram_tensor("c_ptrash", [128, 32], F32, kind="ExternalInput").ap()
    io["c_qgrid"] = nc.dram_tensor("c_qgrid", [128, 16], F32, kind="ExternalInput").ap()
    io["c_qgrid2"] = nc.dram_tensor("c_qgrid2", [128, 16], F32, kind="ExternalInput").ap()
    io["outTOK"] = nc.dram_tensor("outTOK", [T, D], F32, kind="ExternalOutput").ap()
    io["x1T"] = nc.dram_tensor("h_x1T", [D, T], F32, kind="Internal").ap()
    io["x2T"] = nc.dram_tensor("h_x2T", [D, T], F32, kind="Internal").ap()
    io["x3T"] = nc.dram_tensor("h_x3T", [D, T], F32, kind="Internal").ap()

    p = Prog(nc)
    with contextlib.ExitStack() as outer:
        with contextlib.ExitStack() as st0:
            b0 = B0(nc, p, st0, io)
            b0.run()
        emit_early_wcaches(nc, p, io, gates=[("q_wsl", m) for m in b0.wsl_miles])
        p.barrier()
        io1 = dict(io)
        io1["proj"] = io["proj_f"]
        with contextlib.ExitStack() as st1:
            B1(nc, p, st1, io1).run()
        p.barrier()
        io2 = dict(io)
        with contextlib.ExitStack() as st2:
            B2(nc, p, st2, io2).run()
        p.barrier()
        io3 = dict(io)
        with contextlib.ExitStack() as st3:
            B3(nc, p, st3, io3, 0).run()
        p.build(outer)
    return nc


def _get(name, builder):
    if name not in _NCACHE:
        _NCACHE[name] = builder()
    return _NCACHE[name]


def kernel(**ins):
    consts = host_constants()
    p_ = np.arange(128, dtype=np.float32)[:, None]
    x = np.asarray(ins["L0_attn_x"])
    nc = _get("fused", _build_fused)
    maps = []
    for c in range(8):
        m = dict(
            xT=np.ascontiguousarray(x[c].T),
            wq=ins["L0_attn_Wq"], wk=ins["L0_attn_Wk"], wv=ins["L0_attn_Wv"],
            wo=ins["L0_attn_Wo"],
            bq=ins["L0_attn_bq"], bk=ins["L0_attn_bk"], bv=ins["L0_attn_bv"],
            bo=ins["L0_attn_bo"],
            proj_f=ins["L0_ffn_proj"], C1=ins["L0_ffn_C1"], C2=ins["L0_ffn_C2"],
            Win1=ins["L0_ffn_Win1"], Win2=ins["L0_ffn_Win2"],
            Wout1=ins["L0_ffn_Wout1"], Wout2=ins["L0_ffn_Wout2"],
            q_w=ins["L1_attn_q_w"], k_w=ins["L1_attn_k_w"], v_w=ins["L1_attn_v_w"],
            out_w=ins["L1_attn_out_w"],
            q_b=ins["L1_attn_q_b"], k_b=ins["L1_attn_k_b"], v_b=ins["L1_attn_v_b"],
            out_b=ins["L1_attn_out_b"],
            proj2=ins["L1_attn_proj"], gamma=ins["L1_attn_gamma"], beta=ins["L1_attn_beta"],
            Wg=ins["L1_ffn_Wg"],
            w1my=np.ascontiguousarray(np.asarray(ins["L1_ffn_w1"])[2 * c:2 * c + 2]),
            w2my=np.ascontiguousarray(np.asarray(ins["L1_ffn_w2"])[2 * c:2 * c + 2]),
            c_tokid=(4096.0 * c + 128.0 * np.arange(32, dtype=np.float32)[None, :]
                     + p_).astype(np.float32),
            c_offmask=np.zeros((128, 16), dtype=np.float32),
            c_iotatb=np.tile(np.arange(16, dtype=np.float32)[None, :], (128, 32)).reshape(128, 512),
            c_invrow=((2 * c + np.arange(2, dtype=np.uint32)[None, :]) * 128
                      + np.arange(128, dtype=np.uint32)[:, None]).astype(np.uint32),
            c_ptrash=(32768.0 + 128.0 * np.arange(32, dtype=np.float32)[None, :]
                      + p_).astype(np.float32),
            c_qgrid=np.tile((128.0 * np.arange(1, 17, dtype=np.float32)).reshape(1, 16), (128, 1)),
            c_qgrid2=np.tile(np.concatenate(
                [128.0 * np.arange(17, 32, dtype=np.float32), [1e9]]).astype(
                    np.float32).reshape(1, 16), (128, 1)),
        )
        for b in range(c):
            for e in range(16):
                m["c_offmask"][16 * b + e, e] = 1.0
        m.update(consts)
        maps.append(m)
    res = run_bass_kernel_spmd(nc, maps, core_ids=list(range(8)))
    out = np.stack([np.asarray(res.results[c]["outTOK"]) for c in range(8)], axis=0)
    return out.astype(np.float32)


# version 10
# speedup vs baseline: 1.0324x; 1.0066x over previous
import contextlib
import numpy as np
import ml_dtypes
import concourse.bass as bass
import concourse.mybir as mybir
from concourse.bass_core_rust import IndirectOffsetOnAxis

F32 = mybir.dt.float32
BF16 = mybir.dt.bfloat16
U32 = mybir.dt.uint32
AF = mybir.ActivationFunctionType
ALU = mybir.AluOpType
AX = mybir.AxisListType

T = 4096
D = 1024
ENGS = ("sync", "gpsimd", "tensor", "vector", "scalar")
QT = 512
NQT = T // QT
NCH = D // 128


def emit_early_wcaches(nc, p, io, gates=()):
    wch = {}
    m = None
    nd = [0]
    gates = list(gates)

    def gw():
        w = [gates[min(nd[0] // 2, len(gates) - 1)]] if gates else []
        nd[0] += 1
        return w
    if "Win2" in io:
        wch["win2"] = nc.dram_tensor("b1_win2bf", [4096, 512], BF16, kind="Internal")
        for i in range(4):
            m = p.dma("gpsimd", wch["win2"].ap()[i * 1024:(i + 1) * 1024, :],
                      io["Win2"][i * 1024:(i + 1) * 1024, :], sem="q_wch", waits=gw())
    for wn in ["q_w", "k_w"]:
        if wn in io:
            wch[wn] = nc.dram_tensor(f"b2_{wn}bf", [1024, 1024], BF16, kind="Internal")
            for i in range(2):
                m = p.dma("gpsimd", wch[wn].ap()[i * 512:(i + 1) * 512, :],
                          io[wn][i * 512:(i + 1) * 512, :], sem="q_wch", waits=gw())
    if "w1my" in io:
        wch["w1bf"] = nc.dram_tensor("b3_w1bf", [2 * 1024, 4096], BF16, kind="Internal")
        wch["w2bf"] = nc.dram_tensor("b3_w2bf", [2 * 4096, 1024], BF16, kind="Internal")
        for ex in range(2):
            for i in range(2):
                m = p.dma("gpsimd",
                          wch["w1bf"].ap()[ex * 1024 + i * 512:ex * 1024 + (i + 1) * 512, :],
                          io["w1my"][ex, i * 512:(i + 1) * 512, :], sem="q_wch", waits=gw())
        for ex in range(2):
            for i in range(2):
                m = p.dma("gpsimd",
                          wch["w2bf"].ap()[ex * 4096 + i * 2048:ex * 4096 + (i + 1) * 2048, :],
                          io["w2my"][ex, i * 2048:(i + 1) * 2048, :], sem="q_wch", waits=gw())
    io["_wch"] = wch
    io["_wch_m"] = m


class Prog:
    def __init__(self, nc):
        self.nc = nc
        self.ops = []
        self.ctr = {}
        self.sems = {}
        self.sem_names = []

    def sem(self, name):
        if name not in self.ctr:
            self.ctr[name] = 0
            self.sem_names.append(name)
        return name

    SELF_SEM = {"vector": "dve", "scalar": "act", "gpsimd": "gp"}

    def get(self, name):
        self.sem(name)
        return self.ctr[name]

    def emit(self, eng, fn, waits=(), inc=None, inc_n=1, is_dma=False):
        waits = list((s, v) for (s, v) in waits if v)
        selfsem = self.SELF_SEM.get(eng)
        if selfsem is not None and not is_dma:
            assert inc in (None, selfsem), (eng, inc)
            prev = self.get(selfsem)
            if prev:
                waits.append((selfsem, prev))
            inc = selfsem
        incs = []
        val = None
        if inc is not None:
            self.sem(inc)
            self.ctr[inc] += inc_n
            val = self.ctr[inc]
            incs.append((inc, inc_n))
        for (s, v) in waits:
            self.sem(s)
        self.ops.append((eng, fn, tuple(waits), tuple(incs)))
        return val

    def dma(self, queue, out, in_, waits=(), sem=None, **kw):
        semn = self.sem(sem if sem else ("q_" + queue))
        return self.emit(queue, lambda e: e.dma_start(out=out, in_=in_, **kw),
                         waits=waits, inc=semn, inc_n=16, is_dma=True)

    def barrier(self):
        waits = [(name, self.ctr[name]) for name in self.sem_names if self.ctr[name] > 0]
        for eng in ENGS:
            self.emit(eng, lambda e: None, waits=waits, is_dma=True)

    def build(self, stack):
        E = stack.enter_context
        for name in self.sem_names:
            self.sems[name] = E(self.nc.semaphore(name))
        block = E(self.nc.Block())
        prog = self

        def make(engname):
            def body(eng):
                for (e, fn, waits, incs) in prog.ops:
                    if e != engname:
                        continue
                    for (s, v) in waits:
                        eng.wait_ge(prog.sems[s], v)
                    inst = fn(eng)
                    for (s, n) in incs:
                        if inst is None:
                            inst = eng.nop()
                        inst = inst.then_inc(prog.sems[s], n)
            return body

        block.sync(make("sync"))
        block.gpsimd(make("gpsimd"))
        block.tensor(make("tensor"))
        block.vector(make("vector"))
        block.scalar(make("scalar"))


def host_constants():
    c = {}
    c["c_ident"] = np.eye(128, dtype=np.float32)
    P = np.zeros((128, 128), dtype=np.float32)
    for h0 in (0, 64):
        for d_ in range(32):
            P[h0 + d_ + 32, h0 + d_] = -1.0
            P[h0 + d_, h0 + d_ + 32] = 1.0
    c["c_prot"] = P
    hd, base = 64, 10000.0
    inv_freq = 1.0 / base ** (np.arange(0, hd, 2, dtype=np.float32) / hd)
    t = np.arange(T, dtype=np.float32)
    freqs = np.einsum("i,j->ij", t, inv_freq)
    emb = np.concatenate([freqs, freqs], axis=-1)
    cos = np.cos(emb).astype(np.float32).T
    sin = np.sin(emb).astype(np.float32).T
    c["c_cos"] = np.concatenate([cos, cos], axis=0)
    c["c_sin"] = np.concatenate([sin, sin], axis=0)
    dk = np.arange(128)[:, None]
    dq = np.arange(128)[None, :]
    c["c_tri"] = (dk <= dq).astype(np.float32)
    c["c_ones1"] = np.ones((1, 128), dtype=np.float32)
    c["c_onescol"] = np.ones((128, 1), dtype=np.float32)
    c["c_negrow"] = np.full((1, 512), -1.0, dtype=np.float32)
    return c


CONST_SPECS = {
    "c_ident": (128, 128), "c_prot": (128, 128), "c_cos": (128, T), "c_sin": (128, T),
    "c_tri": (128, 128), "c_ones1": (1, 128), "c_onescol": (128, 1), "c_negrow": (1, 512),
}


class B0:

    def __init__(self, nc, p, stack, io):
        self.nc, self.p, self.stack, self.io = nc, p, stack, io
        self.E = stack.enter_context

    def alloc(self):
        nc, E = self.nc, self.E
        s = self
        s.xbf = [E(nc.sbuf_tensor(f"b0_xbf{c}", [128, T], BF16)) for c in range(NCH)]
        s.attnT = [E(nc.sbuf_tensor(f"b0_attnT{c}", [128, T], BF16)) for c in range(NCH)]
        s.qraw = E(nc.sbuf_tensor("b0_qraw", [128, T], BF16))
        s.kraw = E(nc.sbuf_tensor("b0_kraw", [128, T], BF16))
        s.qr = E(nc.sbuf_tensor("b0_qr", [128, T], BF16))
        s.kr = E(nc.sbuf_tensor("b0_kr", [128, T], BF16))
        s.vhat = E(nc.sbuf_tensor("b0_vhat", [128, 32 * 130], BF16))
        s.wsl = [E(nc.sbuf_tensor("b0_wsl0", [128, 3 * 1024], BF16))]
        s.ptile = [E(nc.sbuf_tensor(f"b0_ptile{i}", [128, 1024], BF16)) for i in range(3)]
        s.pscr = E(nc.sbuf_tensor("b0_pscr", [128, 256], BF16))
        s.ropes = E(nc.sbuf_tensor("b0_ropes", [128, 2 * QT], F32))
        s.numbuf = E(nc.sbuf_tensor("b0_numbuf", [128, QT], F32))
        s.lrec = E(nc.sbuf_tensor("b0_lrec", [1, 2 * QT], BF16))
        s.lcat = s.ropes[0:1, 0:2 * QT]
        s.lln = s.ropes[32:33, 0:2 * QT]
        s.resbuf = s.vhat[:, 0:NCH * QT]

        _krf = s.kr[:].bitcast(F32)
        _qrf = s.qr[:].bitcast(F32)
        s.rowbuf = _krf[0:1, 0:2048]
        s.rowbf = None
        s.x1buf = [_qrf[:, 1024 + i * QT:1024 + (i + 1) * QT] for i in range(2)]
        s.x1tmp = None
        s.biases = E(nc.sbuf_tensor("b0_biases", [128, 4 * NCH], F32))
        s.bvrow = E(nc.sbuf_tensor("b0_bvrow", [1, 1024], BF16))
        s.woT = [(s.qraw if c < 4 else s.kraw)[:, (c % 4) * 1024:(c % 4 + 1) * 1024] for c in range(NCH)]
        s.wtmp = [s.qr[:, 0:1024], s.qr[:, 1024:2048]]
        s.stripes = [s.attnT[i][:, 0:1024] for i in range(NCH)]
        s.cst = {}
        for name in ["c_ident", "c_prot", "c_cos", "c_sin", "c_tri", "c_ones1", "c_onescol"]:
            s.cst[name] = E(nc.sbuf_tensor("b0_" + name, list(CONST_SPECS[name]), BF16))
        s.rowbf = s.lrec[0:1, 0:2 * QT]
        s.x1tmp = s.numbuf[:, 0:QT]
        s.rowslot4 = _qrf[0:1, 0:QT]
        s.rowslotE = _qrf[0:1, QT:2 * QT]
        s.sqscr = [s.ptile[0][:, 0:QT], s.ptile[0][:, QT:2 * QT]]
        s.ps_s = [E(nc.psum_tensor(f"b0_ps_s{i}", [128, 1024], F32)) for i in range(2)]
        s.ps_va = E(nc.psum_tensor("b0_ps_va", [128, 512], F32))
        s.ps_vb = E(nc.psum_tensor("b0_ps_vb", [128, 512], F32))
        s.ps_mm = E(nc.psum_tensor("b0_ps_mm", [128, 512], F32))
        s.ps_bc = E(nc.psum_tensor("b0_ps_bc", [128, 512], F32))
        s.ps_tr = [s.ps_s[0].ap().bitcast(BF16), s.ps_s[1].ap().bitcast(BF16)]
        s.ps_bc2 = s.ps_vb

    def load_consts(self):
        p, s, io = self.p, self, self.io
        s.mc = {}
        for name in ["c_ident", "c_prot", "c_cos", "c_sin", "c_tri", "c_ones1", "c_onescol"]:
            p.dma("gpsimd", s.cst[name][:], io[name], sem="q_ig")
        for c in range(NCH):
            p.dma("gpsimd", s.xbf[c][:], io["xT"][c * 128:(c + 1) * 128, :], sem="q_ig")
        for j, bn in enumerate(["bq", "bk", "bv", "bo"]):
            for c in range(NCH):
                p.dma("sync", s.biases[:, j * NCH + c:j * NCH + c + 1],
                      io[bn][c * 128:(c + 1) * 128].rearrange("(p o) -> p o", o=1), sem="q_is")
        s.m_bias = p.get("q_is")
        p.dma("gpsimd", s.bvrow[0:1, :], io["bv"].rearrange("(o p) -> o p", o=1), sem="q_ig")
        m_ig = p.get("q_ig")
        for name in CONST_SPECS:
            s.mc[name] = m_ig
        s.m_bv = m_ig
        s.m_xall = m_ig
        for c in range(NCH):
            s.mc[f"x{c}"] = m_ig

    def bias_ap(self, j, c):
        return self.biases[:, j * NCH + c:j * NCH + c + 1]

    def prep_weights(self):
        p, s, nc, io = self.p, self, self.nc, self.io
        s.wT = {}
        s.prep_done = {}
        ptr_free = [None] * 4
        wtmp_free = [None, None]
        prev_store = None
        for wn in ["wq", "wk", "wv", "wo"]:
            s.wT[wn] = nc.dram_tensor(f"b0_{wn}T", [1024, 1024], BF16, kind="Internal")
            last_evac = None
            for er in range(8):
                buf = s.wtmp[er % 2]
                qp = "q_pa" if er % 2 == 0 else "q_pb"
                mld = p.dma("gpsimd", buf[:], io[wn][er * 128:(er + 1) * 128, :],
                            waits=[("dve", wtmp_free[er % 2])] if wtmp_free[er % 2] else [], sem=qp)
                for dc in range(8):
                    g = dc % 4
                    tr = s.ps_tr[g // 2]
                    go = (g % 2) * 1024
                    waits = [(qp, mld), ("q_ig", s.mc["c_ident"])]
                    if ptr_free[g]:
                        waits.append(("dve", ptr_free[g]))
                    mt = p.emit("tensor",
                                lambda e, buf=buf, dc=dc, tr=tr, go=go: e.transpose(
                                    tr[:, go:go + 128],
                                    buf[:, dc * 128:(dc + 1) * 128], s.cst["c_ident"][:]),
                                waits=waits, inc="pe")
                    vwaits = [("pe", mt)]
                    if er == 0 and prev_store:
                        vwaits.append(("q_ps", prev_store))
                    mv = p.emit("vector",
                                lambda e, dc=dc, er=er, tr=tr, go=go: e.tensor_copy(
                                    s.stripes[dc][:, er * 128:(er + 1) * 128],
                                    tr[:, go:go + 128]),
                                waits=vwaits, inc="dve")
                    ptr_free[g] = mv
                    last_evac = mv
                wtmp_free[er % 2] = last_evac
            mstore = None
            for dc in range(8):
                mstore = p.dma("sync", s.wT[wn].ap()[dc * 128:(dc + 1) * 128, :], s.stripes[dc],
                               waits=[("dve", last_evac)], sem="q_ps")
            s.prep_done[wn] = mstore
            prev_store = mstore
        return

    def run(self):
        p, s = self.p, self
        s.alloc()
        s.load_consts()
        s.prep_weights()
        for hp in range(8):
            self.pair(hp)
        self.oproj_ln()
        p.emit("sync", lambda e: None,
               waits=[("q_sta", p.get("q_sta")), ("q_stb", p.get("q_stb"))])

    def pair(self, hp):
        p, s = self.p, self
        wb = s.wsl[0]
        wsl_ap = lambda j, c: wb[:].rearrange("p (w c e) -> p w c e", w=3, e=128)[:, j, c]
        guard_pe = p.get("pe")
        guard_dve0 = p.get("dve")
        guard_act0 = p.get("act")
        mw = []
        for j, wn in enumerate(["wq", "wk", "wv"]):
            src = s.wT[wn].ap().rearrange("(c p) e -> p c e", p=128)[:, :, hp * 128:(hp + 1) * 128]
            dst = wb[:].rearrange("p (w c e) -> p w c e", w=3, e=128)[:, j]
            mw.append(p.dma("sync", dst, src, sem="q_wsl",
                            waits=[("q_ps", s.prep_done[wn]), ("pe", guard_pe),
                                   ("dve", guard_dve0)]))
        if not hasattr(s, "wsl_miles"):
            s.wsl_miles = []
        s.wsl_miles.append(mw[2])
        ps_mm_free = [None, None]

        def qkv(dst, j, bj):
            qp2 = lambda qt: (s.ps_mm if qt % 2 == 0 else s.ps_bc)
            for qt in range(NQT):
                for c in range(NCH):
                    w = []
                    if c == 0:
                        w = [("q_wsl", mw[2]), ("q_ig", s.m_xall), ("q_is", s.m_bias)]
                        if ps_mm_free[qt % 2]:
                            w.append(("dve", ps_mm_free[qt % 2]))
                    mpe = p.emit("tensor",
                                 lambda e, c=c, qt=qt: e.matmul(
                                     qp2(qt)[:], wsl_ap(j, c), s.xbf[c][:, qt * QT:(qt + 1) * QT],
                                     start=(c == 0), stop=(c == NCH - 1)),
                                 waits=w, inc="pe" if c == NCH - 1 else None)
                mv = p.emit("vector",
                            lambda e, qt=qt: e.tensor_scalar(
                                dst[:, qt * QT:(qt + 1) * QT], qp2(qt)[:], s.bias_ap(bj, hp),
                                None, ALU.add),
                            waits=[("pe", mpe)], inc="dve")
                ps_mm_free[qt % 2] = mv

        qkv(s.qraw, 0, 0)
        qkv(s.kraw, 1, 1)

        for (srcb, dstb) in ((s.qraw, s.qr), (s.kraw, s.kr)):
            for qt in range(NQT):
                w = [("dve", p.get("dve")), ("q_ig", s.mc["c_prot"])]
                if ps_mm_free[0]:
                    w.append(("dve", ps_mm_free[0]))
                if ps_mm_free[1]:
                    w.append(("dve", ps_mm_free[1]))
                    ps_mm_free[1] = None
                mt = p.emit("tensor",
                            lambda e, srcb=srcb, qt=qt: e.matmul(
                                s.ps_mm[:], s.cst["c_prot"][:], srcb[:, qt * QT:(qt + 1) * QT],
                                start=True, stop=True),
                            waits=w, inc="pe")
                p.emit("vector",
                       lambda e, srcb=srcb, qt=qt: e.tensor_tensor(
                           s.ropes[:, 0:QT], srcb[:, qt * QT:(qt + 1) * QT],
                           s.cst["c_cos"][:, qt * QT:(qt + 1) * QT], ALU.mult),
                       waits=[("q_ig", s.mc["c_cos"])], inc="dve")
                mv1 = p.emit("vector",
                             lambda e, qt=qt: e.tensor_tensor(
                                 s.ropes[:, QT:2 * QT], s.ps_mm[:],
                                 s.cst["c_sin"][:, qt * QT:(qt + 1) * QT], ALU.mult),
                             waits=[("pe", mt), ("q_ig", s.mc["c_sin"])], inc="dve")
                ps_mm_free[0] = mv1
                p.emit("vector",
                       lambda e, dstb=dstb, qt=qt: e.tensor_add(
                           dstb[:, qt * QT:(qt + 1) * QT], s.ropes[:, 0:QT], s.ropes[:, QT:2 * QT]),
                       inc="dve")

        vh3 = s.vhat[:].rearrange("p (tt c) -> p tt c", c=130)
        guard_pe = p.get("pe")
        p.emit("vector", lambda e: e.memset(vh3[:, :, 64:65], 1.0), waits=[("pe", guard_pe)], inc="dve")
        p.emit("vector", lambda e: e.memset(vh3[:, :, 129:130], 1.0), inc="dve")
        m_ones = p.get("dve")
        vfree = {0: [("dve", ps_mm_free[0])] if ps_mm_free[0] else [],
                 1: ([("dve", ps_mm_free[1])] if ps_mm_free[1] else [])}
        ps_mm_free[1] = None
        for tt in range(32):
            par = tt % 2
            pso = s.ps_mm if par == 0 else s.ps_bc
            for c in range(NCH):
                w = []
                if c == 0:
                    w = [("q_wsl", mw[2])] + vfree[par]
                p.emit("tensor",
                       lambda e, c=c, tt=tt, pso=pso: e.matmul(
                           pso[:, 0:128], s.xbf[c][:, tt * 128:(tt + 1) * 128], wsl_ap(2, c),
                           start=(c == 0), stop=False),
                       waits=w)
            mpe = p.emit("tensor",
                         lambda e, pso=pso: e.matmul(
                             pso[:, 0:128], s.cst["c_ones1"][0:1, :],
                             s.bvrow[0:1, hp * 128:(hp + 1) * 128], start=False, stop=True),
                         waits=[("q_ig", s.mc["c_ones1"]), ("q_ig", s.m_bv)], inc="pe")
            mv = p.emit("vector",
                        lambda e, tt=tt, pso=pso: e.tensor_copy(vh3[:, tt, 0:64], pso[:, 0:64]),
                        waits=[("pe", mpe)], inc="dve")
            ma = p.emit("vector",
                        lambda e, tt=tt, pso=pso: e.tensor_copy(vh3[:, tt, 65:129],
                                                                pso[:, 64:128]),
                        inc="dve")
            vfree[par] = [("dve", ma)]
        ps_mm_free[0] = None

        ps_s_free = [None, None]
        pscr_free = [None]
        pend_bc = [None]
        mbc_prev = [None]
        for qt in range(NQT):
            K = 4 * (qt + 1)
            ptile_free = [None] * 3
            guard_dve = p.get("dve")
            guard_act = p.get("act")
            mva = None
            kb_order = list(range(4 * qt, K)) + list(range(0, 4 * qt))
            stage = {}

            def emit_st(idx):
                kb = kb_order[idx]
                diag = kb >= 4 * qt
                c0 = 128 * (kb - 4 * qt) if diag else 0
                par = idx % 2
                pt = s.ptile[idx % 3]
                w = [("dve", guard_dve)] if idx == 0 else []
                if ps_s_free[par]:
                    w.append(("act", ps_s_free[par]))
                p.emit("tensor",
                       lambda e, kb=kb, qt=qt, par=par, c0=c0: e.matmul(
                           s.ps_s[par][:, c0:QT],
                           s.kr[0:64, kb * 128:(kb + 1) * 128],
                           s.qr[0:64, qt * QT + c0:(qt + 1) * QT], start=True, stop=True),
                       waits=w)
                mpe = p.emit("tensor",
                             lambda e, kb=kb, qt=qt, par=par, c0=c0: e.matmul(
                                 s.ps_s[par][:, QT + c0:2 * QT],
                                 s.kr[64:128, kb * 128:(kb + 1) * 128],
                                 s.qr[64:128, qt * QT + c0:(qt + 1) * QT], start=True, stop=True),
                             inc="pe")
                ewaits = [("pe", mpe)]
                if ptile_free[idx % 3]:
                    ewaits.append(("pe", ptile_free[idx % 3]))
                last = (idx == K - 1)
                if not diag:
                    mact = p.emit("scalar",
                                  lambda e, pt=pt, par=par: e.activation(
                                      pt[:], s.ps_s[par][:], AF.Exp, scale=0.125),
                                  waits=ewaits, inc="act")
                    ps_s_free[par] = mact
                    stage[idx] = (("act", mact), 0)
                else:
                    if c0 + 128 < QT:
                        p.emit("scalar",
                               lambda e, pt=pt, par=par, c0=c0: e.activation(
                                   pt[:].rearrange("p (h q) -> p h q", h=2)[:, :, c0 + 128:QT],
                                   s.ps_s[par][:].rearrange("p (h q) -> p h q", h=2)[:, :, c0 + 128:QT],
                                   AF.Exp, scale=0.125),
                               waits=ewaits, inc="act")
                        ewaits = []
                    if pscr_free[0]:
                        ewaits = list(ewaits) + [("dve", pscr_free[0])]
                    mact = p.emit("scalar",
                                  lambda e, par=par, c0=c0: e.activation(
                                      s.pscr[:].rearrange("p (h q) -> p h q", h=2),
                                      s.ps_s[par][:].rearrange("p (h q) -> p h q", h=2)[:, :, c0:c0 + 128],
                                      AF.Exp, scale=0.125),
                                  waits=ewaits, inc="act")
                    ps_s_free[par] = mact
                    if last and c0 > 0:
                        p.emit("vector", lambda e, pt=pt, c0=c0: e.memset(pt[:, 0:c0], 0.0),
                               waits=[("act", mact)], inc="dve")
                        p.emit("vector", lambda e, pt=pt, c0=c0: e.memset(pt[:, QT:QT + c0], 0.0),
                               inc="dve")
                    p.emit("vector",
                           lambda e, pt=pt, c0=c0: e.tensor_tensor(
                               pt[:, c0:c0 + 128], s.pscr[:, 0:128], s.cst["c_tri"][:], ALU.mult),
                           waits=[("act", mact), ("q_ig", s.mc["c_tri"])], inc="dve")
                    mmask = p.emit("vector",
                                   lambda e, pt=pt, c0=c0: e.tensor_tensor(
                                       pt[:, QT + c0:QT + c0 + 128], s.pscr[:, 128:256], s.cst["c_tri"][:],
                                       ALU.mult),
                                   inc="dve")
                    pscr_free[0] = mmask
                    stage[idx] = (("dve", mmask), c0 if not last else 0)

            def emit_av(idx):
                nonlocal mva
                kb = kb_order[idx]
                pt = s.ptile[idx % 3]
                mdep, nlo = stage.pop(idx)
                first = (idx == 0)
                last = (idx == K - 1)
                p.emit("tensor",
                       lambda e, kb=kb, pt=pt, nlo=nlo, first=first, last=last: e.matmul(
                           s.ps_va.ap()[0:65, nlo:QT], vh3[:, kb, 0:65], pt[:, nlo:QT],
                           start=first, stop=last),
                       waits=[mdep])
                mva = p.emit("tensor",
                             lambda e, kb=kb, pt=pt, nlo=nlo, first=first, last=last: e.matmul(
                                 s.ps_vb.ap()[0:65, nlo:QT], vh3[:, kb, 65:130], pt[:, QT + nlo:2 * QT],
                                 start=first, stop=last),
                             inc="pe")
                ptile_free[idx % 3] = mva

            emit_st(0)
            for idx in range(K):
                if idx + 1 < K:
                    emit_st(idx + 1)
                emit_av(idx)
                if idx == 0 and pend_bc[0] is not None:
                    pend_bc[0]()
                    pend_bc[0] = None
            p.emit("vector", lambda e: e.tensor_copy(s.numbuf[0:64, :], s.ps_va[0:64, :]),
                   waits=[("pe", mva)], inc="dve")
            mv2 = p.emit("vector", lambda e: e.tensor_copy(s.numbuf[64:128, :], s.ps_vb[0:64, :]), inc="dve")
            p.emit("vector", lambda e: e.tensor_copy(s.lcat[0:1, 0:QT], s.ps_va[64:65, :]),
                   waits=[("act", guard_act)], inc="dve")
            ml2 = p.emit("vector", lambda e: e.tensor_copy(s.lcat[0:1, QT:2 * QT], s.ps_vb[64:65, :]),
                         inc="dve")
            mln = p.emit("scalar", lambda e: e.activation(s.lln[0:1, :], s.lcat[0:1, :], AF.Ln),
                         waits=[("dve", ml2)], inc="act")
            mrw = [("act", mln)]
            if mbc_prev[0]:
                mrw.append(("pe", mbc_prev[0]))
            mrec = p.emit("scalar", lambda e: e.activation(s.lrec[0:1, :], s.lln[0:1, :], AF.Exp, scale=-1.0),
                          waits=mrw, inc="act")
            guard_dve2 = p.get("dve")

            def bc_part(qt=qt, mrec=mrec, mv2=mv2, guard_dve2=guard_dve2):
                p.emit("tensor",
                       lambda e: e.matmul(s.ps_bc[0:64, :], s.cst["c_ones1"][0:1, 0:64],
                                          s.lrec[0:1, 0:QT], start=True, stop=True),
                       waits=[("act", mrec), ("q_ig", s.mc["c_ones1"]), ("dve", guard_dve2)])
                mbc = p.emit("tensor",
                             lambda e: e.matmul(s.ps_bc[64:128, :], s.cst["c_ones1"][0:1, 0:64],
                                                s.lrec[0:1, QT:2 * QT], start=True, stop=True,
                                                tile_position=(0, 64)),
                             inc="pe")
                mbc_prev[0] = mbc
                p.emit("vector",
                       lambda e, hp=hp, qt=qt: e.tensor_tensor(
                           s.attnT[hp][:, qt * QT:(qt + 1) * QT], s.numbuf[:], s.ps_bc[:], ALU.mult),
                       waits=[("pe", mbc), ("dve", mv2), ("q_ps", s.prep_done["wo"])], inc="dve")

            pend_bc[0] = bc_part
        pend_bc[0]()
        pend_bc[0] = None

    def oproj_ln(self):
        p, s = self.p, self
        io = self.io
        mwo = None
        guard_pe = p.get("pe")
        guard_dve0 = p.get("dve")
        for c in range(NCH):
            mwo = p.dma("sync", s.woT[c], s.wT["wo"].ap()[c * 128:(c + 1) * 128, :], sem="q_wo",
                        waits=[("q_ps", s.prep_done["wo"]), ("pe", guard_pe), ("dve", guard_dve0)])
        eps = 1e-5
        ps_mm_free = [None, None]
        mdma_prev = {}
        for qt in range(NQT):
            guard_dve = p.get("dve")
            for oc in range(NCH):
                pso = s.ps_mm[:] if oc % 2 == 0 else s.ps_s[0].ap()[:, 0:QT]
                for ec in range(NCH):
                    w = []
                    if ec == 0:
                        w = [("q_wo", mwo), ("dve", guard_dve)]
                        if ps_mm_free[oc % 2]:
                            w.append(("dve", ps_mm_free[oc % 2]))
                    mpe = p.emit("tensor",
                                 lambda e, oc=oc, ec=ec, qt=qt, pso=pso: e.matmul(
                                     pso, s.woT[ec][:, oc * 128:(oc + 1) * 128],
                                     s.attnT[ec][:, qt * QT:(qt + 1) * QT],
                                     start=(ec == 0), stop=(ec == NCH - 1)),
                                 waits=w, inc="pe" if ec == NCH - 1 else None)
                mres = p.emit("vector",
                              lambda e, oc=oc, qt=qt, pso=pso: e.scalar_tensor_tensor(
                                  s.resbuf[:, oc * QT:(oc + 1) * QT], pso, s.bias_ap(3, oc),
                                  s.xbf[oc][:, qt * QT:(qt + 1) * QT], ALU.add, ALU.add),
                              waits=[("pe", mpe)], inc="dve")
                ps_mm_free[oc % 2] = mres
            guard_pe2 = p.get("pe")
            guard_dve3 = p.get("dve")
            for oc in range(NCH):
                w = [("dve", guard_dve3), ("pe", guard_pe2)] if oc == 0 else []
                p.emit("tensor",
                       lambda e, oc=oc: e.matmul(
                           s.ps_va.ap()[0:1, 0:QT], s.cst["c_onescol"][:, 0:1],
                           s.resbuf[:, oc * QT:(oc + 1) * QT],
                           start=(oc == 0), stop=(oc == NCH - 1)),
                       waits=w, inc="pe" if oc == NCH - 1 else None)
            sq_free = [None, None]
            for oc in range(NCH):
                sp = s.sqscr[oc % 2]
                w = [("dve", guard_dve3)] if oc == 0 else []
                if sq_free[oc % 2]:
                    w = list(w) + [("pe", sq_free[oc % 2])]
                ma = p.emit("scalar",
                            lambda e, oc=oc, sp=sp: e.activation(
                                sp, s.resbuf[:, oc * QT:(oc + 1) * QT], AF.Square),
                            waits=w, inc="act")
                mpe_sq = p.emit("tensor",
                                lambda e, oc=oc, sp=sp: e.matmul(
                                    s.ps_vb.ap()[0:1, 0:QT], s.cst["c_onescol"][:, 0:1], sp,
                                    start=(oc == 0), stop=(oc == NCH - 1)),
                                waits=[("act", ma)], inc="pe")
                sq_free[oc % 2] = mpe_sq
            mpe_stats = p.get("pe")
            p.emit("vector", lambda e: e.tensor_scalar_mul(s.rowbuf[0:1, 0:QT],
                                                           s.ps_va.ap()[0:1, 0:QT], 1.0 / 1024),
                   waits=[("pe", mpe_stats)], inc="dve")
            p.emit("vector", lambda e: e.tensor_scalar_mul(s.rowbuf[0:1, QT:2 * QT],
                                                           s.ps_vb.ap()[0:1, 0:QT], 1.0 / 1024),
                   inc="dve")
            p.emit("vector", lambda e: e.tensor_tensor(s.rowbuf[0:1, 2 * QT:3 * QT],
                                                       s.rowbuf[0:1, 0:QT], s.rowbuf[0:1, 0:QT], ALU.mult),
                   inc="dve")
            p.emit("vector", lambda e: e.tensor_scalar_add(s.rowslotE,
                                                           s.rowbuf[0:1, 2 * QT:3 * QT], -eps), inc="dve")
            mvar = p.emit("vector", lambda e: e.tensor_sub(s.rowbuf[0:1, 3 * QT:4 * QT],
                                                           s.rowbuf[0:1, QT:2 * QT],
                                                           s.rowslotE), inc="dve")
            mln2 = p.emit("scalar", lambda e: e.activation(s.rowslot4,
                                                           s.rowbuf[0:1, 3 * QT:4 * QT], AF.Ln),
                          waits=[("dve", mvar)], inc="act")
            mrstd = p.emit("scalar", lambda e: e.activation(s.rowbf[0:1, 0:QT],
                                                            s.rowslot4, AF.Exp, scale=-0.5),
                           waits=[("act", mln2)], inc="act")
            mb = p.emit("vector", lambda e: e.tensor_tensor(s.rowbf[0:1, QT:2 * QT], s.rowbf[0:1, 0:QT],
                                                            s.rowbuf[0:1, 0:QT], ALU.mult),
                        waits=[("act", mrstd)], inc="dve")
            p.emit("tensor", lambda e: e.matmul(s.ps_bc[:, 0:QT], s.cst["c_ones1"][0:1, :],
                                                s.rowbf[0:1, 0:QT], start=True, stop=True),
                   waits=[("dve", mb)])
            mab = p.emit("tensor", lambda e: e.matmul(s.ps_bc2.ap()[:, 0:QT], s.cst["c_ones1"][0:1, :],
                                                      s.rowbf[0:1, QT:2 * QT], start=True, stop=True),
                         inc="pe")
            for oc in range(NCH):
                xb = s.x1buf[oc % 2]
                w = [("pe", mab)]
                if mdma_prev.get(oc % 2):
                    w.append(("q_sta" if oc % 2 == 0 else "q_stb", mdma_prev[oc % 2]))
                p.emit("vector",
                       lambda e, oc=oc: e.tensor_tensor(
                           s.x1tmp, s.resbuf[:, oc * QT:(oc + 1) * QT], s.ps_bc[:, 0:QT], ALU.mult),
                       waits=w, inc="dve")
                mx1 = p.emit("vector",
                             lambda e, xb=xb: e.tensor_sub(xb, s.x1tmp, s.ps_bc2.ap()[:, 0:QT]),
                             inc="dve")
                mdma_prev[oc % 2] = p.dma("sync",
                                          io["x1T"][oc * 128:(oc + 1) * 128, qt * QT:(qt + 1) * QT],
                                          xb, waits=[("dve", mx1)],
                                          sem="q_sta" if oc % 2 == 0 else "q_stb")


class B1:

    def __init__(self, nc, p, stack, io):
        self.nc, self.p, self.stack, self.io = nc, p, stack, io
        self.E = stack.enter_context

    def alloc(self):
        nc, E = self.nc, self.E
        s = self
        s.xbf = [E(nc.sbuf_tensor(f"b1_xbf{c}", [128, T], BF16)) for c in range(NCH)]
        s.hT = E(nc.sbuf_tensor("b1_hT", [128, 32 * QT], BF16))
        s.win1 = E(nc.sbuf_tensor("b1_win1", [128, 8 * QT], BF16))
        s.wout2T = E(nc.sbuf_tensor("b1_wout2T", [128, 4 * 1024], BF16))
        s.w1buf = [E(nc.sbuf_tensor(f"b1_w1buf{i}", [128, 2048], BF16)) for i in range(2)]
        s.w2buf = [E(nc.sbuf_tensor(f"b1_w2buf{i}", [128, 2048], BF16)) for i in range(2)]
        s.tT = E(nc.sbuf_tensor("b1_tT", [16, T], BF16))
        s.aT = E(nc.sbuf_tensor("b1_aT", [16, T], BF16))
        s.gbuf = [E(nc.sbuf_tensor(f"b1_g{i}", [128, 4 * QT], BF16)) for i in range(2)]
        s.zg = [E(nc.sbuf_tensor(f"b1_zg{i}", [128, 4 * QT], BF16)) for i in range(2)]
        s.t_tb = E(nc.sbuf_tensor("b1_ttb", [128, QT], F32))
        s.e_tmp = E(nc.sbuf_tensor("b1_etmp", [128, QT], F32))
        s.e_c = E(nc.sbuf_tensor("b1_ec", [128, QT], F32))
        s.e_z = E(nc.sbuf_tensor("b1_ez", [128, QT], F32))
        s.a_tb = E(nc.sbuf_tensor("b1_atb", [128, QT], BF16))
        s.esm = E(nc.sbuf_tensor("b1_esm", [128, 16 * 32], F32))
        s.e_zz = E(nc.sbuf_tensor("b1_ezz", [128, QT], F32))
        s.resbuf = E(nc.sbuf_tensor("b1_res", [128, NCH * QT], BF16))
        s.sqscr = [E(nc.sbuf_tensor(f"b1_sq{i}", [128, QT], BF16)) for i in range(2)]
        s.rowbuf = E(nc.sbuf_tensor("b1_rowbuf", [1, 4 * QT], F32))
        s.rowslot4 = E(nc.sbuf_tensor("b1_rs4", [1, QT], F32))
        s.rowslotE = E(nc.sbuf_tensor("b1_rsE", [1, QT], F32))
        s.rowbf = E(nc.sbuf_tensor("b1_rowbf", [1, 2 * QT], BF16))
        s.x1buf = [E(nc.sbuf_tensor(f"b1_x1buf{i}", [128, QT], F32)) for i in range(2)]
        s.x1tmp = E(nc.sbuf_tensor("b1_x1tmp", [128, QT], F32))
        s.projW = E(nc.sbuf_tensor("b1_projW", [128, 8 * 16], BF16))
        s.C1sb = E(nc.sbuf_tensor("b1_C1", [16, QT], BF16))
        s.C2sb = E(nc.sbuf_tensor("b1_C2", [16, QT], BF16))
        s.bcols = E(nc.sbuf_tensor("b1_bcols", [128, 8], F32))
        s.wtmp = [E(nc.sbuf_tensor(f"b1_wtmp{i}", [128, 512], BF16)) for i in range(2)]
        s.cst = {}
        for name in ["c_ident", "c_ones1", "c_onescol"]:
            s.cst[name] = E(nc.sbuf_tensor("b1_" + name, list(CONST_SPECS[name]), BF16))
        s.ps = [E(nc.psum_tensor(f"b1_ps{i}", [128, 512], F32)) for i in range(6)]
        s.ps_bc = E(nc.psum_tensor("b1_ps_bc", [128, 512], F32))
        s.ps_bc2 = E(nc.psum_tensor("b1_ps_bc2", [128, 512], F32))
        s.ps_tr = [s.ps[i].ap().bitcast(BF16) for i in (2, 3, 4, 5)]

    def load_consts(self):
        p, s, io = self.p, self, self.io
        s.mc = {}
        for name in ["c_ident", "c_ones1", "c_onescol"]:
            p.dma("gpsimd", s.cst[name][:], io[name], sem="q_ig")
        for c in range(NCH):
            p.dma("gpsimd", s.xbf[c][:], io["x1T"][c * 128:(c + 1) * 128, :], sem="q_ig")
        for c in range(NCH):
            p.dma("gpsimd", s.win1[:, c * QT:(c + 1) * QT], io["Win1"][c * 128:(c + 1) * 128, :],
                  sem="q_ig")
            p.dma("gpsimd", s.projW[:, c * 16:(c + 1) * 16], io["proj"][c * 128:(c + 1) * 128, :],
                  sem="q_ig")
        p.dma("gpsimd", s.C1sb[:], io["C1"], sem="q_ig")
        p.dma("gpsimd", s.C2sb[:], io["C2"], sem="q_ig")
        for rc in range(4):
            p.dma("sync", s.bcols[:, rc:rc + 1],
                  io["Win1"][1024, rc * 128:(rc + 1) * 128].rearrange("(p o) -> p o", o=1), sem="q_is")
            p.dma("sync", s.bcols[:, 4 + rc:5 + rc],
                  io["Win2"][4096, rc * 128:(rc + 1) * 128].rearrange("(p o) -> p o", o=1), sem="q_is")
        s.m_bias = p.get("q_is")
        m_ig = p.get("q_ig")
        for name in ["c_ident", "c_ones1", "c_onescol"]:
            s.mc[name] = m_ig
        s.m_xall = m_ig
        s.m_smallw = m_ig

    def prep_weights(self):
        p, s, nc, io = self.p, self, self.nc, self.io
        s.wout1T_dram = nc.dram_tensor("b1_wout1T", [512, 4096], BF16, kind="Internal")
        s.win2_dram = io["_wch"]["win2"]
        ptr_free = [None] * 4
        wtmp_free = [None, None]
        for er in range(32):
            buf = s.wtmp[er % 2]
            qp = "q_pa" if er % 2 == 0 else "q_pb"
            mld = p.dma("gpsimd", buf[:], io["Wout1"][er * 128:(er + 1) * 128, :],
                        waits=[("dve", wtmp_free[er % 2])] if wtmp_free[er % 2] else [], sem=qp)
            for rc in range(4):
                tr = s.ps_tr[rc]
                go = 0
                w = [(qp, mld), ("q_ig", s.mc["c_ident"])]
                if ptr_free[rc]:
                    w.append(("dve", ptr_free[rc]))
                mt = p.emit("tensor",
                            lambda e, buf=buf, rc=rc, tr=tr, go=go: e.transpose(
                                tr[:, go:go + 128], buf[:, rc * 128:(rc + 1) * 128],
                                s.cst["c_ident"][:]),
                            waits=w, inc="pe")
                mv = p.emit("vector",
                            lambda e, rc=rc, er=er, tr=tr, go=go: e.tensor_copy(
                                s.hT[:, rc * 4096 + er * 128:rc * 4096 + (er + 1) * 128],
                                tr[:, go:go + 128]),
                            waits=[("pe", mt)], inc="dve")
                ptr_free[rc] = mv
                wtmp_free[er % 2] = mv
        mst = None
        for rc in range(4):
            mst = p.dma("sync", s.wout1T_dram.ap()[rc * 128:(rc + 1) * 128, :],
                        s.hT[:, rc * 4096:(rc + 1) * 4096], waits=[("dve", p.get("dve"))], sem="q_ps")
        s.m_w1T = mst
        for er in range(8):
            buf = s.wtmp[er % 2]
            qp = "q_pa" if er % 2 == 0 else "q_pb"
            mld = p.dma("gpsimd", buf[:], io["Wout2"][er * 128:(er + 1) * 128, :],
                        waits=[("dve", wtmp_free[er % 2])] if wtmp_free[er % 2] else [], sem=qp)
            for rc in range(4):
                tr = s.ps_tr[rc]
                go = 0
                w = [(qp, mld)]
                if ptr_free[rc]:
                    w.append(("dve", ptr_free[rc]))
                mt = p.emit("tensor",
                            lambda e, buf=buf, rc=rc, tr=tr, go=go: e.transpose(
                                tr[:, go:go + 128], buf[:, rc * 128:(rc + 1) * 128],
                                s.cst["c_ident"][:]),
                            waits=w, inc="pe")
                mv = p.emit("vector",
                            lambda e, rc=rc, er=er, tr=tr, go=go: e.tensor_copy(
                                s.wout2T[:, rc * 1024 + er * 128:rc * 1024 + (er + 1) * 128],
                                tr[:, go:go + 128]),
                            waits=[("pe", mt)], inc="dve")
                ptr_free[rc] = mv
                wtmp_free[er % 2] = mv
        s.m_w2T = p.get("dve")
        s.m_w2c = io["_wch_m"]

    def gates(self):
        p, s = self.p, self
        sm = lambda i: s.esm[:, i * 32:(i + 1) * 32]
        sm3 = lambda i: s.esm[:, i * 32:(i + 1) * 32].rearrange("p (c o) -> p c o", o=1)
        for qt in range(NQT):
            for c in range(NCH):
                w = []
                if c == 0:
                    w = [("q_ig", s.m_smallw), ("q_ig", s.m_xall)]
                    if qt >= 2:
                        w.append(("dve", s._tevac[qt - 2]))
                p.emit("tensor",
                       lambda e, c=c, qt=qt: e.matmul(
                           s.ps[qt % 2].ap()[0:16, :], s.projW[:, c * 16:(c + 1) * 16],
                           s.xbf[c][:, qt * QT:(qt + 1) * QT],
                           start=(c == 0), stop=(c == NCH - 1)),
                       waits=w, inc="pe" if c == NCH - 1 else None)
            mpe = p.get("pe")
            mv = p.emit("vector",
                        lambda e, qt=qt: e.tensor_copy(s.tT[:, qt * QT:(qt + 1) * QT],
                                                       s.ps[qt % 2].ap()[0:16, :]),
                        waits=[("pe", mpe)], inc="dve")
            if not hasattr(s, "_tevac"):
                s._tevac = {}
            s._tevac[qt] = mv
        guard = p.get("dve")
        for tt in range(32):
            g = tt % 4
            tr = s.ps_tr[g]
            go = 0
            w = [("dve", guard)] if tt < 4 else [("dve", s._ttb[tt - 4])]
            mt = p.emit("tensor",
                        lambda e, tt=tt, tr=tr, go=go: e.transpose(
                            tr[:, go:go + 16], s.tT[0:16, tt * 128:(tt + 1) * 128],
                            s.cst["c_ident"][0:16, 0:16]),
                        waits=w, inc="pe")
            mv = p.emit("vector",
                        lambda e, tt=tt, tr=tr, go=go: e.tensor_copy(
                            s.t_tb[:, tt * 16:(tt + 1) * 16], tr[:, go:go + 16]),
                        waits=[("pe", mt)], inc="dve")
            if not hasattr(s, "_ttb"):
                s._ttb = {}
            s._ttb[tt] = mv
        t3 = s.t_tb[:].rearrange("p (c e) -> p c e", e=16)
        tmp3 = s.e_tmp[:].rearrange("p (c e) -> p c e", e=16)
        c3 = s.e_c[:].rearrange("p (c e) -> p c e", e=16)
        z3 = s.e_z[:].rearrange("p (c e) -> p c e", e=16)
        zz3 = s.e_zz[:].rearrange("p (c e) -> p c e", e=16)
        bc = lambda i: sm3(i).broadcast_to([128, 32, 16])
        p.emit("vector", lambda e: e.tensor_reduce(sm3(0), t3, AX.X, ALU.add),
               waits=[("dve", p.get("dve"))], inc="dve")
        p.emit("vector", lambda e: e.tensor_scalar_mul(sm(1), sm(0), 1.0 / 16), inc="dve")
        p.emit("vector", lambda e: e.tensor_tensor(tmp3, t3, bc(1), ALU.subtract), inc="dve")
        p.emit("vector", lambda e: e.tensor_mul(s.e_c[:], s.e_tmp[:], s.e_tmp[:]), inc="dve")
        p.emit("vector", lambda e: e.tensor_reduce(sm3(2), c3, AX.X, ALU.add), inc="dve")
        mv = p.emit("vector", lambda e: e.tensor_scalar(sm(3), sm(2), 1.0 / 16, 1e-5, ALU.mult, ALU.add),
                    inc="dve")
        mla = p.emit("scalar", lambda e: e.activation(sm(4), sm(3), AF.Ln),
                     waits=[("dve", mv)], inc="act")
        mrs = p.emit("scalar", lambda e: e.activation(sm(5), sm(4), AF.Exp, scale=-0.5),
                     waits=[("act", mla)], inc="act")
        p.emit("vector", lambda e: e.tensor_tensor(z3, tmp3, bc(5), ALU.mult),
               waits=[("act", mrs)], inc="dve")
        p.emit("vector", lambda e: e.tensor_reduce(sm3(6), z3, AX.X, ALU.max), inc="dve")
        p.emit("vector", lambda e: e.tensor_scalar_mul(sm(7), sm(6), 0.5), inc="dve")
        p.emit("vector", lambda e: e.scalar_tensor_tensor(zz3, z3, 0.5, bc(7), ALU.mult, ALU.subtract),
               inc="dve")
        p.emit("vector", lambda e: e.memset(sm(8), -1.0), inc="dve")
        tcur, tnxt = 8, 9
        for it in range(24):
            p.emit("vector", lambda e, tcur=tcur: e.tensor_tensor(tmp3, zz3, bc(tcur), ALU.subtract),
                   inc="dve")
            p.emit("vector", lambda e: e.tensor_scalar_max(s.e_c[:], s.e_tmp[:], 0.0), inc="dve")
            p.emit("vector", lambda e: e.tensor_mul(s.e_tmp[:], s.e_c[:], s.e_c[:]), inc="dve")
            p.emit("vector", lambda e: e.tensor_reduce(sm3(10), tmp3, AX.X, ALU.add), inc="dve")
            p.emit("vector", lambda e: e.tensor_reduce(sm3(11), c3, AX.X, ALU.add), inc="dve")
            p.emit("vector", lambda e: e.tensor_scalar_add(sm(12), sm(10), -1.0), inc="dve")
            p.emit("vector", lambda e: e.reciprocal(sm(13), sm(11)), inc="dve")
            p.emit("vector", lambda e: e.tensor_tensor(sm(14), sm(12), sm(13), ALU.mult), inc="dve")
            p.emit("vector", lambda e, tcur=tcur, tnxt=tnxt: e.scalar_tensor_tensor(
                sm(tnxt), sm(14), 0.5, sm(tcur), ALU.mult, ALU.add), inc="dve")
            tcur, tnxt = tnxt, tcur
        p.emit("vector", lambda e, tcur=tcur: e.tensor_tensor(tmp3, zz3, bc(tcur), ALU.subtract),
               inc="dve")
        p.emit("vector", lambda e: e.tensor_scalar_max(s.e_c[:], s.e_tmp[:], 0.0), inc="dve")
        ma = p.emit("vector", lambda e: e.tensor_mul(s.a_tb[:], s.e_c[:], s.e_c[:]), inc="dve")
        for tt in range(32):
            g = tt % 4
            tr = s.ps_tr[g]
            go = 0
            w = [("dve", ma)] if tt < 4 else [("dve", s._atb[tt - 4])]
            mt = p.emit("tensor",
                        lambda e, tt=tt, tr=tr, go=go: e.transpose(
                            tr[0:16, go:go + 128], s.a_tb[:, tt * 16:(tt + 1) * 16],
                            s.cst["c_ident"][:]),
                        waits=w, inc="pe")
            mv = p.emit("vector",
                        lambda e, tt=tt, tr=tr, go=go: e.tensor_copy(
                            s.aT[0:16, tt * 128:(tt + 1) * 128], tr[0:16, go:go + 128]),
                        waits=[("pe", mt)], inc="dve")
            if not hasattr(s, "_atb"):
                s._atb = {}
            s._atb[tt] = mv
        s.m_gates = p.get("dve")

    def mumoe(self):
        p, s, io = self.p, self, self.io
        bias1 = lambda rc: s.bcols[:, rc:rc + 1]
        bias2 = lambda rc: s.bcols[:, 4 + rc:5 + rc]
        w1ld_prev = {}
        w2ld_prev = {}
        mdma_prev = {}
        eps = 1e-5
        for qt in range(NQT):
            guard = p.get("dve")
            for j, Csb in ((0, s.C1sb), (1, s.C2sb)):
                mgv = None
                for rc in range(4):
                    w = [("dve", s.m_gates)] if (qt == 0 and rc == 0 and j == 0) else []
                    if rc == 0:
                        w.append(("dve", guard))
                    if mgv:
                        w = list(w) + [("dve", mgv)]
                    mpe = p.emit("tensor",
                                 lambda e, j=j, rc=rc, qt=qt, Csb=Csb: e.matmul(
                                     s.ps[4 + j][:], Csb[:, rc * 128:(rc + 1) * 128],
                                     s.aT[0:16, qt * QT:(qt + 1) * QT],
                                     start=True, stop=True),
                                 waits=w, inc="pe")
                    mgv = p.emit("vector",
                                 lambda e, j=j, rc=rc: e.tensor_copy(
                                     s.gbuf[j][:, rc * QT:(rc + 1) * QT], s.ps[4 + j][:]),
                                 waits=[("pe", mpe)], inc="dve")
            if not hasattr(s, "_gz"):
                s._gz = {}
            s._gz[0] = p.get("dve")
            s._gz[1] = p.get("dve")
            zfree = getattr(s, "_z1free", {})
            for rc in range(4):
                for c in range(NCH):
                    w = []
                    if c == 0:
                        if qt == 0 and rc == 0:
                            w = [("q_ig", s.m_smallw), ("q_is", s.m_bias)]
                        if zfree.get(rc % 2):
                            w.append(("dve", zfree[rc % 2]))
                    p.emit("tensor",
                           lambda e, rc=rc, c=c, qt=qt: e.matmul(
                               s.ps[rc % 2][:], s.win1[:, c * QT + rc * 128:c * QT + (rc + 1) * 128],
                               s.xbf[c][:, qt * QT:(qt + 1) * QT],
                               start=(c == 0), stop=(c == NCH - 1)),
                           waits=w, inc="pe" if c == NCH - 1 else None)
                mpe = p.get("pe")
                zfree[rc % 2] = p.emit("vector",
                                       lambda e, rc=rc: e.scalar_tensor_tensor(
                                           s.zg[0][:, rc * QT:(rc + 1) * QT], s.ps[rc % 2][:], bias1(rc),
                                           s.gbuf[0][:, rc * QT:(rc + 1) * QT], ALU.add, ALU.mult),
                                       waits=[("pe", mpe)], inc="dve")
            s._z1free = zfree
            mzg1 = p.get("dve")
            for go in range(8):
                wb = s.w1buf[go % 2]
                w = [("q_ps", s.m_w1T), ("dve", mzg1)]
                if w1ld_prev.get(go % 2):
                    w.append(("pe", w1ld_prev[go % 2]))
                mld = p.dma("sync", wb[:],
                            s.wout1T_dram.ap().rearrange("(rc p) o -> p rc o", p=128)[
                                :, :, go * 512:(go + 1) * 512], waits=w,
                            sem="q_w1a" if go % 2 == 0 else "q_w1b")
                hact_loc = {}
                for oc4 in range(4):
                    oc = go * 4 + oc4
                    for rc in range(4):
                        w2 = [(("q_w1a" if go % 2 == 0 else "q_w1b"), mld)] if rc == 0 else []
                        if rc == 0 and oc4 == 0 and s._hact.get((qt - 1, go)):
                            w2.append(("act", s._hact[(qt - 1, go)]))
                        if rc == 0 and oc4 >= 2:
                            w2.append(("act", hact_loc[oc4 - 2]))
                        if rc == 0 and oc4 < 2 and s._hact.get((qt, go - 1)):
                            w2.append(("act", s._hact[(qt, go - 1)]))
                        p.emit("tensor",
                               lambda e, oc4=oc4, rc=rc, wb=wb: e.matmul(
                                   s.ps[2 + oc4 % 2][:],
                                   wb[:].rearrange("p (rc o) -> p rc o", rc=4)[:, rc,
                                                                              oc4 * 128:(oc4 + 1) * 128],
                                   s.zg[0][:, rc * QT:(rc + 1) * QT],
                                   start=(rc == 0), stop=(rc == 3)),
                               waits=w2, inc="pe" if rc == 3 else None)
                    mpe = p.get("pe")
                    mact = p.emit("scalar",
                                  lambda e, oc=oc, oc4=oc4: e.activation(
                                      s.hT[:, oc * QT:(oc + 1) * QT], s.ps[2 + oc4 % 2][:], AF.Gelu),
                                  waits=[("pe", mpe)], inc="act")
                    hact_loc[oc4] = mact
                w1ld_prev[go % 2] = mpe
                s._hact[(qt, go)] = mact
            for go in range(8):
                wb = s.w2buf[go % 2]
                w = [("q_wch", s.m_w2c), ("act", s._hact[(qt, 7)])]
                if w2ld_prev.get(go % 2):
                    w.append(("pe", w2ld_prev[go % 2]))
                mld = p.dma("sync", wb[:],
                            s.win2_dram.ap().rearrange("(g p) r -> p g r", p=128)[
                                :, go * 4:(go + 1) * 4, :], waits=w,
                            sem="q_w2a" if go % 2 == 0 else "q_w2b")
                for oc4 in range(4):
                    oc = go * 4 + oc4
                    for rc in range(4):
                        w2 = []
                        if rc == 0 and oc4 == 0:
                            w2 = [(("q_w2a" if go % 2 == 0 else "q_w2b"), mld)]
                            if go == 0:
                                w2.append(("dve", s._zg2free))
                        mpe = p.emit("tensor",
                                     lambda e, oc=oc, oc4=oc4, rc=rc, wb=wb: e.matmul(
                                         s.ps[rc][:],
                                         wb[:].rearrange("p (g r) -> p g r", g=4)[
                                             :, oc4, rc * 128:(rc + 1) * 128],
                                         s.hT[:, oc * QT:(oc + 1) * QT],
                                         start=(oc == 0), stop=(oc == 31)),
                                     waits=w2, inc="pe" if (rc == 3 and oc4 == 3) else None)
                w2ld_prev[go % 2] = mpe
            mpe_z2 = p.get("pe")
            for rc in range(4):
                p.emit("vector",
                       lambda e, rc=rc: e.scalar_tensor_tensor(
                           s.zg[1][:, rc * QT:(rc + 1) * QT], s.ps[rc][:], bias2(rc),
                           s.gbuf[1][:, rc * QT:(rc + 1) * QT], ALU.add, ALU.mult),
                       waits=[("pe", mpe_z2)] if rc == 0 else [], inc="dve")
            s._zg2free = p.get("dve")
            guard2 = p.get("dve")
            ps_mm_free = [None, None]
            for dc in range(NCH):
                for rc in range(4):
                    w = []
                    if rc == 0:
                        w = [("dve", guard2)]
                        if ps_mm_free[dc % 2]:
                            w.append(("dve", ps_mm_free[dc % 2]))
                    p.emit("tensor",
                           lambda e, dc=dc, rc=rc: e.matmul(
                               s.ps[4 + dc % 2][:],
                               s.wout2T[:, rc * 1024 + dc * 128:rc * 1024 + (dc + 1) * 128],
                               s.zg[1][:, rc * QT:(rc + 1) * QT],
                               start=(rc == 0), stop=(rc == 3)),
                           waits=w, inc="pe" if rc == 3 else None)
                mpe = p.get("pe")
                mres = p.emit("vector",
                              lambda e, dc=dc, qt=qt: e.tensor_tensor(
                                  s.resbuf[:, dc * QT:(dc + 1) * QT], s.ps[4 + dc % 2][:],
                                  s.xbf[dc][:, qt * QT:(qt + 1) * QT], ALU.add),
                              waits=[("pe", mpe)], inc="dve")
                ps_mm_free[dc % 2] = mres
            guard_pe2 = p.get("pe")
            guard_dve3 = p.get("dve")
            for oc in range(NCH):
                w = [("dve", guard_dve3), ("pe", guard_pe2)] if oc == 0 else []
                p.emit("tensor",
                       lambda e, oc=oc: e.matmul(
                           s.ps_bc.ap()[0:1, 0:QT], s.cst["c_onescol"][:, 0:1],
                           s.resbuf[:, oc * QT:(oc + 1) * QT],
                           start=(oc == 0), stop=(oc == NCH - 1)),
                       waits=w, inc="pe" if oc == NCH - 1 else None)
            sq_free = [None, None]
            for oc in range(NCH):
                sp = s.sqscr[oc % 2]
                w = [("dve", guard_dve3)] if oc == 0 else []
                if sq_free[oc % 2]:
                    w = list(w) + [("pe", sq_free[oc % 2])]
                mact2 = p.emit("scalar",
                               lambda e, oc=oc, sp=sp: e.activation(
                                   sp[:], s.resbuf[:, oc * QT:(oc + 1) * QT], AF.Square),
                               waits=w, inc="act")
                mpe_sq = p.emit("tensor",
                                lambda e, oc=oc, sp=sp: e.matmul(
                                    s.ps_bc2.ap()[0:1, 0:QT], s.cst["c_onescol"][:, 0:1], sp[:],
                                    start=(oc == 0), stop=(oc == NCH - 1)),
                                waits=[("act", mact2)], inc="pe")
                sq_free[oc % 2] = mpe_sq
            mpe_stats = p.get("pe")
            p.emit("vector", lambda e: e.tensor_scalar_mul(s.rowbuf[0:1, 0:QT],
                                                           s.ps_bc.ap()[0:1, 0:QT], 1.0 / 1024),
                   waits=[("pe", mpe_stats)], inc="dve")
            p.emit("vector", lambda e: e.tensor_scalar_mul(s.rowbuf[0:1, QT:2 * QT],
                                                           s.ps_bc2.ap()[0:1, 0:QT], 1.0 / 1024),
                   inc="dve")
            p.emit("vector", lambda e: e.tensor_tensor(s.rowbuf[0:1, 2 * QT:3 * QT],
                                                       s.rowbuf[0:1, 0:QT], s.rowbuf[0:1, 0:QT],
                                                       ALU.mult), inc="dve")
            p.emit("vector", lambda e: e.tensor_scalar_add(s.rowslotE[0:1, :],
                                                           s.rowbuf[0:1, 2 * QT:3 * QT], -eps),
                   inc="dve")
            mvar = p.emit("vector", lambda e: e.tensor_sub(s.rowbuf[0:1, 3 * QT:4 * QT],
                                                           s.rowbuf[0:1, QT:2 * QT],
                                                           s.rowslotE[0:1, :]), inc="dve")
            mln2 = p.emit("scalar", lambda e: e.activation(s.rowslot4[0:1, :],
                                                           s.rowbuf[0:1, 3 * QT:4 * QT], AF.Ln),
                          waits=[("dve", mvar)], inc="act")
            mrstd = p.emit("scalar", lambda e: e.activation(s.rowbf[0:1, 0:QT],
                                                            s.rowslot4[0:1, :], AF.Exp, scale=-0.5),
                           waits=[("act", mln2)], inc="act")
            mb = p.emit("vector", lambda e: e.tensor_tensor(s.rowbf[0:1, QT:2 * QT],
                                                            s.rowbf[0:1, 0:QT],
                                                            s.rowbuf[0:1, 0:QT], ALU.mult),
                        waits=[("act", mrstd)], inc="dve")
            p.emit("tensor", lambda e: e.matmul(s.ps_bc[:, 0:QT], s.cst["c_ones1"][0:1, :],
                                                s.rowbf[0:1, 0:QT], start=True, stop=True),
                   waits=[("dve", mb), ("q_ig", s.mc["c_ones1"])])
            mab = p.emit("tensor", lambda e: e.matmul(s.ps_bc2.ap()[:, 0:QT], s.cst["c_ones1"][0:1, :],
                                                      s.rowbf[0:1, QT:2 * QT], start=True, stop=True),
                         inc="pe")
            for oc in range(NCH):
                xb = s.x1buf[oc % 2]
                w = [("pe", mab)]
                if mdma_prev.get(oc % 2):
                    w.append(("q_sta" if oc % 2 == 0 else "q_stb", mdma_prev[oc % 2]))
                p.emit("vector",
                       lambda e, oc=oc: e.tensor_tensor(
                           s.x1tmp[:], s.resbuf[:, oc * QT:(oc + 1) * QT], s.ps_bc[:, 0:QT],
                           ALU.mult),
                       waits=w, inc="dve")
                mx1 = p.emit("vector",
                             lambda e, xb=xb: e.tensor_sub(xb[:], s.x1tmp[:], s.ps_bc2.ap()[:, 0:QT]),
                             inc="dve")
                mdma_prev[oc % 2] = p.dma("sync",
                                          io["x2T"][oc * 128:(oc + 1) * 128, qt * QT:(qt + 1) * QT],
                                          xb[:], waits=[("dve", mx1)],
                                          sem="q_sta" if oc % 2 == 0 else "q_stb")

    def run(self):
        p, s = self.p, self
        s.alloc()
        s._hact = {}
        s._zg2free = None
        s.load_consts()
        s.prep_weights()
        self.gates()
        self.mumoe()
        p.emit("sync", lambda e: None,
               waits=[("q_sta", p.get("q_sta")), ("q_stb", p.get("q_stb"))])


class B2:

    def __init__(self, nc, p, stack, io):
        self.nc, self.p, self.stack, self.io = nc, p, stack, io
        self.E = stack.enter_context

    def alloc(self):
        nc, E = self.nc, self.E
        s = self
        s.xbf = [E(nc.sbuf_tensor(f"b2_xbf{c}", [128, T], BF16)) for c in range(NCH)]
        s.vhat = E(nc.sbuf_tensor("b2_vhat", [128, 32 * 1040], BF16))
        s.wbuf = E(nc.sbuf_tensor("b2_wbuf", [128, 8 * 1024], BF16))
        s.kbuf = E(nc.sbuf_tensor("b2_kbuf", [128, 8 * 1024], BF16))
        s.qr = E(nc.sbuf_tensor("b2_qr", [128, T], BF16))
        s.kr = E(nc.sbuf_tensor("b2_kr", [128, T], BF16))
        s.sqbuf = E(nc.sbuf_tensor("b2_sqbuf", [128, T], BF16))
        s.kptok = [E(nc.sbuf_tensor(f"b2_kptok{i}", [128, 256], BF16)) for i in range(2)]
        s.kvbf = E(nc.sbuf_tensor("b2_kvbf", [128, 130], BF16))
        s.kvf32 = E(nc.sbuf_tensor("b2_kvf32", [128, 2 * 130], F32))
        s.atile = [E(nc.sbuf_tensor(f"b2_atile{i}", [128, 128], BF16)) for i in range(2)]
        s.attntok = E(nc.sbuf_tensor("b2_attntok", [128, 32 * 64], BF16))

        s.dbuf = E(nc.sbuf_tensor("b2_dbuf", [128, 8], F32))
        s.rbuf = E(nc.sbuf_tensor("b2_rbuf", [128, 8], F32))
        s.mapW = E(nc.sbuf_tensor("b2_mapW", [128, 256], BF16))
        s.negones = E(nc.sbuf_tensor("b2_negones", [128, 256], BF16))
        s.biases = E(nc.sbuf_tensor("b2_biases", [128, 4 * NCH], F32))
        s.vbrow = E(nc.sbuf_tensor("b2_vbrow", [1, 1024], BF16))
        s.gb2 = E(nc.sbuf_tensor("b2_gb2", [2, 1024], BF16))
        s.onesrow = E(nc.sbuf_tensor("b2_onesrow", [2, QT], BF16))
        s.resbuf = s.vhat[:, 0:NCH * QT]
        s.res2buf = s.vhat[:, NCH * QT:2 * NCH * QT]
        _qf = s.qr[:].bitcast(F32)
        _kf = s.kr[:].bitcast(F32)
        s.sqscr = [s.attntok[:, 0:QT], s.attntok[:, QT:2 * QT]]
        s.ybuf = s.attntok[:, 2 * QT:4 * QT].bitcast(F32)
        s.rowbuf = _qf[0:1, 0:2048]
        s.rowslot4 = _qf[32:33, 0:QT]
        s.rowslotE = E(nc.sbuf_tensor("b2_rsE", [1, QT], F32))[:]
        s.rowbf = s.kr[0:1, 0:2 * QT]
        s.x1buf = [_kf[:, 512 + i * QT:512 + (i + 1) * QT] for i in range(2)]
        s.x1tmp = _kf[:, 1536:2048]
        s.cst = {}
        for name in ["c_ident", "c_ones1", "c_onescol", "c_tri", "c_negrow"]:
            shp = CONST_SPECS.get(name, (1, QT))
            s.cst[name] = E(nc.sbuf_tensor("b2_" + name, list(shp), BF16))
        s.ps_m = [E(nc.psum_tensor(f"b2_ps_m{i}", [128, 1024], F32)) for i in range(2)]
        s.ps_a = E(nc.psum_tensor("b2_ps_a", [128, 512], F32))
        s.ps_o4 = E(nc.psum_tensor("b2_ps_o4", [128, 512], F32))
        s.ps_kv = E(nc.psum_tensor("b2_ps_kv", [128, 512], F32))
        s.ps_bc = E(nc.psum_tensor("b2_ps_bc", [128, 512], F32))
        s.ps_tr = [s.ps_m[0].ap().bitcast(BF16), s.ps_m[1].ap().bitcast(BF16)]
        s.ps_bc2 = s.ps_kv

    def load_consts(self):
        p, s, io = self.p, self, self.io
        for name in ["c_ident", "c_ones1", "c_onescol", "c_tri", "c_negrow"]:
            p.dma("gpsimd", s.cst[name][:], io[name], sem="q_ig")
        for c in range(NCH):
            p.dma("gpsimd", s.xbf[c][:], io["x2T"][c * 128:(c + 1) * 128, :], sem="q_ig")
        for j, bn in enumerate(["q_b", "k_b", "v_b", "out_b"]):
            for c in range(NCH):
                p.dma("sync", s.biases[:, j * NCH + c:j * NCH + c + 1],
                      io[bn][c * 128:(c + 1) * 128].rearrange("(p o) -> p o", o=1), sem="q_is")
        p.dma("gpsimd", s.gb2[0:1, :], io["gamma"].rearrange("(o p) -> o p", o=1), sem="q_ig")
        p.dma("gpsimd", s.gb2[1:2, :], io["beta"].rearrange("(o p) -> o p", o=1), sem="q_ig")
        p.dma("gpsimd", s.vbrow[0:1, :], io["v_b"].rearrange("(o p) -> o p", o=1), sem="q_ig")
        p.dma("gpsimd", s.onesrow[1:2, :], io["c_negrow"], sem="q_ig")
        s.m_bias = p.get("q_is")
        s.m_ig = p.get("q_ig")

    def bias_ap(self, j, c):
        return self.biases[:, j * NCH + c:j * NCH + c + 1]

    def prep(self):
        p, s, nc, io = self.p, self, self.nc, self.io
        m1 = p.dma("gpsimd", s.sqbuf[:, 0:64], io["proj"][0:128, :], sem="q_pa")
        m2 = p.dma("gpsimd", s.sqbuf[:, 64:128], io["proj"][128:256, :], sem="q_pb")
        mn = p.emit("vector", lambda e: e.memset(s.negones[:], -1.0 / 128))
        prev_evac = {}
        for half in range(2):
            qw = ("q_pa", m1) if half == 0 else ("q_pb", m2)
            for tp in (0, 64):
                tw = [qw, ("q_ig", s.m_ig)]
                if prev_evac.get(tp):
                    tw.append(("dve", prev_evac[tp]))
                mt = p.emit("tensor",
                            lambda e, half=half, tp=tp: e.transpose(
                                (s.ps_tr[0][0:64, 0:128] if tp == 0 else s.ps_tr[0][64:128, 1024:1152]),
                                s.sqbuf[:, half * 64:half * 64 + 64],
                                s.cst["c_ident"][:], tile_position=(0, tp)),
                            waits=tw, inc="pe")
                prev_evac[tp] = p.emit("vector",
                       lambda e, half=half, tp=tp: e.tensor_scalar_mul(
                           s.mapW[tp:tp + 64, half * 128:(half + 1) * 128],
                           (s.ps_tr[0][:, 0:128] if tp == 0 else s.ps_tr[0][:, 1024:1152])[tp:tp + 64, :],
                           0.125),
                       waits=[("pe", mt)], inc="dve")
        s.m_mapw = p.get("dve")
        s.wc = {wn: io["_wch"][wn] for wn in ["q_w", "k_w"]}
        s.m_wc = io["_wch_m"]

    def build_vhat(self):
        p, s, io = self.p, self, self.io
        vh3 = s.vhat[:].rearrange("p (tt pr c) -> p tt pr c", pr=8, c=130)
        p.emit("vector", lambda e: e.memset(vh3[:, :, :, 64:65], 1.0))
        p.emit("vector", lambda e: e.memset(vh3[:, :, :, 129:130], 1.0))
        mones = p.get("dve")
        for c in range(NCH):
            p.dma("gpsimd", s.kbuf[:, c * 1024:(c + 1) * 1024],
                  io["v_w"][c * 128:(c + 1) * 128, :], sem="q_ig2")
        mvl = p.get("q_ig2")
        ps_free = {}
        for tt in range(32):
            for half in range(2):
                for c in range(NCH):
                    w = []
                    if c == 0:
                        w = [("q_ig2", mvl), ("q_ig", s.m_ig)]
                        if tt == 0 and half == 0:
                            w.append(("dve", mones))
                        if ps_free.get(tt % 2):
                            w.append(("dve", ps_free[tt % 2]))
                    p.emit("tensor",
                           lambda e, tt=tt, c=c, half=half: e.matmul(
                               s.ps_m[tt % 2].ap()[:, half * 512:(half + 1) * 512],
                               s.xbf[c][:, tt * 128:(tt + 1) * 128],
                               s.kbuf[:, c * 1024 + half * 512:c * 1024 + (half + 1) * 512],
                               start=(c == 0), stop=False),
                           waits=w)
                mpx = p.emit("tensor",
                             lambda e, tt=tt, half=half: e.matmul(
                                 s.ps_m[tt % 2].ap()[:, half * 512:(half + 1) * 512],
                                 s.cst["c_ones1"][0:1, :],
                                 s.vbrow[0:1, half * 512:(half + 1) * 512],
                                 start=False, stop=True),
                             inc="pe")
                src = s.ps_m[tt % 2].ap()[:, half * 512:(half + 1) * 512].rearrange(
                    "p (pr h d) -> p pr h d", pr=4, h=2, d=64)
                dst3 = vh3[:, tt, half * 4:(half + 1) * 4, :]
                p.emit("vector",
                       lambda e, src=src, dst3=dst3: e.tensor_copy(dst3[:, :, 0:64], src[:, :, 0, :]),
                       waits=[("pe", mpx)], inc="dve")
                mv = p.emit("vector",
                            lambda e, src=src, dst3=dst3: e.tensor_copy(dst3[:, :, 65:129], src[:, :, 1, :]),
                            inc="dve")
                ps_free[tt % 2] = mv
        s.m_vhat = p.get("dve")

    def pair_qk(self, hp, mat_prev=None):
        p, s = self.p, self
        guard_pe = p.get("pe")
        guard_dve = p.get("dve")
        mws = []
        for j, wn in enumerate(["q_w", "k_w"]):
            src = s.wc[wn].ap().rearrange("(c p) e -> p c e", p=128)[:, :, hp * 128:(hp + 1) * 128]
            dst = s.sqbuf[:].rearrange("p (w c e) -> p w c e", w=4, e=128)[:, j]
            ww = [("q_wch", s.m_wc),
                  ("pe", guard_pe), ("dve", guard_dve)]
            if mat_prev:
                ww.append((mat_prev[0], mat_prev[1]))
            mws.append(p.dma("sync", dst, src, sem="q_wsl", waits=ww))
        mwsl = mws[-1]
        wsl_ap = lambda j, c: s.sqbuf[:].rearrange("p (w c e) -> p w c e", w=4, e=128)[:, j, c]
        ps_free = {}
        for j, dstb in ((0, s.qr), (1, s.kr)):
            for qt in range(NQT):
                for c in range(NCH):
                    w = []
                    if c == 0:
                        w = [("q_wsl", mwsl), ("q_ig", s.m_ig), ("q_is", s.m_bias)]
                        if ps_free.get(qt % 2):
                            w.append(("dve", ps_free[qt % 2]))
                    mpe = p.emit("tensor",
                                 lambda e, j=j, qt=qt, c=c: e.matmul(
                                     s.ps_m[qt % 2].ap()[:, 0:512],
                                     wsl_ap(j, c),
                                     s.xbf[c][:, qt * QT:(qt + 1) * QT],
                                     start=(c == 0), stop=(c == NCH - 1)),
                                 waits=w, inc="pe" if c == NCH - 1 else None)
                mv = p.emit("vector",
                            lambda e, j=j, qt=qt, dstb=dstb, hp=hp: e.tensor_scalar(
                                dstb[:, qt * QT:(qt + 1) * QT], s.ps_m[qt % 2].ap()[:, 0:512],
                                s.bias_ap(j, hp), None, ALU.add),
                            waits=[("pe", mpe)], inc="dve")
                ps_free[qt % 2] = mv

    def head(self, hp, half, adram, mat_prev):
        p, s = self.p, self
        h = 2 * hp + half
        hb = half * 64
        qpT = lambda mc: s.wbuf[:, mc * 4096:(mc + 1) * 4096]
        kpT = lambda mc: s.kbuf[:, mc * 4096:(mc + 1) * 4096]
        vh3 = s.vhat[:].rearrange("p (tt pr c) -> p tt pr c", pr=8, c=130)
        guard = p.get("dve")
        guard_pe = p.get("pe")
        for j, (srcb, dstT) in enumerate(((s.qr, qpT), (s.kr, kpT))):
            sqw = [("pe", guard_pe)]
            if mat_prev:
                sqw.append((mat_prev[0], mat_prev[1]))
            msq = p.emit("vector",
                         lambda e, srcb=srcb, hb=hb: e.tensor_mul(
                             s.sqbuf[hb:hb + 64, :] if hb == 0 else s.sqbuf[64:128, :],
                             srcb[hb:hb + 64, :] if hb == 0 else srcb[64:128, :],
                             srcb[hb:hb + 64, :] if hb == 0 else srcb[64:128, :]),
                         waits=sqw, inc="dve")
            if j == 0:
                ps_free = {}
            mlast_mm = None
            for st in range(NQT):
                for mc in range(2):
                    w = []
                    if mc == 0:
                        w = [("dve", msq), ("dve", s.m_mapw)]
                        if ps_free.get(st % 2):
                            w.append(("act", ps_free[st % 2]))
                    p.emit("tensor",
                           lambda e, st=st, mc=mc, hb=hb, srcb=srcb: e.matmul(
                               s.ps_m[st % 2].ap()[:, mc * 512:(mc + 1) * 512],
                               s.mapW[hb:hb + 64, mc * 128:(mc + 1) * 128],
                               srcb[hb:hb + 64, st * QT:(st + 1) * QT],
                               start=True, stop=False),
                           waits=w)
                    mlast_mm = p.emit("tensor",
                                      lambda e, st=st, mc=mc, hb=hb: e.matmul(
                                          s.ps_m[st % 2].ap()[:, mc * 512:(mc + 1) * 512],
                                          s.negones[hb:hb + 64, mc * 128:(mc + 1) * 128],
                                          s.sqbuf[hb:hb + 64, st * QT:(st + 1) * QT],
                                          start=False, stop=True),
                                      inc="pe")
                dst = (s.wbuf if j == 0 else s.kbuf)[:].rearrange(
                    "p (mc x) -> p mc x", mc=2)[:, :, st * QT:(st + 1) * QT]
                mex = p.emit("scalar",
                             lambda e, st=st, dst=dst: e.activation(
                                 dst, s.ps_m[st % 2].ap().rearrange("p (mc x) -> p mc x", mc=2),
                                 AF.Exp),
                             waits=[("pe", mlast_mm)], inc="act")
                ps_free[st % 2] = mex
            if j == 0:
                guard_pe = p.get("pe")
        m_exp_done = p.get("act")
        ms_ktok = {}
        m_ktok_last = None
        m_mask_last = None
        m_atile_free = {}
        m_kptok_free = {}
        m_kv_evac = None
        m_norm = None
        m_out_last = None
        for i in range(32):
            par = i % 2
            w = [("act", m_exp_done)] if i == 0 else [("pe", m_out_last)]
            if m_ktok_last:
                w.append(("dve", m_ktok_last))
            for mc in range(2):
                mt = p.emit("tensor",
                            lambda e, i=i, mc=mc: e.transpose(
                                s.ps_tr[0][:, mc * 1024:mc * 1024 + 128],
                                kpT(mc)[:, i * 128:(i + 1) * 128],
                                s.cst["c_ident"][:]),
                            waits=w if mc == 0 else [], inc="pe" if mc == 1 else None)
            kw = [("pe", mt)]
            if m_kptok_free.get(par):
                kw.append(("pe", m_kptok_free[par]))
            mktok = p.emit("vector",
                           lambda e, i=i, par=par: e.tensor_copy(
                               s.kptok[par][:].rearrange("p (mc x) -> p mc x", mc=2),
                               s.ps_tr[0].rearrange("p (mc x) -> p mc x", mc=2)[:, :, 0:128]),
                           waits=kw, inc="dve")
            ms_ktok[par] = mktok
            m_ktok_last = mktok
            w = [("dve", m_mask_last)] if m_mask_last else []
            for mc in range(2):
                ma = p.emit("tensor",
                            lambda e, i=i, mc=mc: e.matmul(
                                s.ps_a.ap()[:, 0:128],
                                kpT(mc)[:, i * 128:(i + 1) * 128],
                                qpT(mc)[:, i * 128:(i + 1) * 128],
                                start=(mc == 0), stop=(mc == 1)),
                            waits=w if mc == 0 else [], inc="pe" if mc == 1 else None)
            mw2 = [("pe", ma), ("q_ig", s.m_ig)]
            if m_atile_free.get(par):
                mw2.append(("pe", m_atile_free[par]))
            mmask = p.emit("vector",
                           lambda e, par=par: e.tensor_tensor(
                               s.atile[par][:], s.ps_a.ap()[:, 0:128], s.cst["c_tri"][:], ALU.mult),
                           waits=mw2, inc="dve")
            m_mask_last = mmask
            ob = (i % 4) * 65
            w = [("dve", mmask)]
            if i % 4 == 0 and m_norm:
                w.append(("dve", m_norm))
            m_out_last = p.emit("tensor",
                                lambda e, i=i, ob=ob, par=par, hp=hp, half=half: e.matmul(
                                    s.ps_o4.ap()[:, ob:ob + 65], s.atile[par][:],
                                    vh3[:, i, hp, half * 65:half * 65 + 65],
                                    start=True, stop=(i == 0)),
                                waits=w, inc="pe" if i == 0 else None)
            if i > 0:
                for mc in range(2):
                    m_out_last = p.emit("tensor",
                                        lambda e, i=i, ob=ob, mc=mc: e.matmul(
                                            s.ps_o4.ap()[:, ob:ob + 65],
                                            qpT(mc)[:, i * 128:(i + 1) * 128],
                                            s.kvbf[:, mc * 65:(mc + 1) * 65],
                                            start=False, stop=(mc == 1)),
                                        waits=[m_kv_evac] if mc == 0 else [],
                                        inc="pe" if mc == 1 else None)
            m_atile_free[par] = m_out_last
            w = [("dve", mktok)]
            if m_kv_evac:
                w.append(m_kv_evac)
            for mc in range(2):
                mkv = p.emit("tensor",
                             lambda e, i=i, mc=mc, par=par, hp=hp, half=half: e.matmul(
                                 s.ps_kv.ap()[:, mc * 65:(mc + 1) * 65],
                                 s.kptok[par][:, mc * 128:(mc + 1) * 128],
                                 vh3[:, i, hp, half * 65:half * 65 + 65],
                                 start=True, stop=True),
                             waits=w if mc == 0 else [], inc="pe" if mc == 1 else None)
            m_kptok_free[par] = mkv
            if i == 0:
                madd = p.emit("vector",
                              lambda e, par=par: e.tensor_copy(s.kvf32[:, par * 130:(par + 1) * 130],
                                                               s.ps_kv.ap()[:, 0:130]),
                              waits=[("pe", mkv)], inc="dve")
            else:
                madd = p.emit("vector",
                              lambda e, par=par: e.tensor_add(s.kvf32[:, par * 130:(par + 1) * 130],
                                                              s.kvf32[:, (1 - par) * 130:(2 - par) * 130],
                                                              s.ps_kv.ap()[:, 0:130]),
                              waits=[("pe", mkv)], inc="dve")
            m_kv_evac = ("act", p.emit("scalar",
                                       lambda e, par=par: e.activation(
                                           s.kvbf[:], s.kvf32[:, par * 130:(par + 1) * 130],
                                           AF.Copy),
                                       waits=[("dve", madd)], inc="act"))
            if i % 4 == 3:
                g0 = i - 3
                p.emit("vector",
                       lambda e: e.tensor_copy(
                           s.dbuf[:, 0:4],
                           s.ps_o4.ap()[:, 0:260].rearrange("p (g c) -> p g c", c=65)[:, :, 64:65]
                           .rearrange("p g o -> p (g o)")),
                       waits=[("pe", m_out_last)], inc="dve")
                p.emit("vector", lambda e: e.reciprocal(s.rbuf[:, 0:4], s.dbuf[:, 0:4]), inc="dve")
                m_norm = p.emit("vector",
                                lambda e, g0=g0: e.tensor_tensor(
                                    s.attntok[:, g0 * 64:(g0 + 4) * 64].rearrange(
                                        "p (g d) -> p g d", g=4),
                                    s.ps_o4.ap()[:, 0:260].rearrange("p (g c) -> p g c", c=65)[:, :, 0:64],
                                    s.rbuf[:, 0:4].rearrange("p (g o) -> p g o", o=1).broadcast_to(
                                        [128, 4, 64]),
                                    ALU.mult),
                                inc="dve")
        ab = (1 - half) * 64
        matp = {}
        for tt in range(32):
            par = tt % 2
            w = [("dve", m_norm)] if tt < 2 else []
            if matp.get(par):
                w.append(("dve", matp[par]))
            if tt == 0 and mat_prev:
                w.append((mat_prev[0], mat_prev[1]))
            mt = p.emit("tensor",
                        lambda e, tt=tt, par=par, ab=ab: e.transpose(
                            s.ps_tr[1][ab:ab + 64, par * 1024:par * 1024 + 128],
                            s.attntok[:, tt * 64:(tt + 1) * 64],
                            s.cst["c_ident"][:], tile_position=(0, ab)),
                        waits=w, inc="pe")
            mv = p.emit("vector",
                        lambda e, tt=tt, par=par, ab=ab: e.tensor_copy(
                            s.sqbuf[ab:ab + 64, tt * 128:(tt + 1) * 128],
                            s.ps_tr[1][ab:ab + 64, par * 1024:par * 1024 + 128]),
                        waits=[("pe", mt)], inc="dve")
            matp[par] = mv
        mat = p.dma("sync", adram[h * 64:(h + 1) * 64, :], s.sqbuf[ab:ab + 64, :],
                    sem="q_ata" if h % 2 == 0 else "q_atb", waits=[("dve", p.get("dve"))])
        return (("q_ata" if h % 2 == 0 else "q_atb"), mat)

    def oproj_ln(self, adram):
        p, s, io = self.p, self, self.io
        guard_pe = p.get("pe")
        guard_dve = p.get("dve")
        for c in range(NCH):
            p.dma("gpsimd", s.kbuf[:, c * 1024:(c + 1) * 1024],
                  io["out_w"][c * 128:(c + 1) * 128, :], sem="q_ow",
                  waits=[("pe", guard_pe), ("dve", guard_dve)] if c == 0 else [])
        m_ow = p.get("q_ow")
        eps = 1e-5
        mab_prev = {}
        mdma_prev = {}
        mat_ld = {}
        for qt in range(NQT):
            par = qt % 2
            w = ([("pe", guard_pe), ("dve", guard_dve),
                  ("q_ata", p.get("q_ata")), ("q_atb", p.get("q_atb"))]
                 if qt < 2 else [("pe", mat_ld[par])])
            mld = p.dma("sync", s.wbuf[:, par * 4096:(par + 1) * 4096].rearrange(
                            "p (c t) -> p c t", c=8),
                        adram.rearrange("(c p) t -> p c t", p=128)[:, :, qt * QT:(qt + 1) * QT],
                        sem="q_aa" if par == 0 else "q_ab", waits=w)
            guard2 = p.get("dve")
            ps_free = {}
            for oc in range(NCH):
                for ec in range(NCH):
                    w2 = []
                    if ec == 0:
                        w2 = [(("q_aa" if par == 0 else "q_ab"), mld), ("q_ow", m_ow)]
                        if ps_free.get(oc % 2):
                            w2.append(("dve", ps_free[oc % 2]))
                    mpe = p.emit("tensor",
                                 lambda e, oc=oc, ec=ec, par=par: e.matmul(
                                     s.ps_m[oc % 2].ap()[:, 0:512],
                                     s.kbuf[:, ec * 1024 + oc * 128:ec * 1024 + (oc + 1) * 128],
                                     s.wbuf[:, par * 4096 + ec * 512:par * 4096 + (ec + 1) * 512],
                                     start=(ec == 0), stop=(ec == NCH - 1)),
                                 waits=w2, inc="pe" if ec == NCH - 1 else None)
                mres = p.emit("vector",
                              lambda e, oc=oc, qt=qt: e.scalar_tensor_tensor(
                                  s.resbuf[:, oc * QT:(oc + 1) * QT], s.ps_m[oc % 2].ap()[:, 0:512],
                                  self.bias_ap(3, oc),
                                  s.xbf[oc][:, qt * QT:(qt + 1) * QT], ALU.add, ALU.add),
                              waits=[("pe", mpe)], inc="dve")
                ps_free[oc % 2] = mres
            mat_ld[par] = p.get("pe")
            self._ln_stats(s.resbuf, qt)
            mb = p.emit("vector", lambda e: e.tensor_tensor(s.onesrow[0:1, :], s.rowbf[0:1, 0:QT],
                                                            s.rowbuf[0:1, 0:QT], ALU.mult),
                        inc="dve")
            mab_last = None
            for oc in range(NCH):
                w2 = [("dve", mb)]
                if mab_last:
                    w2.append(("dve", mab_last))
                p.emit("tensor",
                       lambda e, oc=oc: e.matmul(
                           s.ps_bc[:, 0:QT], s.gb2[0:1, oc * 128:(oc + 1) * 128],
                           s.rowbf[0:1, 0:QT], start=True, stop=True),
                       waits=w2)
                mabx = p.emit("tensor",
                              lambda e, oc=oc: e.matmul(
                                  s.ps_bc2.ap()[:, 0:QT], s.gb2[:, oc * 128:(oc + 1) * 128],
                                  s.onesrow[:], start=True, stop=True),
                              inc="pe")
                p.emit("vector",
                       lambda e, oc=oc: e.tensor_tensor(
                           s.x1tmp, s.resbuf[:, oc * QT:(oc + 1) * QT], s.ps_bc[:, 0:QT], ALU.mult),
                       waits=[("pe", mabx)], inc="dve")
                p.emit("vector",
                       lambda e: e.tensor_sub(s.ybuf, s.x1tmp, s.ps_bc2.ap()[:, 0:QT]),
                       inc="dve")
                mab_last = p.emit("vector",
                                  lambda e, oc=oc, qt=qt: e.tensor_tensor(
                                      s.res2buf[:, oc * QT:(oc + 1) * QT], s.ybuf,
                                      s.xbf[oc][:, qt * QT:(qt + 1) * QT], ALU.add),
                                  inc="dve")
            self._ln_stats(s.res2buf, qt)
            mbb = p.emit("vector", lambda e: e.tensor_tensor(s.rowbf[0:1, QT:2 * QT],
                                                             s.rowbf[0:1, 0:QT],
                                                             s.rowbuf[0:1, 0:QT], ALU.mult),
                         inc="dve")
            p.emit("tensor", lambda e: e.matmul(s.ps_bc[:, 0:QT], s.cst["c_ones1"][0:1, :],
                                                s.rowbf[0:1, 0:QT], start=True, stop=True),
                   waits=[("dve", mbb)])
            mab = p.emit("tensor", lambda e: e.matmul(s.ps_bc2.ap()[:, 0:QT], s.cst["c_ones1"][0:1, :],
                                                      s.rowbf[0:1, QT:2 * QT], start=True, stop=True),
                         inc="pe")
            for oc in range(NCH):
                xb = s.x1buf[oc % 2]
                w2 = [("pe", mab)]
                if mdma_prev.get(oc % 2):
                    w2.append(("q_sta" if oc % 2 == 0 else "q_stb", mdma_prev[oc % 2]))
                p.emit("vector",
                       lambda e, oc=oc: e.tensor_tensor(
                           s.x1tmp, s.res2buf[:, oc * QT:(oc + 1) * QT], s.ps_bc[:, 0:QT],
                           ALU.mult),
                       waits=w2, inc="dve")
                mx1 = p.emit("vector",
                             lambda e, xb=xb: e.tensor_sub(xb, s.x1tmp, s.ps_bc2.ap()[:, 0:QT]),
                             inc="dve")
                mdma_prev[oc % 2] = p.dma("sync",
                                          io["x3T"][oc * 128:(oc + 1) * 128, qt * QT:(qt + 1) * QT],
                                          xb, waits=[("dve", mx1)],
                                          sem="q_sta" if oc % 2 == 0 else "q_stb")

    def _ln_stats(self, resb, qt, eps=1e-5):
        p, s = self.p, self
        guard_pe2 = p.get("pe")
        guard_dve3 = p.get("dve")
        for oc in range(NCH):
            w = [("dve", guard_dve3), ("pe", guard_pe2)] if oc == 0 else []
            p.emit("tensor",
                   lambda e, oc=oc, resb=resb: e.matmul(
                       s.ps_bc.ap()[0:1, 0:QT], s.cst["c_onescol"][:, 0:1],
                       resb[:, oc * QT:(oc + 1) * QT],
                       start=(oc == 0), stop=(oc == NCH - 1)),
                   waits=w, inc="pe" if oc == NCH - 1 else None)
        sq_free = {}
        for oc in range(NCH):
            sp = s.sqscr[oc % 2]
            w = [("dve", guard_dve3)] if oc == 0 else []
            if sq_free.get(oc % 2):
                w = list(w) + [("pe", sq_free[oc % 2])]
            mact2 = p.emit("scalar",
                           lambda e, oc=oc, sp=sp, resb=resb: e.activation(
                               sp, resb[:, oc * QT:(oc + 1) * QT], AF.Square),
                           waits=w, inc="act")
            mpe_sq = p.emit("tensor",
                            lambda e, oc=oc, sp=sp: e.matmul(
                                s.ps_bc2.ap()[0:1, 0:QT], s.cst["c_onescol"][:, 0:1], sp,
                                start=(oc == 0), stop=(oc == NCH - 1)),
                            waits=[("act", mact2)], inc="pe")
            sq_free[oc % 2] = mpe_sq
        mpe_stats = p.get("pe")
        p.emit("vector", lambda e: e.tensor_scalar_mul(s.rowbuf[0:1, 0:QT],
                                                       s.ps_bc.ap()[0:1, 0:QT], 1.0 / 1024),
               waits=[("pe", mpe_stats)], inc="dve")
        p.emit("vector", lambda e: e.tensor_scalar_mul(s.rowbuf[0:1, QT:2 * QT],
                                                       s.ps_bc2.ap()[0:1, 0:QT], 1.0 / 1024),
               inc="dve")
        p.emit("vector", lambda e: e.tensor_tensor(s.rowbuf[0:1, 2 * QT:3 * QT],
                                                   s.rowbuf[0:1, 0:QT], s.rowbuf[0:1, 0:QT],
                                                   ALU.mult), inc="dve")
        p.emit("vector", lambda e: e.tensor_scalar_add(s.rowslotE,
                                                       s.rowbuf[0:1, 2 * QT:3 * QT], -eps),
               inc="dve")
        mvar = p.emit("vector", lambda e: e.tensor_sub(s.rowbuf[0:1, 3 * QT:4 * QT],
                                                       s.rowbuf[0:1, QT:2 * QT],
                                                       s.rowslotE), inc="dve")
        mln2 = p.emit("scalar", lambda e: e.activation(s.rowslot4,
                                                       s.rowbuf[0:1, 3 * QT:4 * QT], AF.Ln),
                      waits=[("dve", mvar)], inc="act")
        p.emit("scalar", lambda e: e.activation(s.rowbf[0:1, 0:QT],
                                                s.rowslot4, AF.Exp, scale=-0.5),
               waits=[("act", mln2)], inc="act")
        p.emit("vector", lambda e: e.engine_nop(), waits=[("act", p.get("act"))], inc="dve")

    def run(self):
        p, s, nc = self.p, self, self.nc
        s.alloc()
        s.load_consts()
        self.prep()
        self.build_vhat()
        adram = nc.dram_tensor("b2_attnT", [1024, T], BF16, kind="Internal").ap()
        mat_prev = None
        for hp in range(8):
            self.pair_qk(hp, mat_prev)
            for half in range(2):
                mat_prev = self.head(hp, half, adram, mat_prev)
        self.oproj_ln(adram)
        p.emit("sync", lambda e: None,
               waits=[("q_sta", p.get("q_sta")), ("q_stb", p.get("q_stb"))])


class B3:

    def __init__(self, nc, p, stack, io, core_id):
        self.nc, self.p, self.stack, self.io = nc, p, stack, io
        self.E = stack.enter_context
        self.cid = core_id

    def alloc(self):
        nc, E = self.nc, self.E
        s = self
        s.xbf = [E(nc.sbuf_tensor(f"b3_xbf{c}", [128, T], BF16)) for c in range(NCH)]
        s.hT = E(nc.sbuf_tensor("b3_hT", [128, 16 * 1024], BF16))
        s.xtok = E(nc.sbuf_tensor("b3_xtok", [128, 2048], BF16))
        s.wst = [E(nc.sbuf_tensor(f"b3_wst{i}", [128, 4096], BF16)) for i in range(2)]
        s.gl = E(nc.sbuf_tensor("b3_gl", [16, T], F32))
        s.gltb = E(nc.sbuf_tensor("b3_gltb", [128, QT], F32))
        s.etb = [E(nc.sbuf_tensor(f"b3_etb{i}", [128, QT], F32)) for i in range(2)]
        s.esm = E(nc.sbuf_tensor("b3_esm", [128, 16 * 32], F32))
        s.idxu = E(nc.sbuf_tensor("b3_idxu", [128, 8 * 32], U32))
        s.tokid = E(nc.sbuf_tensor("b3_tokid", [128, 32], F32))
        s.offmask = E(nc.sbuf_tensor("b3_offmask", [128, 16], F32))
        s.iotatb = E(nc.sbuf_tensor("b3_iotatb", [128, QT], F32))
        s.big16 = E(nc.sbuf_tensor("b3_big16", [128, QT], F32))
        s.wgf = E(nc.sbuf_tensor("b3_wgf", [128, 8 * 16], F32))
        s.ctriF = E(nc.sbuf_tensor("b3_ctriF", [128, 128], F32))
        s.conescolF = E(nc.sbuf_tensor("b3_onescolF", [128, 1], F32))
        s.cidentF = E(nc.sbuf_tensor("b3_cidentF", [128, 128], F32))
        s.pt16 = E(nc.sbuf_tensor("b3_pt16", [16, 96], F32))
        s.onesrowF = E(nc.sbuf_tensor("b3_ones1F", [1, 128], F32))
        s.invrow = E(nc.sbuf_tensor("b3_invrow", [128, 2], U32))
        s.prow = E(nc.sbuf_tensor("b3_prow", [1, QT], F32))
        s.cnt = E(nc.sbuf_tensor("b3_cnt", [16, 1], F32))
        s.cntall = E(nc.sbuf_tensor("b3_cntall", [128, 1], F32))
        s.offs = E(nc.sbuf_tensor("b3_offs", [16, 1], F32))
        s.invf = E(nc.sbuf_tensor("b3_invf", [128, 16], F32))
        s.invu = E(nc.sbuf_tensor("b3_invu", [128, 16], U32))
        s.zero = E(nc.sbuf_tensor("b3_zero", [128, 144], F32))
        s.ptrash = E(nc.sbuf_tensor("b3_ptrash", [128, 32], F32))
        s.qgrid = E(nc.sbuf_tensor("b3_qgrid", [128, 16], F32))
        s.qgrid2 = E(nc.sbuf_tensor("b3_qgrid2", [128, 16], F32))
        s.gout = E(nc.sbuf_tensor("b3_gout", [128, 1024], BF16))
        s.sdump = E(nc.sbuf_tensor("b3_sdump", [128, 1024], BF16))
        s.resf = E(nc.sbuf_tensor("b3_resf", [128, 1024], F32))
        s.outf = [E(nc.sbuf_tensor(f"b3_outf{i}", [128, 1024], F32)) for i in range(2)]
        s.otok = [E(nc.sbuf_tensor(f"b3_otok{i}", [128, 1024], BF16)) for i in range(2)]
        s.lnm = E(nc.sbuf_tensor("b3_lnm", [128, 8], F32))
        s.cst = {}
        s.ident = E(nc.sbuf_tensor("b3_ident", [128, 128], BF16))
        s.ps = [E(nc.psum_tensor(f"b3_ps{i}", [128, 512], F32)) for i in range(6)]
        s.ps_r = E(nc.psum_tensor("b3_ps_r", [128, 512], F32))
        s.ps_r2 = E(nc.psum_tensor("b3_ps_r2", [128, 512], F32))
        s.ps_tr = [s.ps[i].ap().bitcast(BF16) for i in range(4)]

    def load_consts(self):
        p, s, io = self.p, self, self.io
        for c in range(NCH):
            p.dma("gpsimd", s.xbf[c][:], io["x3T"][c * 128:(c + 1) * 128, :], sem="q_ig")
        p.dma("gpsimd", s.ident[:], io["c_ident"], sem="q_ig")
        p.dma("sync", s.tokid[:], io["c_tokid"], sem="q_is")
        p.dma("sync", s.offmask[:], io["c_offmask"], sem="q_is")
        p.dma("sync", s.iotatb[:], io["c_iotatb"], sem="q_is")
        p.dma("sync", s.ctriF[:], io["c_tri"], sem="q_is")
        p.dma("sync", s.conescolF[:], io["c_onescol"], sem="q_is")
        p.dma("sync", s.cidentF[:], io["c_ident"], sem="q_is")
        for c in range(NCH):
            p.dma("sync", s.wgf[:, c * 16:(c + 1) * 16], io["Wg"][c * 128:(c + 1) * 128, :], sem="q_is")
        p.dma("sync", s.onesrowF[:], io["c_ones1"], sem="q_is")
        p.dma("sync", s.invrow[:], io["c_invrow"], sem="q_is")
        p.dma("sync", s.ptrash[:], io["c_ptrash"], sem="q_is")
        p.dma("sync", s.qgrid[:], io["c_qgrid"], sem="q_is")
        p.dma("sync", s.qgrid2[:], io["c_qgrid2"], sem="q_is")
        s.m_ig = p.get("q_ig")
        s.m_is = p.get("q_is")
        p.emit("vector", lambda e: e.memset(s.big16[:], 16.0))
        p.emit("vector", lambda e: e.memset(s.zero[:], 0.0))
        s.m_memset = p.get("dve")

    def routing(self):
        p, s, nc, io = self.p, self, self.nc, self.io
        sm = lambda i: s.esm[:, i * 32:(i + 1) * 32]
        sm3 = lambda i: s.esm[:, i * 32:(i + 1) * 32].rearrange("p (c o) -> p c o", o=1)
        bc = lambda i: sm3(i).broadcast_to([128, 32, 16])
        s.agx_in = nc.dram_tensor("b3_agx_in", [T, 1024], BF16, kind="Internal")
        s.agx_out = nc.dram_tensor("b3_agx_out", [8 * T, 1024], BF16, kind="Internal",
                                   addr_space="Shared")
        mstore_prev = {}
        for tt in range(32):
            for c in range(NCH):
                g = c % 4
                w = [("q_ig", s.m_ig)]
                if c >= 4:
                    w.append(s._xtev[(tt, c - 4)])
                elif tt > 0:
                    w.append(s._xtev[(tt - 1, c + 4)])
                mt = p.emit("tensor",
                            lambda e, tt=tt, c=c, g=g: e.transpose(
                                s.ps_tr[g][:, 0:128], s.xbf[c][:, tt * 128:(tt + 1) * 128],
                                s.ident[:]),
                            waits=w, inc="pe")
                if not hasattr(s, "_xtev"):
                    s._xtev = {}
                vw = [("pe", mt)]
                if tt > 0 and c <= 1:
                    par = tt % 2
                    if mstore_prev.get(par):
                        vw.append((("q_xta" if par == 0 else "q_xtb"), mstore_prev[par]))
                dst_lo = (tt % 2) * 1024 + c * 128
                if c % 2 == 0:
                    mcp = p.emit("vector",
                                 lambda e, dst_lo=dst_lo, g=g: e.tensor_copy(
                                     s.xtok[:, dst_lo:dst_lo + 128], s.ps_tr[g][:, 0:128]),
                                 waits=vw, inc="dve")
                    s._xtev[(tt, c)] = ("dve", mcp)
                else:
                    mcp = p.emit("scalar",
                                 lambda e, dst_lo=dst_lo, g=g: e.activation(
                                     s.xtok[:, dst_lo:dst_lo + 128], s.ps_tr[g][:, 0:128],
                                     AF.Copy),
                                 waits=vw, inc="act")
                    s._xtev[(tt, c)] = ("act", mcp)
            par = tt % 2
            mstore_prev[par] = p.dma("sync", s.agx_in.ap()[tt * 128:(tt + 1) * 128, :],
                                     s.xtok[:, par * 1024:(par + 1) * 1024],
                                     waits=[s._xtev[(tt, NCH - 2)], s._xtev[(tt, NCH - 1)]],
                                     sem="q_xta" if par == 0 else "q_xtb")
        m_xtok = (mstore_prev[0], mstore_prev[1])
        mag2 = p.emit("gpsimd",
                      lambda e: e.collective_compute(
                          "AllGather", ALU.bypass, replica_groups=[list(range(8))],
                          ins=[s.agx_in.ap().opt()], outs=[s.agx_out.ap().opt()]),
                      waits=[("q_xta", m_xtok[0]), ("q_xtb", m_xtok[1])], inc="cc2", is_dma=True)
        xf = s.hT[:].bitcast(F32)[:, 0:4096]
        mgl_prev = None
        for qt in range(NQT):
            mld = p.dma("sync", xf.rearrange("p (c t) -> p c t", c=NCH),
                        io["x3T"].rearrange("(c p) t -> p c t", p=128)[:, :, qt * QT:(qt + 1) * QT],
                        sem="q_xf", waits=[("pe", mgl_prev)] if mgl_prev else [])
            for c in range(NCH):
                w = [("q_xf", mld), ("q_is", s.m_is)] if c == 0 else []
                if c == 0 and getattr(s, "_mglv", None):
                    w.append(("dve", s._mglv))
                mgl = p.emit("tensor",
                             lambda e, qt=qt, c=c: e.matmul(
                                 s.ps_r.ap()[0:16, 0:512], s.wgf[:, c * 16:(c + 1) * 16],
                                 xf[:, c * 512:(c + 1) * 512],
                                 start=(c == 0), stop=(c == NCH - 1)),
                             waits=w, inc="pe" if c == NCH - 1 else None)
            mgl_prev = mgl
            s._mglv = p.emit("vector",
                             lambda e, qt=qt: e.tensor_copy(s.gl[0:16, qt * QT:(qt + 1) * QT],
                                                            s.ps_r.ap()[0:16, 0:512]),
                             waits=[("pe", mgl)], inc="dve")
        mgt = {}
        for tt in range(32):
            g = tt % 4
            w = [("dve", p.get("dve"))] if tt < 4 else [("dve", mgt[tt - 4])]
            mt = p.emit("tensor",
                        lambda e, tt=tt, g=g: e.transpose(
                            s.ps[g].ap()[:, 0:16], s.gl[0:16, tt * 128:(tt + 1) * 128],
                            s.cidentF[0:16, 0:16]),
                        waits=w, inc="pe")
            mgt[tt] = p.emit("vector",
                             lambda e, tt=tt, g=g: e.tensor_copy(
                                 s.gltb[:, tt * 16:(tt + 1) * 16], s.ps[g].ap()[:, 0:16]),
                             waits=[("pe", mt)], inc="dve")
        gl3 = s.gltb[:].rearrange("p (c e) -> p c e", e=16)
        t0 = s.etb[0][:].rearrange("p (c e) -> p c e", e=16)
        t1 = s.etb[1][:].rearrange("p (c e) -> p c e", e=16)
        iota3 = s.iotatb[:].rearrange("p (c e) -> p c e", e=16)
        p.emit("vector", lambda e: e.tensor_reduce(sm3(0), gl3, AX.X, ALU.max), inc="dve")
        p.emit("vector", lambda e: e.tensor_tensor(t0, gl3, bc(0), ALU.subtract), inc="dve")
        mex = p.emit("scalar", lambda e: e.activation(s.etb[1][:], s.etb[0][:], AF.Exp),
                     waits=[("dve", p.get("dve"))], inc="act")
        p.emit("vector", lambda e: e.tensor_reduce(sm3(1), t1, AX.X, ALU.add),
               waits=[("act", mex)], inc="dve")
        p.emit("vector", lambda e: e.reciprocal(sm(2), sm(1)), inc="dve")
        p.emit("vector", lambda e: e.tensor_tensor(t0, gl3, bc(0), ALU.is_ge), inc="dve")
        p.emit("vector", lambda e: e.tensor_scalar_add(s.gltb[:], s.iotatb[:], -16.0),
               waits=[("q_is", s.m_is)], inc="dve")
        p.emit("vector", lambda e: e.tensor_tensor(t1, gl3, t0, ALU.mult), inc="dve")
        p.emit("vector", lambda e: e.tensor_reduce(sm3(13), t1, AX.X, ALU.min), inc="dve")
        p.emit("vector", lambda e: e.tensor_scalar_add(sm(3), sm(13), 16.0), inc="dve")
        moh = p.emit("vector", lambda e: e.tensor_tensor(t0, iota3, bc(3), ALU.is_equal), inc="dve")
        for tt in range(32):
            w = [("dve", moh)] if tt == 0 else []
            if tt >= 2:
                w.append(("dve", s._cumev[tt - 2]))
            p.emit("tensor",
                   lambda e, tt=tt: e.matmul(
                       s.ps[4 + tt % 2].ap()[:, 0:16], s.ctriF[:],
                       t0[:, tt, :], start=True, stop=True),
                   waits=w, inc=None)
            mcc = p.emit("tensor",
                         lambda e, tt=tt: e.matmul(
                             s.ps_r2.ap()[0:16, tt:tt + 1], t0[:, tt, :], s.conescolF[:, 0:1],
                             start=True, stop=True),
                         inc="pe")
            if not hasattr(s, "_cumev"):
                s._cumev = {}
            s._cumev[tt] = p.emit("vector",
                                  lambda e, tt=tt: e.tensor_copy(
                                      s.etb[1][:].rearrange("p (c e) -> p c e", e=16)[:, tt, :],
                                      s.ps[4 + tt % 2].ap()[:, 0:16]),
                                  waits=[("pe", mcc)], inc="dve")
        m_cum = p.get("dve")
        mtt = p.emit("vector", lambda e: e.tensor_copy(s.pt16[0:16, 0:32], s.ps_r2.ap()[0:16, 0:32]),
                     waits=[("pe", p.get("pe"))], inc="dve")
        p.emit("vector", lambda e: e.tensor_reduce(s.cnt[0:16, 0:1], s.pt16[0:16, 0:32],
                                                   AX.X, ALU.add),
               inc="dve")
        p.emit("vector", lambda e: e.tensor_tensor_scan(s.pt16[0:16, 32:64], s.pt16[0:16, 0:32],
                                                        s.pt16[0:16, 0:32], 0.0, ALU.add, ALU.bypass),
               inc="dve")
        mpref = p.emit("vector", lambda e: e.tensor_sub(s.pt16[0:16, 64:96], s.pt16[0:16, 32:64],
                                                        s.pt16[0:16, 0:32]), inc="dve")
        s.agc_in = nc.dram_tensor("b3_agc_in", [16, 1], F32, kind="Internal")
        s.agc_out = nc.dram_tensor("b3_agc_out", [128, 1], F32, kind="Internal", addr_space="Shared")
        mcnt = p.dma("sync", s.agc_in.ap(), s.cnt[0:16, 0:1], sem="q_cn",
                     waits=[("dve", p.get("dve"))])
        mag1 = p.emit("gpsimd",
                      lambda e: e.collective_compute(
                          "AllGather", ALU.bypass, replica_groups=[list(range(8))],
                          ins=[s.agc_in.ap().opt()], outs=[s.agc_out.ap().opt()]),
                      waits=[("q_cn", mcnt), ("cc2", 1)], inc="cc1", is_dma=True)
        mld = p.dma("sync", s.cntall[:, 0:1], s.agc_out.ap(), sem="q_cn", waits=[("cc1", 1)])
        moff = p.emit("tensor",
                      lambda e: e.matmul(s.ps_r.ap()[0:16, 0:1], s.offmask[:], s.cntall[:, 0:1],
                                         start=True, stop=True),
                      waits=[("q_cn", mld), ("q_is", s.m_is)], inc="pe")
        p.emit("vector", lambda e: e.tensor_copy(s.offs[0:16, 0:1], s.ps_r.ap()[0:16, 0:1]),
               waits=[("pe", moff)], inc="dve")
        p.emit("vector", lambda e: e.tensor_tensor(
            s.pt16[0:16, 0:32].rearrange("p (o c) -> p o c", o=1),
            s.pt16[0:16, 64:96].rearrange("p (o c) -> p o c", o=1),
            s.offs[0:16, 0:1].rearrange("p (c o) -> p c o", o=1).broadcast_to([16, 1, 32]),
            ALU.add), inc="dve")
        mpoff = p.emit("vector", lambda e: e.tensor_scalar_add(s.pt16[0:16, 32:64],
                                                               s.pt16[0:16, 0:32], -1.0), inc="dve")
        s.poff_dram = nc.dram_tensor("b3_poff", [32, 16], F32, kind="Internal")
        mtp = p.emit("tensor",
                     lambda e: e.transpose(s.ps_r.ap()[0:32, 0:16], s.pt16[0:16, 32:64],
                                           s.cidentF[0:16, 0:16]),
                     waits=[("dve", mpoff)], inc="pe")
        mtv = p.emit("vector", lambda e: e.tensor_copy(s.big16[0:32, 0:16], s.ps_r.ap()[0:32, 0:16]),
                     waits=[("pe", mtp)], inc="dve")
        mst = p.dma("sync", s.poff_dram.ap(), s.big16[0:32, 0:16], sem="q_cn", waits=[("dve", mtv)])
        mld2 = p.dma("sync", s.prow[0:1, :], s.poff_dram.ap().rearrange("(o a) b -> o (a b)", o=1),
                     sem="q_cn", waits=[("q_cn", mst)])
        mbc = p.emit("tensor",
                     lambda e: e.matmul(s.ps_r.ap()[:, 0:512], s.onesrowF[0:1, :], s.prow[0:1, :],
                                        start=True, stop=True),
                     waits=[("q_cn", mld2), ("q_is", s.m_is)], inc="pe")
        p.emit("vector", lambda e: e.tensor_tensor(
            s.gltb[:], s.etb[1][:], s.ps_r.ap()[:, 0:512], ALU.add),
               waits=[("pe", mbc), ("dve", m_cum)], inc="dve")
        p.emit("vector", lambda e: e.tensor_tensor(t1, gl3, t0, ALU.mult), inc="dve")
        mslot = p.emit("vector", lambda e: e.tensor_reduce(sm3(4), t1, AX.X, ALU.add), inc="dve")
        p.emit("vector", lambda e: e.tensor_single_scalar(sm(5), sm(4), 2048.0, ALU.is_lt), inc="dve")
        p.emit("vector", lambda e: e.tensor_tensor(sm(6), sm(2), sm(5), ALU.mult), inc="dve")
        grid3 = s.etb[0][:].rearrange("p (c j) -> p c j", j=16)
        p.emit("vector", lambda e: e.tensor_tensor(
            grid3, sm3(4).broadcast_to([128, 32, 16]),
            s.qgrid[:].rearrange("p (o j) -> p o j", o=1).broadcast_to([128, 32, 16]),
            ALU.is_ge), inc="dve")
        p.emit("vector", lambda e: e.tensor_reduce(sm3(7), grid3, AX.X, ALU.add), inc="dve")
        p.emit("vector", lambda e: e.tensor_tensor(
            grid3, sm3(4).broadcast_to([128, 32, 16]),
            s.qgrid[:].rearrange("p (o j) -> p o j", o=1).broadcast_to([128, 32, 16])
            if False else
            s.qgrid2[:].rearrange("p (o j) -> p o j", o=1).broadcast_to([128, 32, 16]),
            ALU.is_ge), inc="dve")
        p.emit("vector", lambda e: e.tensor_reduce(sm3(8), grid3, AX.X, ALU.add), inc="dve")
        p.emit("vector", lambda e: e.tensor_tensor(sm(9), sm(7), sm(8), ALU.add), inc="dve")
        p.emit("vector", lambda e: e.scalar_tensor_tensor(sm(7), sm(9), -128.0, sm(4),
                                                          ALU.mult, ALU.add), inc="dve")
        p.emit("vector", lambda e: e.scalar_tensor_tensor(sm(10), sm(7), 16.0, sm(9),
                                                          ALU.mult, ALU.add), inc="dve")
        p.emit("vector", lambda e: e.scalar_tensor_tensor(sm(11), sm(3), 2048.0, sm(10),
                                                          ALU.mult, ALU.add), inc="dve")
        p.emit("vector", lambda e: e.tensor_tensor(sm(13), sm(11), s.ptrash[:], ALU.subtract),
               inc="dve")
        p.emit("vector", lambda e: e.tensor_tensor(sm(8), sm(13), sm(5), ALU.mult), inc="dve")
        p.emit("vector", lambda e: e.tensor_tensor(sm(12), sm(8), s.ptrash[:], ALU.add), inc="dve")
        mdu = p.emit("vector", lambda e: e.tensor_copy(s.idxu[:, 0:32], sm(12)), inc="dve")
        p.emit("vector", lambda e: e.scalar_tensor_tensor(sm(14), sm(3), 2048.0, sm(4),
                                                          ALU.mult, ALU.add), inc="dve")
        p.emit("vector", lambda e: e.tensor_tensor(sm(15), sm(14), sm(5), ALU.mult), inc="dve")
        mgu = p.emit("vector", lambda e: e.tensor_copy(s.idxu[:, 32:64], sm(15)), inc="dve")
        s.inv_dram = nc.dram_tensor("b3_inv", [36864, 1], F32, kind="Internal")
        s.inv_out = nc.dram_tensor("b3_inv_out", [36864, 1], F32, kind="Internal",
                                   addr_space="Shared")
        mz = None
        for i in range(2):
            mz = p.dma("sync", s.inv_dram.ap().rearrange("(a p f) o -> a p (f o)", a=2, p=128)[i],
                       s.zero[:, 0:144], sem="q_zf", waits=[("dve", s.m_memset)])
        msc = None
        for tt in range(32):
            w = ([("dve", mdu), ("q_zf", p.get("q_zf")), ("q_is", s.m_is)]
                 if tt == 0 else [("q_sc", msc)])
            msc = p.emit("gpsimd",
                         lambda e, tt=tt: e.indirect_dma_start(
                             out=s.inv_dram.ap(),
                             out_offset=IndirectOffsetOnAxis(s.idxu[:, tt:tt + 1], 0),
                             in_=s.tokid[:, tt:tt + 1], in_offset=None),
                         waits=w, inc="q_sc", inc_n=16, is_dma=True)
        mar = p.emit("gpsimd",
                     lambda e: e.collective_compute(
                         "AllReduce", ALU.add, replica_groups=[list(range(8))],
                         ins=[s.inv_dram.ap().opt()], outs=[s.inv_out.ap().opt()]),
                     waits=[("q_sc", p.get("q_sc")), ("cc1", 1)], inc="cc3", is_dma=True)
        s.m_routing = (mgu, mar)

    def experts(self):
        p, s, nc, io = self.p, self, self.nc, self.io
        s.w1bf = io["_wch"]["w1bf"]
        s.w2bf = io["_wch"]["w2bf"]
        s.m_wcache = io["_wch_m"]
        s.ago_in = nc.dram_tensor("b3_ago_in", [T, 1024], BF16, kind="Internal")
        s.ago_out = nc.dram_tensor("b3_ago_out", [8 * T, 1024], BF16, kind="Internal",
                                   addr_space="Shared")
        inv2d = s.inv_out.ap().rearrange("(r j) o -> r (j o)", j=16)
        dispT = lambda dc: s.xbf[4 + dc // 2][:, (dc % 2) * 2048:(dc % 2 + 1) * 2048]
        exp_guard = (None, None)
        for ex in range(2):
            mgi = p.emit("gpsimd",
                         lambda e, ex=ex: e.indirect_dma_start(
                             out=s.invf[:], out_offset=None,
                             in_=inv2d, in_offset=IndirectOffsetOnAxis(s.invrow[:, ex:ex + 1], 0)),
                         waits=[("cc3", 1), ("q_is", s.m_is)] + (
                             [("dve", exp_guard[1])] if exp_guard[1] else []),
                         inc="q_gi", inc_n=16, is_dma=True)
            mcv = p.emit("vector", lambda e: e.tensor_copy(s.invu[:], s.invf[:]),
                         waits=[("q_gi", p.get("q_gi"))], inc="dve")
            mg_miles = []
            for g in range(16):
                dst = s.xbf[g // 4][:, (g % 4) * 1024:(g % 4 + 1) * 1024]
                w = [("dve", mcv), ("cc2", 1)]
                if exp_guard[0]:
                    w.append(("pe", exp_guard[0]))
                mg_miles.append(p.emit("gpsimd",
                                       lambda e, g=g, dst=dst: e.indirect_dma_start(
                                           out=dst, out_offset=None,
                                           in_=s.agx_out.ap(),
                                           in_offset=IndirectOffsetOnAxis(s.invu[:, g:g + 1], 0)),
                                       waits=w if g == 0 else [], inc="q_gx", inc_n=16,
                                       is_dma=True))
            m_gx = mg_miles[-1]
            mtr = {}
            for g in range(16):
                for dc in range(8):
                    q = (g * 8 + dc) % 4
                    w = ([("q_gx", m_gx), ("act", p.get("act"))]
                         if (g == 0 and dc == 0) else [])
                    if mtr.get(q):
                        w.append(mtr[q])
                    mt = p.emit("tensor",
                                lambda e, g=g, dc=dc, q=q: e.transpose(
                                    s.ps_tr[q][:, 0:128],
                                    s.xbf[g // 4][:, (g % 4) * 1024 + dc * 128:
                                                  (g % 4) * 1024 + (dc + 1) * 128],
                                    s.ident[:]),
                                waits=w, inc="pe")
                    db, dlo = 4 + dc // 2, (dc % 2) * 2048 + g * 128
                    if dc % 2 == 0:
                        mtr[q] = ("dve", p.emit("vector",
                                                lambda e, db=db, dlo=dlo, q=q: e.tensor_copy(
                                                    s.xbf[db][:, dlo:dlo + 128],
                                                    s.ps_tr[q][:, 0:128]),
                                                waits=[("pe", mt)], inc="dve"))
                    else:
                        mtr[q] = ("act", p.emit("scalar",
                                                lambda e, db=db, dlo=dlo, q=q: e.activation(
                                                    s.xbf[db][:, dlo:dlo + 128],
                                                    s.ps_tr[q][:, 0:128], AF.Copy),
                                                waits=[("pe", mt)], inc="act"))
            m_dispT = (p.get("dve"), p.get("act"))
            for st in range(4):
                wld_prev = {}
                relu_prev = {}
                mpe = None
                for fc8 in range(8):
                    par = fc8 % 2
                    w = [("q_wch", s.m_wcache)]
                    if wld_prev.get(par):
                        w.append(("pe", wld_prev[par]))
                    if st > 0 or ex > 0:
                        w.append(("pe", s._w2done)) if getattr(s, "_w2done", None) else None
                    mldw = p.dma("sync", s.wst[par][:].rearrange("p (c f) -> p c f", c=8),
                                 s.w1bf.ap()[ex * 1024:(ex + 1) * 1024,
                                             fc8 * 512:(fc8 + 1) * 512]
                                 .rearrange("(c p) f -> p c f", p=128),
                                 sem="q_w1a" if par == 0 else "q_w1b", waits=w)
                    for f2 in range(4):
                        fc = fc8 * 4 + f2
                        for dc in range(8):
                            w2 = []
                            if dc == 0:
                                w2 = [(("q_w1a" if par == 0 else "q_w1b"), mldw)]
                                if fc8 == 0 and f2 == 0:
                                    w2.append(("dve", m_dispT[0]))
                                    w2.append(("act", m_dispT[1]))
                                if relu_prev.get(f2 % 2):
                                    w2.append(("act", relu_prev[f2 % 2]))
                            mpe = p.emit("tensor",
                                         lambda e, f2=f2, dc=dc, st=st, par=par: e.matmul(
                                             s.ps[4 + f2 % 2].ap()[:, 0:512],
                                             s.wst[par][:].rearrange("p (c f) -> p c f", c=8)[
                                                 :, dc, f2 * 128:(f2 + 1) * 128],
                                             dispT(dc)[:, st * 512:(st + 1) * 512],
                                             start=(dc == 0), stop=(dc == 7)),
                                         waits=w2, inc="pe" if dc == 7 else None)
                        relu_prev[f2 % 2] = p.emit("scalar",
                                                   lambda e, fc=fc, f2=f2: e.activation(
                                                       s.hT[:, fc * 512:(fc + 1) * 512],
                                                       s.ps[4 + f2 % 2].ap()[:, 0:512], AF.Relu),
                                                   waits=[("pe", mpe)], inc="act")
                    wld_prev[par] = mpe
                m_h = p.get("act")
                psof = lambda oc: (s.ps[oc] if oc < 6 else (s.ps_r if oc == 6 else s.ps_r2))
                mld2 = None
                mpo = None
                for w2g in range(8):
                    par = w2g % 2
                    w = [("act", m_h)]
                    if w2g >= 2:
                        w.append(("pe", s._w2mm[par]))
                    mld2 = p.dma("sync", s.wst[par][:].rearrange("p (c o) -> p c o", c=4),
                                 s.w2bf.ap()[ex * 4096 + w2g * 512:ex * 4096 + (w2g + 1) * 512, :]
                                 .rearrange("(c p) o -> p c o", p=128),
                                 sem="q_w2a" if par == 0 else "q_w2b", waits=w)
                    for oc in range(8):
                        for c4 in range(4):
                            fch = w2g * 4 + c4
                            w3 = []
                            if c4 == 0 and oc == 0:
                                w3 = [(("q_w2a" if par == 0 else "q_w2b"), mld2)]
                                if w2g == 0:
                                    w3.append(("dve", getattr(s, "_oev_last", None)))
                            mpo = p.emit("tensor",
                                         lambda e, oc=oc, c4=c4, fch=fch, par=par: e.matmul(
                                             psof(oc).ap()[:, 0:512],
                                             s.wst[par][:].rearrange("p (c o) -> p c o", c=4)[
                                                 :, c4, oc * 128:(oc + 1) * 128],
                                             s.hT[:, fch * 512:(fch + 1) * 512],
                                             start=(fch == 0), stop=(fch == 31)),
                                         waits=[x for x in w3 if x],
                                         inc="pe" if (oc == 7 and c4 == 3) else None)
                    if not hasattr(s, "_w2mm"):
                        s._w2mm = {}
                    s._w2mm[par] = mpo
                s._w2done = mpo
                oev = None
                for oc in range(8):
                    oev = p.emit("vector",
                                 lambda e, oc=oc: e.tensor_copy(
                                     s.xbf[0][:, oc * 512:(oc + 1) * 512], psof(oc).ap()[:, 0:512]),
                                 waits=[("pe", s._w2done)] if oc == 0 else [], inc="dve")
                s._oev_last = oev
                mot = getattr(s, "_mot", None)
                for tl in range(4):
                    for oc in range(8):
                        q = (tl * 8 + oc) % 4
                        w = [("dve", oev)] if (tl == 0 and oc == 0) else []
                        if mtr.get(("o", q)):
                            w.append(mtr[("o", q)])
                        mt = p.emit("tensor",
                                    lambda e, tl=tl, oc=oc, q=q: e.transpose(
                                        s.ps_tr[q][:, 0:128],
                                        s.xbf[0][:, oc * 512 + tl * 128:oc * 512 + (tl + 1) * 128],
                                        s.ident[:]),
                                    waits=w, inc="pe")
                        cw = [("pe", mt)]
                        if oc <= 1 and mot:
                            cw.append(("q_oa", mot))
                        if oc % 2 == 0:
                            mtr[("o", q)] = ("dve", p.emit("vector",
                                                           lambda e, oc=oc, q=q: e.tensor_copy(
                                                               s.gout[:, oc * 128:(oc + 1) * 128],
                                                               s.ps_tr[q][:, 0:128]),
                                                           waits=cw, inc="dve"))
                        else:
                            mtr[("o", q)] = ("act", p.emit("scalar",
                                                           lambda e, oc=oc, q=q: e.activation(
                                                               s.gout[:, oc * 128:(oc + 1) * 128],
                                                               s.ps_tr[q][:, 0:128], AF.Copy),
                                                           waits=cw, inc="act"))
                    mot = p.dma("sync",
                                s.ago_in.ap()[ex * 2048 + st * 512 + tl * 128:
                                              ex * 2048 + st * 512 + (tl + 1) * 128, :],
                                s.gout[:], sem="q_oa",
                                waits=[("dve", p.get("dve")), ("act", p.get("act"))])
                    s._mot = mot
            exp_guard = (p.get("pe"), p.get("dve"))
        s.m_ago = p.get("q_oa")
        mag3 = p.emit("gpsimd",
                      lambda e: e.collective_compute(
                          "AllGather", ALU.bypass, replica_groups=[list(range(8))],
                          ins=[s.ago_in.ap().opt()], outs=[s.ago_out.ap().opt()]),
                      waits=[("q_oa", s.m_ago), ("cc3", 1)], inc="cc4", is_dma=True)

    def final(self):
        p, s, io = self.p, self, self.io
        gout2 = s.wst[1][:, 0:1024]
        resf2 = s.wst[0][:].bitcast(F32)[:, 0:1024]
        mres_prev = {}
        mact_prev = {}
        mo_prev = {}
        eps = 1e-5
        for tt in range(32):
            par = tt % 2
            gb = s.gout[:] if par == 0 else gout2
            rb = s.resf[:] if par == 0 else resf2
            xt = s.xtok[:, par * 1024:(par + 1) * 1024]
            Lt = s.lnm if par == 0 else s.invf
            L = lambda i: Lt[:, i:i + 1]
            w = ([("cc4", 1), ("dve", p.get("dve")), ("act", p.get("act"))]
                 if tt == 0 else [])
            if mres_prev.get(par):
                w.append(("dve", mres_prev[par]))
            qgb = "q_gb" if par == 0 else "q_gb2"
            qxr = "q_xr" if par == 0 else "q_xr2"
            mgr = p.emit("gpsimd",
                         lambda e, tt=tt, gb=gb: e.indirect_dma_start(
                             out=gb, out_offset=None,
                             in_=s.ago_out.ap(),
                             in_offset=IndirectOffsetOnAxis(s.idxu[:, 32 + tt:33 + tt], 0)),
                         waits=w, inc=qgb, inc_n=16, is_dma=True)
            mxl = p.dma("gpsimd", xt, s.agx_in.ap()[tt * 128:(tt + 1) * 128, :],
                        sem=qxr, waits=[("q_xta", p.get("q_xta")), ("q_xtb", p.get("q_xtb"))]
                        if tt == 0 else [])
            rw = [(qgb, p.get(qgb)), (qxr, p.get(qxr))]
            if mact_prev.get(par):
                rw.append(("act", mact_prev[par]))
            mres = p.emit("vector",
                          lambda e, tt=tt, gb=gb, rb=rb, xt=xt: e.scalar_tensor_tensor(
                              rb, gb, s.esm[:, 6 * 32 + tt:6 * 32 + tt + 1],
                              xt, ALU.mult, ALU.add),
                          waits=rw, inc="dve")
            mres_prev[par] = mres
            p.emit("scalar", lambda e, rb=rb, L=L: e.activation(s.sdump[:], rb, AF.Square,
                                                                accum_out=L(2)),
                   waits=[("dve", mres)], inc="act")
            mcp = p.emit("scalar", lambda e, rb=rb, L=L: e.activation(s.sdump[:], rb, AF.Copy,
                                                                      accum_out=L(0)),
                         inc="act")
            mact_prev[par] = mcp
            p.emit("scalar", lambda e, L=L: e.activation(L(1), L(0), AF.Copy, scale=1.0 / 1024),
                   inc="act")
            mmsq = p.emit("scalar", lambda e, L=L: e.activation(L(3), L(2), AF.Copy,
                                                                scale=1.0 / 1024, bias=eps),
                          inc="act")
            p.emit("vector", lambda e, L=L: e.tensor_tensor(L(4), L(1), L(1), ALU.mult),
                   waits=[("act", mmsq)], inc="dve")
            mvar = p.emit("vector", lambda e, L=L: e.tensor_sub(L(5), L(3), L(4)), inc="dve")
            p.emit("scalar", lambda e, L=L: e.activation(L(6), L(5), AF.Ln),
                   waits=[("dve", mvar)], inc="act")
            mrs = p.emit("scalar", lambda e, L=L: e.activation(L(7), L(6), AF.Exp, scale=-0.5),
                         inc="act")
            ob = s.outf[par]
            w4 = [("act", mrs)]
            if mo_prev.get(par):
                w4.append((("q_fa" if par == 0 else "q_fb"), mo_prev[par]))
            mfin = p.emit("vector",
                          lambda e, ob=ob, rb=rb, L=L: e.tensor_scalar(
                              ob[:], rb, L(1), L(7), ALU.subtract, ALU.mult),
                          waits=w4, inc="dve")
            mo_prev[par] = p.dma("sync", io["outTOK"][tt * 128:(tt + 1) * 128, :], ob[:],
                                 waits=[("dve", mfin)], sem="q_fa" if par == 0 else "q_fb")
        p.emit("sync", lambda e: None,
               waits=[("q_fa", p.get("q_fa")), ("q_fb", p.get("q_fb"))])

    def run(self):
        p, s = self.p, self
        s.alloc()
        s.load_consts()
        self.routing()
        self.experts()
        self.final()


from concourse.bass_utils import run_bass_kernel_spmd

_NCACHE = {}


def _build_fused():
    nc = bass.Bass("TRN2", target_bir_lowering=False)
    io = {}
    io["xT"] = nc.dram_tensor("xT", [D, T], F32, kind="ExternalInput").ap()
    for wn in ["wq", "wk", "wv", "wo"]:
        io[wn] = nc.dram_tensor(wn, [D, D], F32, kind="ExternalInput").ap()
    for bn in ["bq", "bk", "bv", "bo"]:
        io[bn] = nc.dram_tensor(bn, [D], F32, kind="ExternalInput").ap()
    for cn in CONST_SPECS:
        io[cn] = nc.dram_tensor(cn, list(CONST_SPECS[cn]), F32, kind="ExternalInput").ap()
    io["proj_f"] = nc.dram_tensor("proj_f", [D, 16], F32, kind="ExternalInput").ap()
    io["C1"] = nc.dram_tensor("C1", [16, 512], F32, kind="ExternalInput").ap()
    io["C2"] = nc.dram_tensor("C2", [16, 512], F32, kind="ExternalInput").ap()
    io["Win1"] = nc.dram_tensor("Win1", [1025, 512], F32, kind="ExternalInput").ap()
    io["Win2"] = nc.dram_tensor("Win2", [4097, 512], F32, kind="ExternalInput").ap()
    io["Wout1"] = nc.dram_tensor("Wout1", [4096, 512], F32, kind="ExternalInput").ap()
    io["Wout2"] = nc.dram_tensor("Wout2", [1024, 512], F32, kind="ExternalInput").ap()
    for wn in ["q_w", "k_w", "v_w", "out_w"]:
        io[wn] = nc.dram_tensor(wn, [D, D], F32, kind="ExternalInput").ap()
    for bn in ["q_b", "k_b", "v_b", "out_b"]:
        io[bn] = nc.dram_tensor(bn, [D], F32, kind="ExternalInput").ap()
    io["proj"] = nc.dram_tensor("proj2", [256, 64], F32, kind="ExternalInput").ap()
    io["gamma"] = nc.dram_tensor("gamma", [D], F32, kind="ExternalInput").ap()
    io["beta"] = nc.dram_tensor("beta", [D], F32, kind="ExternalInput").ap()
    io["Wg"] = nc.dram_tensor("Wg", [D, 16], F32, kind="ExternalInput").ap()
    io["w1my"] = nc.dram_tensor("w1my", [2, 1024, 4096], F32, kind="ExternalInput").ap()
    io["w2my"] = nc.dram_tensor("w2my", [2, 4096, 1024], F32, kind="ExternalInput").ap()
    io["c_tokid"] = nc.dram_tensor("c_tokid", [128, 32], F32, kind="ExternalInput").ap()
    io["c_offmask"] = nc.dram_tensor("c_offmask", [128, 16], F32, kind="ExternalInput").ap()
    io["c_iotatb"] = nc.dram_tensor("c_iotatb", [128, 512], F32, kind="ExternalInput").ap()
    io["c_invrow"] = nc.dram_tensor("c_invrow", [128, 2], U32, kind="ExternalInput").ap()
    io["c_ptrash"] = nc.dram_tensor("c_ptrash", [128, 32], F32, kind="ExternalInput").ap()
    io["c_qgrid"] = nc.dram_tensor("c_qgrid", [128, 16], F32, kind="ExternalInput").ap()
    io["c_qgrid2"] = nc.dram_tensor("c_qgrid2", [128, 16], F32, kind="ExternalInput").ap()
    io["outTOK"] = nc.dram_tensor("outTOK", [T, D], F32, kind="ExternalOutput").ap()
    io["x1T"] = nc.dram_tensor("h_x1T", [D, T], F32, kind="Internal").ap()
    io["x2T"] = nc.dram_tensor("h_x2T", [D, T], F32, kind="Internal").ap()
    io["x3T"] = nc.dram_tensor("h_x3T", [D, T], F32, kind="Internal").ap()

    p = Prog(nc)
    with contextlib.ExitStack() as outer:
        with contextlib.ExitStack() as st0:
            b0 = B0(nc, p, st0, io)
            b0.run()
        emit_early_wcaches(nc, p, io, gates=[("q_wsl", m) for m in b0.wsl_miles])
        p.barrier()
        io1 = dict(io)
        io1["proj"] = io["proj_f"]
        with contextlib.ExitStack() as st1:
            B1(nc, p, st1, io1).run()
        p.barrier()
        io2 = dict(io)
        with contextlib.ExitStack() as st2:
            B2(nc, p, st2, io2).run()
        p.barrier()
        io3 = dict(io)
        with contextlib.ExitStack() as st3:
            B3(nc, p, st3, io3, 0).run()
        p.build(outer)
    return nc


def _get(name, builder):
    if name not in _NCACHE:
        _NCACHE[name] = builder()
    return _NCACHE[name]


def kernel(**ins):
    consts = host_constants()
    p_ = np.arange(128, dtype=np.float32)[:, None]
    x = np.asarray(ins["L0_attn_x"])
    nc = _get("fused", _build_fused)
    maps = []
    for c in range(8):
        m = dict(
            xT=np.ascontiguousarray(x[c].T),
            wq=ins["L0_attn_Wq"], wk=ins["L0_attn_Wk"], wv=ins["L0_attn_Wv"],
            wo=ins["L0_attn_Wo"],
            bq=ins["L0_attn_bq"], bk=ins["L0_attn_bk"], bv=ins["L0_attn_bv"],
            bo=ins["L0_attn_bo"],
            proj_f=ins["L0_ffn_proj"], C1=ins["L0_ffn_C1"], C2=ins["L0_ffn_C2"],
            Win1=ins["L0_ffn_Win1"], Win2=ins["L0_ffn_Win2"],
            Wout1=ins["L0_ffn_Wout1"], Wout2=ins["L0_ffn_Wout2"],
            q_w=ins["L1_attn_q_w"], k_w=ins["L1_attn_k_w"], v_w=ins["L1_attn_v_w"],
            out_w=ins["L1_attn_out_w"],
            q_b=ins["L1_attn_q_b"], k_b=ins["L1_attn_k_b"], v_b=ins["L1_attn_v_b"],
            out_b=ins["L1_attn_out_b"],
            proj2=ins["L1_attn_proj"], gamma=ins["L1_attn_gamma"], beta=ins["L1_attn_beta"],
            Wg=ins["L1_ffn_Wg"],
            w1my=np.ascontiguousarray(np.asarray(ins["L1_ffn_w1"])[2 * c:2 * c + 2]),
            w2my=np.ascontiguousarray(np.asarray(ins["L1_ffn_w2"])[2 * c:2 * c + 2]),
            c_tokid=(4096.0 * c + 128.0 * np.arange(32, dtype=np.float32)[None, :]
                     + p_).astype(np.float32),
            c_offmask=np.zeros((128, 16), dtype=np.float32),
            c_iotatb=np.tile(np.arange(16, dtype=np.float32)[None, :], (128, 32)).reshape(128, 512),
            c_invrow=((2 * c + np.arange(2, dtype=np.uint32)[None, :]) * 128
                      + np.arange(128, dtype=np.uint32)[:, None]).astype(np.uint32),
            c_ptrash=(32768.0 + 128.0 * np.arange(32, dtype=np.float32)[None, :]
                      + p_).astype(np.float32),
            c_qgrid=np.tile((128.0 * np.arange(1, 17, dtype=np.float32)).reshape(1, 16), (128, 1)),
            c_qgrid2=np.tile(np.concatenate(
                [128.0 * np.arange(17, 32, dtype=np.float32), [1e9]]).astype(
                    np.float32).reshape(1, 16), (128, 1)),
        )
        for b in range(c):
            for e in range(16):
                m["c_offmask"][16 * b + e, e] = 1.0
        m.update(consts)
        maps.append(m)
    res = run_bass_kernel_spmd(nc, maps, core_ids=list(range(8)))
    out = np.stack([np.asarray(res.results[c]["outTOK"]) for c in range(8)], axis=0)
    return out.astype(np.float32)
